# version 07
# baseline (speedup 1.0000x reference)
import contextlib
import math
from collections import defaultdict

import numpy as np
import ml_dtypes

import concourse.bass as bass
import concourse.mybir as mybir

F32 = mybir.dt.float32
BF16 = mybir.dt.bfloat16
AF = mybir.ActivationFunctionType
ALU = mybir.AluOpType
AX = mybir.AxisListType.X

T = 1024
D = 2048
DI = 4096
DXB = 1024
HD = 64
DS = 64
NH = 64
NKV = 16
INTER = 8192
HID = 5632
HID0 = 5461
NH1 = 16
HD1 = 128
BLK = 256
NBLK = T // BLK
EPS0 = 1e-5
EPS1 = 1e-6
Q = 128
NCH = T // Q
NCORES = 8

DINP = 10304


def _get_slopes(n):
    def p2(n):
        start = 2 ** (-(2 ** (-(math.log2(n) - 3))))
        return [start * start**i for i in range(n)]
    if math.log2(n).is_integer():
        return p2(n)
    c = 2 ** math.floor(math.log2(n))
    return p2(c) + _get_slopes(2 * c)[0::2][: n - c]


SLOPES = np.asarray(_get_slopes(NH1), dtype=np.float32)
BDECAY = np.exp(-SLOPES * BLK).astype(np.float64)
CORE_DECAY = np.exp(-SLOPES.astype(np.float64) * T)


def host_consts():
    c = {}
    c["eye"] = np.eye(128, dtype=np.float32)
    c["eyeb"] = np.eye(128, dtype=ml_dtypes.bfloat16)
    c["onescol"] = np.ones((1, 128), dtype=np.float32)
    c["ones128"] = np.ones((128, 1), dtype=np.float32)
    s, t = np.meshgrid(np.arange(128), np.arange(128), indexing="ij")
    c["causal"] = np.where(t >= s, 0.0, -1e30).astype(np.float32)
    c["causT"] = np.ascontiguousarray(c["causal"].T)
    sel = np.zeros((64, 64 * 128), dtype=np.float32)
    for h in range(64):
        sel[h, h * 128:(h + 1) * 128] = 1.0
    c["sel"] = sel
    sel2 = np.zeros((64, 32 * 128), dtype=np.float32)
    for pc in range(32):
        sel2[2 * pc, pc * 128:pc * 128 + 64] = 1.0
        sel2[2 * pc + 1, pc * 128 + 64:(pc + 1) * 128] = 1.0
    c["sel2"] = sel2
    arr = np.arange(BLK, dtype=np.float64) + 1.0
    qdec = np.zeros((NH1 * HD1, BLK), dtype=np.float32)
    kdec = np.zeros((NH1 * HD1, BLK), dtype=np.float32)
    ddecT = np.zeros((NH1 * BLK, BLK), dtype=np.float32)
    for h in range(NH1):
        sl = float(SLOPES[h])
        qdec[h * HD1:(h + 1) * HD1, :] = np.exp(-sl * arr)[None, :]
        kdec[h * HD1:(h + 1) * HD1, :] = np.exp(-sl * (BLK - arr))[None, :]
        idx = arr[None, :] - arr[:, None]
        ddecT[h * BLK:(h + 1) * BLK, :] = np.where(idx >= 0, np.exp(-sl * idx), 0.0)
    c["qdec"] = qdec
    c["kdec"] = kdec
    c["ddecT"] = ddecT
    return c


def host_prep(inputs):
    f = np.ascontiguousarray
    sh = dict(host_consts())
    sh["wip"] = f(inputs["L0_attn_in_proj_w"].T)
    sh["wop"] = f(inputs["L0_attn_out_proj_w"].T)
    sh["wfc1"] = f(inputs["L0_attn_fc1_w"].T)
    sh["wfc2"] = f(inputs["L0_attn_fc2_w"].T)
    sh["wg"] = f(inputs["L0_ffn_w_gate"])
    sh["wu"] = f(inputs["L0_ffn_w_up"])
    sh["wd"] = f(inputs["L0_ffn_w_down"])
    sh["wqkv"] = f(inputs["L1_attn_Wqkv"])
    sh["wgt"] = f(inputs["L1_attn_Wg"])
    sh["wo1"] = f(inputs["L1_attn_Wout"])
    w1t = np.zeros((D, HID), dtype=np.float32)
    w1t[:, :HID0] = inputs["L1_ffn_w1"].T
    sh["w1t"] = w1t
    w3t = np.zeros((D, HID), dtype=np.float32)
    w3t[:, :HID0] = inputs["L1_ffn_w3"].T
    sh["w3t"] = w3t
    w2t = np.zeros((HID, D), dtype=np.float32)
    w2t[:HID0, :] = inputs["L1_ffn_w2"].T
    sh["w2t"] = w2t
    sh["ln2w"] = f(inputs["L0_attn_ln_w"].reshape(1, D))
    sh["ln2b"] = f(inputs["L0_attn_ln_b"].reshape(1, D))
    sh["dtb"] = f(inputs["L0_attn_dt_bias"].reshape(64, 1))
    sh["alog"] = f(inputs["L0_attn_A_log"].reshape(64, 1))
    j = np.arange(DI).reshape(32, 128).T
    sh["dpcols"] = f(inputs["L0_attn_Dp"][j // HD])
    sh["nwcols"] = f(inputs["L0_attn_norm_w"].reshape(32, 128).T)
    sh["dpc64"] = np.ascontiguousarray(np.broadcast_to(
        np.asarray(inputs["L0_attn_Dp"], np.float32)[None, :], (64, 64)))
    sel64 = np.zeros((64, 64 * 64), dtype=np.float32)
    for h in range(64):
        sel64[h, h * 64:(h + 1) * 64] = 1.0
    sh["sel64"] = sel64
    sh["nw64"] = f(inputs["L0_attn_norm_w"].reshape(64, 64).T)
    sh["nw1cols"] = f(inputs["L1_attn_norm_w"].reshape(16, 128).T)
    sh["fc1bc"] = f(inputs["L0_attn_fc1_b"].reshape(64, 128).T)
    sh["fc2bc"] = f(inputs["L0_attn_fc2_b"].reshape(16, 128).T)
    x = np.asarray(inputs["L0_attn_hidden_states"])[0]
    per_core = []
    for i in range(NCORES):
        em = (np.arange(8) < i).astype(np.float32)
        l1w = np.ones((128, NH1 * 8), dtype=np.float32)
        for h in range(NH1):
            for k in range(8):
                if k < i:
                    l1w[:, h * 8 + k] = CORE_DECAY[h]
        per_core.append({
            "xs": f(x[i * T:(i + 1) * T]),
            "emask64": np.ascontiguousarray(np.broadcast_to(em[None, :], (64, 8))),
            "emask128": np.ascontiguousarray(np.broadcast_to(em[None, :], (128, 8))),
            "l1w": l1w,
        })
    return sh, per_core



class Ctx:
    def __init__(self, nc, io):
        self.nc = nc
        self.io = io
        self._sems = {}
        self.cnt = defaultdict(int)
        self.es = contextlib.ExitStack()

    def sem(self, name):
        if name not in self._sems:
            self._sems[name] = self.es.enter_context(self.nc.semaphore(name))
        return self._sems[name]

    def close(self):
        self.es.close()


SEMS = ["wld", "ald", "mm", "ev", "evs", "sto", "wf0", "wf1", "aux", "aux2", "cc"]



def linear(ctx, cfg):
    nc = ctx.nc
    K, N, Tn = cfg["K"], cfg["N"], cfg["T"]
    tw = cfg.get("tw", 512)
    Kc = K // 128
    nw = cfg.get("nw", 128)
    jobs = cfg.get("n0s") or [(n0, min(nw, N - n0)) for n0 in range(0, N, nw)]
    tis = [(t0, min(tw, Tn - t0)) for t0 in range(0, Tn, tw)]
    ntis = len(tis)
    items = [(ji, ti) for ji in range(len(jobs)) for ti in range(ntis)]
    psums = cfg["psums"]
    P = len(psums)
    obufs = cfg.get("obufs")
    O = len(obufs) if obufs else 0
    sbufs = cfg.get("sbufs")
    S = len(sbufs) if sbufs else 0
    store = cfg.get("out_ap") is not None
    two_stage = cfg.get("evac_s") is not None and cfg.get("evac_v") is not None

    mm, ev, evs = (ctx.sem(s) for s in ["mm", "ev", "evs"])
    NW = len(cfg["wbufs"])
    wld = [ctx.sem(f"wld{i}") for i in range(NW)]
    stos = [ctx.sem(f"sto{r}") for r in range(O)] if store else []
    mm0, ev0, evs0 = (ctx.cnt[s] for s in ["mm", "ev", "evs"])
    wld0 = [ctx.cnt[f"wld{i}"] for i in range(NW)]
    sto0 = [ctx.cnt[f"sto{r}"] for r in range(O)] if store else []

    KTILE_PER_DMA = 8
    ndma_job = (Kc + KTILE_PER_DMA - 1) // KTILE_PER_DMA
    wsrc = cfg["wt"].rearrange("(a p) n -> p a n", p=128)

    W = len(cfg["wbufs"])
    with nc.Block() as blk:
        @blk.gpsimd
        def _(g):
            for ji, (n0, nwj) in enumerate(jobs):
                par = ji % W
                if ji >= W:
                    g.wait_ge(ev, ev0 + (ji - W + 1) * ntis)
                wb = cfg["wbufs"][par]
                for di in range(ndma_job):
                    k0 = di * KTILE_PER_DMA
                    k1 = min(Kc, k0 + KTILE_PER_DMA)
                    dst = wb[:, k0 * nwj:k1 * nwj].rearrange("p (a n) -> p a n", n=nwj)
                    g.dma_start(dst, wsrc[:, k0:k1, n0:n0 + nwj]).then_inc(wld[par], 16)


        @blk.tensor
        def _(te):
            for idx, (ji, ti) in enumerate(items):
                n0, nwj = jobs[ji]
                t0, twj = tis[ti]
                wb = cfg["wbufs"][ji % W]
                if ti == 0:
                    te.wait_ge(wld[ji % W], wld0[ji % W] + 16 * ndma_job * (ji // W + 1))
                ps = psums[idx % P]
                if idx >= P:
                    te.wait_ge(ev, ev0 + idx - P + 1)
                ins = None
                for kc in range(Kc):
                    ins = te.matmul(
                        ps[0:nwj, 0:twj],
                        wb[:, kc * nwj:kc * nwj + nwj],
                        cfg["x_tile"](kc)[:, t0:t0 + twj],
                        start=(kc == 0), stop=(kc == Kc - 1))
                ins.then_inc(mm, 1)

        if cfg.get("evac_s") is not None:
            @blk.scalar
            def _(se):
                for idx, (ji, ti) in enumerate(items):
                    ps = psums[idx % P]
                    se.wait_ge(mm, mm0 + idx + 1)
                    ot = obufs[idx % O] if obufs else None
                    stile = sbufs[idx % S] if sbufs else None
                    if two_stage:
                        if idx >= S and S:
                            se.wait_ge(ev, ev0 + idx - S + 1)
                    elif store and idx >= O:
                        se.wait_ge(stos[idx % O], sto0[idx % O] + 16 * (idx // O))
                    ins = cfg["evac_s"](se, ji, ti, ps, ot, stile)
                    ins.then_inc(evs if two_stage else ev, 1)

        if cfg.get("evac_v") is not None:
            @blk.vector
            def _(ve):
                for idx, (ji, ti) in enumerate(items):
                    ps = psums[idx % P]
                    if two_stage:
                        ve.wait_ge(evs, evs0 + idx + 1)
                    else:
                        ve.wait_ge(mm, mm0 + idx + 1)
                    ot = obufs[idx % O] if obufs else None
                    stile = sbufs[idx % S] if sbufs else None
                    if store and idx >= O:
                        ve.wait_ge(stos[idx % O], sto0[idx % O] + 16 * (idx // O))
                    ins = cfg["evac_v"](ve, ji, ti, ps, ot, stile)
                    ins.then_inc(ev, 1)

        if store:
            @blk.sync
            def _(sy):
                for idx, (ji, ti) in enumerate(items):
                    sy.wait_ge(ev, ev0 + idx + 1)
                    ap = cfg["out_ap"](ji, ti)
                    sy.dma_start(ap, obufs[idx % O][0:jobs[ji][1], 0:tis[ti][1]]).then_inc(stos[idx % O], 16)
                for r in range(O):
                    n_r = sum(1 for i2 in range(len(items)) if i2 % O == r)
                    sy.wait_ge(stos[r], sto0[r] + 16 * n_r)

    for i in range(W):
        nji = sum(1 for j2 in range(len(jobs)) if j2 % W == i)
        ctx.cnt[f"wld{i}"] = wld0[i] + 16 * ndma_job * nji
    ctx.cnt["mm"] = mm0 + len(items)
    ctx.cnt["ev"] = ev0 + len(items)
    if two_stage:
        ctx.cnt["evs"] = evs0 + len(items)
    if store:
        for r in range(O):
            ctx.cnt[f"sto{r}"] = sto0[r] + 16 * sum(1 for i2 in range(len(items)) if i2 % O == r)


def load_fm_tiles(ctx, tm_ap, ncol, Tn, bufs, sem="ald"):
    nc = ctx.nc
    s0 = ctx.cnt[sem]
    n = ncol // 128
    with nc.Block() as blk:
        @blk.sync
        def _(sy):
            for kc in range(n):
                sy.dma_start(bufs[kc][:, 0:Tn], tm_ap[0:Tn, kc * 128:(kc + 1) * 128],
                             transpose=True).then_inc(ctx.sem(sem), 16)
            sy.wait_ge(ctx.sem(sem), s0 + 16 * n)
    ctx.cnt[sem] = s0 + 16 * n


def bcast_rows(ctx, blk_engines, row_ap, out_sb, psum, width):
    raise NotImplementedError




CONSTS = [1.0 / 2048, EPS0, -0.5, 1.0 / 4096, EPS1, 1.0 / HID0 / 2048 / 2, 0.0, 1.0]

def emit_setup(ctx):
    nc = ctx.nc
    cp = ctx.es.enter_context(nc.sbuf_tensor("cpool", [128, len(CONSTS)], F32))
    with nc.Block() as blk:
        @blk.vector
        def _(v):
            for j, val in enumerate(CONSTS):
                v.memset(cp[:, j:j + 1], float(val))
    for j, val in enumerate(CONSTS):
        nc.const_aps.register(F32, float(val), cp[:, j:j + 1])
    ctx.cpool = cp
    nc2 = nc
    ctx.eye = ctx.es.enter_context(nc2.sbuf_tensor("c_eye", [128, 128], F32))
    ctx.eyeb = ctx.es.enter_context(nc2.sbuf_tensor("c_eyeb", [128, 128], BF16))
    ctx.onec = ctx.es.enter_context(nc2.sbuf_tensor("c_onec", [1, 128], F32))
    ctx.ones128 = ctx.es.enter_context(nc2.sbuf_tensor("c_ones128", [128, 1], F32))
    s0 = ctx.cnt["ald"]
    with nc.Block() as blk:
        @blk.sync
        def _(sy):
            sy.dma_start(ctx.eye[:, :], ctx.io["eye"][:, :]).then_inc(ctx.sem("ald"), 16)
            sy.dma_start(ctx.eyeb[:, :], ctx.io["eyeb"][:, :]).then_inc(ctx.sem("ald"), 16)
            sy.dma_start(ctx.onec[0:1, :], ctx.io["onescol"][0:1, :]).then_inc(ctx.sem("ald"), 16)
            sy.dma_start(ctx.ones128[:, 0:1], ctx.io["ones128"][:, 0:1]).then_inc(ctx.sem("ald"), 16)
            sy.wait_ge(ctx.sem("ald"), s0 + 64)
    ctx.cnt["ald"] = s0 + 64



def emit_p1(ctx):
    nc, io = ctx.nc, ctx.io
    es = contextlib.ExitStack()
    xt = [es.enter_context(nc.sbuf_tensor(f"p1xt{i}", [128, D], F32)) for i in range(2)]
    xc = es.enter_context(nc.sbuf_tensor("p1xc", [128, D], F32))
    sq = es.enter_context(nc.sbuf_tensor("p1sq", [128, D], F32))
    xn = es.enter_context(nc.sbuf_tensor("p1xn", [128, D], F32))
    xnb = es.enter_context(nc.sbuf_tensor("p1xnb", [128, D], BF16))
    ht = es.enter_context(nc.sbuf_tensor("p1ht", [128, D], BF16))
    st_ = es.enter_context(nc.sbuf_tensor("p1st", [128, 12], F32))
    wbc = es.enter_context(nc.sbuf_tensor("p1wbc", [128, D], F32))
    bbc = es.enter_context(nc.sbuf_tensor("p1bbc", [128, D], F32))
    row = es.enter_context(nc.sbuf_tensor("p1row", [1, D], F32))
    row2 = es.enter_context(nc.sbuf_tensor("p1row2", [1, D], F32))
    onec = es.enter_context(nc.sbuf_tensor("p1one", [1, 128], F32))
    ot = [es.enter_context(nc.sbuf_tensor(f"p1ot{i}", [128, 128], F32)) for i in range(4)]
    pt = [es.enter_context(nc.psum_tensor(f"p1pt{i}", [128, 128], F32)) for i in range(4)]
    pb = es.enter_context(nc.psum_tensor("p1pb", [128, 512], F32))

    ald, aux, evs, mm, ev, sto, aux2 = (ctx.sem(s) for s in
        ["ald", "aux", "evs", "mm", "ev", "sto", "aux2"])
    b = {s: ctx.cnt[s] for s in ["ald", "aux", "evs", "mm", "ev", "sto", "aux2"]}
    NTI = T // 128

    with nc.Block() as blk:
        @blk.sync
        def _(sy):
            sy.dma_start(row[0:1, :], io["ln2w"][0:1, :]).then_inc(ald, 16)
            sy.dma_start(row2[0:1, :], io["ln2b"][0:1, :]).then_inc(ald, 16)
            sy.dma_start(onec[0:1, :], io["onescol"][0:1, :]).then_inc(ald, 16)
        @blk.tensor
        def _(te):
            te.wait_ge(ald, b["ald"] + 48)
            for j in range(4):
                if j > 0:
                    te.wait_ge(ev, b["ev"] + 2 * j)
                te.matmul(pb[:, :], onec[0:1, :], row[0:1, j * 512:(j + 1) * 512],
                          start=True, stop=True).then_inc(mm, 1)
                te.wait_ge(ev, b["ev"] + 2 * j + 1)
                te.matmul(pb[:, :], onec[0:1, :], row2[0:1, j * 512:(j + 1) * 512],
                          start=True, stop=True).then_inc(mm, 1)
        @blk.vector
        def _(v):
            for j in range(4):
                v.wait_ge(mm, b["mm"] + 2 * j + 1)
                v.tensor_copy(wbc[:, j * 512:(j + 1) * 512], pb[:, :]).then_inc(ev, 1)
                v.wait_ge(mm, b["mm"] + 2 * j + 2)
                v.tensor_copy(bbc[:, j * 512:(j + 1) * 512], pb[:, :]).then_inc(ev, 1)
    ctx.cnt["ald"] += 48
    ctx.cnt["mm"] += 8
    ctx.cnt["ev"] += 8
    b = {s: ctx.cnt[s] for s in ["ald", "aux", "evs", "mm", "ev", "sto", "aux2"]}

    al = [ctx.sem("ald0"), ctx.sem("ald1")]
    stoa, stob = ctx.sem("stoa"), ctx.sem("stob")
    stot = [ctx.sem(f"sto{r}") for r in range(4)]
    b2 = {s: ctx.cnt[s] for s in
          ["ald0", "ald1", "stoa", "stob", "sto0", "sto1", "sto2", "sto3",
           "aux", "aux2", "evs", "mm", "ev"]}
    A, E = b2["aux"], b2["evs"]

    def altgt(i):
        return b2[f"ald{i % 2}"] + 16 * (i // 2 + 1)

    def stotgt(gi):
        return b2[f"sto{gi % 4}"] + 16 * (gi // 4 + 1)

    with nc.Block() as blk:
        @blk.sync
        def _(sy):
            sy.dma_start(xt[0][:, :], io["xs"][0:128, :]).then_inc(al[0], 16)
            if NTI > 1:
                sy.dma_start(xt[1][:, :], io["xs"][128:256, :]).then_inc(al[1], 16)
            for i in range(NTI):
                sy.wait_ge(evs, E + 4 * i + 4)
                sy.dma_start(io["xntm"][i * 128:(i + 1) * 128, :], xnb[:, :]).then_inc(stoa, 16)
                sy.wait_ge(aux2, b2["aux2"] + i + 1)
                sy.dma_start(io["htm"][i * 128:(i + 1) * 128, :], ht[:, :]).then_inc(stob, 16)
                for j in range(32):
                    gi = 32 * i + j
                    sy.wait_ge(ev, b2["ev"] + gi + 1)
                    kc = j % 16
                    dst = io["xfm"] if j < 16 else io["xnfm"]
                    sy.dma_start(dst[kc * 128:(kc + 1) * 128, i * 128:(i + 1) * 128],
                                 ot[gi % 4][:, :]).then_inc(stot[gi % 4], 16)
                if i + 2 < NTI:
                    sy.wait_ge(aux, A + 8 * i + 3)
                    sy.wait_ge(mm, b2["mm"] + 32 * i + 16)
                    sy.dma_start(xt[i % 2][:, :],
                                 io["xs"][(i + 2) * 128:(i + 3) * 128, :]).then_inc(al[i % 2], 16)
            sy.wait_ge(stoa, b2["stoa"] + 16 * NTI)
            sy.wait_ge(stob, b2["stob"] + 16 * NTI)
            for r in range(4):
                sy.wait_ge(stot[r], b2[f"sto{r}"] + 16 * 8 * NTI)

        @blk.vector
        def _(v):
            for i in range(NTI):
                x_ = xt[i % 2]
                v.wait_ge(al[i % 2], altgt(i))
                if i > 0:
                    v.wait_ge(evs, E + 4 * i - 1)
                v.reduce_sum(st_[:, 0:1], x_[:, :], axis=AX).then_inc(aux, 1)
                v.wait_ge(aux, A + 8 * i + 1)
                v.tensor_scalar(st_[:, 1:2], st_[:, 0:1], 1.0 / D, None,
                                ALU.mult).then_inc(aux, 1)
                v.wait_ge(aux, A + 8 * i + 2)
                v.tensor_scalar(xc[:, :], x_[:, :], st_[:, 1:2], None,
                                ALU.subtract).then_inc(aux, 1)
                v.wait_ge(evs, E + 4 * i + 1)
                v.reduce_sum(st_[:, 2:3], sq[:, :], axis=AX).then_inc(aux, 1)
                v.wait_ge(evs, E + 4 * i + 2)
                if i > 0:
                    v.wait_ge(mm, b2["mm"] + 32 * i)
                v.tensor_scalar(xn[:, :], xc[:, :], st_[:, 4:5], None, ALU.mult)
                v.reduce_sum(st_[:, 5:6], xn[:, :], axis=AX).then_inc(aux, 1)
                v.wait_ge(aux, A + 8 * i + 5)
                v.tensor_scalar(st_[:, 6:7], st_[:, 5:6], 1.0 / D, None,
                                ALU.mult).then_inc(aux, 1)
                v.wait_ge(aux, A + 8 * i + 6)
                v.tensor_scalar(xc[:, :], xn[:, :], st_[:, 6:7], None,
                                ALU.subtract).then_inc(aux, 1)
                v.wait_ge(evs, E + 4 * i + 3)
                v.reduce_sum(st_[:, 7:8], sq[:, :], axis=AX).then_inc(aux, 1)
                v.wait_ge(evs, E + 4 * i + 4)
                if i > 0:
                    v.wait_ge(stob, b2["stob"] + 16 * i)
                v.scalar_tensor_tensor(sq[:, :], xc[:, :], st_[:, 9:10], wbc[:, :],
                                       ALU.mult, ALU.mult)
                v.tensor_tensor(ht[:, :], sq[:, :], bbc[:, :], ALU.add).then_inc(aux2, 1)
                for j in range(32):
                    gi = 32 * i + j
                    v.wait_ge(mm, b2["mm"] + gi + 1)
                    if gi >= 4:
                        v.wait_ge(stot[gi % 4], stotgt(gi - 4))
                    v.tensor_copy(ot[gi % 4][:, :], pt[gi % 4][:, :]).then_inc(ev, 1)

        @blk.scalar
        def _(s):
            for i in range(NTI):
                s.wait_ge(aux, A + 8 * i + 3)
                s.activation(sq[:, :], xc[:, :], AF.Square).then_inc(evs, 1)
                s.wait_ge(aux, A + 8 * i + 4)
                s.activation(st_[:, 3:4], st_[:, 2:3], AF.Ln, scale=1.0 / D, bias=EPS0)
                s.activation(st_[:, 4:5], st_[:, 3:4], AF.Exp, scale=-0.5).then_inc(evs, 1)
                s.wait_ge(aux, A + 8 * i + 7)
                s.activation(sq[:, :], xc[:, :], AF.Square).then_inc(evs, 1)
                s.wait_ge(aux, A + 8 * i + 8)
                s.activation(st_[:, 8:9], st_[:, 7:8], AF.Ln, scale=1.0 / D, bias=EPS0)
                s.activation(st_[:, 9:10], st_[:, 8:9], AF.Exp, scale=-0.5)
                if i > 0:
                    s.wait_ge(stoa, b2["stoa"] + 16 * i)
                s.copy(xnb[:, :], xn[:, :]).then_inc(evs, 1)

        @blk.tensor
        def _(te):
            eye = ctx.eye
            for i in range(NTI):
                te.wait_ge(al[i % 2], altgt(i))
                for j in range(32):
                    gi = 32 * i + j
                    if j == 16:
                        te.wait_ge(aux, A + 8 * i + 5)
                    if gi >= 4:
                        te.wait_ge(ev, b2["ev"] + gi - 3)
                    src = xt[i % 2] if j < 16 else xn
                    kc = j % 16
                    te.transpose(pt[gi % 4][:, :], src[:, kc * 128:(kc + 1) * 128],
                                 eye[:, :]).then_inc(mm, 1)

    ctx.cnt["ald0"] = b2["ald0"] + 16 * ((NTI + 1) // 2)
    ctx.cnt["ald1"] = b2["ald1"] + 16 * (NTI // 2)
    ctx.cnt["aux"] = A + 8 * NTI
    ctx.cnt["aux2"] = b2["aux2"] + NTI
    ctx.cnt["evs"] = E + 4 * NTI
    ctx.cnt["mm"] = b2["mm"] + 32 * NTI
    ctx.cnt["ev"] = b2["ev"] + 32 * NTI
    ctx.cnt["stoa"] = b2["stoa"] + 16 * NTI
    ctx.cnt["stob"] = b2["stob"] + 16 * NTI
    for r in range(4):
        ctx.cnt[f"sto{r}"] = b2[f"sto{r}"] + 16 * 8 * NTI
    es.close()


def emit_p2(ctx):
    nc, io = ctx.nc, ctx.io
    es = contextlib.ExitStack()
    hfm = [es.enter_context(nc.sbuf_tensor(f"p2h{k}", [128, T], BF16)) for k in range(16)]
    gu = es.enter_context(nc.sbuf_tensor("p2gu", [128, 64 * 512], BF16))
    wb1 = [es.enter_context(nc.sbuf_tensor(f"p2w1{i}", [128, 16 * 128], BF16)) for i in range(3)]
    wb2 = [es.enter_context(nc.sbuf_tensor(f"p2w2{i}", [128, 64 * 128], BF16)) for i in range(3)]
    b1c = es.enter_context(nc.sbuf_tensor("p2b1", [128, 64], F32))
    b2c = es.enter_context(nc.sbuf_tensor("p2b2", [128, 16], F32))
    ob = [es.enter_context(nc.sbuf_tensor(f"p2ob{i}", [128, 512], F32)) for i in range(4)]
    ps = [es.enter_context(nc.psum_tensor(f"p2ps{i}", [128, 512], F32)) for i in range(4)]

    s0 = ctx.cnt["ald"]
    with nc.Block() as blk:
        @blk.sync
        def _(sy):
            sy.dma_start(b1c[:, :], io["fc1bc"][:, :]).then_inc(ctx.sem("ald"), 16)
            sy.dma_start(b2c[:, :], io["fc2bc"][:, :]).then_inc(ctx.sem("ald"), 16)
            sy.wait_ge(ctx.sem("ald"), s0 + 32)
    ctx.cnt["ald"] = s0 + 32
    load_fm_tiles(ctx, io["htm"], D, T, hfm)

    for th in range(2):
        t0 = th * 512

        def ev1(se, ji, ti, p, ot, stile):
            return se.activation(gu[:, ji * 512:(ji + 1) * 512], p[:, :],
                                 AF.Gelu_apprx_tanh, bias=b1c[:, ji:ji + 1])

        linear(ctx, dict(
            name="fc1", wt=io["wfc1"], K=D, N=INTER, T=512, tw=512,
            x_tile=lambda kc, t0=t0: hfm[kc][:, t0:t0 + 512],
            psums=ps, wbufs=wb1, evac_s=ev1, evac_v=None, out_ap=None))

        def ev2(se, ji, ti, p, ot, stile):
            return se.activation(ot[:, :], p[:, :], AF.Identity, bias=b2c[:, ji:ji + 1])

        linear(ctx, dict(
            name="fc2", wt=io["wfc2"], K=INTER, N=D, T=512, tw=512,
            x_tile=lambda kc: gu[:, kc * 512:(kc + 1) * 512],
            psums=ps, wbufs=wb2, obufs=ob, evac_s=ev2, evac_v=None,
            out_ap=lambda ji, ti, t0=t0: io["mlp"][ji * 128:(ji + 1) * 128, t0:t0 + 512]))
    es.close()



def emit_p3(ctx):
    nc, io = ctx.nc, ctx.io
    es = contextlib.ExitStack()
    hfm = [es.enter_context(nc.sbuf_tensor(f"p3h{k}", [128, T], BF16)) for k in range(16)]
    wb = [es.enter_context(nc.sbuf_tensor(f"p3w{i}", [128, 16 * 128], BF16)) for i in range(3)]
    ob = [es.enter_context(nc.sbuf_tensor(f"p3ob{i}", [128, 512], BF16)) for i in range(4)]
    obf = [es.enter_context(nc.sbuf_tensor(f"p3of{i}", [64, 512], F32)) for i in range(2)]
    ps = [es.enter_context(nc.psum_tensor(f"p3ps{i}", [128, 512], F32)) for i in range(4)]
    load_fm_tiles(ctx, io["htm"], D, T, hfm)

    def out_ap(ji, ti):
        t0 = ti * 512
        if ji < 32:
            return io["zf"][ji * 128:(ji + 1) * 128, t0:t0 + 512]
        if ji < 40:
            return io["xmf"][(ji - 32) * 128:(ji - 31) * 128, t0:t0 + 512]
        if ji < 48:
            return io["bmf"][(ji - 40) * 128:(ji - 39) * 128, t0:t0 + 512]
        return io["cmf"][(ji - 48) * 128:(ji - 47) * 128, t0:t0 + 512]

    def ev(ve, ji, ti, p, ot, stile):
        return ve.tensor_copy(ot[:, :], p[:, :])

    linear(ctx, dict(
        name="inproj", wt=io["wip"], K=D, N=80 * 128, T=T, tw=512,
        x_tile=lambda kc: hfm[kc],
        psums=ps, wbufs=wb, obufs=ob, evac_v=ev, out_ap=out_ap))

    def evdt(ve, ji, ti, p, ot, stile):
        return ve.tensor_copy(ot[0:64, :], p[0:64, :])

    linear(ctx, dict(
        name="dt", wt=io["wip"], K=D, N=64, T=T, tw=512,
        n0s=[(10240, 64)],
        x_tile=lambda kc: hfm[kc],
        psums=ps, wbufs=wb, obufs=obf, evac_v=evdt,
        out_ap=lambda ji, ti: io["dtr"][0:64, ti * 512:(ti + 1) * 512]))
    es.close()



def emit_p4(ctx):
    nc, io = ctx.nc, ctx.io
    es = contextlib.ExitStack()
    sb = lambda n, s, d: es.enter_context(nc.sbuf_tensor(n, s, d))
    xfc = [sb(f"p4xf{i}", [64, 16 * 128], BF16) for i in range(2)]
    bfc = [sb(f"p4bf{i}", [64, 16 * 128], BF16) for i in range(2)]
    xtm = [sb(f"p4xt{i}", [128, 1024], BF16) for i in range(8)]
    btm = [sb(f"p4bt{i}", [128, 1024], BF16) for i in range(8)]
    cc = [sb(f"p4cc{i}", [64, 64 * 128], BF16) for i in range(2)]
    clb = sb("p4cl", [64, 64 * 128], BF16)
    ylb = sb("p4yl", [64, 64 * 128], BF16)
    Sf = sb("p4S", [64, 4096], F32)
    Sb = sb("p4Sb", [64, 4096], BF16)
    dtr_ = sb("p4dtr", [64, T], F32)
    dts = sb("p4dts", [64, T], F32)
    tmpa = sb("p4tma", [64, T], F32)
    tmpb = sb("p4tmb", [64, T], F32)
    lndt = sb("p4ln", [64, T], F32)
    logdA = sb("p4da", [64, T], F32)
    Lg = sb("p4lg", [64, T], F32)
    Rt = sb("p4r", [64, T], F32)
    eloc = dtr_
    eglo = logdA
    wsc = sb("p4w", [64, T], F32)
    Ecol = sb("p4E", [64, 16], F32)
    dE = sb("p4dE", [64, 8], F32)
    Dex = sb("p4Dx", [64, 8], F32)
    Dflat = sb("p4Df", [1, 512], F32)
    Dbc = sb("p4Dbc", [64, 64], F32)
    LgT = sb("p4lgt", [128, 8 * 64], F32)
    RT = sb("p4rt", [128, 8 * 64], F32)
    wT = sb("p4wt", [128, 8 * 64], F32)
    dtb = sb("p4dtb", [64, 1], F32)
    alog = sb("p4al", [64, 1], F32)
    Acol = sb("p4A", [64, 1], F32)
    dpc = sb("p4dp", [64, 64], F32)
    sel64 = sb("p4s64", [64, 64 * 64], F32)
    caus = sb("p4ca", [128, 128], F32)
    zt = sb("p4z", [128, 128], F32)
    bct = [sb(f"p4bc{i}", [128, 128], F32) for i in range(2)]
    msk = [sb(f"p4mk{i}", [128, 128], F32) for i in range(2)]
    Mp = [sb(f"p4Mp{i}", [128, 128], F32) for i in range(2)]
    Gm = [sb(f"p4Gm{i}", [128, 128], BF16) for i in range(2)]
    xsc = [sb(f"p4xs{i}", [64, 128], F32) for i in range(2)]
    bw = [sb(f"p4bw{i}", [128, 64], BF16) for i in range(2)]
    st1 = sb("p4s1", [64, 64], F32)
    psA = [es.enter_context(nc.psum_tensor(f"p4pA{i}", [128, 128], F32)) for i in range(2)]
    psG = [es.enter_context(nc.psum_tensor(f"p4pG{i}", [128, 128], F32)) for i in range(2)]
    psY = [es.enter_context(nc.psum_tensor(f"p4pY{i}", [64, 128], F32)) for i in range(2)]
    psS = [es.enter_context(nc.psum_tensor(f"p4pS{i}", [64, 64], F32)) for i in range(2)]
    dram_tmp = nc.dram_tensor("p4dram", [64, 8], F32).ap()

    a0 = ctx.cnt["ald"]
    with nc.Block() as blk:
        @blk.sync
        def _(sy):
            n = 0
            for i in range(8):
                sy.dma_start(xtm[i][:, :], io["xmf"][0:1024, i * 128:(i + 1) * 128], transpose=True).then_inc(ctx.sem("ald"), 16); n += 1
                sy.dma_start(btm[i][:, :], io["bmf"][0:1024, i * 128:(i + 1) * 128], transpose=True).then_inc(ctx.sem("ald"), 16); n += 1
            for ap, name in [(dtr_, "dtr"), (dtb, "dtb"), (alog, "alog"), (dpc, "dpc64"),
                             (caus, "causal"), (sel64, "sel64")]:
                sy.dma_start(ap[:, :], io[name][:, :]).then_inc(ctx.sem("ald"), 16); n += 1
            sy.wait_ge(ctx.sem("ald"), a0 + 16 * n)
            ctx.cnt["ald"] = a0 + 16 * n
        @blk.vector
        def _(v):
            v.memset(zt[:, :], 0.0)
            v.memset(Ecol[:, :], 0.0)
            v.memset(Sf[:, :], 0.0)
            v.memset(Sb[:, :], 0.0)

    steps = []
    def S(eng, fn):
        steps.append((eng, fn))
    S("s", lambda e: e.activation(dts[:, :], dtr_[:, :], AF.Identity, bias=dtb[:, 0:1]))
    S("s", lambda e: e.activation(tmpa[:, :], dts[:, :], AF.Abs))
    S("s", lambda e: e.activation(tmpb[:, :], tmpa[:, :], AF.Exp, scale=-1.0))
    S("s", lambda e: e.activation(tmpa[:, :], tmpb[:, :], AF.Ln, bias=1.0))
    S("v", lambda e: e.tensor_relu(tmpb[:, :], dts[:, :]))
    S("v", lambda e: e.tensor_add(dts[:, :], tmpa[:, :], tmpb[:, :]))
    S("s", lambda e: e.activation(lndt[:, :], dts[:, :], AF.Ln))
    S("s", lambda e: e.activation(Acol[:, 0:1], alog[:, 0:1], AF.Exp))
    S("v", lambda e: e.tensor_scalar(Acol[:, 0:1], Acol[:, 0:1], -1.0, None, ALU.mult))
    S("v", lambda e: e.tensor_scalar(logdA[:, :], dts[:, :], Acol[:, 0:1], None, ALU.mult))
    S("v", lambda e: e.tensor_tensor_scan(Lg[:, :], logdA[:, :], logdA[:, :], 0.0, ALU.add, ALU.bypass))
    S("v", lambda e: e.tensor_copy(Ecol[:, 1:9], Lg[:, 127::128]))
    S("v", lambda e: e.tensor_sub(Rt[:, :], lndt[:, :], Lg[:, :]))
    for c in range(NCH):
        S("v", lambda e, c=c: e.tensor_scalar(tmpa[:, c * 128:(c + 1) * 128], Lg[:, c * 128:(c + 1) * 128], Ecol[:, c:c + 1], None, ALU.subtract))
        S("v", lambda e, c=c: e.tensor_scalar(tmpb[:, c * 128:(c + 1) * 128], Rt[:, c * 128:(c + 1) * 128], Ecol[:, c + 1:c + 2], None, ALU.add))
    S("s", lambda e: e.activation(eloc[:, :], tmpa[:, :], AF.Exp))
    S("s", lambda e: e.activation(eglo[:, :], Lg[:, :], AF.Exp))
    S("s", lambda e: e.activation(wsc[:, :], tmpb[:, :], AF.Exp))
    S("v", lambda e: e.tensor_sub(dE[:, :], Ecol[:, 1:9], Ecol[:, 0:8]))
    S("s", lambda e: e.activation(Dex[:, :], dE[:, :], AF.Exp))
    S("g", lambda e: e.dma_start(dram_tmp[:, :], Dex[:, :]))
    for c in range(NCH):
        S("g", lambda e, c=c: e.dma_start(Dflat[0:1, c * 64:(c + 1) * 64],
                                          dram_tmp[:, c:c + 1].rearrange("h c -> c h"),
                                          allow_slow_non_contiguous=True))
    for c in range(NCH):
        S("t", lambda e, c=c: e.transpose(psA[0][:, 0:64], Lg[:, c * 128:(c + 1) * 128], ctx.eye[0:64, 0:64]))
        S("v", lambda e, c=c: e.tensor_copy(LgT[:, c * 64:(c + 1) * 64], psA[0][:, 0:64]))
        S("t", lambda e, c=c: e.transpose(psA[0][:, 0:64], Rt[:, c * 128:(c + 1) * 128], ctx.eye[0:64, 0:64]))
        S("v", lambda e, c=c: e.tensor_copy(RT[:, c * 64:(c + 1) * 64], psA[0][:, 0:64]))
        S("t", lambda e, c=c: e.transpose(psA[0][:, 0:64], wsc[:, c * 128:(c + 1) * 128], ctx.eye[0:64, 0:64]))
        S("v", lambda e, c=c: e.tensor_copy(wT[:, c * 64:(c + 1) * 64], psA[0][:, 0:64]))

    aux = ctx.sem("aux")
    A0 = ctx.cnt["aux"]
    incs = [16 if en == "g" else 1 for en, _ in steps]
    tgts = [A0 + sum(incs[:i]) for i in range(len(steps))]
    with nc.Block() as blk:
        for eng_name, body in [("v", "vector"), ("s", "scalar"), ("t", "tensor"), ("g", "gpsimd")]:
            def mk(eng_name=eng_name):
                def emit(e):
                    for i, (en, fn) in enumerate(steps):
                        if en != eng_name:
                            continue
                        e.wait_ge(aux, tgts[i])
                        fn(e).then_inc(aux, incs[i])
                return emit
            getattr(blk, body)(mk())
    ctx.cnt["aux"] = A0 + sum(incs)

    mm, ev, evs, mmp, evp = (ctx.sem(s) for s in ["mm", "ev", "evs", "mmp", "evp"])
    al = [ctx.sem("ald0"), ctx.sem("ald1")]
    stot = [ctx.sem(f"sto{r}") for r in range(4)]
    b = {s: ctx.cnt[s] for s in ["mm", "ev", "evs", "mmp", "evp", "ald0", "ald1",
                                 "sto0", "sto1", "sto2", "sto3"]}
    M0, E0, V0, MP0, EP0 = b["mm"], b["evs"], b["ev"], b["mmp"], b["evp"]
    csrc = io["cmf"].rearrange("(a p) t -> p a t", p=64)
    xsrc = io["xmf"].rearrange("(a p) t -> p a t", p=64)
    bsrc = io["bmf"].rearrange("(a p) t -> p a t", p=64)

    def kvg(h):
        return h // 4

    def sfv(buf, h):
        return buf[:, h * 64:(h + 1) * 64]

    def hv(buf, h, w=128):
        return buf[:, h * 128:h * 128 + w]

    def cload(sy, c):
        for d in range(4):
            sy.dma_start(cc[c % 2][:, d * 16 * 128:(d + 1) * 16 * 128].rearrange("p (a t) -> p a t", t=128),
                         csrc[:, d * 16:(d + 1) * 16, c * 128:(c + 1) * 128]).then_inc(al[c % 2], 16)
        sy.dma_start(xfc[c % 2][:, :].rearrange("p (a t) -> p a t", t=128),
                     xsrc[:, :, c * 128:(c + 1) * 128]).then_inc(al[c % 2], 16)
        sy.dma_start(bfc[c % 2][:, :].rearrange("p (a t) -> p a t", t=128),
                     bsrc[:, :, c * 128:(c + 1) * 128]).then_inc(al[c % 2], 16)

    NPRE = 65

    with nc.Block() as blk:
        @blk.sync
        def _(sy):
            for c in range(min(2, NCH)):
                cload(sy, c)
            for c in range(NCH):
                for h in range(NH):
                    g = c * 64 + h
                    sy.wait_ge(ev, V0 + 4 * g + 3)
                    sy.dma_start(io["yloc"][h * 64:(h + 1) * 64, c * 128:(c + 1) * 128],
                                 hv(ylb, h)).then_inc(stot[h % 4], 16)
                if c + 2 < NCH:
                    sy.wait_ge(mm, M0 + 4 * 64 * (c + 1))
                    sy.wait_ge(evp, EP0 + NPRE * (c + 1))
                    cload(sy, c + 2)
            for r in range(4):
                sy.wait_ge(stot[r], b[f"sto{r}"] + 16 * 16 * 8)
            sy.wait_ge(ev, V0 + 4 * 64 * NCH)
            for h in range(NH):
                sy.dma_start(io["sfin"][h * 64:(h + 1) * 64, :], sfv(Sf, h)).then_inc(stot[0], 16)
            sy.dma_start(io["elast"][:, 0:1], Ecol[:, 8:9]).then_inc(stot[0], 16)
            sy.dma_start(io["eglob"][:, :], eglo[:, :]).then_inc(stot[0], 16)
            sy.wait_ge(stot[0], b["sto0"] + 16 * 16 * 8 + 16 * 66)

        @blk.tensor
        def _(te):
            nmm = M0
            for c in range(NCH):
                par = c % 2
                te.wait_ge(al[par], b[f"ald{par}"] + 16 * 6 * (c // 2 + 1))
                if c > 0:
                    te.wait_ge(ev, V0 + 4 * 64 * c)
                for h in range(NH):
                    if h >= 2:
                        te.wait_ge(evp, EP0 + NPRE * c + h - 1)
                    te.matmul(psY[h % 2][:, :], sel64[:, h * 64:(h + 1) * 64],
                              eloc[:, c * 128:(c + 1) * 128],
                              start=True, stop=True).then_inc(mmp, 1)
                te.wait_ge(evp, EP0 + NPRE * c + 64)
                te.matmul(psS[0][:, :], ctx.onec[0:1, 0:64], Dflat[0:1, c * 64:(c + 1) * 64],
                          start=True, stop=True).then_inc(mmp, 1)
                for h in range(NH):
                    g = c * 64 + h
                    if h == 0:
                        te.wait_ge(evp, EP0 + NPRE * (c + 1))
                    te.wait_ge(evs, E0 + 5 * g + 1)
                    if g >= 2:
                        te.wait_ge(ev, V0 + 4 * (g - 2) + 2)
                    te.transpose(psA[g % 2][:, :], bct[g % 2][:, :], ctx.eye[:, :]).then_inc(mm, 1)
                    te.matmul(psG[g % 2][:, :], bfc[par][:, kvg(h) * 128:(kvg(h) + 1) * 128],
                              hv(cc[par], h), start=True, stop=True).then_inc(mm, 1)
                    te.wait_ge(ev, V0 + 4 * g + 2)
                    te.matmul(psY[g % 2][:, :], xtm[c][:, kvg(h) * 64:(kvg(h) + 1) * 64],
                              Gm[g % 2][:, :], start=True, stop=False)
                    if c > 0:
                        te.wait_ge(evs, E0 + 5 * (g - 64) + 5)
                    te.matmul(psY[g % 2][:, :], sfv(Sb, h), hv(clb, h),
                              start=False, stop=True).then_inc(mm, 1)
                    te.wait_ge(evs, E0 + 5 * g + 4)
                    te.matmul(psS[g % 2][:, :], bw[g % 2][:, :],
                              xtm[c][:, kvg(h) * 64:(kvg(h) + 1) * 64],
                              start=True, stop=True).then_inc(mm, 1)

        @blk.vector
        def _(v):
            for c in range(NCH):
                if c > 0:
                    v.wait_ge(mm, M0 + 4 * 64 * c)
                for h in range(NH):
                    v.wait_ge(mmp, MP0 + NPRE * c + h + 1)
                    v.tensor_tensor(hv(clb, h), hv(cc[c % 2], h),
                                    psY[h % 2][:, :], ALU.mult).then_inc(evp, 1)
                v.wait_ge(mmp, MP0 + NPRE * c + 65)
                v.tensor_copy(Dbc[:, :], psS[0][:, :]).then_inc(evp, 1)
                for h in range(NH):
                    g = c * 64 + h
                    v.wait_ge(mm, M0 + 4 * g + 1)
                    if g >= 2:
                        v.wait_ge(evs, E0 + 5 * (g - 2) + 2)
                    v.tensor_tensor(msk[g % 2][:, :], psA[g % 2][:, :], caus[:, :], ALU.add).then_inc(ev, 1)
                    v.wait_ge(mm, M0 + 4 * g + 2)
                    v.wait_ge(evs, E0 + 5 * g + 2)
                    v.tensor_tensor(Gm[g % 2][:, :], psG[g % 2][:, :], Mp[g % 2][:, :], ALU.mult).then_inc(ev, 1)
                    v.wait_ge(mm, M0 + 4 * g + 3)
                    v.wait_ge(evs, E0 + 5 * g + 3)
                    if c > 0:
                        v.wait_ge(stot[h % 4], b[f"sto{h % 4}"] + 16 * (16 * (c - 1) + h // 4 + 1))
                    v.tensor_tensor(hv(ylb, h), psY[g % 2][:, :], xsc[g % 2][:, :], ALU.add).then_inc(ev, 1)
                    v.wait_ge(mm, M0 + 4 * g + 4)
                    v.wait_ge(evp, EP0 + NPRE * (c + 1))
                    v.tensor_scalar(st1[:, :], sfv(Sf, h), Dbc[:, h:h + 1], None, ALU.mult)
                    v.tensor_tensor(sfv(Sf, h), st1[:, :], psS[g % 2][:, :], ALU.add).then_inc(ev, 1)

        @blk.scalar
        def _(s):
            for c in range(NCH):
                for h in range(NH):
                    g = c * 64 + h
                    if g >= 2:
                        s.wait_ge(mm, M0 + 4 * (g - 2) + 1)
                    s.activation(bct[g % 2][:, :], zt[:, :], AF.Identity,
                                 bias=LgT[:, c * 64 + h:c * 64 + h + 1]).then_inc(evs, 1)
                    s.wait_ge(ev, V0 + 4 * g + 1)
                    s.activation(Mp[g % 2][:, :], msk[g % 2][:, :], AF.Exp,
                                 bias=RT[:, c * 64 + h:c * 64 + h + 1]).then_inc(evs, 1)
                    if h == 0:
                        s.wait_ge(al[c % 2], b[f"ald{c % 2}"] + 16 * 6 * (c // 2 + 1))
                    if g >= 2:
                        s.wait_ge(ev, V0 + 4 * (g - 2) + 3)
                    s.activation(xsc[g % 2][:, :], xfc[c % 2][:, kvg(h) * 128:(kvg(h) + 1) * 128],
                                 AF.Identity, scale=dpc[:, h:h + 1]).then_inc(evs, 1)
                    if g >= 2:
                        s.wait_ge(mm, M0 + 4 * (g - 2) + 4)
                    s.activation(bw[g % 2][:, :],
                                 btm[c][:, kvg(h) * 64:(kvg(h) + 1) * 64], AF.Identity,
                                 scale=wT[:, c * 64 + h:c * 64 + h + 1]).then_inc(evs, 1)
                    s.wait_ge(ev, V0 + 4 * g + 4)
                    s.copy(sfv(Sb, h), sfv(Sf, h)).then_inc(evs, 1)

    ctx.cnt["mm"] = M0 + 4 * 64 * NCH
    ctx.cnt["ev"] = V0 + 4 * 64 * NCH
    ctx.cnt["evs"] = E0 + 5 * 64 * NCH
    ctx.cnt["mmp"] = MP0 + NPRE * NCH
    ctx.cnt["evp"] = EP0 + NPRE * NCH
    ctx.cnt["ald0"] = b["ald0"] + 16 * 6 * ((NCH + 1) // 2)
    ctx.cnt["ald1"] = b["ald1"] + 16 * 6 * (NCH // 2)
    for r in range(4):
        ctx.cnt[f"sto{r}"] = b[f"sto{r}"] + 16 * 16 * 8 + (16 * 66 if r == 0 else 0)
    es.close()


def emit_p5(ctx):
    nc, io = ctx.nc, ctx.io
    es0 = contextlib.ExitStack()
    esA = contextlib.ExitStack()
    sb = lambda st, n, s, d: st.enter_context(nc.sbuf_tensor(n, s, d))
    esA2 = contextlib.ExitStack()
    ssq = es0.enter_context(nc.sbuf_tensor("p5ssq", [1, T], F32))
    sel64 = sb(esA, "p5s64", [64, 64 * 64], F32)
    Sin = sb(esA, "p5sin", [64, 4096], BF16)
    Esb = sb(esA2, "p5E", [64, 8], F32)
    emk = sb(esA2, "p5em", [64, 8], F32)
    ExpE = sb(esA2, "p5xE", [64, 8], F32)
    Wbc = sb(esA2, "p5W", [64, 64 * 8], F32)
    acc = sb(esA2, "p5acc", [64, 4096], F32)
    skb = sb(esA2, "p5skb", [64, 4096], F32)
    sgg = sb(esA2, "p5sgg", [64, 4096], F32)
    psW = esA2.enter_context(nc.psum_tensor("p5pW", [64, 8], F32))

    a0 = ctx.cnt["ald"]
    with nc.Block() as blk:
        @blk.sync
        def _(sy):
            n = 0
            for k in range(8):
                sy.dma_start(Esb[:, k:k + 1], io["egath"][k * 64:(k + 1) * 64, 0:1]).then_inc(ctx.sem("ald"), 16); n += 1
            sy.dma_start(emk[:, :], io["emask64"][:, :]).then_inc(ctx.sem("ald"), 16); n += 1
            sy.dma_start(sel64[:, :], io["sel64"][:, :]).then_inc(ctx.sem("ald"), 16); n += 1
            sy.wait_ge(ctx.sem("ald"), a0 + 16 * n)
            ctx.cnt["ald"] = a0 + 16 * n
        @blk.vector
        def _(v):
            v.memset(acc[:, :], 0.0)

    steps = []
    def S(eng, fn):
        steps.append((eng, fn))
    S("v", lambda e: e.tensor_mul(Esb[:, :], Esb[:, :], emk[:, :]))
    S("s", lambda e: e.activation(ExpE[:, :], Esb[:, :], AF.Exp))
    for h in range(64):
        S("t", lambda e, h=h: e.matmul(psW[:, :], sel64[:, h * 64:(h + 1) * 64], ExpE[:, :],
                                       start=True, stop=True))
        S("v", lambda e, h=h: e.tensor_copy(Wbc[:, h * 8:(h + 1) * 8], psW[:, :]))
    for k in range(7):
        sgsrc = io["sgath"].rearrange("(c hh n) p -> c n hh p", c=8, n=64)
        for d in range(4):
            S("g", lambda e, k=k, d=d: e.dma_start(
                skb[:, d * 16 * 64:(d + 1) * 16 * 64].rearrange("n (hh p) -> n hh p", p=64),
                sgsrc[k, :, d * 16:(d + 1) * 16, :]))
        for h in range(64):
            S("v", lambda e, k=k, h=h: e.tensor_scalar(
                acc[:, h * 64:(h + 1) * 64], acc[:, h * 64:(h + 1) * 64],
                Wbc[:, h * 8 + k:h * 8 + k + 1], None, ALU.mult))
        S("v", lambda e, k=k: e.tensor_scalar(sgg[:, :], skb[:, :], emk[:, k:k + 1], None, ALU.mult))
        S("v", lambda e, k=k: e.tensor_add(acc[:, :], acc[:, :], sgg[:, :]))
    S("v", lambda e: e.tensor_copy(Sin[:, :], acc[:, :]))
    if "dbg3" in io:
        S("g", lambda e: e.dma_start(io["dbg3"][:, :], acc[:, :]))
        S("g", lambda e: e.dma_start(io["dbg4"][:, :], Wbc[:, :]))

    aux = ctx.sem("aux")
    A0 = ctx.cnt["aux"]
    incs = [16 if en == "g" else 1 for en, _ in steps]
    tgts = [A0 + sum(incs[:i]) for i in range(len(steps))]
    with nc.Block() as blk:
        for eng_name, body in [("v", "vector"), ("s", "scalar"), ("t", "tensor"), ("g", "gpsimd")]:
            def mk(eng_name=eng_name):
                def emit(e):
                    for i, (en, fn) in enumerate(steps):
                        if en != eng_name:
                            continue
                        e.wait_ge(aux, tgts[i])
                        fn(e).then_inc(aux, incs[i])
                return emit
            getattr(blk, body)(mk())
    ctx.cnt["aux"] = A0 + sum(incs)
    esA2.close()

    esB = contextlib.ExitStack()
    cc = [sb(esB, f"p5cc{i}", [64, 64 * 128], BF16) for i in range(2)]
    yl = [sb(esB, f"p5yl{i}", [64, 64 * 128], BF16) for i in range(2)]
    zc = [sb(esB, f"p5zc{i}", [64, 64 * 128], BF16) for i in range(2)]
    yfb = [sb(esB, f"p5yf{i}", [64, 64 * 128], BF16) for i in range(2)]
    eglo = sb(esB, "p5eg", [64, T], F32)
    nw64 = sb(esB, "p5nw", [64, 64], F32)
    CGl = [sb(esB, f"p5cg{i}", [64, 128], BF16) for i in range(2)]
    yt = [sb(esB, f"p5yt{i}", [64, 128], F32) for i in range(2)]
    sz = [sb(esB, f"p5sz{i}", [64, 128], F32) for i in range(2)]
    sq = [sb(esB, f"p5sq{i}", [64, 128], F32) for i in range(2)]
    psA = [esB.enter_context(nc.psum_tensor(f"p5pA{i}", [64, 128], F32)) for i in range(2)]
    psY = [esB.enter_context(nc.psum_tensor(f"p5pY{i}", [64, 128], F32)) for i in range(2)]
    psQ = [esB.enter_context(nc.psum_tensor(f"p5pQ{i}", [1, 128], F32)) for i in range(2)]

    a0 = ctx.cnt["ald"]
    with nc.Block() as blk:
        @blk.sync
        def _(sy):
            sy.dma_start(eglo[:, :], io["eglob"][:, :]).then_inc(ctx.sem("ald"), 16)
            sy.dma_start(nw64[:, :], io["nw64"][:, :]).then_inc(ctx.sem("ald"), 16)
            sy.wait_ge(ctx.sem("ald"), a0 + 32)
            ctx.cnt["ald"] = a0 + 32
        @blk.vector
        def _(v):
            v.memset(ssq[0:1, :], 0.0)
    mm, ev, evs, evp = (ctx.sem(s) for s in ["mm", "ev", "evs", "evp"])
    al = [ctx.sem("ald0"), ctx.sem("ald1")]
    stot = [ctx.sem(f"sto{r}") for r in range(2)]
    b = {s: ctx.cnt[s] for s in ["mm", "ev", "evs", "evp", "ald0", "ald1", "sto0", "sto1"]}
    M0, V0, E0, EP0 = b["mm"], b["ev"], b["evs"], b["evp"]

    def hv(buf, h, w=128):
        return buf[:, h * 128:h * 128 + w]

    def chload(sy, c, tens, dst):
        src = tens.rearrange("(a p) t -> p a t", p=64)
        for d in range(4):
            sy.dma_start(dst[c % 2][:, d * 16 * 128:(d + 1) * 16 * 128].rearrange("p (a t) -> p a t", t=128),
                         src[:, d * 16:(d + 1) * 16, c * 128:(c + 1) * 128]).then_inc(al[c % 2], 16)

    with nc.Block() as blk:
        @blk.sync
        def _(sy):
            for c in range(min(2, NCH)):
                chload(sy, c, io["cmf"], cc)
                chload(sy, c, io["yloc"], yl)
                chload(sy, c, io["zf"], zc)
            for c in range(NCH):
                g9 = c * 64 + 63
                sy.wait_ge(ev, V0 + 5 * g9 + 4)
                ydst = io["yfh"].rearrange("(a p) t -> p a t", p=64)
                for d in range(4):
                    sy.dma_start(ydst[:, d * 16:(d + 1) * 16, c * 128:(c + 1) * 128],
                                 yfb[c % 2][:, d * 16 * 128:(d + 1) * 16 * 128].rearrange("p (a t) -> p a t", t=128)
                                 ).then_inc(stot[c % 2], 16)
                if c + 2 < NCH:
                    sy.wait_ge(ev, V0 + 5 * 64 * (c + 1))
                    sy.wait_ge(evs, E0 + 2 * 64 * (c + 1))
                    chload(sy, c + 2, io["cmf"], cc)
                    chload(sy, c + 2, io["yloc"], yl)
                    chload(sy, c + 2, io["zf"], zc)
            for r in range(2):
                sy.wait_ge(stot[r], b[f"sto{r}"] + 16 * 4 * (4 if r == 0 else 4))

        @blk.tensor
        def _(te):
            for c in range(NCH):
                par = c % 2
                te.wait_ge(al[par], b[f"ald{par}"] + 16 * 12 * (c // 2 + 1))
                for h in range(NH):
                    g = c * 64 + h
                    if g >= 2:
                        te.wait_ge(ev, V0 + 5 * (g - 2) + 1)
                    te.matmul(psA[g % 2][:, :], sel64[:, h * 64:(h + 1) * 64],
                              eglo[:, c * 128:(c + 1) * 128], start=True, stop=True).then_inc(mm, 1)
                    te.wait_ge(ev, V0 + 5 * g + 1)
                    te.matmul(psY[g % 2][:, :], Sin[:, h * 64:(h + 1) * 64],
                              CGl[g % 2][:, :], start=True, stop=True).then_inc(mm, 1)
                    te.wait_ge(evs, E0 + 2 * g + 2)
                    if g >= 2:
                        te.wait_ge(ev, V0 + 5 * (g - 2) + 5)
                    te.matmul(psQ[g % 2][:, :], ctx.ones128[0:64, 0:1], sq[g % 2][:, :],
                              start=True, stop=True).then_inc(mm, 1)

        @blk.vector
        def _(v):
            for c in range(NCH):
                for h in range(NH):
                    g = c * 64 + h
                    v.wait_ge(mm, M0 + 3 * g + 1)
                    if g >= 2:
                        v.wait_ge(mm, M0 + 3 * (g - 2) + 2)
                    v.tensor_tensor(CGl[g % 2][:, :], hv(cc[c % 2], h), psA[g % 2][:, :],
                                    ALU.mult).then_inc(ev, 1)
                    v.wait_ge(mm, M0 + 3 * g + 2)
                    v.tensor_tensor(yt[g % 2][:, :], psY[g % 2][:, :], hv(yl[c % 2], h),
                                    ALU.add).then_inc(ev, 1)
                    v.wait_ge(evs, E0 + 2 * g + 1)
                    if c >= 2 and h == 0:
                        v.wait_ge(stot[c % 2], b[f"sto{c % 2}"] + 16 * 4 * (c // 2))
                    v.tensor_tensor(hv(yfb[c % 2], h), yt[g % 2][:, :], sz[g % 2][:, :],
                                    ALU.mult).then_inc(ev, 1)
                    v.wait_ge(evs, E0 + 2 * g + 2)
                    v.tensor_scalar(hv(yfb[c % 2], h), hv(yfb[c % 2], h),
                                    nw64[:, h:h + 1], None, ALU.mult).then_inc(ev, 1)
                    v.wait_ge(mm, M0 + 3 * g + 3)
                    v.tensor_tensor(ssq[0:1, c * 128:(c + 1) * 128],
                                    ssq[0:1, c * 128:(c + 1) * 128],
                                    psQ[g % 2][0:1, :], ALU.add).then_inc(ev, 1)

        @blk.scalar
        def _(s):
            for c in range(NCH):
                for h in range(NH):
                    g = c * 64 + h
                    if h == 0:
                        s.wait_ge(al[c % 2], b[f"ald{c % 2}"] + 16 * 12 * (c // 2 + 1))
                    if g >= 2:
                        s.wait_ge(ev, V0 + 5 * (g - 2) + 3)
                    s.activation(sz[g % 2][:, :], hv(zc[c % 2], h), AF.Silu).then_inc(evs, 1)
                    s.wait_ge(ev, V0 + 5 * g + 3)
                    s.activation(sq[g % 2][:, :], hv(yfb[c % 2], h), AF.Square).then_inc(evs, 1)

    ctx.cnt["mm"] = M0 + 3 * 64 * NCH
    ctx.cnt["ev"] = V0 + 5 * 64 * NCH
    ctx.cnt["evs"] = E0 + 2 * 64 * NCH
    ctx.cnt["ald0"] = b["ald0"] + 16 * 12 * ((NCH + 1) // 2)
    ctx.cnt["ald1"] = b["ald1"] + 16 * 12 * (NCH // 2)
    ctx.cnt["sto0"] = b["sto0"] + 16 * 4 * 4
    ctx.cnt["sto1"] = b["sto1"] + 16 * 4 * 4
    esB.close()
    esA.close()

    esC = contextlib.ExitStack()
    sbC = lambda n, s, d: esC.enter_context(nc.sbuf_tensor(n, s, d))
    yft = [sbC(f"p5y{k}", [128, T], BF16) for k in range(32)]
    mos = [sbC(f"p5mo{k}", [128, T], F32) for k in range(16)]
    rrow = sbC("p5rr", [1, T], F32)
    rbc = sbC("p5rb", [128, T], F32)
    wb = [sbC(f"p5wb{i}", [128, 32 * 128], BF16) for i in range(2)]
    mt = sbC("p5mt", [128, T], F32)
    xt_ = sbC("p5xt2", [128, T], F32)
    ob = [sbC(f"p5ob{i}", [128, 512], F32) for i in range(4)]
    ps = [esC.enter_context(nc.psum_tensor(f"p5ps{i}", [128, 512], F32)) for i in range(4)]
    psB = esC.enter_context(nc.psum_tensor("p5psB", [128, 512], F32))

    a0 = ctx.cnt["ald"]
    with nc.Block() as blk:
        @blk.sync
        def _(sy):
            for k in range(32):
                sy.dma_start(yft[k][:, :], io["yfh"][k * 128:(k + 1) * 128, :]).then_inc(ctx.sem("ald"), 16)
            sy.wait_ge(ctx.sem("ald"), a0 + 16 * 32)
            ctx.cnt["ald"] = a0 + 16 * 32

    steps = []
    def S(eng, fn):
        steps.append((eng, fn))
    if "dbg1" in io:
        S("g", lambda e: e.dma_start(io["dbg1"][0:1, :], ssq[0:1, :]))
        S("g", lambda e: e.dma_start(io["dbg2"][:, :], Sin[:, :]))

    S("s", lambda e: e.activation(rrow[0:1, :], ssq[0:1, :], AF.Ln, scale=1.0 / DI, bias=EPS0))
    S("s", lambda e: e.activation(rrow[0:1, :], rrow[0:1, :], AF.Exp, scale=-0.5))
    for j in range(2):
        S("t", lambda e, j=j: e.matmul(psB[:, :], ctx.onec[0:1, :], rrow[0:1, j * 512:(j + 1) * 512],
                                       start=True, stop=True))
        S("v", lambda e, j=j: e.tensor_copy(rbc[:, j * 512:(j + 1) * 512], psB[:, :]))
    aux = ctx.sem("aux")
    A0 = ctx.cnt["aux"]
    incsd = [16 if en == "g" else 1 for en, _ in steps]
    tgts = [A0 + sum(incsd[:i]) for i in range(len(steps))]
    with nc.Block() as blk:
        for eng_name, body in [("v", "vector"), ("s", "scalar"), ("t", "tensor"), ("g", "gpsimd")]:
            def mk(eng_name=eng_name):
                def emit(e):
                    for i, (en, fn) in enumerate(steps):
                        if en != eng_name:
                            continue
                        e.wait_ge(aux, tgts[i])
                        fn(e).then_inc(aux, incsd[i])
                return emit
            getattr(blk, body)(mk())
    ctx.cnt["aux"] = A0 + sum(incsd)

    def evop(ve, ji, ti, p, ot, stile):
        t0 = ti * 512
        return ve.tensor_tensor(mos[ji][:, t0:t0 + 512], p[:, :], rbc[:, t0:t0 + 512], ALU.mult)

    linear(ctx, dict(
        name="outproj", wt=io["wop"], K=DI, N=D, T=T, tw=512,
        x_tile=lambda kc: yft[kc],
        psums=ps, wbufs=wb, evac_v=evop, out_ap=None))

    steps = []
    for kc in range(16):
        S("g", lambda e, kc=kc: e.dma_start(mt[:, :], io["mlp"][kc * 128:(kc + 1) * 128, :]))
        S("g", lambda e, kc=kc: e.dma_start(xt_[:, :], io["xnfm"][kc * 128:(kc + 1) * 128, :]))
        S("v", lambda e, kc=kc: e.tensor_add(mos[kc][:, :], mos[kc][:, :], mt[:, :]))
        S("v", lambda e, kc=kc: e.tensor_add(mos[kc][:, :], mos[kc][:, :], xt_[:, :]))
        S("g", lambda e, kc=kc: e.dma_start(io["sfm"][kc * 128:(kc + 1) * 128, :], mos[kc][:, :]))
    A0 = ctx.cnt["aux"]
    incs = [16 if en == "g" else 1 for en, _ in steps]
    tgts = [A0 + sum(incs[:i]) for i in range(len(steps))]
    with nc.Block() as blk:
        for eng_name, body in [("v", "vector"), ("g", "gpsimd")]:
            def mk(eng_name=eng_name):
                def emit(e):
                    for i, (en, fn) in enumerate(steps):
                        if en != eng_name:
                            continue
                        e.wait_ge(aux, tgts[i])
                        fn(e).then_inc(aux, incs[i])
                return emit
            getattr(blk, body)(mk())
        @blk.sync
        def _(sy):
            sy.wait_ge(aux, A0 + sum(incs))
    ctx.cnt["aux"] = A0 + sum(incs)
    esC.close()
    es0.close()


def emit_p6(ctx):
    nc, io = ctx.nc, ctx.io
    esA = contextlib.ExitStack()
    sbA = lambda n, s, d: esA.enter_context(nc.sbuf_tensor(n, s, d))
    xnf = [sbA(f"p6x{k}", [128, T], BF16) for k in range(16)]
    wgb = [sbA(f"p6wg{i}", [128, 16 * 512], BF16) for i in range(2)]
    wub = [sbA(f"p6wu{i}", [128, 16 * 512], BF16) for i in range(2)]
    RNG = 3
    sg = [sbA(f"p6sg{i}", [128, 512], F32) for i in range(RNG)]
    hh = [sbA(f"p6hh{i}", [128, 512], F32) for i in range(RNG)]
    sqv = [sbA(f"p6sq{i}", [128, 512], F32) for i in range(RNG)]
    cb = sbA("p6cb", [128, 6 * 128], F32)
    tb = sbA("p6tb", [128, 2 * 128], F32)
    kp = sbA("p6kp", [128, 128], F32)
    hspb = [sbA(f"p6hs{i}", [128, 512], BF16) for i in range(RNG)]
    psG = [esA.enter_context(nc.psum_tensor(f"p6pG{i}", [128, 512], F32)) for i in range(RNG)]
    psU = [esA.enter_context(nc.psum_tensor(f"p6pU{i}", [128, 512], F32)) for i in range(RNG)]
    load_fm_tiles(ctx, io["xntm"], D, T, xnf)

    mm, ev, evs = (ctx.sem(s) for s in ["mm", "ev", "evs"])
    wld = [ctx.sem("wld0"), ctx.sem("wld1")]
    stot = [ctx.sem(f"sto{r}") for r in range(3)]
    b = {s: ctx.cnt[s] for s in ["mm", "ev", "evs", "wld0", "wld1", "sto0", "sto1", "sto2"]}
    M0, V0, E0 = b["mm"], b["ev"], b["evs"]
    wgsrc = io["wg"].rearrange("(a p) n -> p a n", p=128)
    wusrc = io["wu"].rearrange("(a p) n -> p a n", p=128)
    NIB, NTP = 16, 8

    with nc.Block() as blk:
        @blk.gpsimd
        def _(g):
            for ib in range(NIB):
                par = ib % 2
                if ib >= 2:
                    g.wait_ge(ev, V0 + 2 * 8 * (ib - 1))
                for d in range(2):
                    g.dma_start(wgb[par][:, d * 8 * 512:(d + 1) * 8 * 512].rearrange("p (a n) -> p a n", n=512),
                                wgsrc[:, d * 8:(d + 1) * 8, ib * 512:(ib + 1) * 512]).then_inc(wld[par], 16)
                    g.dma_start(wub[par][:, d * 8 * 512:(d + 1) * 8 * 512].rearrange("p (a n) -> p a n", n=512),
                                wusrc[:, d * 8:(d + 1) * 8, ib * 512:(ib + 1) * 512]).then_inc(wld[par], 16)

        @blk.tensor
        def _(te):
            for ib in range(NIB):
                par = ib % 2
                te.wait_ge(wld[par], b[f"wld{par}"] + 16 * 4 * (ib // 2 + 1))
                for tp in range(NTP):
                    idx = ib * 8 + tp
                    if idx >= 1:
                        te.wait_ge(ev, V0 + 2 * (idx - 1) + 2)
                    if idx >= 3:
                        te.wait_ge(evs, E0 + (idx - 3) + 1)
                    ins = None
                    for kc in range(16):
                        ins = te.matmul(psG[idx % 3][:, :], xnf[kc][:, tp * 128:(tp + 1) * 128],
                                        wgb[par][:, kc * 512:(kc + 1) * 512],
                                        start=(kc == 0), stop=(kc == 15))
                    ins.then_inc(mm, 1)
                    for kc in range(16):
                        ins = te.matmul(psU[idx % 3][:, :], xnf[kc][:, tp * 128:(tp + 1) * 128],
                                        wub[par][:, kc * 512:(kc + 1) * 512],
                                        start=(kc == 0), stop=(kc == 15))
                    ins.then_inc(mm, 1)

        @blk.scalar
        def _(s):
            for idx in range(NIB * NTP):
                s.wait_ge(mm, M0 + 2 * idx + 1)
                if idx >= 3:
                    s.wait_ge(ev, V0 + 2 * (idx - 3) + 1)
                s.activation(sg[idx % 3][:, :], psG[idx % 3][:, :], AF.Silu).then_inc(evs, 1)

        @blk.vector
        def _(v):
            pairs = [(0, 1), (0, 2), (0, 3), (1, 2), (1, 3), (2, 3)]
            for idx in range(NIB * NTP):
                v.wait_ge(evs, E0 + idx + 1)
                v.wait_ge(mm, M0 + 2 * idx + 2)
                v.tensor_tensor(hh[idx % 3][:, :], sg[idx % 3][:, :], psU[idx % 3][:, :],
                                ALU.mult).then_inc(ev, 1)
                v.tensor_mul(sqv[idx % 3][:, :], hh[idx % 3][:, :], hh[idx % 3][:, :])
                if idx >= 3:
                    v.wait_ge(stot[idx % 3], b[f"sto{idx % 3}"] + 16 * (idx // 3))
                sq_ = sqv[idx % 3]
                for j, (a, bb_) in enumerate(pairs):
                    v.tensor_tensor(cb[:, j * 128:(j + 1) * 128], sq_[:, a::4], sq_[:, bb_::4], ALU.is_gt)
                h_ = hh[idx % 3]
                o_ = hspb[idx % 3]
                v.tensor_add(tb[:, 0:128], cb[:, 0:128], cb[:, 128:256])
                v.tensor_add(tb[:, 0:128], tb[:, 0:128], cb[:, 256:384])
                v.tensor_single_scalar(kp[:, :], tb[:, 0:128], 1.5, ALU.is_gt)
                v.tensor_tensor(o_[:, 0::4], h_[:, 0::4], kp[:, :], ALU.mult)
                v.tensor_add(tb[:, 0:128], cb[:, 384:512], cb[:, 512:640])
                v.tensor_sub(tb[:, 0:128], tb[:, 0:128], cb[:, 0:128])
                v.tensor_single_scalar(kp[:, :], tb[:, 0:128], 0.5, ALU.is_gt)
                v.tensor_tensor(o_[:, 1::4], h_[:, 1::4], kp[:, :], ALU.mult)
                v.tensor_sub(tb[:, 0:128], cb[:, 640:768], cb[:, 128:256])
                v.tensor_sub(tb[:, 0:128], tb[:, 0:128], cb[:, 384:512])
                v.tensor_single_scalar(kp[:, :], tb[:, 0:128], -0.5, ALU.is_gt)
                v.tensor_tensor(o_[:, 2::4], h_[:, 2::4], kp[:, :], ALU.mult)
                v.tensor_add(tb[:, 0:128], cb[:, 256:384], cb[:, 512:640])
                v.tensor_add(tb[:, 0:128], tb[:, 0:128], cb[:, 640:768])
                v.tensor_single_scalar(kp[:, :], tb[:, 0:128], 1.5, ALU.is_lt)
                v.tensor_tensor(o_[:, 3::4], h_[:, 3::4], kp[:, :], ALU.mult).then_inc(ev, 1)

        @blk.sync
        def _(sy):
            for idx in range(NIB * NTP):
                ib, tp = idx // 8, idx % 8
                sy.wait_ge(ev, V0 + 2 * idx + 2)
                sy.dma_start(io["hsptm"][tp * 128:(tp + 1) * 128, ib * 512:(ib + 1) * 512],
                             hspb[idx % 3][:, :]).then_inc(stot[idx % 3], 16)
            for r in range(3):
                nr = sum(1 for i2 in range(NIB * NTP) if i2 % 3 == r)
                sy.wait_ge(stot[r], b[f"sto{r}"] + 16 * nr)

    ctx.cnt["mm"] = M0 + 2 * NIB * NTP
    ctx.cnt["ev"] = V0 + 2 * NIB * NTP
    ctx.cnt["evs"] = E0 + NIB * NTP
    ctx.cnt["wld0"] = b["wld0"] + 16 * 4 * 8
    ctx.cnt["wld1"] = b["wld1"] + 16 * 4 * 8
    for r in range(3):
        ctx.cnt[f"sto{r}"] = b[f"sto{r}"] + 16 * sum(1 for i2 in range(NIB * NTP) if i2 % 3 == r)
    esA.close()

    for th in range(2):
        t0 = th * 512
        esB = contextlib.ExitStack()
        sbB = lambda n, s, d: esB.enter_context(nc.sbuf_tensor(n, s, d))
        hspf = [sbB(f"p6hf{th}_{k}", [128, 512], BF16) for k in range(64)]
        sT = [sbB(f"p6sT{th}_{k}", [128, 512], F32) for k in range(16)]
        xT = [sbB(f"p6xT{th}_{k}", [128, 512], F32) for k in range(16)]
        wb = [sbB(f"p6wb{th}_{i}", [128, 64 * 128], BF16) for i in range(2)]
        gsc = [sbB(f"p6gs{th}_{i}", [128, 512], F32) for i in range(2)]
        ob = [sbB(f"p6ob{th}_{i}", [128, 512], F32) for i in range(4)]
        ps = [esB.enter_context(nc.psum_tensor(f"p6ps{th}_{i}", [128, 512], F32)) for i in range(4)]
        a0 = ctx.cnt["ald"]
        with nc.Block() as blk:
            @blk.sync
            def _(sy):
                n = 0
                for k in range(64):
                    sy.dma_start(hspf[k][:, :], io["hsptm"][t0:t0 + 512, k * 128:(k + 1) * 128],
                                 transpose=True).then_inc(ctx.sem("ald"), 16); n += 1
                for k in range(16):
                    sy.dma_start(sT[k][:, :], io["sfm"][k * 128:(k + 1) * 128, t0:t0 + 512]).then_inc(ctx.sem("ald"), 16); n += 1
                    sy.dma_start(xT[k][:, :], io["xfm"][k * 128:(k + 1) * 128, t0:t0 + 512]).then_inc(ctx.sem("ald"), 16); n += 1
                sy.wait_ge(ctx.sem("ald"), a0 + 16 * n)
                ctx.cnt["ald"] = a0 + 16 * n

        def evs_(se, ji, ti, p, ot, stile):
            return se.activation(stile[:, :], p[:, :], AF.Silu)

        def evv_(ve, ji, ti, p, ot, stile):
            ve.tensor_tensor(ot[:, :], stile[:, :], sT[ji][:, :], ALU.mult)
            return ve.tensor_tensor(ot[:, :], ot[:, :], xT[ji][:, :], ALU.add)

        linear(ctx, dict(
            name=f"down{th}", wt=io["wd"], K=INTER, N=D, T=512, tw=512,
            x_tile=lambda kc: hspf[kc],
            psums=ps, wbufs=wb, obufs=ob, sbufs=gsc,
            evac_s=evs_, evac_v=evv_,
            out_ap=lambda ji, ti, t0=t0: io["x1fm"][ji * 128:(ji + 1) * 128, t0:t0 + 512]))
        esB.close()


def emit_p7(ctx):
    nc, io = ctx.nc, ctx.io
    es = contextlib.ExitStack()
    sb = lambda n, s, d: es.enter_context(nc.sbuf_tensor(n, s, d))
    xr = sb("p7xr", [128, 16 * 128], F32)
    xt = sb("p7xt", [128, D], F32)
    xc = sb("p7xc", [128, D], F32)
    sq = sb("p7sq", [128, D], F32)
    xnb = sb("p7xnb", [128, D], BF16)
    st_ = sb("p7st", [128, 8], F32)
    psT = es.enter_context(nc.psum_tensor("p7ps", [128, 128], F32))
    xsrc = io["x1fm"].rearrange("(a p) t -> p a t", p=128)

    steps = []
    def S(eng, fn):
        steps.append((eng, fn))
    for i in range(8):
        S("g", lambda e, i=i: e.dma_start(xr[:, :].rearrange("p (a t) -> p a t", t=128),
                                          xsrc[:, :, i * 128:(i + 1) * 128]))
        for kc in range(16):
            S("t", lambda e, kc=kc: e.transpose(psT[:, :], xr[:, kc * 128:(kc + 1) * 128], ctx.eye[:, :]))
            S("v", lambda e, kc=kc: e.tensor_copy(xt[:, kc * 128:(kc + 1) * 128], psT[:, :]))
        S("v", lambda e: e.reduce_sum(st_[:, 0:1], xt[:, :], axis=AX))
        S("v", lambda e: e.tensor_scalar(st_[:, 1:2], st_[:, 0:1], 1.0 / D, None, ALU.mult))
        S("v", lambda e: e.tensor_scalar(xc[:, :], xt[:, :], st_[:, 1:2], None, ALU.subtract))
        S("s", lambda e: e.activation(sq[:, :], xc[:, :], AF.Square))
        S("v", lambda e: e.reduce_sum(st_[:, 2:3], sq[:, :], axis=AX))
        S("s", lambda e: e.activation(st_[:, 3:4], st_[:, 2:3], AF.Ln, scale=1.0 / D, bias=EPS0))
        S("s", lambda e: e.activation(st_[:, 4:5], st_[:, 3:4], AF.Exp, scale=-0.5))
        S("v", lambda e: e.tensor_scalar(xnb[:, :], xc[:, :], st_[:, 4:5], None, ALU.mult))
        S("g", lambda e, i=i: e.dma_start(io["xn1tm"][i * 128:(i + 1) * 128, :], xnb[:, :]))
    aux = ctx.sem("aux")
    A0 = ctx.cnt["aux"]
    incs = [16 if en == "g" else 1 for en, _ in steps]
    tgts = [A0 + sum(incs[:i]) for i in range(len(steps))]
    with nc.Block() as blk:
        for eng_name, body in [("v", "vector"), ("s", "scalar"), ("t", "tensor"), ("g", "gpsimd")]:
            def mk(eng_name=eng_name):
                def emit(e):
                    for i, (en, fn) in enumerate(steps):
                        if en != eng_name:
                            continue
                        e.wait_ge(aux, tgts[i])
                        fn(e).then_inc(aux, incs[i])
                return emit
            getattr(blk, body)(mk())
        @blk.sync
        def _(sy):
            sy.wait_ge(aux, A0 + sum(incs))
    ctx.cnt["aux"] = A0 + sum(incs)
    es.close()



def emit_p8(ctx):
    nc, io = ctx.nc, ctx.io
    es = contextlib.ExitStack()
    xnf = [es.enter_context(nc.sbuf_tensor(f"p8x{k}", [128, T], BF16)) for k in range(16)]
    wb = [es.enter_context(nc.sbuf_tensor(f"p8w{i}", [128, 16 * 128], BF16)) for i in range(3)]
    ob = [es.enter_context(nc.sbuf_tensor(f"p8o{i}", [128, 512], BF16)) for i in range(4)]
    ps = [es.enter_context(nc.psum_tensor(f"p8p{i}", [128, 512], F32)) for i in range(4)]
    load_fm_tiles(ctx, io["xn1tm"], D, T, xnf)

    def out_ap(ji, ti):
        h, part = ji // 3, ji % 3
        dst = [io["qf"], io["kf"], io["vf"]][part]
        return dst[h * 128:(h + 1) * 128, ti * 512:(ti + 1) * 512]

    def ev(se, ji, ti, p, ot, stile):
        return se.activation(ot[:, :], p[:, :], AF.Silu)

    linear(ctx, dict(
        name="qkv", wt=io["wqkv"], K=D, N=48 * 128, T=T, tw=512,
        x_tile=lambda kc: xnf[kc],
        psums=ps, wbufs=wb, obufs=ob, evac_s=ev, out_ap=out_ap))
    es.close()


def emit_p9(ctx):
    nc, io = ctx.nc, ctx.io
    es = contextlib.ExitStack()
    sb = lambda n, s, d: es.enter_context(nc.sbuf_tensor(n, s, d))
    qf = [sb(f"p9q{k}", [128, T], BF16) for k in range(16)]
    kf = [sb(f"p9k{k}", [128, T], BF16) for k in range(16)]
    vf = [sb(f"p9v{k}", [128, T], BF16) for k in range(16)]
    qdech = sb("p9qd", [128, BLK], F32)
    kdech = sb("p9kd", [128, BLK], F32)
    ddech = sb("p9dd", [128, 2 * BLK], F32)
    kv = sb("p9kv", [128, 128], F32)
    kvt = sb("p9kvt", [128, 128], F32)
    kvb = sb("p9kvb", [128, 128], BF16)
    qd = sb("p9qdt", [128, BLK], BF16)
    qkm = [sb(f"p9qk{i}", [128, BLK], BF16) for i in range(2)]
    kd = sb("p9kdt", [128, BLK], BF16)
    vtm = [sb(f"p9vt{i}", [128, 128], BF16) for i in range(2)]
    ktm = [sb(f"p9kt{i}", [128, 128], BF16) for i in range(2)]
    yb = sb("p9yb", [128, BLK], BF16)
    psY = es.enter_context(nc.psum_tensor("p9pY", [128, BLK], F32))
    psK = [es.enter_context(nc.psum_tensor(f"p9pK{i}", [128, BLK], F32)) for i in range(2)]
    psKV = es.enter_context(nc.psum_tensor("p9pKV", [128, 128], F32))
    psT = es.enter_context(nc.psum_tensor("p9pT", [128, 128], BF16))

    a0 = ctx.cnt["ald"]
    with nc.Block() as blk:
        @blk.sync
        def _(sy):
            n = 0
            for k in range(16):
                sy.dma_start(qf[k][:, :], io["qf"][k * 128:(k + 1) * 128, :]).then_inc(ctx.sem("ald"), 16); n += 1
                sy.dma_start(kf[k][:, :], io["kf"][k * 128:(k + 1) * 128, :]).then_inc(ctx.sem("ald"), 16); n += 1
                sy.dma_start(vf[k][:, :], io["vf"][k * 128:(k + 1) * 128, :]).then_inc(ctx.sem("ald"), 16); n += 1
            sy.wait_ge(ctx.sem("ald"), a0 + 16 * n)
            ctx.cnt["ald"] = a0 + 16 * n

    ds = DepSteps(ctx)
    S = ds.S
    P = 2
    qdechs = [qdech, kdech]
    kdechs = [sb(f"p9kd2{i}", [128, BLK], F32) for i in range(P)]
    ddechs = [ddech] + [sb("p9dd2", [128, 2 * BLK], F32)]
    kvs_ = [kv, sb("p9kv2", [128, 128], F32)]
    kvts = [kvt, sb("p9kvt2", [128, 128], F32)]
    kvbs = [kvb, sb("p9kvb2", [128, 128], BF16)]
    qds = [qd, sb("p9qd2", [128, BLK], BF16)]
    kds = [kd, sb("p9kd3", [128, BLK], BF16)]
    qkms = [qkm, [sb(f"p9qk2{i}", [128, BLK], BF16) for i in range(2)]]
    vtms = [vtm, [sb(f"p9vt2{i}", [128, 128], BF16) for i in range(2)]]
    ktms = [ktm, [sb(f"p9kt2{i}", [128, 128], BF16) for i in range(2)]]
    ybs = [yb, sb("p9yb2", [128, BLK], BF16)]
    psYs = [psY, es.enter_context(nc.psum_tensor("p9pY2", [128, BLK], F32))]
    psKVs = [psKV, es.enter_context(nc.psum_tensor("p9pKV2", [128, 128], F32))]
    last = {}
    for h in range(NH1):
        p_ = h % 2
        bd = float(BDECAY[h])
        ql, kl, dl = qdechs[p_], kdechs[p_], ddechs[p_]
        i_ql = S("g", lambda e, h=h, ql=ql: e.dma_start(ql[:, :], io["qdec"][h * 128:(h + 1) * 128, :]),
                 deps=[last.get(("qluse", p_))])
        i_kl = S("g", lambda e, h=h, kl=kl: e.dma_start(kl[:, :], io["kdec"][h * 128:(h + 1) * 128, :]),
                 deps=[last.get(("kluse", p_))])
        i_dl = S("g", lambda e, h=h, dl=dl: e.dma_start(dl[:, :].rearrange("p (a t) -> p a t", t=BLK),
                 io["ddecT"].rearrange("(a p) t -> p a t", p=128)[:, 2 * h:2 * h + 2, :]),
                 deps=[last.get(("dluse", p_))])
        i_z1 = S("v", lambda e, p_=p_: e.memset(kvs_[p_][:, :], 0.0), deps=[last.get(("kvuse", p_))])
        i_z2 = S("v", lambda e, p_=p_: e.memset(kvbs[p_][:, :], 0.0), deps=[last.get(("kvbuse", p_))])
        for b in range(NBLK):
            t0 = b * BLK
            i_qd = S("v", lambda e, h=h, t0=t0, p_=p_: e.tensor_tensor(
                qds[p_][:, :], qf[h][:, t0:t0 + BLK], qdechs[p_][:, :], ALU.mult),
                deps=[i_ql, last.get(("qduse", p_))])
            qk_is = []
            vt_is = []
            for sh_ in range(2):
                i_qk = S("t", lambda e, h=h, t0=t0, sh_=sh_: e.matmul(
                    psK[sh_][:, :], kf[h][:, t0 + sh_ * 128:t0 + sh_ * 128 + 128],
                    qf[h][:, t0:t0 + BLK], start=True, stop=True),
                    deps=[last.get(("psK", sh_))])
                i_qkm = S("v", lambda e, sh_=sh_, p_=p_: e.tensor_tensor(
                    qkms[p_][sh_][:, :], psK[sh_][:, :],
                    ddechs[p_][:, sh_ * BLK:(sh_ + 1) * BLK], ALU.mult),
                    deps=[i_qk, i_dl, last.get(("qkmuse", p_, sh_))])
                last[("psK", sh_)] = i_qkm
                i_vt = S("t", lambda e, h=h, t0=t0, sh_=sh_: e.transpose(
                    psT[:, :], vf[h][:, t0 + sh_ * 128:t0 + sh_ * 128 + 128], ctx.eyeb[:, :]),
                    deps=[last.get("psT")])
                i_vte = S("v", lambda e, sh_=sh_, p_=p_: e.tensor_copy(vtms[p_][sh_][:, :], psT[:, :]),
                          deps=[i_vt, last.get(("vtmuse", p_, sh_))])
                last["psT"] = i_vte
                qk_is.append(i_qkm)
                vt_is.append(i_vte)
            i_y0 = S("t", lambda e, p_=p_: e.matmul(psYs[p_][:, :], kvbs[p_][:, :], qds[p_][:, :],
                                                    start=True, stop=False),
                     deps=[i_qd, i_z2 if b == 0 else last[("kvb", p_)], last.get(("psY", p_))])
            last[("qduse", p_)] = i_y0
            i_y1 = S("t", lambda e, p_=p_: e.matmul(psYs[p_][:, :], vtms[p_][0][:, :], qkms[p_][0][:, :],
                                                    start=False, stop=False),
                     deps=[qk_is[0], vt_is[0]])
            i_y2 = S("t", lambda e, p_=p_: e.matmul(psYs[p_][:, :], vtms[p_][1][:, :], qkms[p_][1][:, :],
                                                    start=False, stop=True),
                     deps=[qk_is[1], vt_is[1]])
            last[("qkmuse", p_, 0)] = i_y1
            last[("qkmuse", p_, 1)] = i_y2
            i_ye = S("v", lambda e, p_=p_: e.tensor_copy(ybs[p_][:, :], psYs[p_][:, :]),
                     deps=[i_y2, last.get(("ybuse", p_))])
            last[("psY", p_)] = i_ye
            i_yst = S("g", lambda e, h=h, t0=t0, p_=p_: e.dma_start(
                io["y1loc"][h * 128:(h + 1) * 128, t0:t0 + BLK], ybs[p_][:, :]), deps=[i_ye])
            last[("ybuse", p_)] = i_yst
            i_kd = S("v", lambda e, h=h, t0=t0, p_=p_: e.tensor_tensor(
                kds[p_][:, :], kf[h][:, t0:t0 + BLK], kdechs[p_][:, :], ALU.mult),
                deps=[i_kl, last.get(("kduse", p_))])
            kt_is = []
            for sh_ in range(2):
                i_kt = S("t", lambda e, sh_=sh_, p_=p_: e.transpose(
                    psT[:, :], kds[p_][:, sh_ * 128:(sh_ + 1) * 128], ctx.eyeb[:, :]),
                    deps=[i_kd, last.get("psT")])
                i_kte = S("v", lambda e, sh_=sh_, p_=p_: e.tensor_copy(ktms[p_][sh_][:, :], psT[:, :]),
                          deps=[i_kt, last.get(("ktmuse", p_, sh_))])
                last["psT"] = i_kte
                kt_is.append(i_kte)
            last[("kduse", p_)] = kt_is[1]
            i_kv1 = S("t", lambda e, p_=p_: e.matmul(psKVs[p_][:, :], ktms[p_][0][:, :], vtms[p_][0][:, :],
                                                     start=True, stop=False),
                      deps=[kt_is[0], vt_is[0], last.get(("psKV", p_))])
            i_kv2 = S("t", lambda e, p_=p_: e.matmul(psKVs[p_][:, :], ktms[p_][1][:, :], vtms[p_][1][:, :],
                                                     start=False, stop=True),
                      deps=[kt_is[1], vt_is[1]])
            last[("ktmuse", p_, 0)] = i_kv1
            last[("ktmuse", p_, 1)] = i_kv2
            last[("vtmuse", p_, 0)] = i_kv1
            last[("vtmuse", p_, 1)] = i_kv2
            i_ts = S("v", lambda e, bd=bd, p_=p_: e.tensor_scalar(kvts[p_][:, :], kvs_[p_][:, :],
                                                                  bd, None, ALU.mult),
                     deps=[i_z1 if b == 0 else last[("kv", p_)]])
            i_tt = S("v", lambda e, p_=p_: e.tensor_tensor(kvs_[p_][:, :], kvts[p_][:, :],
                                                           psKVs[p_][:, :], ALU.add),
                     deps=[i_kv2])
            last[("psKV", p_)] = i_tt
            last[("kv", p_)] = i_tt
            i_kb = S("v", lambda e, p_=p_: e.tensor_copy(kvbs[p_][:, :], kvs_[p_][:, :]), deps=[i_tt])
            last[("kvb", p_)] = i_kb
        i_fin = S("g", lambda e, h=h, p_=p_: e.dma_start(io["kvfin"][h * 128:(h + 1) * 128, :],
                                                         kvs_[p_][:, :]), deps=[last[("kv", p_)]])
        last[("kvuse", p_)] = i_fin
        last[("kvbuse", p_)] = last[("kvb", p_)]
        last[("qluse", p_)] = last[("qduse", p_)]
        last[("kluse", p_)] = last[("kduse", p_)]
        last[("dluse", p_)] = last[("qkmuse", p_, 1)]
    ds.emit()
    es.close()



def emit_p10(ctx):
    nc, io = ctx.nc, ctx.io
    es = contextlib.ExitStack()
    sb = lambda n, s, d: es.enter_context(nc.sbuf_tensor(n, s, d))
    qf = [sb(f"pAq{k}", [128, T], BF16) for k in range(16)]
    y1l = sb("pAy", [128, T], BF16)
    accs = sb("pAacc", [128, 16 * 128], F32)
    kvk = sb("pAkvk", [128, 16 * 128], F32)
    l1w = sb("pAl1w", [128, NH1 * 8], F32)
    emk = sb("pAem", [128, 8], F32)
    qdech = sb("pAqd", [128, BLK], F32)
    kvs = sb("pAkvs", [128, 128], F32)
    kvsb = sb("pAkvsb", [128, 128], BF16)
    qd = sb("pAqdt", [128, BLK], BF16)
    ot = sb("pAot", [128, BLK], F32)
    sqo = sb("pAsq", [128, BLK], F32)
    ssq1 = sb("pAssq", [1, T], F32)
    obuf = [sb(f"pAo{k}", [128, T], BF16) for k in range(16)]
    esP = contextlib.ExitStack()
    psY = esP.enter_context(nc.psum_tensor("pApY", [128, BLK], F32))
    psQ = esP.enter_context(nc.psum_tensor("pApQ", [1, BLK], F32))

    a0 = ctx.cnt["ald"]
    with nc.Block() as blk:
        @blk.sync
        def _(sy):
            n = 0
            for k in range(16):
                sy.dma_start(qf[k][:, :], io["qf"][k * 128:(k + 1) * 128, :]).then_inc(ctx.sem("ald"), 16); n += 1
            sy.dma_start(l1w[:, :], io["l1w"][:, :]).then_inc(ctx.sem("ald"), 16); n += 1
            sy.dma_start(emk[:, :], io["emask128"][:, :]).then_inc(ctx.sem("ald"), 16); n += 1
            sy.wait_ge(ctx.sem("ald"), a0 + 16 * n)
            ctx.cnt["ald"] = a0 + 16 * n
        @blk.vector
        def _(v):
            v.memset(accs[:, :], 0.0)
            v.memset(ssq1[0:1, :], 0.0)

    ds = DepSteps(ctx)
    S = ds.S
    last = {}
    for k in range(7):
        i_ld = S("g", lambda e, k=k: e.dma_start(
            kvk[:, :].rearrange("p (a t) -> p a t", t=128),
            io["kvgath"].rearrange("(a p) t -> p a t", p=128)[:, k * 16:(k + 1) * 16, :]),
            deps=[last.get(("kvkuse", h2)) for h2 in range(NH1)])
        for h in range(NH1):
            i_m = S("v", lambda e, h=h, k=k: e.tensor_scalar(
                accs[:, h * 128:(h + 1) * 128], accs[:, h * 128:(h + 1) * 128],
                l1w[:, h * 8 + k:h * 8 + k + 1], None, ALU.mult))
            i_g = S("v", lambda e, h=h, k=k: e.tensor_scalar(
                kvk[:, h * 128:(h + 1) * 128], kvk[:, h * 128:(h + 1) * 128],
                emk[:, k:k + 1], None, ALU.mult), deps=[i_ld])
            i_a = S("v", lambda e, h=h: e.tensor_add(
                accs[:, h * 128:(h + 1) * 128], accs[:, h * 128:(h + 1) * 128],
                kvk[:, h * 128:(h + 1) * 128]))
            last[("kvkuse", h)] = i_a
    qdechs = [qdech, sb("pAqd2", [128, BLK], F32)]
    kvss = [kvs, sb("pAkvs2", [128, 128], F32)]
    kvsbs = [kvsb, sb("pAkvsb2", [128, 128], BF16)]
    qds = [qd, sb("pAqdt2", [128, BLK], BF16)]
    ots = [ot, sb("pAot2", [128, BLK], F32)]
    sqos = [sqo, sb("pAsq2", [128, BLK], F32)]
    y1ls = [y1l, sb("pAy2", [128, T], BF16)]
    psYs = [psY, esP.enter_context(nc.psum_tensor("pApY2", [128, BLK], F32))]
    psQs = [psQ, esP.enter_context(nc.psum_tensor("pApQ2", [1, BLK], F32))]
    i_ssqlast = None
    for h in range(NH1):
        p_ = h % 2
        i_ql = S("g", lambda e, h=h, p_=p_: e.dma_start(qdechs[p_][:, :], io["qdec"][h * 128:(h + 1) * 128, :]),
                 deps=[last.get(("qluse", p_))])
        i_yl = S("g", lambda e, h=h, p_=p_: e.dma_start(y1ls[p_][:, :], io["y1loc"][h * 128:(h + 1) * 128, :]),
                 deps=[last.get(("yluse", p_))])
        for b in range(NBLK):
            t0 = b * BLK
            bdb = float(BDECAY[h] ** b)
            i_kvs = S("v", lambda e, h=h, bdb=bdb, p_=p_: e.tensor_scalar(
                kvss[p_][:, :], accs[:, h * 128:(h + 1) * 128], bdb, None, ALU.mult),
                deps=[last[("kvkuse", h)], last.get(("kvsuse", p_))])
            i_kvb = S("v", lambda e, p_=p_: e.tensor_copy(kvsbs[p_][:, :], kvss[p_][:, :]),
                      deps=[i_kvs, last.get(("kvsbuse", p_))])
            i_qd = S("v", lambda e, h=h, t0=t0, p_=p_: e.tensor_tensor(
                qds[p_][:, :], qf[h][:, t0:t0 + BLK], qdechs[p_][:, :], ALU.mult),
                deps=[i_ql, last.get(("qduse", p_))])
            i_mm = S("t", lambda e, p_=p_: e.matmul(psYs[p_][:, :], kvsbs[p_][:, :], qds[p_][:, :],
                                                    start=True, stop=True),
                     deps=[i_kvb, i_qd, last.get(("psY", p_))])
            last[("kvsuse", p_)] = i_mm
            last[("kvsbuse", p_)] = i_mm
            last[("qduse", p_)] = i_mm
            i_ot = S("v", lambda e, t0=t0, p_=p_: e.tensor_tensor(
                ots[p_][:, :], psYs[p_][:, :], y1ls[p_][:, t0:t0 + BLK], ALU.add),
                deps=[i_mm, i_yl, last.get(("otuse", p_))])
            last[("psY", p_)] = i_ot
            i_sq = S("v", lambda e, p_=p_: e.tensor_mul(sqos[p_][:, :], ots[p_][:, :], ots[p_][:, :]),
                     deps=[last.get(("squse", p_))])
            i_mq = S("t", lambda e, p_=p_: e.matmul(psQs[p_][:, :], ctx.ones128[:, 0:1], sqos[p_][:, :],
                                                    start=True, stop=True),
                     deps=[i_sq, last.get(("psQ", p_))])
            last[("squse", p_)] = i_mq
            i_sa = S("v", lambda e, t0=t0, p_=p_: e.tensor_tensor(
                ssq1[0:1, t0:t0 + BLK], ssq1[0:1, t0:t0 + BLK], psQs[p_][0:1, :], ALU.add),
                deps=[i_mq, i_ssqlast])
            i_ssqlast = i_sa
            last[("psQ", p_)] = i_sa
            i_oc = S("v", lambda e, h=h, t0=t0, p_=p_: e.tensor_copy(obuf[h][:, t0:t0 + BLK], ots[p_][:, :]))
            last[("otuse", p_)] = i_oc
        last[("qluse", p_)] = last[("qduse", p_)]
        last[("yluse", p_)] = last[("otuse", p_)]
    ds.emit()
    esP.close()

    rrow = sb("pArr", [1, T], F32)
    rb1 = sb("pArb", [128, T], F32)
    nw1 = sb("pAnw", [128, 16], F32)
    psB = es.enter_context(nc.psum_tensor("pApB", [128, 512], F32))
    a0 = ctx.cnt["ald"]
    with nc.Block() as blk:
        @blk.sync
        def _(sy):
            sy.dma_start(nw1[:, :], io["nw1cols"][:, :]).then_inc(ctx.sem("ald"), 16)
            sy.wait_ge(ctx.sem("ald"), a0 + 16)
            ctx.cnt["ald"] = a0 + 16
    steps2 = []
    def S2(eng, fn):
        steps2.append((eng, fn))
    S2("s", lambda e: e.activation(rrow[0:1, :], ssq1[0:1, :], AF.Ln, scale=1.0 / D, bias=EPS1))
    S2("s", lambda e: e.activation(rrow[0:1, :], rrow[0:1, :], AF.Exp, scale=-0.5))
    for j in range(2):
        S2("t", lambda e, j=j: e.matmul(psB[:, :], ctx.onec[0:1, :], rrow[0:1, j * 512:(j + 1) * 512],
                                        start=True, stop=True))
        S2("v", lambda e, j=j: e.tensor_copy(rb1[:, j * 512:(j + 1) * 512], psB[:, :]))
    aux = ctx.sem("aux")
    A0 = ctx.cnt["aux"]
    tgts = [A0 + i for i in range(len(steps2))]
    with nc.Block() as blk:
        for eng_name, body in [("v", "vector"), ("s", "scalar"), ("t", "tensor")]:
            def mk(eng_name=eng_name):
                def emit(e):
                    for i, (en, fn) in enumerate(steps2):
                        if en != eng_name:
                            continue
                        e.wait_ge(aux, tgts[i])
                        fn(e).then_inc(aux, 1)
                return emit
            getattr(blk, body)(mk())
    ctx.cnt["aux"] = A0 + len(steps2)

    esD = contextlib.ExitStack()
    xnf = [esD.enter_context(nc.sbuf_tensor(f"pAx{k}", [128, T], BF16)) for k in range(16)]
    wb = [esD.enter_context(nc.sbuf_tensor(f"pAw{i}", [128, 16 * 128], BF16)) for i in range(3)]
    sgt = [esD.enter_context(nc.sbuf_tensor(f"pAsg{i}", [128, 512], F32)) for i in range(2)]
    ofb = [esD.enter_context(nc.sbuf_tensor(f"pAof{k}", [128, T], BF16)) for k in range(16)]
    ob = [esD.enter_context(nc.sbuf_tensor(f"pAob{i}", [128, 512], F32)) for i in range(4)]
    ps = [esD.enter_context(nc.psum_tensor(f"pAp{i}", [128, 512], F32)) for i in range(4)]
    load_fm_tiles(ctx, io["xn1tm"], D, T, xnf)

    def evs_(se, ji, ti, p, ot_, stile):
        return se.activation(stile[:, :], p[:, :], AF.Sigmoid)

    def evv_(ve, ji, ti, p, ot_, stile):
        t0 = ti * 512
        ve.scalar_tensor_tensor(ofb[ji][:, t0:t0 + 512], obuf[ji][:, t0:t0 + 512],
                                nw1[:, ji:ji + 1], stile[:, :], ALU.mult, ALU.mult)
        return ve.tensor_tensor(ofb[ji][:, t0:t0 + 512], ofb[ji][:, t0:t0 + 512],
                                rb1[:, t0:t0 + 512], ALU.mult)

    linear(ctx, dict(
        name="wg1", wt=io["wgt"], K=D, N=D, T=T, tw=512,
        x_tile=lambda kc: xnf[kc],
        psums=ps, wbufs=wb, sbufs=sgt, evac_s=evs_, evac_v=evv_, out_ap=None))

    def evo_(ve, ji, ti, p, ot_, stile):
        return ve.tensor_copy(ot_[:, :], p[:, :])

    linear(ctx, dict(
        name="wo1", wt=io["wo1"], K=D, N=D, T=T, tw=512,
        x_tile=lambda kc: ofb[kc],
        psums=ps, wbufs=wb, obufs=ob, evac_v=evo_,
        out_ap=lambda ji, ti: io["s1fm"][ji * 128:(ji + 1) * 128, ti * 512:(ti + 1) * 512]))
    esD.close()
    es.close()


W2ROWS = HID // 8

def emit_qa(ctx):
    nc, io = ctx.nc, ctx.io
    es = contextlib.ExitStack()
    sb = lambda n, s, d: es.enter_context(nc.sbuf_tensor(n, s, d))
    wbuf = sb("qaw", [128, HID], F32)
    cs = sb("qacs", [128, 1], F32)
    row = sb("qarow", [1, 4], F32)
    ps1 = es.enter_context(nc.psum_tensor("qap", [1, 1], F32))
    steps = []
    def S(eng, fn):
        steps.append((eng, fn))
    jobs = [("w1s", [(0, 128), (128, 128)], 0),
            ("w3s", [(0, 128), (128, 128)], 1),
            ("w2s", [(128 * j, 128) for j in range(5)] + [(640, 64)], 2)]
    for name, slices, col in jobs:
        width = HID if name != "w2s" else D
        nsl = len(slices)
        for si, (r0, nr) in enumerate(slices):
            S("g", lambda e, name=name, r0=r0, nr=nr, width=width: e.dma_start(
                wbuf[0:nr, 0:width], io[name][r0:r0 + nr, 0:width]))
            S("v", lambda e, nr=nr, width=width: e.reduce_sum(
                cs[0:nr, 0:1], wbuf[0:nr, 0:width], axis=AX, apply_absolute_value=True))
            S("t", lambda e, nr=nr, si=si, nsl=nsl: e.matmul(
                ps1[0:1, 0:1], cs[0:nr, 0:1], ctx.ones128[0:nr, 0:1],
                start=(si == 0), stop=(si == nsl - 1)))
        S("v", lambda e, col=col: e.tensor_copy(row[0:1, col:col + 1], ps1[0:1, 0:1]))
    S("g", lambda e: e.dma_start(io["absq"][0:1, :], row[0:1, :]))
    aux = ctx.sem("aux")
    A0 = ctx.cnt["aux"]
    incs = [16 if en == "g" else 1 for en, _ in steps]
    tgts = [A0 + sum(incs[:i]) for i in range(len(steps))]
    with nc.Block() as blk:
        for eng_name, body in [("v", "vector"), ("t", "tensor"), ("g", "gpsimd")]:
            def mk(eng_name=eng_name):
                def emit(e):
                    for i, (en, fn) in enumerate(steps):
                        if en != eng_name:
                            continue
                        e.wait_ge(aux, tgts[i])
                        fn(e).then_inc(aux, incs[i])
                return emit
            getattr(blk, body)(mk())
        @blk.sync
        def _(sy):
            sy.wait_ge(aux, A0 + sum(incs))
    ctx.cnt["aux"] = A0 + sum(incs)
    es.close()


def emit_qb(ctx):
    nc, io = ctx.nc, ctx.io
    es = contextlib.ExitStack()
    sb = lambda n, s, d: es.enter_context(nc.sbuf_tensor(n, s, d))
    flat = sb("qbf", [1, 32], F32)
    acc = sb("qbacc", [1, 4], F32)
    scr = sb("qbsc", [1, 4], F32)
    thbc = sb("qbth", [128, 4], F32)
    wbuf = sb("qbw", [128, HID], F32)
    sg = sb("qbsg", [128, HID], F32)
    ab = sb("qbab", [128, HID], F32)
    tq = sb("qbtq", [128, HID], BF16)
    psB = es.enter_context(nc.psum_tensor("qbp", [128, 4], F32))
    NORM = 1.0 / (HID0 * D)
    steps = []
    def S(eng, fn):
        steps.append((eng, fn))
    S("g", lambda e: e.dma_start(flat[0:1, :], io["absg"].rearrange("c v -> (c v)").rearrange("(a f) -> a f", a=1)))
    S("v", lambda e: e.tensor_copy(acc[0:1, :], flat[0:1, 0:4]))
    for k in range(1, 8):
        S("v", lambda e, k=k: e.tensor_add(acc[0:1, :], acc[0:1, :], flat[0:1, 4 * k:4 * k + 4]))
    S("v", lambda e: e.tensor_scalar(scr[0:1, :], acc[0:1, :], NORM, None, ALU.mult))
    S("g", lambda e: e.dma_start(io["sc"][0:1, :], scr[0:1, :]))
    S("t", lambda e: e.matmul(psB[:, :], ctx.onec[0:1, :], scr[0:1, :], start=True, stop=True))
    S("v", lambda e: e.tensor_scalar(thbc[:, :], psB[:, :], 0.5, None, ALU.mult))
    jobs = [("w1s", "tq1s", [(128 * j, 128, 128 * j) for j in range(2)], 0),
            ("w3s", "tq3s", [(128 * j, 128, 128 * j) for j in range(2)], 1),
            ("w2s", "tq2s", [(128 * j, 128 if j < 5 else 64, 128 * j) for j in range(6)], 2)]
    for name, oname, slices, col in jobs:
        width = HID if name != "w2s" else D
        for (r0, nr, o0) in slices:
            S("g", lambda e, name=name, r0=r0, nr=nr, width=width: e.dma_start(
                wbuf[0:nr, 0:width], io[name][r0:r0 + nr, 0:width]))
            S("s", lambda e, nr=nr, width=width: e.activation(sg[0:nr, 0:width], wbuf[0:nr, 0:width], AF.Sign))
            S("s", lambda e, nr=nr, width=width: e.activation(ab[0:nr, 0:width], wbuf[0:nr, 0:width], AF.Abs))
            S("v", lambda e, nr=nr, width=width, col=col: e.tensor_scalar(
                ab[0:nr, 0:width], ab[0:nr, 0:width], thbc[0:nr, col:col + 1], None, ALU.is_gt))
            S("v", lambda e, nr=nr, width=width: e.tensor_tensor(
                tq[0:nr, 0:width], sg[0:nr, 0:width], ab[0:nr, 0:width], ALU.mult))
            S("g", lambda e, oname=oname, o0=o0, nr=nr, width=width: e.dma_start(
                io[oname][o0:o0 + nr, 0:width], tq[0:nr, 0:width]))
    aux = ctx.sem("aux")
    A0 = ctx.cnt["aux"]
    incs = [16 if en == "g" else 1 for en, _ in steps]
    tgts = [A0 + sum(incs[:i]) for i in range(len(steps))]
    with nc.Block() as blk:
        for eng_name, body in [("v", "vector"), ("s", "scalar"), ("t", "tensor"), ("g", "gpsimd")]:
            def mk(eng_name=eng_name):
                def emit(e):
                    for i, (en, fn) in enumerate(steps):
                        if en != eng_name:
                            continue
                        e.wait_ge(aux, tgts[i])
                        fn(e).then_inc(aux, incs[i])
                return emit
            getattr(blk, body)(mk())
        @blk.sync
        def _(sy):
            sy.wait_ge(aux, A0 + sum(incs))
    ctx.cnt["aux"] = A0 + sum(incs)
    es.close()



def emit_p12(ctx):
    nc, io = ctx.nc, ctx.io
    es0 = contextlib.ExitStack()
    sb0 = lambda n, s, d: es0.enter_context(nc.sbuf_tensor(n, s, d))
    scb = sb0("pCsc", [128, 4], F32)
    scrow = sb0("pCsr", [1, 4], F32)
    psB = es0.enter_context(nc.psum_tensor("pCpB", [128, 4], F32))
    a0 = ctx.cnt["ald"]
    with nc.Block() as blk:
        @blk.sync
        def _(sy):
            sy.dma_start(scrow[0:1, :], io["sc"][0:1, :]).then_inc(ctx.sem("ald"), 16)
            sy.wait_ge(ctx.sem("ald"), a0 + 16)
            ctx.cnt["ald"] = a0 + 16
    steps = []
    def S(eng, fn):
        steps.append((eng, fn))
    S("t", lambda e: e.matmul(psB[:, :], ctx.onec[0:1, :], scrow[0:1, :], start=True, stop=True))
    S("v", lambda e: e.tensor_copy(scb[:, :], psB[:, :]))
    aux = ctx.sem("aux")
    A0 = ctx.cnt["aux"]
    tgts = [A0 + i for i in range(len(steps))]
    with nc.Block() as blk:
        for eng_name, body in [("v", "vector"), ("t", "tensor")]:
            def mk(eng_name=eng_name):
                def emit(e):
                    for i, (en, fn) in enumerate(steps):
                        if en != eng_name:
                            continue
                        e.wait_ge(aux, tgts[i])
                        fn(e).then_inc(aux, 1)
                return emit
            getattr(blk, body)(mk())
    ctx.cnt["aux"] = A0 + len(steps)

    for th in range(2):
        t0 = th * 512
        esT = contextlib.ExitStack()
        sbT = lambda n, s, d: esT.enter_context(nc.sbuf_tensor(n, s, d))
        g1 = [sbT(f"pCg{th}_{k}", [128, 512], BF16) for k in range(44)]
        xnf = [sbT(f"pCxn{th}_{k}", [128, 512], BF16) for k in range(16)]
        s1T = [sbT(f"pCs1{th}_{k}", [128, 512], F32) for k in range(16)]
        x1T = [sbT(f"pCx1{th}_{k}", [128, 512], F32) for k in range(16)]
        wb = [sbT(f"pCwb{th}_{i}", [128, 44 * 128], BF16) for i in range(3)]
        gsc = [sbT(f"pCgs{th}_{i}", [128, 512], F32) for i in range(2)]
        ps = [esT.enter_context(nc.psum_tensor(f"pCp{th}_{i}", [128, 512], F32)) for i in range(4)]
        a0 = ctx.cnt["ald"]
        with nc.Block() as blk:
            @blk.sync
            def _(sy):
                n = 0
                for k in range(16):
                    sy.dma_start(xnf[k][:, :], io["xn1tm"][t0:t0 + 512, k * 128:(k + 1) * 128],
                                 transpose=True).then_inc(ctx.sem("ald"), 16); n += 1
                    sy.dma_start(s1T[k][:, :], io["s1fm"][k * 128:(k + 1) * 128, t0:t0 + 512]).then_inc(ctx.sem("ald"), 16); n += 1
                    sy.dma_start(x1T[k][:, :], io["x1fm"][k * 128:(k + 1) * 128, t0:t0 + 512]).then_inc(ctx.sem("ald"), 16); n += 1
                sy.wait_ge(ctx.sem("ald"), a0 + 16 * n)
                ctx.cnt["ald"] = a0 + 16 * n
        def ev3(ve, ji, ti, p, ot_, stile):
            return ve.tensor_scalar(g1[ji][:, :], p[:, :], scb[:, 1:2], None, ALU.mult)

        linear(ctx, dict(
            name=f"h3_{th}", wt=io["tq3"], K=D, N=HID, T=512, tw=512,
            x_tile=lambda kc: xnf[kc],
            psums=ps, wbufs=wb, evac_v=ev3, out_ap=None))

        def ev1s(se, ji, ti, p, ot_, stile):
            return se.activation(stile[:, :], p[:, :], AF.Silu, scale=scb[:, 0:1])

        def ev1v(ve, ji, ti, p, ot_, stile):
            return ve.tensor_tensor(g1[ji][:, :], g1[ji][:, :], stile[:, :], ALU.mult)

        linear(ctx, dict(
            name=f"h1_{th}", wt=io["tq1"], K=D, N=HID, T=512, tw=512,
            x_tile=lambda kc: xnf[kc],
            psums=ps, wbufs=wb, sbufs=gsc, evac_s=ev1s, evac_v=ev1v, out_ap=None))

        wb2 = [esT.enter_context(nc.sbuf_tensor(f"pCw2{th}_{i}", [128, 44 * 128], BF16)) for i in range(2)]
        ob = [esT.enter_context(nc.sbuf_tensor(f"pCob{th}_{i}", [128, 512], F32)) for i in range(4)]

        def evg(se, ji, ti, p, ot_, stile):
            return se.activation(stile[:, :], p[:, :], AF.Silu, scale=scb[:, 2:3])

        def evx(ve, ji, ti, p, ot_, stile):
            ve.tensor_tensor(ot_[:, :], stile[:, :], s1T[ji][:, :], ALU.mult)
            return ve.tensor_tensor(ot_[:, :], ot_[:, :], x1T[ji][:, :], ALU.add)

        linear(ctx, dict(
            name=f"w2_{th}", wt=io["tq2"], K=HID, N=D, T=512, tw=512,
            x_tile=lambda kc: g1[kc],
            psums=ps, wbufs=wb2, sbufs=gsc, obufs=ob, evac_s=evg, evac_v=evx,
            out_ap=lambda ji, ti, t0=t0: io["x2fm"][ji * 128:(ji + 1) * 128, t0:t0 + 512]))
        esT.close()

    esF = contextlib.ExitStack()
    xr = esF.enter_context(nc.sbuf_tensor("pCxr", [128, 16 * 128], F32))
    otile = esF.enter_context(nc.sbuf_tensor("pCot", [128, D], F32))
    psT = esF.enter_context(nc.psum_tensor("pCpT", [128, 128], F32))
    x2src = io["x2fm"].rearrange("(a p) t -> p a t", p=128)
    steps = []
    for i in range(8):
        S("g", lambda e, i=i: e.dma_start(xr[:, :].rearrange("p (a t) -> p a t", t=128),
                                          x2src[:, :, i * 128:(i + 1) * 128]))
        for kc in range(16):
            S("t", lambda e, kc=kc: e.transpose(psT[:, :], xr[:, kc * 128:(kc + 1) * 128],
                                                ctx.eye[:, :]))
            S("v", lambda e, kc=kc: e.tensor_copy(otile[:, kc * 128:(kc + 1) * 128], psT[:, :]))
        S("g", lambda e, i=i: e.dma_start(io["ys"][i * 128:(i + 1) * 128, :], otile[:, :]))
    A0 = ctx.cnt["aux"]
    incs = [16 if en == "g" else 1 for en, _ in steps]
    tgts = [A0 + sum(incs[:i]) for i in range(len(steps))]
    with nc.Block() as blk:
        for eng_name, body in [("v", "vector"), ("t", "tensor"), ("g", "gpsimd")]:
            def mk(eng_name=eng_name):
                def emit(e):
                    for i, (en, fn) in enumerate(steps):
                        if en != eng_name:
                            continue
                        e.wait_ge(aux, tgts[i])
                        fn(e).then_inc(aux, incs[i])
                return emit
            getattr(blk, body)(mk())
        @blk.sync
        def _(sy):
            sy.wait_ge(aux, A0 + sum(incs))
    ctx.cnt["aux"] = A0 + sum(incs)
    esF.close()
    es0.close()


class DepSteps:

    ENGS = [("v", "vector"), ("s", "scalar"), ("t", "tensor"), ("g", "gpsimd"), ("p", "gpsimd")]

    def __init__(self, ctx):
        self.ctx = ctx
        self.steps = []

    def S(self, eng, fn, deps=()):
        self.steps.append((eng, fn, [d for d in deps if d is not None]))
        return len(self.steps) - 1

    def emit(self):
        ctx = self.ctx
        nc = ctx.nc
        sems = {e: ctx.sem("x" + e) for e, _ in self.ENGS}
        base = {e: ctx.cnt["x" + e] for e, _ in self.ENGS}
        cum = {}
        run = dict(base)
        for i, (en, fn, deps) in enumerate(self.steps):
            run[en] += 16 if en == "g" else 1
            cum[i] = run[en]
        gp_classes = [e for e in ("g", "p") if any(en == e for en, _, _ in self.steps)]
        with nc.Block() as blk:
            for eng, body in [("v", "vector"), ("s", "scalar"), ("t", "tensor")]:
                if not any(en == eng for en, _, _ in self.steps):
                    continue

                def mk(eng=eng):
                    def emit_body(e):
                        water = {o: base[o] for o, _ in self.ENGS}
                        for i, (en, fn, deps) in enumerate(self.steps):
                            if en != eng:
                                continue
                            for d in deps:
                                den = self.steps[d][0]
                                if den == eng:
                                    continue
                                tgt = cum[d]
                                if tgt > water[den]:
                                    e.wait_ge(sems[den], tgt)
                                    water[den] = tgt
                            fn(e).then_inc(sems[eng], 1)
                    return emit_body
                getattr(blk, body)(mk())
            if gp_classes:
                @blk.gpsimd
                def _(e):
                    water = {o: base[o] for o, _ in self.ENGS}
                    for i, (en, fn, deps) in enumerate(self.steps):
                        if en not in ("g", "p"):
                            continue
                        for d in deps:
                            den = self.steps[d][0]
                            if den in ("g", "p") and den == en:
                                continue
                            if den in ("g", "p") and den != en:
                                pass
                            elif den == en:
                                continue
                            tgt = cum[d]
                            if self.steps[d][0] in ("g", "p"):
                                continue
                            if tgt > water[den]:
                                e.wait_ge(sems[den], tgt)
                                water[den] = tgt
                        fn(e).then_inc(sems[en], 16 if en == "g" else 1)
                        if en == "g":
                            e.wait_ge(sems[en], cum[i])
            @blk.sync
            def _(sy):
                for eng in set(en for en, _, _ in self.steps):
                    if run[eng] > base[eng]:
                        sy.wait_ge(sems[eng], run[eng])
        for eng, _ in self.ENGS:
            ctx.cnt["x" + eng] = run[eng]


def emit_p45(ctx, gather_cb):
    nc, io = ctx.nc, ctx.io
    ctx.p45_es0 = contextlib.ExitStack()
    ssq = ctx.p45_es0.enter_context(nc.sbuf_tensor("fssq", [1, T], F32))
    esP = contextlib.ExitStack()
    sbP = lambda n, s, d: esP.enter_context(nc.sbuf_tensor(n, s, d))
    eloc = sbP("fEl", [64, T], F32)
    eglo = sbP("fEg", [64, T], F32)
    LgT = sbP("fLgT", [128, 8 * 64], F32)
    RT = sbP("fRT", [128, 8 * 64], F32)
    wT = sbP("fwT", [128, 8 * 64], F32)
    Ecol = sbP("fE", [64, 16], F32)
    Dflat = sbP("fDf", [1, 512], F32)
    Eflat = sbP("fEf", [1, 576], F32)
    dpc = sbP("fdp", [64, 64], F32)
    caus = sbP("fca", [128, 128], F32)
    zt = sbP("fz", [128, 128], F32)
    elocT = sbP("felT", [128, 8 * 64], F32)
    nw64 = sbP("fnw", [64, 64], F32)
    Sf = sbP("fS", [64, 4096], F32)
    Sin = sbP("fsin", [64, 4096], BF16)
    xtm = [sbP(f"fxt{i}", [128, 1024], BF16) for i in range(8)]
    btm = [sbP(f"fbt{i}", [128, 1024], BF16) for i in range(8)]
    ctx.p45 = {"ssq": ssq}

    esQ = contextlib.ExitStack()
    sbQ = lambda n, s, d: esQ.enter_context(nc.sbuf_tensor(n, s, d))
    dtr_ = sbQ("fdtr", [64, T], F32)
    dts = sbQ("fdts", [64, T], F32)
    tmpa = sbQ("ftma", [64, T], F32)
    tmpb = sbQ("ftmb", [64, T], F32)
    lndt = sbQ("fln", [64, T], F32)
    Lg = sbQ("flg", [64, T], F32)
    Rt = sbQ("fr", [64, T], F32)
    wsc = sbQ("fw", [64, T], F32)
    dE = sbQ("fdE", [64, 9], F32)
    dtb = sbQ("fdtb", [64, 1], F32)
    alog = sbQ("fal", [64, 1], F32)
    Acol = sbQ("fA", [64, 1], F32)
    psP = esQ.enter_context(nc.psum_tensor("fpP", [128, 128], F32))
    dram_tmp = nc.dram_tensor("f45dram", [64, 9], F32).ap()

    a0 = ctx.cnt["ald"]
    with nc.Block() as blk:
        @blk.sync
        def _(sy):
            n = 0
            for i in range(8):
                sy.dma_start(xtm[i][:, :], io["xmf"][0:1024, i * 128:(i + 1) * 128], transpose=True).then_inc(ctx.sem("ald"), 16); n += 1
                sy.dma_start(btm[i][:, :], io["bmf"][0:1024, i * 128:(i + 1) * 128], transpose=True).then_inc(ctx.sem("ald"), 16); n += 1
            for ap, name in [(dtr_, "dtr"), (dtb, "dtb"), (alog, "alog"), (dpc, "dpc64"),
                             (caus, "causT"), (nw64, "nw64")]:
                sy.dma_start(ap[:, :], io[name][:, :]).then_inc(ctx.sem("ald"), 16); n += 1
            sy.wait_ge(ctx.sem("ald"), a0 + 16 * n)
            ctx.cnt["ald"] = a0 + 16 * n
        @blk.vector
        def _(v):
            v.memset(zt[:, :], 0.0)
            v.memset(Ecol[:, :], 0.0)
            v.memset(Sf[:, :], 0.0)
            v.memset(ssq[0:1, :], 0.0)

    steps = []
    def S(eng, fn):
        steps.append((eng, fn))
    S("s", lambda e: e.activation(dts[:, :], dtr_[:, :], AF.Identity, bias=dtb[:, 0:1]))
    S("s", lambda e: e.activation(tmpa[:, :], dts[:, :], AF.Abs))
    S("s", lambda e: e.activation(tmpb[:, :], tmpa[:, :], AF.Exp, scale=-1.0))
    S("s", lambda e: e.activation(tmpa[:, :], tmpb[:, :], AF.Ln, bias=1.0))
    S("v", lambda e: e.tensor_relu(tmpb[:, :], dts[:, :]))
    S("v", lambda e: e.tensor_add(dts[:, :], tmpa[:, :], tmpb[:, :]))
    S("s", lambda e: e.activation(lndt[:, :], dts[:, :], AF.Ln))
    S("s", lambda e: e.activation(Acol[:, 0:1], alog[:, 0:1], AF.Exp))
    S("v", lambda e: e.tensor_scalar(Acol[:, 0:1], Acol[:, 0:1], -1.0, None, ALU.mult))
    S("v", lambda e: e.tensor_scalar(tmpa[:, :], dts[:, :], Acol[:, 0:1], None, ALU.mult))
    S("v", lambda e: e.tensor_tensor_scan(Lg[:, :], tmpa[:, :], tmpa[:, :], 0.0, ALU.add, ALU.bypass))
    S("v", lambda e: e.tensor_copy(Ecol[:, 1:9], Lg[:, 127::128]))
    S("v", lambda e: e.tensor_sub(Rt[:, :], lndt[:, :], Lg[:, :]))
    for c in range(NCH):
        S("v", lambda e, c=c: e.tensor_scalar(tmpa[:, c * 128:(c + 1) * 128], Lg[:, c * 128:(c + 1) * 128], Ecol[:, c:c + 1], None, ALU.subtract))
        S("v", lambda e, c=c: e.tensor_scalar(tmpb[:, c * 128:(c + 1) * 128], Rt[:, c * 128:(c + 1) * 128], Ecol[:, c + 1:c + 2], None, ALU.add))
    S("s", lambda e: e.activation(eloc[:, :], tmpa[:, :], AF.Exp))
    S("s", lambda e: e.activation(eglo[:, :], Lg[:, :], AF.Exp))
    S("s", lambda e: e.activation(wsc[:, :], tmpb[:, :], AF.Exp))
    S("v", lambda e: e.tensor_sub(dE[:, 0:8], Ecol[:, 1:9], Ecol[:, 0:8]))
    S("s", lambda e: e.activation(dE[:, 0:8], dE[:, 0:8], AF.Exp))
    S("s", lambda e: e.activation(dE[:, 8:9], Ecol[:, 0:8][:, 0:1], AF.Identity))
    S("g", lambda e: e.dma_start(dram_tmp[:, 0:8], dE[:, 0:8]))
    for c in range(NCH):
        S("g", lambda e, c=c: e.dma_start(Dflat[0:1, c * 64:(c + 1) * 64],
                                          dram_tmp[:, c:c + 1].rearrange("h c -> c h"),
                                          allow_slow_non_contiguous=True))
    S("s", lambda e: e.activation(dE[:, :], Ecol[:, 0:9], AF.Exp))
    S("g", lambda e: e.dma_start(dram_tmp[:, 0:9], dE[:, 0:9]))
    for c in range(9):
        S("g", lambda e, c=c: e.dma_start(Eflat[0:1, c * 64:(c + 1) * 64],
                                          dram_tmp[:, c:c + 1].rearrange("h c -> c h"),
                                          allow_slow_non_contiguous=True))
    for c in range(NCH):
        S("t", lambda e, c=c: e.transpose(psP[:, 0:64], Lg[:, c * 128:(c + 1) * 128], ctx.eye[0:64, 0:64]))
        S("v", lambda e, c=c: e.tensor_copy(LgT[:, c * 64:(c + 1) * 64], psP[:, 0:64]))
        S("t", lambda e, c=c: e.transpose(psP[:, 0:64], Rt[:, c * 128:(c + 1) * 128], ctx.eye[0:64, 0:64]))
        S("v", lambda e, c=c: e.tensor_copy(RT[:, c * 64:(c + 1) * 64], psP[:, 0:64]))
        S("t", lambda e, c=c: e.transpose(psP[:, 0:64], wsc[:, c * 128:(c + 1) * 128], ctx.eye[0:64, 0:64]))
        S("v", lambda e, c=c: e.tensor_copy(wT[:, c * 64:(c + 1) * 64], psP[:, 0:64]))
        S("t", lambda e, c=c: e.transpose(psP[:, 0:64], eloc[:, c * 128:(c + 1) * 128], ctx.eye[0:64, 0:64]))
        S("v", lambda e, c=c: e.tensor_copy(elocT[:, c * 64:(c + 1) * 64], psP[:, 0:64]))
    aux = ctx.sem("aux")
    A0 = ctx.cnt["aux"]
    incs = [16 if en == "g" else 1 for en, _ in steps]
    tgts = [A0 + sum(incs[:i]) for i in range(len(steps))]
    with nc.Block() as blk:
        for eng_name, body in [("v", "vector"), ("s", "scalar"), ("t", "tensor"), ("g", "gpsimd")]:
            def mk(eng_name=eng_name):
                def emit(e):
                    for i, (en, fn) in enumerate(steps):
                        if en != eng_name:
                            continue
                        e.wait_ge(aux, tgts[i])
                        fn(e).then_inc(aux, incs[i])
                return emit
            getattr(blk, body)(mk())
    ctx.cnt["aux"] = A0 + sum(incs)
    esQ.close()

    esS = contextlib.ExitStack()
    sbS = lambda n, s, d: esS.enter_context(nc.sbuf_tensor(n, s, d))
    bw = [sbS(f"fbw{i}", [128, 64], BF16) for i in range(2)]
    st1 = [sbS(f"fst{i}", [64, 64], F32) for i in range(2)]
    Dbc = sbS("fDbc", [64, 64], F32)
    Dbt = sbS("fDbt", [64, 64], F32)
    psS = [esS.enter_context(nc.psum_tensor(f"fpS{i}", [64, 64], F32)) for i in range(2)]
    psD = esS.enter_context(nc.psum_tensor("fpD", [64, 64], F32))
    ds = DepSteps(ctx)
    Sd = ds.S
    last = {}
    for c in range(NCH):
        i_dm = Sd("t", lambda e, c=c: e.matmul(psD[:, :], ctx.onec[0:1, 0:64],
                                               Dflat[0:1, c * 64:(c + 1) * 64], start=True, stop=True),
                  deps=[last.get("Dbcuse")])
        i_db0 = Sd("v", lambda e: e.tensor_copy(Dbt[:, :], psD[:, :]), deps=[i_dm, last.get("Dbtuse")])
        i_db = Sd("v", lambda e: e.tensor_copy(Dbc[:, :], Dbt[:, :]), deps=[last.get("Dbcuse")])
        last["Dbtuse"] = i_db
        lastdb = i_db
        for h in range(NH):
            p_ = h % 2
            i_bw = Sd("s", lambda e, c=c, h=h, p_=p_: e.activation(
                bw[p_][:, :], btm[c][:, (h // 4) * 64:(h // 4) * 64 + 64], AF.Identity,
                scale=wT[:, c * 64 + h:c * 64 + h + 1]), deps=[last.get(("bwuse", p_))])
            i_ds = Sd("t", lambda e, c=c, h=h, p_=p_: e.matmul(
                psS[p_][:, :], bw[p_][:, :], xtm[c][:, (h // 4) * 64:(h // 4) * 64 + 64],
                start=True, stop=True), deps=[i_bw, last.get(("psS", p_))])
            last[("bwuse", p_)] = i_ds
            i_t1 = Sd("v", lambda e, h=h, p_=p_: e.tensor_scalar(
                st1[p_][:, :], Sf[:, h * 64:(h + 1) * 64], Dbc[:, h:h + 1], None, ALU.mult),
                deps=[i_db, last.get(("Sup", h)), last.get(("st1use", p_))])
            i_t2 = Sd("v", lambda e, h=h, p_=p_: e.tensor_tensor(
                Sf[:, h * 64:(h + 1) * 64], st1[p_][:, :], psS[p_][:, :], ALU.add),
                deps=[i_ds, i_t1])
            last[("psS", p_)] = i_t2
            last[("st1use", p_)] = i_t2
            last[("Sup", h)] = i_t2
        last["Dbcuse"] = last[("Sup", 63)]
    prev = None
    for h in range(NH):
        prev = Sd("g", lambda e, h=h: e.dma_start(io["sfin"][h * 64:(h + 1) * 64, :],
                                                  Sf[:, h * 64:(h + 1) * 64]),
                  deps=[last[("Sup", h)], prev])
    Sd("g", lambda e: e.dma_start(io["elast"][:, 0:1], Ecol[:, 8:9]), deps=[prev])
    ds.emit()
    esS.close()

    gather_cb(ctx)

    esH = contextlib.ExitStack()
    sbH = lambda n, s, d: esH.enter_context(nc.sbuf_tensor(n, s, d))
    sel64 = sbH("fhs64", [64, 64 * 64], F32)
    Esb = sbH("fhE", [64, 8], F32)
    emk = sbH("fhem", [64, 8], F32)
    Wbc = sbH("fhW", [64, 64 * 8], F32)
    acc = sbH("fhacc", [64, 4096], F32)
    skb = sbH("fhskb", [64, 4096], F32)
    sgg = sbH("fhsgg", [64, 4096], F32)
    psW = esH.enter_context(nc.psum_tensor("fhpW", [64, 8], F32))
    a0 = ctx.cnt["ald"]
    with nc.Block() as blk:
        @blk.sync
        def _(sy):
            n = 0
            for k in range(8):
                sy.dma_start(Esb[:, k:k + 1], io["egath"][k * 64:(k + 1) * 64, 0:1]).then_inc(ctx.sem("ald"), 16); n += 1
            sy.dma_start(emk[:, :], io["emask64"][:, :]).then_inc(ctx.sem("ald"), 16); n += 1
            sy.dma_start(sel64[:, :], io["sel64"][:, :]).then_inc(ctx.sem("ald"), 16); n += 1
            sy.wait_ge(ctx.sem("ald"), a0 + 16 * n)
            ctx.cnt["ald"] = a0 + 16 * n
        @blk.vector
        def _(v):
            v.memset(acc[:, :], 0.0)
    steps = []
    S("v", lambda e: e.tensor_mul(Esb[:, :], Esb[:, :], emk[:, :]))
    S("s", lambda e: e.activation(Esb[:, :], Esb[:, :], AF.Exp))
    for h in range(64):
        S("t", lambda e, h=h: e.matmul(psW[:, :], sel64[:, h * 64:(h + 1) * 64], Esb[:, :],
                                       start=True, stop=True))
        S("v", lambda e, h=h: e.tensor_copy(Wbc[:, h * 8:(h + 1) * 8], psW[:, :]))
    for k in range(7):
        sgsrc = io["sgath"].rearrange("(c hh n) p -> c n hh p", c=8, n=64)
        for d in range(4):
            S("g", lambda e, k=k, d=d: e.dma_start(
                skb[:, d * 16 * 64:(d + 1) * 16 * 64].rearrange("n (hh p) -> n hh p", p=64),
                sgsrc[k, :, d * 16:(d + 1) * 16, :]))
        for h in range(64):
            S("v", lambda e, k=k, h=h: e.tensor_scalar(
                acc[:, h * 64:(h + 1) * 64], acc[:, h * 64:(h + 1) * 64],
                Wbc[:, h * 8 + k:h * 8 + k + 1], None, ALU.mult))
        S("v", lambda e, k=k: e.tensor_scalar(sgg[:, :], skb[:, :], emk[:, k:k + 1], None, ALU.mult))
        S("v", lambda e: e.tensor_add(acc[:, :], acc[:, :], sgg[:, :]))
    S("v", lambda e: e.tensor_copy(Sin[:, :], acc[:, :]))
    if "fdbg1" in io:
        S("g", lambda e: e.dma_start(io["fdbg1"][0:1, :], Eflat[0:1, :]))
        S("g", lambda e: e.dma_start(io["fdbg2"][:, :], Sin[:, :]))
    A0 = ctx.cnt["aux"]
    incs = [16 if en == "g" else 1 for en, _ in steps]
    tgts = [A0 + sum(incs[:i]) for i in range(len(steps))]
    with nc.Block() as blk:
        for eng_name, body in [("v", "vector"), ("s", "scalar"), ("t", "tensor"), ("g", "gpsimd")]:
            def mk(eng_name=eng_name):
                def emit(e):
                    for i, (en, fn) in enumerate(steps):
                        if en != eng_name:
                            continue
                        e.wait_ge(aux, tgts[i])
                        fn(e).then_inc(aux, incs[i])
                return emit
            getattr(blk, body)(mk())
    ctx.cnt["aux"] = A0 + sum(incs)
    esH.close()

    esY = contextlib.ExitStack()
    sbY = lambda n, s, d: esY.enter_context(nc.sbuf_tensor(n, s, d))
    cc = [sbY(f"fcc{i}", [64, 64 * 128], BF16) for i in range(2)]
    zc = sbY("fzc", [64, 64 * 128], BF16)
    clb = sbY("fcl", [64, 64 * 128], BF16)
    yfb = [sbY(f"fyf{i}", [64, 64 * 128], BF16) for i in range(2)]
    xfc0 = sbY("fxf", [64, 16 * 128], BF16)
    xfc = [xfc0, xfc0]
    bfc0 = sbY("fbf", [64, 16 * 128], BF16)
    bfc = [bfc0, bfc0]
    Sb = sbY("fSb", [64, 4096], BF16)
    Sgb = [sbY(f"fSg{i}", [64, 64], BF16) for i in range(2)]
    Sgs = [sbY(f"fSgs{i}", [64, 64], F32) for i in range(2)]
    Ebc = sbY("fEbc", [64, 64], F32)
    Dbc2 = sbY("fDbc2", [64, 64], F32)
    Ebt2 = sbY("fEbt2", [64, 64], F32)
    Dbt2 = sbY("fDbt2", [64, 64], F32)
    bct = [sbY(f"fbc{i}", [128, 128], F32) for i in range(2)]
    msk = [sbY(f"fmk{i}", [128, 128], F32) for i in range(2)]
    Mp = [sbY(f"fMp{i}", [128, 128], F32) for i in range(2)]
    Gm = [sbY(f"fGm{i}", [128, 128], BF16) for i in range(2)]
    Gm2 = [sbY(f"fGm2{i}", [128, 128], BF16) for i in range(2)]
    cltmp = [sbY(f"fclt{i}", [64, 128], BF16) for i in range(2)]
    xsc = [sbY(f"fxs{i}", [64, 128], F32) for i in range(2)]
    szb = [sbY(f"fsz{i}", [64, 128], F32) for i in range(2)]
    sqb = [sbY(f"fsq{i}", [64, 128], F32) for i in range(2)]
    ytb = [sbY(f"fyt{i}", [64, 128], F32) for i in range(2)]
    bw2 = [sbY(f"fbw2{i}", [128, 64], BF16) for i in range(2)]
    st2 = [sbY(f"fst2{i}", [64, 64], F32) for i in range(2)]
    psA = [esY.enter_context(nc.psum_tensor(f"fpA{i}", [128, 128], F32)) for i in range(2)]
    psG = [esY.enter_context(nc.psum_tensor(f"fpG{i}", [128, 128], F32)) for i in range(2)]
    psY = [esY.enter_context(nc.psum_tensor(f"fpY{i}", [64, 128], F32)) for i in range(2)]
    psS2 = [esY.enter_context(nc.psum_tensor(f"fpS2{i}", [64, 128], F32)) for i in range(2)]
    csrc = io["cmf"].rearrange("(a p) t -> p a t", p=64)
    zsrc = io["zf"].rearrange("(a p) t -> p a t", p=64)
    xsrc = io["xmf"].rearrange("(a p) t -> p a t", p=64)
    bsrc = io["bmf"].rearrange("(a p) t -> p a t", p=64)
    ydst = io["yfh"].rearrange("(a p) t -> p a t", p=64)

    def hv(buf, h, w=128):
        return buf[:, h * 128:h * 128 + w]

    ds = DepSteps(ctx)
    Sd = ds.S
    last = {}
    with nc.Block() as blk0:
        @blk0.vector
        def _(v):
            v.memset(Sf[:, :], 0.0)
            v.memset(Sb[:, :], 0.0)
    for c in range(NCH):
        p_ = c % 2
        ld_c = []
        for dd in range(4):
            ld_c.append(Sd("g", lambda e, c=c, dd=dd, p_=p_: e.dma_start(
                cc[p_][:, dd * 16 * 128:(dd + 1) * 16 * 128].rearrange("p (a t) -> p a t", t=128),
                csrc[:, dd * 16:(dd + 1) * 16, c * 128:(c + 1) * 128]),
                deps=[last.get(("ccuse", p_))]))
        ld_z = []
        for dd in range(4):
            ld_z.append(Sd("g", lambda e, c=c, dd=dd: e.dma_start(
                zc[:, dd * 16 * 128:(dd + 1) * 16 * 128].rearrange("p (a t) -> p a t", t=128),
                zsrc[:, dd * 16:(dd + 1) * 16, c * 128:(c + 1) * 128]),
                deps=[last.get("zcuse")]))
        i_lx = Sd("g", lambda e, c=c, p_=p_: e.dma_start(
            xfc[p_][:, :].rearrange("p (a t) -> p a t", t=128),
            xsrc[:, :, c * 128:(c + 1) * 128]), deps=[last.get(("xfcuse", 0)), last.get(("xfcuse", 1))])
        i_lb = Sd("g", lambda e, c=c, p_=p_: e.dma_start(
            bfc[p_][:, :].rearrange("p (a t) -> p a t", t=128),
            bsrc[:, :, c * 128:(c + 1) * 128]), deps=[last.get(("bfcuse", 0)), last.get(("bfcuse", 1))])
        i_dm = Sd("t", lambda e, c=c: e.matmul(psS2[0][:, 0:64], ctx.onec[0:1, 0:64],
                                               Dflat[0:1, c * 64:(c + 1) * 64], start=True, stop=True),
                  deps=[last.get(("psS2", 0))])
        i_db0 = Sd("v", lambda e: e.tensor_copy(Dbt2[:, :], psS2[0][:, 0:64]),
                   deps=[i_dm, last.get("Dbt2use")])
        i_db = Sd("v", lambda e: e.tensor_copy(Dbc2[:, :], Dbt2[:, :]),
                  deps=[last.get("Dbc2use")])
        i_em = Sd("t", lambda e, c=c: e.matmul(psS2[1][:, 0:64], ctx.onec[0:1, 0:64],
                                               Eflat[0:1, c * 64:(c + 1) * 64], start=True, stop=True),
                  deps=[last.get(("psS2", 1))])
        i_eb0 = Sd("v", lambda e: e.tensor_copy(Ebt2[:, :], psS2[1][:, 0:64]),
                   deps=[i_em, last.get("Ebt2use")])
        i_eb = Sd("v", lambda e: e.tensor_copy(Ebc[:, :], Ebt2[:, :]),
                  deps=[last.get("Ebcuse")])
        last["Dbt2use"] = i_db
        last["Ebt2use"] = i_eb
        last[("psS2", 0)] = i_db
        last[("psS2", 1)] = i_eb
        cl_done = []
        for h in range(NH):
            hp = h % 2
            i_b2 = Sd("s", lambda e, c=c, h=h, hp=hp: e.activation(
                bct[hp][:, 0:64], zt[:, 0:64], AF.Identity,
                bias=elocT[:, c * 64 + h:c * 64 + h + 1]), deps=[last.get(("bctuse", hp))])
            i_sc = Sd("t", lambda e, hp=hp: e.transpose(
                psA[hp][0:64, :], bct[hp][:, 0:64], ctx.eye[:, :]),
                deps=[i_b2, last.get(("psA", hp))])
            last[("bctuse", hp)] = i_sc
            i_cl0 = Sd("v", lambda e, h=h, hp=hp, p_=p_: e.tensor_tensor(
                cltmp[hp][:, :], hv(cc[p_], h), psA[hp][0:64, :], ALU.mult),
                deps=[i_sc, ld_c[h // 16], last.get(("cltmpuse", hp))])
            last[("psA", hp)] = i_cl0
            i_cl = Sd("p", lambda e, h=h, hp=hp: e.tensor_copy(hv(clb, h), cltmp[hp][:, :]),
                      deps=[i_cl0, last.get(("clbuse", h))])
            last[("cltmpuse", hp)] = i_cl
            cl_done.append(i_cl)
        for h in range(NH):
            hp = h % 2
            kv = h // 4
            g = c * 64 + h
            i_b = Sd("s", lambda e, c=c, h=h, hp=hp: e.activation(
                bct[hp][:, :], caus[:, :], AF.Identity, bias=LgT[:, c * 64 + h:c * 64 + h + 1]),
                deps=[last.get(("bctuse", hp))])
            i_tr = Sd("t", lambda e, hp=hp: e.transpose(psA[hp][:, :], bct[hp][:, :], ctx.eye[:, :]),
                      deps=[i_b, last.get(("psA", hp))])
            last[("bctuse", hp)] = i_tr
            i_mp = Sd("s", lambda e, c=c, h=h, hp=hp: e.activation(
                Mp[hp][:, :], psA[hp][:, :], AF.Exp, bias=RT[:, c * 64 + h:c * 64 + h + 1]),
                deps=[i_tr, last.get(("Mpuse", hp))])
            last[("psA", hp)] = i_mp
            i_gp = Sd("t", lambda e, c=c, h=h, hp=hp, p_=p_: e.matmul(
                psG[hp][:, :], bfc[p_][0:64, (h // 4) * 128:(h // 4) * 128 + 128],
                hv(cc[p_], h), start=True, stop=True),
                deps=[i_lb, ld_c[h // 16], last.get(("psG", hp))])
            i_gm0 = Sd("v", lambda e, hp=hp: e.tensor_tensor(Gm[hp][:, :], psG[hp][:, :], Mp[hp][:, :], ALU.mult),
                       deps=[i_gp, i_mp, last.get(("Gmuse", hp))])
            i_gm = Sd("p", lambda e, hp=hp: e.tensor_copy(Gm2[hp][:, :], Gm[hp][:, :]),
                      deps=[i_gm0, last.get(("Gm2use", hp))])
            last[("psG", hp)] = i_gm0
            last[("Mpuse", hp)] = i_gm0
            last[("Gmuse", hp)] = i_gm
            i_sg = Sd("v", lambda e, h=h, hp=hp: e.tensor_scalar(
                Sgs[hp][:, :], Sin[:, h * 64:(h + 1) * 64], Ebc[:, h:h + 1], None, ALU.mult),
                deps=[i_eb, last.get(("Sgsuse", hp))])
            i_sgb = Sd("p", lambda e, hp=hp: e.tensor_copy(Sgb[hp][:, :], Sgs[hp][:, :]),
                       deps=[i_sg, last.get(("Sgbuse", hp))])
            last[("Sgsuse", hp)] = i_sgb
            if c == 0 and h == 6 and "fdbg3" in io:
                Sd("g", lambda e, hp=hp: e.dma_start(io["fdbg3"][:, :], Sgs[hp][:, :]), deps=[i_sg])
                Sd("g", lambda e: e.dma_start(io["fdbg4"][:, :], Ebc[:, :]), deps=[i_eb])
                Sd("g", lambda e, h=h: e.dma_start(io["fdbg5"][:, :], hv(clb, h)), deps=[cl_done[h]])
            i_y1 = Sd("t", lambda e, c=c, h=h, hp=hp: e.matmul(
                psY[hp][:, :], xtm[c][:, (h // 4) * 64:(h // 4) * 64 + 64], Gm2[hp][:, :],
                start=True, stop=False), deps=[i_gm, last.get(("psY", hp))])
            last[("Gm2use", hp)] = i_y1
            i_y2 = Sd("t", lambda e, h=h, hp=hp: e.matmul(
                psY[hp][:, :], Sb[:, h * 64:(h + 1) * 64], hv(clb, h),
                start=False, stop=False), deps=[cl_done[h], last.get(("Sbup", h))])
            i_y3 = Sd("t", lambda e, h=h, hp=hp: e.matmul(
                psY[hp][:, :], Sgb[hp][:, :], hv(clb, h), start=False, stop=True),
                deps=[i_sgb])
            last[("Sgbuse", hp)] = i_y3
            i_xs = Sd("s", lambda e, c=c, h=h, hp=hp, p_=p_: e.activation(
                xsc[hp][:, :], xfc[p_][0:64, (h // 4) * 128:(h // 4) * 128 + 128],
                AF.Identity, scale=dpc[:, h:h + 1]), deps=[i_lx, last.get(("xscuse", hp))])
            i_yt = Sd("v", lambda e, hp=hp: e.tensor_tensor(ytb[hp][:, :], psY[hp][:, :], xsc[hp][:, :], ALU.add),
                      deps=[i_y3, i_xs, last.get(("ytuse", hp))])
            last[("psY", hp)] = i_yt
            last[("xscuse", hp)] = i_yt
            i_sz = Sd("s", lambda e, h=h, hp=hp: e.activation(szb[hp][:, :], hv(zc, h), AF.Silu),
                      deps=[ld_z[h // 16], last.get(("szuse", hp))])
            i_yf = Sd("v", lambda e, h=h, hp=hp, p_=p_: e.tensor_tensor(
                hv(yfb[p_], h), ytb[hp][:, :], szb[hp][:, :], ALU.mult),
                deps=[i_sz, i_yt, last.get(("yfbuse", p_, h))])
            last[("ytuse", hp)] = i_yf
            last[("szuse", hp)] = i_yf
            i_sq = Sd("v", lambda e, h=h, hp=hp, p_=p_: e.tensor_mul(sqb[hp][:, :], hv(yfb[p_], h), hv(yfb[p_], h)),
                      deps=[last.get(("squse", hp))])
            i_nw = Sd("v", lambda e, h=h, p_=p_: e.tensor_scalar(
                hv(yfb[p_], h), hv(yfb[p_], h), nw64[:, h:h + 1], None, ALU.mult))
            i_mq = Sd("t", lambda e, hp=hp: e.matmul(psS2[hp][0:1, 0:128], ctx.ones128[0:64, 0:1], sqb[hp][:, :],
                                                     start=True, stop=True),
                      deps=[i_sq, last.get(("psS2", hp))])
            last[("squse", hp)] = i_mq
            i_sa = Sd("v", lambda e, c=c, hp=hp: e.tensor_tensor(
                ssq[0:1, c * 128:(c + 1) * 128], ssq[0:1, c * 128:(c + 1) * 128],
                psS2[hp][0:1, 0:128], ALU.add), deps=[i_mq, last.get("ssqlast")])
            last[("psS2", hp)] = i_sa
            last["ssqlast"] = i_sa
            last[("yfbuse2", p_, h)] = i_nw
            i_bw = Sd("s", lambda e, c=c, h=h, hp=hp: e.activation(
                bw2[hp][:, :], btm[c][:, (h // 4) * 64:(h // 4) * 64 + 64], AF.Identity,
                scale=wT[:, c * 64 + h:c * 64 + h + 1]), deps=[last.get(("bw2use", hp))])
            i_ds2 = Sd("t", lambda e, c=c, h=h, hp=hp: e.matmul(
                psS2[hp][:, 0:64], bw2[hp][:, :], xtm[c][:, (h // 4) * 64:(h // 4) * 64 + 64],
                start=True, stop=True), deps=[i_bw, last.get(("psS2", hp))])
            last[("bw2use", hp)] = i_ds2
            i_t1 = Sd("v", lambda e, h=h, hp=hp: e.tensor_scalar(
                st2[hp][:, :], Sf[:, h * 64:(h + 1) * 64], Dbc2[:, h:h + 1], None, ALU.mult),
                deps=[i_db, last.get(("Sup2", h)), last.get(("st2use", hp))])
            i_t2 = Sd("v", lambda e, h=h, hp=hp: e.tensor_tensor(
                Sf[:, h * 64:(h + 1) * 64], st2[hp][:, :], psS2[hp][:, 0:64], ALU.add),
                deps=[i_ds2, i_t1])
            last[("psS2", hp)] = i_t2
            last[("st2use", hp)] = i_t2
            last[("Sup2", h)] = i_t2
            i_sb = Sd("p", lambda e, h=h: e.tensor_copy(Sb[:, h * 64:(h + 1) * 64], Sf[:, h * 64:(h + 1) * 64]),
                      deps=[i_y2, i_t2])
            last[("Sbup", h)] = i_sb
            last[("clbuse", h)] = i_y3
        last["Dbc2use"] = last[("Sup2", 63)]
        last["Ebcuse"] = last[("Sgsuse", 1)]
        st_c = None
        for dd in range(4):
            st_c = Sd("g", lambda e, c=c, dd=dd, p_=p_: e.dma_start(
                ydst[:, dd * 16:(dd + 1) * 16, c * 128:(c + 1) * 128],
                yfb[p_][:, dd * 16 * 128:(dd + 1) * 16 * 128].rearrange("p (a t) -> p a t", t=128)),
                deps=[last[("yfbuse2", p_, 16 * dd + 15)], last["ssqlast"], st_c])
        for h in range(NH):
            last[("yfbuse", p_, h)] = st_c
        last[("ccuse", p_)] = last[("psG", 1)]
        last["zcuse"] = last[("szuse", 1)]
        last[("xfcuse", p_)] = last[("xscuse", 1)]
        last[("bfcuse", p_)] = last[("psG", 1)]
    ds.emit()
    esY.close()
    esP.close()

def emit_p5c(ctx):
    nc, io = ctx.nc, ctx.io
    ssq = ctx.p45["ssq"]
    esC = contextlib.ExitStack()
    sbC = lambda n, s, d: esC.enter_context(nc.sbuf_tensor(n, s, d))
    yft = [sbC(f"c5y{k}", [128, T], BF16) for k in range(32)]
    mos = [sbC(f"c5mo{k}", [128, T], F32) for k in range(16)]
    rrow = sbC("c5rr", [1, T], F32)
    rbc = sbC("c5rb", [128, T], F32)
    wb = [sbC(f"c5wb{i}", [128, 32 * 128], BF16) for i in range(3)]
    mt = sbC("c5mt", [128, T], F32)
    xt_ = sbC("c5xt2", [128, T], F32)
    ps = [esC.enter_context(nc.psum_tensor(f"c5ps{i}", [128, 512], F32)) for i in range(4)]
    psB = esC.enter_context(nc.psum_tensor("c5psB", [128, 512], F32))
    a0 = ctx.cnt["ald"]
    with nc.Block() as blk:
        @blk.sync
        def _(sy):
            for k in range(32):
                sy.dma_start(yft[k][:, :], io["yfh"][k * 128:(k + 1) * 128, :]).then_inc(ctx.sem("ald"), 16)
            sy.wait_ge(ctx.sem("ald"), a0 + 16 * 32)
            ctx.cnt["ald"] = a0 + 16 * 32
    steps = []
    def S(eng, fn):
        steps.append((eng, fn))
    S("s", lambda e: e.activation(rrow[0:1, :], ssq[0:1, :], AF.Ln, scale=1.0 / DI, bias=EPS0))
    S("s", lambda e: e.activation(rrow[0:1, :], rrow[0:1, :], AF.Exp, scale=-0.5))
    for j in range(2):
        S("t", lambda e, j=j: e.matmul(psB[:, :], ctx.onec[0:1, :], rrow[0:1, j * 512:(j + 1) * 512],
                                       start=True, stop=True))
        S("v", lambda e, j=j: e.tensor_copy(rbc[:, j * 512:(j + 1) * 512], psB[:, :]))
    aux = ctx.sem("aux")
    A0 = ctx.cnt["aux"]
    tgts = [A0 + i for i in range(len(steps))]
    with nc.Block() as blk:
        for eng_name, body in [("v", "vector"), ("s", "scalar"), ("t", "tensor")]:
            def mk(eng_name=eng_name):
                def emit(e):
                    for i, (en, fn) in enumerate(steps):
                        if en != eng_name:
                            continue
                        e.wait_ge(aux, tgts[i])
                        fn(e).then_inc(aux, 1)
                return emit
            getattr(blk, body)(mk())
    ctx.cnt["aux"] = A0 + len(steps)

    def evop(ve, ji, ti, p, ot, stile):
        t0 = ti * 512
        return ve.tensor_tensor(mos[ji][:, t0:t0 + 512], p[:, :], rbc[:, t0:t0 + 512], ALU.mult)

    linear(ctx, dict(
        name="outproj", wt=io["wop"], K=DI, N=D, T=T, tw=512,
        x_tile=lambda kc: yft[kc],
        psums=ps, wbufs=wb, evac_v=evop, out_ap=None))

    steps = []
    for kc in range(16):
        S("g", lambda e, kc=kc: e.dma_start(mt[:, :], io["mlp"][kc * 128:(kc + 1) * 128, :]))
        S("g", lambda e, kc=kc: e.dma_start(xt_[:, :], io["xnfm"][kc * 128:(kc + 1) * 128, :]))
        S("v", lambda e, kc=kc: e.tensor_add(mos[kc][:, :], mos[kc][:, :], mt[:, :]))
        S("v", lambda e, kc=kc: e.tensor_add(mos[kc][:, :], mos[kc][:, :], xt_[:, :]))
        S("g", lambda e, kc=kc: e.dma_start(io["sfm"][kc * 128:(kc + 1) * 128, :], mos[kc][:, :]))
    A0 = ctx.cnt["aux"]
    incs = [16 if en == "g" else 1 for en, _ in steps]
    tgts = [A0 + sum(incs[:i]) for i in range(len(steps))]
    with nc.Block() as blk:
        for eng_name, body in [("v", "vector"), ("g", "gpsimd")]:
            def mk(eng_name=eng_name):
                def emit(e):
                    for i, (en, fn) in enumerate(steps):
                        if en != eng_name:
                            continue
                        e.wait_ge(aux, tgts[i])
                        fn(e).then_inc(aux, incs[i])
                return emit
            getattr(blk, body)(mk())
        @blk.sync
        def _(sy):
            sy.wait_ge(aux, A0 + sum(incs))
    ctx.cnt["aux"] = A0 + sum(incs)
    esC.close()
    ctx.p45_es0.close()



from concourse.bass_utils import run_bass_kernel_spmd

_BF = ml_dtypes.bfloat16
_DT = {F32: np.float32, BF16: _BF}

CONSTS_SPEC = [("eye", (128, 128), F32), ("eyeb", (128, 128), BF16),
               ("onescol", (1, 128), F32), ("ones128", (128, 1), F32)]

INTERNAL = {
    "absq": ((1, 4), F32), "sc": ((1, 4), F32),
    "tq1s": ((256, HID), BF16), "tq3s": ((256, HID), BF16), "tq2s": ((W2ROWS, D), BF16),
    "xfm": ((D, T), F32), "xnfm": ((D, T), F32), "xntm": ((T, D), BF16), "htm": ((T, D), BF16),
    "mlp": ((D, T), F32),
    "zf": ((DI, T), BF16), "xmf": ((DXB, T), BF16), "bmf": ((DXB, T), BF16),
    "cmf": ((DI, T), BF16), "dtr": ((64, T), F32),
    "yloc": ((DI, T), BF16), "eglob": ((64, T), F32), "sfin": ((DI, 64), F32), "elast": ((64, 1), F32),
    "sfm": ((D, T), F32), "yfh": ((DI, T), BF16),
    "x1fm": ((D, T), F32), "hsptm": ((T, INTER), BF16), "xn1tm": ((T, D), BF16),
    "qf": ((2048, T), BF16), "kf": ((2048, T), BF16), "vf": ((2048, T), BF16),
    "y1loc": ((2048, T), BF16), "kvfin": ((2048, 128), F32),
    "s1fm": ((D, T), F32), "x2fm": ((D, T), F32),
    "absg": ((8, 4), F32), "sgath": ((8 * DI, 64), F32), "egath": ((512, 1), F32),
    "kvgath": ((8 * 2048, 128), F32),
    "tq1": ((D, HID), BF16), "tq3": ((D, HID), BF16), "tq2": ((HID, D), BF16),
}
WSHAPES = {
    "wip": (D, DINP), "wop": (DI, D), "wfc1": (D, INTER), "wfc2": (INTER, D),
    "wg": (D, INTER), "wu": (D, INTER), "wd": (INTER, D),
    "wqkv": (D, 6144), "wgt": (D, D), "wo1": (D, D),
    "ln2w": (1, D), "ln2b": (1, D), "dtb": (64, 1), "alog": (64, 1),
    "dpc64": (64, 64), "sel64": (64, 64 * 64), "nw64": (64, 64),
    "causal": (128, 128), "causT": (128, 128), "qdec": (2048, 256), "kdec": (2048, 256), "ddecT": (4096, 256),
    "nw1cols": (128, 16), "fc1bc": (128, 64), "fc2bc": (128, 16),
}
PCSHAPES = {"xs": ((T, D), F32), "emask64": ((64, 8), F32), "emask128": ((128, 8), F32),
            "l1w": ((128, NH1 * 8), F32),
            "w1s": ((256, HID), F32), "w3s": ((256, HID), F32), "w2s": ((W2ROWS, D), F32)}


def emit_gather(ctx, pairs, wait=True):
    nc = ctx.nc
    cc = ctx.sem("cc")
    c0 = ctx.cnt["cc"]
    with nc.Block() as blk:
        @blk.gpsimd
        def _(g):
            n = 0
            for src, dst in pairs:
                g.collective_compute(
                    "AllGather", ALU.bypass,
                    replica_groups=[list(range(NCORES))],
                    ins=[src.ap().opt()],
                    outs=[dst.ap().opt()],
                ).then_inc(cc, 1)
                n += 1
            if wait:
                g.wait_ge(cc, c0 + n)
    ctx.cnt["cc"] = c0 + len(pairs)


def emit_gather_wait(ctx):
    nc = ctx.nc
    cc = ctx.sem("cc")
    tgt = ctx.cnt["cc"]
    with nc.Block() as blk:
        @blk.gpsimd
        def _(g):
            g.wait_ge(cc, tgt)


def kernel(**inputs):
    sh, per_core = host_prep(inputs)
    for i in range(NCORES):
        per_core[i]["w1s"] = np.ascontiguousarray(sh["w1t"][256 * i:256 * (i + 1)])
        per_core[i]["w3s"] = np.ascontiguousarray(sh["w3t"][256 * i:256 * (i + 1)])
        per_core[i]["w2s"] = np.ascontiguousarray(sh["w2t"][W2ROWS * i:W2ROWS * (i + 1)])

    nc = bass.Bass("TRN2", target_bir_lowering=False, detect_race_conditions=False)
    io = {}
    handles = {}
    ins_spec = list(CONSTS_SPEC)
    for name, shape in WSHAPES.items():
        ins_spec.append((name, shape, F32))
    for name, (shape, dt) in PCSHAPES.items():
        ins_spec.append((name, shape, dt))
    for name, shape, dt in ins_spec:
        io[name] = nc.dram_tensor(name, list(shape), dt, kind="ExternalInput").ap()
    for name, (shape, dt) in INTERNAL.items():
        h = nc.dram_tensor(name, list(shape), dt, kind="Internal")
        handles[name] = h
        io[name] = h.ap()
    ysh = nc.dram_tensor("ys", [T, D], F32, kind="ExternalOutput")
    io["ys"] = ysh.ap()

    ctx = Ctx(nc, io)
    phases = [
        ("setup", lambda: emit_setup(ctx)),
        ("qa", lambda: emit_qa(ctx)),
        ("cc1", lambda: emit_gather(ctx, [(handles["absq"], handles["absg"])])),
        ("qb", lambda: emit_qb(ctx)),
        ("p1", lambda: emit_p1(ctx)),
        ("cctq", lambda: emit_gather(ctx, [(handles["tq1s"], handles["tq1"]),
                                           (handles["tq3s"], handles["tq3"]),
                                           (handles["tq2s"], handles["tq2"])], wait=False)),
        ("p2", lambda: emit_p2(ctx)),
        ("p3", lambda: emit_p3(ctx)),
        ("p45", lambda: emit_p45(ctx, lambda c2: emit_gather(
            c2, [(handles["sfin"], handles["sgath"]), (handles["elast"], handles["egath"])]))),
        ("p5c", lambda: emit_p5c(ctx)),
        ("p6", lambda: emit_p6(ctx)),
        ("p7", lambda: emit_p7(ctx)),
        ("p8", lambda: emit_p8(ctx)),
        ("p9", lambda: emit_p9(ctx)),
        ("cc3", lambda: emit_gather(ctx, [(handles["kvfin"], handles["kvgath"])])),
        ("p10", lambda: emit_p10(ctx)),
        ("cctqw", lambda: emit_gather_wait(ctx)),
        ("p12", lambda: emit_p12(ctx)),
    ]
    for name, fn in phases:
        with nc.named_scope(name):
            fn()
    ctx.close()

    in_maps = []
    for c in range(NCORES):
        m = {}
        for name, shape, dt in ins_spec:
            a = per_core[c][name] if name in per_core[c] else sh[name]
            m[name] = np.ascontiguousarray(np.asarray(a, dtype=_DT[dt]))
        in_maps.append(m)
    res = run_bass_kernel_spmd(nc, in_maps, core_ids=list(range(NCORES)))
    out = np.concatenate([np.asarray(r["ys"], np.float32) for r in res.results], axis=0)
    return out[None]


# version 08
# speedup vs baseline: 1.0319x; 1.0319x over previous
import contextlib
import math
from collections import defaultdict

import numpy as np
import ml_dtypes

import concourse.bass as bass
import concourse.mybir as mybir

F32 = mybir.dt.float32
BF16 = mybir.dt.bfloat16
AF = mybir.ActivationFunctionType
ALU = mybir.AluOpType
AX = mybir.AxisListType.X

T = 1024
D = 2048
DI = 4096
DXB = 1024
HD = 64
DS = 64
NH = 64
NKV = 16
INTER = 8192
HID = 5632
HID0 = 5461
NH1 = 16
HD1 = 128
BLK = 256
NBLK = T // BLK
EPS0 = 1e-5
EPS1 = 1e-6
Q = 128
NCH = T // Q
NCORES = 8

DINP = 10304


def _get_slopes(n):
    def p2(n):
        start = 2 ** (-(2 ** (-(math.log2(n) - 3))))
        return [start * start**i for i in range(n)]
    if math.log2(n).is_integer():
        return p2(n)
    c = 2 ** math.floor(math.log2(n))
    return p2(c) + _get_slopes(2 * c)[0::2][: n - c]


SLOPES = np.asarray(_get_slopes(NH1), dtype=np.float32)
BDECAY = np.exp(-SLOPES * BLK).astype(np.float64)
CORE_DECAY = np.exp(-SLOPES.astype(np.float64) * T)


def host_consts():
    c = {}
    c["eye"] = np.eye(128, dtype=np.float32)
    c["eyeb"] = np.eye(128, dtype=ml_dtypes.bfloat16)
    c["onescol"] = np.ones((1, 128), dtype=np.float32)
    c["ones128"] = np.ones((128, 1), dtype=np.float32)
    s, t = np.meshgrid(np.arange(128), np.arange(128), indexing="ij")
    c["causal"] = np.where(t >= s, 0.0, -1e30).astype(np.float32)
    c["causT"] = np.ascontiguousarray(c["causal"].T)
    sel = np.zeros((64, 64 * 128), dtype=np.float32)
    for h in range(64):
        sel[h, h * 128:(h + 1) * 128] = 1.0
    c["sel"] = sel
    sel2 = np.zeros((64, 32 * 128), dtype=np.float32)
    for pc in range(32):
        sel2[2 * pc, pc * 128:pc * 128 + 64] = 1.0
        sel2[2 * pc + 1, pc * 128 + 64:(pc + 1) * 128] = 1.0
    c["sel2"] = sel2
    arr = np.arange(BLK, dtype=np.float64) + 1.0
    qdec = np.zeros((NH1 * HD1, BLK), dtype=np.float32)
    kdec = np.zeros((NH1 * HD1, BLK), dtype=np.float32)
    ddecT = np.zeros((NH1 * BLK, BLK), dtype=np.float32)
    for h in range(NH1):
        sl = float(SLOPES[h])
        qdec[h * HD1:(h + 1) * HD1, :] = np.exp(-sl * arr)[None, :]
        kdec[h * HD1:(h + 1) * HD1, :] = np.exp(-sl * (BLK - arr))[None, :]
        idx = arr[None, :] - arr[:, None]
        ddecT[h * BLK:(h + 1) * BLK, :] = np.where(idx >= 0, np.exp(-sl * idx), 0.0)
    c["qdec"] = qdec
    c["kdec"] = kdec
    c["ddecT"] = ddecT
    return c


def host_prep(inputs):
    f = np.ascontiguousarray
    sh = dict(host_consts())
    sh["wip"] = f(inputs["L0_attn_in_proj_w"].T)
    sh["wop"] = f(inputs["L0_attn_out_proj_w"].T)
    sh["wfc1"] = f(inputs["L0_attn_fc1_w"].T)
    sh["wfc2"] = f(inputs["L0_attn_fc2_w"].T)
    sh["wg"] = f(inputs["L0_ffn_w_gate"])
    sh["wu"] = f(inputs["L0_ffn_w_up"])
    sh["wd"] = f(inputs["L0_ffn_w_down"])
    sh["wqkv"] = f(inputs["L1_attn_Wqkv"])
    sh["wgt"] = f(inputs["L1_attn_Wg"])
    sh["wo1"] = f(inputs["L1_attn_Wout"])
    w1t = np.zeros((D, HID), dtype=np.float32)
    w1t[:, :HID0] = inputs["L1_ffn_w1"].T
    sh["w1t"] = w1t
    w3t = np.zeros((D, HID), dtype=np.float32)
    w3t[:, :HID0] = inputs["L1_ffn_w3"].T
    sh["w3t"] = w3t
    w2t = np.zeros((HID, D), dtype=np.float32)
    w2t[:HID0, :] = inputs["L1_ffn_w2"].T
    sh["w2t"] = w2t
    sh["ln2w"] = f(inputs["L0_attn_ln_w"].reshape(1, D))
    sh["ln2b"] = f(inputs["L0_attn_ln_b"].reshape(1, D))
    sh["dtb"] = f(inputs["L0_attn_dt_bias"].reshape(64, 1))
    sh["alog"] = f(inputs["L0_attn_A_log"].reshape(64, 1))
    j = np.arange(DI).reshape(32, 128).T
    sh["dpcols"] = f(inputs["L0_attn_Dp"][j // HD])
    sh["nwcols"] = f(inputs["L0_attn_norm_w"].reshape(32, 128).T)
    sh["dpc64"] = np.ascontiguousarray(np.broadcast_to(
        np.asarray(inputs["L0_attn_Dp"], np.float32)[None, :], (64, 64)))
    sel64 = np.zeros((64, 64 * 64), dtype=np.float32)
    for h in range(64):
        sel64[h, h * 64:(h + 1) * 64] = 1.0
    sh["sel64"] = sel64
    sh["nw64"] = f(inputs["L0_attn_norm_w"].reshape(64, 64).T)
    sh["nw1cols"] = f(inputs["L1_attn_norm_w"].reshape(16, 128).T)
    sh["fc1bc"] = f(inputs["L0_attn_fc1_b"].reshape(64, 128).T)
    sh["fc2bc"] = f(inputs["L0_attn_fc2_b"].reshape(16, 128).T)
    x = np.asarray(inputs["L0_attn_hidden_states"])[0]
    per_core = []
    for i in range(NCORES):
        em = (np.arange(8) < i).astype(np.float32)
        l1w = np.ones((128, NH1 * 8), dtype=np.float32)
        for h in range(NH1):
            for k in range(8):
                if k < i:
                    l1w[:, h * 8 + k] = CORE_DECAY[h]
        per_core.append({
            "xs": f(x[i * T:(i + 1) * T]),
            "emask64": np.ascontiguousarray(np.broadcast_to(em[None, :], (64, 8))),
            "emask128": np.ascontiguousarray(np.broadcast_to(em[None, :], (128, 8))),
            "l1w": l1w,
        })
    return sh, per_core



class Ctx:
    def __init__(self, nc, io):
        self.nc = nc
        self.io = io
        self._sems = {}
        self.cnt = defaultdict(int)
        self.es = contextlib.ExitStack()

    def sem(self, name):
        if name not in self._sems:
            self._sems[name] = self.es.enter_context(self.nc.semaphore(name))
        return self._sems[name]

    def close(self):
        self.es.close()


SEMS = ["wld", "ald", "mm", "ev", "evs", "sto", "wf0", "wf1", "aux", "aux2", "cc"]



def linear(ctx, cfg):
    nc = ctx.nc
    K, N, Tn = cfg["K"], cfg["N"], cfg["T"]
    tw = cfg.get("tw", 512)
    Kc = K // 128
    nw = cfg.get("nw", 128)
    jobs = cfg.get("n0s") or [(n0, min(nw, N - n0)) for n0 in range(0, N, nw)]
    tis = [(t0, min(tw, Tn - t0)) for t0 in range(0, Tn, tw)]
    ntis = len(tis)
    items = [(ji, ti) for ji in range(len(jobs)) for ti in range(ntis)]
    psums = cfg["psums"]
    P = len(psums)
    obufs = cfg.get("obufs")
    O = len(obufs) if obufs else 0
    sbufs = cfg.get("sbufs")
    S = len(sbufs) if sbufs else 0
    store = cfg.get("out_ap") is not None
    two_stage = cfg.get("evac_s") is not None and cfg.get("evac_v") is not None

    mm, ev, evs = (ctx.sem(s) for s in ["mm", "ev", "evs"])
    NW = len(cfg["wbufs"])
    wld = [ctx.sem(f"wld{i}") for i in range(NW)]
    stos = [ctx.sem(f"sto{r}") for r in range(O)] if store else []
    mm0, ev0, evs0 = (ctx.cnt[s] for s in ["mm", "ev", "evs"])
    wld0 = [ctx.cnt[f"wld{i}"] for i in range(NW)]
    sto0 = [ctx.cnt[f"sto{r}"] for r in range(O)] if store else []

    KTILE_PER_DMA = 8
    ndma_job = (Kc + KTILE_PER_DMA - 1) // KTILE_PER_DMA
    wsrc = cfg["wt"].rearrange("(a p) n -> p a n", p=128)

    W = len(cfg["wbufs"])
    with nc.Block() as blk:
        @blk.gpsimd
        def _(g):
            for ji, (n0, nwj) in enumerate(jobs):
                par = ji % W
                if ji >= W:
                    g.wait_ge(ev, ev0 + (ji - W + 1) * ntis)
                wb = cfg["wbufs"][par]
                for di in range(ndma_job):
                    k0 = di * KTILE_PER_DMA
                    k1 = min(Kc, k0 + KTILE_PER_DMA)
                    dst = wb[:, k0 * nwj:k1 * nwj].rearrange("p (a n) -> p a n", n=nwj)
                    g.dma_start(dst, wsrc[:, k0:k1, n0:n0 + nwj]).then_inc(wld[par], 16)


        @blk.tensor
        def _(te):
            for idx, (ji, ti) in enumerate(items):
                n0, nwj = jobs[ji]
                t0, twj = tis[ti]
                wb = cfg["wbufs"][ji % W]
                if ti == 0:
                    te.wait_ge(wld[ji % W], wld0[ji % W] + 16 * ndma_job * (ji // W + 1))
                ps = psums[idx % P]
                if idx >= P:
                    te.wait_ge(ev, ev0 + idx - P + 1)
                ins = None
                for kc in range(Kc):
                    ins = te.matmul(
                        ps[0:nwj, 0:twj],
                        wb[:, kc * nwj:kc * nwj + nwj],
                        cfg["x_tile"](kc)[:, t0:t0 + twj],
                        start=(kc == 0), stop=(kc == Kc - 1))
                ins.then_inc(mm, 1)

        if cfg.get("evac_s") is not None:
            @blk.scalar
            def _(se):
                for idx, (ji, ti) in enumerate(items):
                    ps = psums[idx % P]
                    se.wait_ge(mm, mm0 + idx + 1)
                    ot = obufs[idx % O] if obufs else None
                    stile = sbufs[idx % S] if sbufs else None
                    if two_stage:
                        if idx >= S and S:
                            se.wait_ge(ev, ev0 + idx - S + 1)
                    elif store and idx >= O:
                        se.wait_ge(stos[idx % O], sto0[idx % O] + 16 * (idx // O))
                    ins = cfg["evac_s"](se, ji, ti, ps, ot, stile)
                    ins.then_inc(evs if two_stage else ev, 1)

        if cfg.get("evac_v") is not None:
            @blk.vector
            def _(ve):
                for idx, (ji, ti) in enumerate(items):
                    ps = psums[idx % P]
                    if two_stage:
                        ve.wait_ge(evs, evs0 + idx + 1)
                    else:
                        ve.wait_ge(mm, mm0 + idx + 1)
                    ot = obufs[idx % O] if obufs else None
                    stile = sbufs[idx % S] if sbufs else None
                    if store and idx >= O:
                        ve.wait_ge(stos[idx % O], sto0[idx % O] + 16 * (idx // O))
                    ins = cfg["evac_v"](ve, ji, ti, ps, ot, stile)
                    ins.then_inc(ev, 1)

        if store:
            @blk.sync
            def _(sy):
                for idx, (ji, ti) in enumerate(items):
                    sy.wait_ge(ev, ev0 + idx + 1)
                    ap = cfg["out_ap"](ji, ti)
                    sy.dma_start(ap, obufs[idx % O][0:jobs[ji][1], 0:tis[ti][1]]).then_inc(stos[idx % O], 16)
                for r in range(O):
                    n_r = sum(1 for i2 in range(len(items)) if i2 % O == r)
                    sy.wait_ge(stos[r], sto0[r] + 16 * n_r)

    for i in range(W):
        nji = sum(1 for j2 in range(len(jobs)) if j2 % W == i)
        ctx.cnt[f"wld{i}"] = wld0[i] + 16 * ndma_job * nji
    ctx.cnt["mm"] = mm0 + len(items)
    ctx.cnt["ev"] = ev0 + len(items)
    if two_stage:
        ctx.cnt["evs"] = evs0 + len(items)
    if store:
        for r in range(O):
            ctx.cnt[f"sto{r}"] = sto0[r] + 16 * sum(1 for i2 in range(len(items)) if i2 % O == r)


def load_fm_tiles(ctx, tm_ap, ncol, Tn, bufs, sem="ald"):
    nc = ctx.nc
    s0 = ctx.cnt[sem]
    n = ncol // 128
    with nc.Block() as blk:
        @blk.sync
        def _(sy):
            for kc in range(n):
                sy.dma_start(bufs[kc][:, 0:Tn], tm_ap[0:Tn, kc * 128:(kc + 1) * 128],
                             transpose=True).then_inc(ctx.sem(sem), 16)
            sy.wait_ge(ctx.sem(sem), s0 + 16 * n)
    ctx.cnt[sem] = s0 + 16 * n


def bcast_rows(ctx, blk_engines, row_ap, out_sb, psum, width):
    raise NotImplementedError




CONSTS = [1.0 / 2048, EPS0, -0.5, 1.0 / 4096, EPS1, 1.0 / HID0 / 2048 / 2, 0.0, 1.0]

def emit_setup(ctx):
    nc = ctx.nc
    cp = ctx.es.enter_context(nc.sbuf_tensor("cpool", [128, len(CONSTS)], F32))
    with nc.Block() as blk:
        @blk.vector
        def _(v):
            for j, val in enumerate(CONSTS):
                v.memset(cp[:, j:j + 1], float(val))
    for j, val in enumerate(CONSTS):
        nc.const_aps.register(F32, float(val), cp[:, j:j + 1])
    ctx.cpool = cp
    nc2 = nc
    ctx.eye = ctx.es.enter_context(nc2.sbuf_tensor("c_eye", [128, 128], F32))
    ctx.eyeb = ctx.es.enter_context(nc2.sbuf_tensor("c_eyeb", [128, 128], BF16))
    ctx.onec = ctx.es.enter_context(nc2.sbuf_tensor("c_onec", [1, 128], F32))
    ctx.ones128 = ctx.es.enter_context(nc2.sbuf_tensor("c_ones128", [128, 1], F32))
    s0 = ctx.cnt["ald"]
    with nc.Block() as blk:
        @blk.sync
        def _(sy):
            sy.dma_start(ctx.eye[:, :], ctx.io["eye"][:, :]).then_inc(ctx.sem("ald"), 16)
            sy.dma_start(ctx.eyeb[:, :], ctx.io["eyeb"][:, :]).then_inc(ctx.sem("ald"), 16)
            sy.dma_start(ctx.onec[0:1, :], ctx.io["onescol"][0:1, :]).then_inc(ctx.sem("ald"), 16)
            sy.dma_start(ctx.ones128[:, 0:1], ctx.io["ones128"][:, 0:1]).then_inc(ctx.sem("ald"), 16)
            sy.wait_ge(ctx.sem("ald"), s0 + 64)
    ctx.cnt["ald"] = s0 + 64



def emit_p1(ctx):
    nc, io = ctx.nc, ctx.io
    es = contextlib.ExitStack()
    xt = [es.enter_context(nc.sbuf_tensor(f"p1xt{i}", [128, D], F32)) for i in range(2)]
    xc = es.enter_context(nc.sbuf_tensor("p1xc", [128, D], F32))
    sq = es.enter_context(nc.sbuf_tensor("p1sq", [128, D], F32))
    xn = es.enter_context(nc.sbuf_tensor("p1xn", [128, D], F32))
    xnb = es.enter_context(nc.sbuf_tensor("p1xnb", [128, D], BF16))
    ht = es.enter_context(nc.sbuf_tensor("p1ht", [128, D], BF16))
    st_ = es.enter_context(nc.sbuf_tensor("p1st", [128, 12], F32))
    wbc = es.enter_context(nc.sbuf_tensor("p1wbc", [128, D], F32))
    bbc = es.enter_context(nc.sbuf_tensor("p1bbc", [128, D], F32))
    row = es.enter_context(nc.sbuf_tensor("p1row", [1, D], F32))
    row2 = es.enter_context(nc.sbuf_tensor("p1row2", [1, D], F32))
    onec = es.enter_context(nc.sbuf_tensor("p1one", [1, 128], F32))
    ot = [es.enter_context(nc.sbuf_tensor(f"p1ot{i}", [128, 128], F32)) for i in range(4)]
    pt = [es.enter_context(nc.psum_tensor(f"p1pt{i}", [128, 128], F32)) for i in range(4)]
    pb = es.enter_context(nc.psum_tensor("p1pb", [128, 512], F32))

    ald, aux, evs, mm, ev, sto, aux2 = (ctx.sem(s) for s in
        ["ald", "aux", "evs", "mm", "ev", "sto", "aux2"])
    b = {s: ctx.cnt[s] for s in ["ald", "aux", "evs", "mm", "ev", "sto", "aux2"]}
    NTI = T // 128

    with nc.Block() as blk:
        @blk.sync
        def _(sy):
            sy.dma_start(row[0:1, :], io["ln2w"][0:1, :]).then_inc(ald, 16)
            sy.dma_start(row2[0:1, :], io["ln2b"][0:1, :]).then_inc(ald, 16)
            sy.dma_start(onec[0:1, :], io["onescol"][0:1, :]).then_inc(ald, 16)
        @blk.tensor
        def _(te):
            te.wait_ge(ald, b["ald"] + 48)
            for j in range(4):
                if j > 0:
                    te.wait_ge(ev, b["ev"] + 2 * j)
                te.matmul(pb[:, :], onec[0:1, :], row[0:1, j * 512:(j + 1) * 512],
                          start=True, stop=True).then_inc(mm, 1)
                te.wait_ge(ev, b["ev"] + 2 * j + 1)
                te.matmul(pb[:, :], onec[0:1, :], row2[0:1, j * 512:(j + 1) * 512],
                          start=True, stop=True).then_inc(mm, 1)
        @blk.vector
        def _(v):
            for j in range(4):
                v.wait_ge(mm, b["mm"] + 2 * j + 1)
                v.tensor_copy(wbc[:, j * 512:(j + 1) * 512], pb[:, :]).then_inc(ev, 1)
                v.wait_ge(mm, b["mm"] + 2 * j + 2)
                v.tensor_copy(bbc[:, j * 512:(j + 1) * 512], pb[:, :]).then_inc(ev, 1)
    ctx.cnt["ald"] += 48
    ctx.cnt["mm"] += 8
    ctx.cnt["ev"] += 8
    b = {s: ctx.cnt[s] for s in ["ald", "aux", "evs", "mm", "ev", "sto", "aux2"]}

    al = [ctx.sem("ald0"), ctx.sem("ald1")]
    stoa, stob = ctx.sem("stoa"), ctx.sem("stob")
    stot = [ctx.sem(f"sto{r}") for r in range(4)]
    b2 = {s: ctx.cnt[s] for s in
          ["ald0", "ald1", "stoa", "stob", "sto0", "sto1", "sto2", "sto3",
           "aux", "aux2", "evs", "mm", "ev"]}
    A, E = b2["aux"], b2["evs"]

    def altgt(i):
        return b2[f"ald{i % 2}"] + 16 * (i // 2 + 1)

    def stotgt(gi):
        return b2[f"sto{gi % 4}"] + 16 * (gi // 4 + 1)

    with nc.Block() as blk:
        @blk.sync
        def _(sy):
            sy.dma_start(xt[0][:, :], io["xs"][0:128, :]).then_inc(al[0], 16)
            if NTI > 1:
                sy.dma_start(xt[1][:, :], io["xs"][128:256, :]).then_inc(al[1], 16)
            for i in range(NTI):
                sy.wait_ge(evs, E + 4 * i + 4)
                sy.dma_start(io["xntm"][i * 128:(i + 1) * 128, :], xnb[:, :]).then_inc(stoa, 16)
                sy.wait_ge(aux2, b2["aux2"] + i + 1)
                sy.dma_start(io["htm"][i * 128:(i + 1) * 128, :], ht[:, :]).then_inc(stob, 16)
                for j in range(32):
                    gi = 32 * i + j
                    sy.wait_ge(ev, b2["ev"] + gi + 1)
                    kc = j % 16
                    dst = io["xfm"] if j < 16 else io["xnfm"]
                    sy.dma_start(dst[kc * 128:(kc + 1) * 128, i * 128:(i + 1) * 128],
                                 ot[gi % 4][:, :]).then_inc(stot[gi % 4], 16)
                if i + 2 < NTI:
                    sy.wait_ge(aux, A + 8 * i + 3)
                    sy.wait_ge(mm, b2["mm"] + 32 * i + 16)
                    sy.dma_start(xt[i % 2][:, :],
                                 io["xs"][(i + 2) * 128:(i + 3) * 128, :]).then_inc(al[i % 2], 16)
            sy.wait_ge(stoa, b2["stoa"] + 16 * NTI)
            sy.wait_ge(stob, b2["stob"] + 16 * NTI)
            for r in range(4):
                sy.wait_ge(stot[r], b2[f"sto{r}"] + 16 * 8 * NTI)

        @blk.vector
        def _(v):
            for i in range(NTI):
                x_ = xt[i % 2]
                v.wait_ge(al[i % 2], altgt(i))
                if i > 0:
                    v.wait_ge(evs, E + 4 * i - 1)
                v.reduce_sum(st_[:, 0:1], x_[:, :], axis=AX).then_inc(aux, 1)
                v.wait_ge(aux, A + 8 * i + 1)
                v.tensor_scalar(st_[:, 1:2], st_[:, 0:1], 1.0 / D, None,
                                ALU.mult).then_inc(aux, 1)
                v.wait_ge(aux, A + 8 * i + 2)
                v.tensor_scalar(xc[:, :], x_[:, :], st_[:, 1:2], None,
                                ALU.subtract).then_inc(aux, 1)
                v.wait_ge(evs, E + 4 * i + 1)
                v.reduce_sum(st_[:, 2:3], sq[:, :], axis=AX).then_inc(aux, 1)
                v.wait_ge(evs, E + 4 * i + 2)
                if i > 0:
                    v.wait_ge(mm, b2["mm"] + 32 * i)
                v.tensor_scalar(xn[:, :], xc[:, :], st_[:, 4:5], None, ALU.mult)
                v.reduce_sum(st_[:, 5:6], xn[:, :], axis=AX).then_inc(aux, 1)
                v.wait_ge(aux, A + 8 * i + 5)
                v.tensor_scalar(st_[:, 6:7], st_[:, 5:6], 1.0 / D, None,
                                ALU.mult).then_inc(aux, 1)
                v.wait_ge(aux, A + 8 * i + 6)
                v.tensor_scalar(xc[:, :], xn[:, :], st_[:, 6:7], None,
                                ALU.subtract).then_inc(aux, 1)
                v.wait_ge(evs, E + 4 * i + 3)
                v.reduce_sum(st_[:, 7:8], sq[:, :], axis=AX).then_inc(aux, 1)
                v.wait_ge(evs, E + 4 * i + 4)
                if i > 0:
                    v.wait_ge(stob, b2["stob"] + 16 * i)
                v.scalar_tensor_tensor(sq[:, :], xc[:, :], st_[:, 9:10], wbc[:, :],
                                       ALU.mult, ALU.mult)
                v.tensor_tensor(ht[:, :], sq[:, :], bbc[:, :], ALU.add).then_inc(aux2, 1)
                for j in range(32):
                    gi = 32 * i + j
                    v.wait_ge(mm, b2["mm"] + gi + 1)
                    if gi >= 4:
                        v.wait_ge(stot[gi % 4], stotgt(gi - 4))
                    v.tensor_copy(ot[gi % 4][:, :], pt[gi % 4][:, :]).then_inc(ev, 1)

        @blk.scalar
        def _(s):
            for i in range(NTI):
                s.wait_ge(aux, A + 8 * i + 3)
                s.activation(sq[:, :], xc[:, :], AF.Square).then_inc(evs, 1)
                s.wait_ge(aux, A + 8 * i + 4)
                s.activation(st_[:, 3:4], st_[:, 2:3], AF.Ln, scale=1.0 / D, bias=EPS0)
                s.activation(st_[:, 4:5], st_[:, 3:4], AF.Exp, scale=-0.5).then_inc(evs, 1)
                s.wait_ge(aux, A + 8 * i + 7)
                s.activation(sq[:, :], xc[:, :], AF.Square).then_inc(evs, 1)
                s.wait_ge(aux, A + 8 * i + 8)
                s.activation(st_[:, 8:9], st_[:, 7:8], AF.Ln, scale=1.0 / D, bias=EPS0)
                s.activation(st_[:, 9:10], st_[:, 8:9], AF.Exp, scale=-0.5)
                if i > 0:
                    s.wait_ge(stoa, b2["stoa"] + 16 * i)
                s.copy(xnb[:, :], xn[:, :]).then_inc(evs, 1)

        @blk.tensor
        def _(te):
            eye = ctx.eye
            for i in range(NTI):
                te.wait_ge(al[i % 2], altgt(i))
                for j in range(32):
                    gi = 32 * i + j
                    if j == 16:
                        te.wait_ge(aux, A + 8 * i + 5)
                    if gi >= 4:
                        te.wait_ge(ev, b2["ev"] + gi - 3)
                    src = xt[i % 2] if j < 16 else xn
                    kc = j % 16
                    te.transpose(pt[gi % 4][:, :], src[:, kc * 128:(kc + 1) * 128],
                                 eye[:, :]).then_inc(mm, 1)

    ctx.cnt["ald0"] = b2["ald0"] + 16 * ((NTI + 1) // 2)
    ctx.cnt["ald1"] = b2["ald1"] + 16 * (NTI // 2)
    ctx.cnt["aux"] = A + 8 * NTI
    ctx.cnt["aux2"] = b2["aux2"] + NTI
    ctx.cnt["evs"] = E + 4 * NTI
    ctx.cnt["mm"] = b2["mm"] + 32 * NTI
    ctx.cnt["ev"] = b2["ev"] + 32 * NTI
    ctx.cnt["stoa"] = b2["stoa"] + 16 * NTI
    ctx.cnt["stob"] = b2["stob"] + 16 * NTI
    for r in range(4):
        ctx.cnt[f"sto{r}"] = b2[f"sto{r}"] + 16 * 8 * NTI
    es.close()


def emit_p2(ctx):
    nc, io = ctx.nc, ctx.io
    es = contextlib.ExitStack()
    hfm = [es.enter_context(nc.sbuf_tensor(f"p2h{k}", [128, T], BF16)) for k in range(16)]
    gu = es.enter_context(nc.sbuf_tensor("p2gu", [128, 64 * 512], BF16))
    wb1 = [es.enter_context(nc.sbuf_tensor(f"p2w1{i}", [128, 16 * 128], BF16)) for i in range(3)]
    wb2 = [es.enter_context(nc.sbuf_tensor(f"p2w2{i}", [128, 64 * 128], BF16)) for i in range(3)]
    b1c = es.enter_context(nc.sbuf_tensor("p2b1", [128, 64], F32))
    b2c = es.enter_context(nc.sbuf_tensor("p2b2", [128, 16], F32))
    ob = [es.enter_context(nc.sbuf_tensor(f"p2ob{i}", [128, 512], F32)) for i in range(4)]
    ps = [es.enter_context(nc.psum_tensor(f"p2ps{i}", [128, 512], F32)) for i in range(4)]

    s0 = ctx.cnt["ald"]
    with nc.Block() as blk:
        @blk.sync
        def _(sy):
            sy.dma_start(b1c[:, :], io["fc1bc"][:, :]).then_inc(ctx.sem("ald"), 16)
            sy.dma_start(b2c[:, :], io["fc2bc"][:, :]).then_inc(ctx.sem("ald"), 16)
            sy.wait_ge(ctx.sem("ald"), s0 + 32)
    ctx.cnt["ald"] = s0 + 32
    load_fm_tiles(ctx, io["htm"], D, T, hfm)

    for th in range(2):
        t0 = th * 512

        def ev1(se, ji, ti, p, ot, stile):
            return se.activation(gu[:, ji * 512:(ji + 1) * 512], p[:, :],
                                 AF.Gelu_apprx_tanh, bias=b1c[:, ji:ji + 1])

        linear(ctx, dict(
            name="fc1", wt=io["wfc1"], K=D, N=INTER, T=512, tw=512,
            x_tile=lambda kc, t0=t0: hfm[kc][:, t0:t0 + 512],
            psums=ps, wbufs=wb1, evac_s=ev1, evac_v=None, out_ap=None))

        def ev2(se, ji, ti, p, ot, stile):
            return se.activation(ot[:, :], p[:, :], AF.Identity, bias=b2c[:, ji:ji + 1])

        linear(ctx, dict(
            name="fc2", wt=io["wfc2"], K=INTER, N=D, T=512, tw=512,
            x_tile=lambda kc: gu[:, kc * 512:(kc + 1) * 512],
            psums=ps, wbufs=wb2, obufs=ob, evac_s=ev2, evac_v=None,
            out_ap=lambda ji, ti, t0=t0: io["mlp"][ji * 128:(ji + 1) * 128, t0:t0 + 512]))
    es.close()



def emit_p3(ctx):
    nc, io = ctx.nc, ctx.io
    es = contextlib.ExitStack()
    hfm = [es.enter_context(nc.sbuf_tensor(f"p3h{k}", [128, T], BF16)) for k in range(16)]
    wb = [es.enter_context(nc.sbuf_tensor(f"p3w{i}", [128, 16 * 128], BF16)) for i in range(3)]
    ob = [es.enter_context(nc.sbuf_tensor(f"p3ob{i}", [128, 512], BF16)) for i in range(4)]
    obf = [es.enter_context(nc.sbuf_tensor(f"p3of{i}", [64, 512], F32)) for i in range(2)]
    ps = [es.enter_context(nc.psum_tensor(f"p3ps{i}", [128, 512], F32)) for i in range(4)]
    load_fm_tiles(ctx, io["htm"], D, T, hfm)

    def out_ap(ji, ti):
        t0 = ti * 512
        if ji < 32:
            return io["zf"][ji * 128:(ji + 1) * 128, t0:t0 + 512]
        if ji < 40:
            return io["xmf"][(ji - 32) * 128:(ji - 31) * 128, t0:t0 + 512]
        if ji < 48:
            return io["bmf"][(ji - 40) * 128:(ji - 39) * 128, t0:t0 + 512]
        return io["cmf"][(ji - 48) * 128:(ji - 47) * 128, t0:t0 + 512]

    def ev(ve, ji, ti, p, ot, stile):
        return ve.tensor_copy(ot[:, :], p[:, :])

    linear(ctx, dict(
        name="inproj", wt=io["wip"], K=D, N=80 * 128, T=T, tw=512,
        x_tile=lambda kc: hfm[kc],
        psums=ps, wbufs=wb, obufs=ob, evac_v=ev, out_ap=out_ap))

    def evdt(ve, ji, ti, p, ot, stile):
        return ve.tensor_copy(ot[0:64, :], p[0:64, :])

    linear(ctx, dict(
        name="dt", wt=io["wip"], K=D, N=64, T=T, tw=512,
        n0s=[(10240, 64)],
        x_tile=lambda kc: hfm[kc],
        psums=ps, wbufs=wb, obufs=obf, evac_v=evdt,
        out_ap=lambda ji, ti: io["dtr"][0:64, ti * 512:(ti + 1) * 512]))
    es.close()



def emit_p4(ctx):
    nc, io = ctx.nc, ctx.io
    es = contextlib.ExitStack()
    sb = lambda n, s, d: es.enter_context(nc.sbuf_tensor(n, s, d))
    xfc = [sb(f"p4xf{i}", [64, 16 * 128], BF16) for i in range(2)]
    bfc = [sb(f"p4bf{i}", [64, 16 * 128], BF16) for i in range(2)]
    xtm = [sb(f"p4xt{i}", [128, 1024], BF16) for i in range(8)]
    btm = [sb(f"p4bt{i}", [128, 1024], BF16) for i in range(8)]
    cc = [sb(f"p4cc{i}", [64, 64 * 128], BF16) for i in range(2)]
    clb = sb("p4cl", [64, 64 * 128], BF16)
    ylb = sb("p4yl", [64, 64 * 128], BF16)
    Sf = sb("p4S", [64, 4096], F32)
    Sb = sb("p4Sb", [64, 4096], BF16)
    dtr_ = sb("p4dtr", [64, T], F32)
    dts = sb("p4dts", [64, T], F32)
    tmpa = sb("p4tma", [64, T], F32)
    tmpb = sb("p4tmb", [64, T], F32)
    lndt = sb("p4ln", [64, T], F32)
    logdA = sb("p4da", [64, T], F32)
    Lg = sb("p4lg", [64, T], F32)
    Rt = sb("p4r", [64, T], F32)
    eloc = dtr_
    eglo = logdA
    wsc = sb("p4w", [64, T], F32)
    Ecol = sb("p4E", [64, 16], F32)
    dE = sb("p4dE", [64, 8], F32)
    Dex = sb("p4Dx", [64, 8], F32)
    Dflat = sb("p4Df", [1, 512], F32)
    Dbc = sb("p4Dbc", [64, 64], F32)
    LgT = sb("p4lgt", [128, 8 * 64], F32)
    RT = sb("p4rt", [128, 8 * 64], F32)
    wT = sb("p4wt", [128, 8 * 64], F32)
    dtb = sb("p4dtb", [64, 1], F32)
    alog = sb("p4al", [64, 1], F32)
    Acol = sb("p4A", [64, 1], F32)
    dpc = sb("p4dp", [64, 64], F32)
    sel64 = sb("p4s64", [64, 64 * 64], F32)
    caus = sb("p4ca", [128, 128], F32)
    zt = sb("p4z", [128, 128], F32)
    bct = [sb(f"p4bc{i}", [128, 128], F32) for i in range(2)]
    msk = [sb(f"p4mk{i}", [128, 128], F32) for i in range(2)]
    Mp = [sb(f"p4Mp{i}", [128, 128], F32) for i in range(2)]
    Gm = [sb(f"p4Gm{i}", [128, 128], BF16) for i in range(2)]
    xsc = [sb(f"p4xs{i}", [64, 128], F32) for i in range(2)]
    bw = [sb(f"p4bw{i}", [128, 64], BF16) for i in range(2)]
    st1 = sb("p4s1", [64, 64], F32)
    psA = [es.enter_context(nc.psum_tensor(f"p4pA{i}", [128, 128], F32)) for i in range(2)]
    psG = [es.enter_context(nc.psum_tensor(f"p4pG{i}", [128, 128], F32)) for i in range(2)]
    psY = [es.enter_context(nc.psum_tensor(f"p4pY{i}", [64, 128], F32)) for i in range(2)]
    psS = [es.enter_context(nc.psum_tensor(f"p4pS{i}", [64, 64], F32)) for i in range(2)]
    dram_tmp = nc.dram_tensor("p4dram", [64, 8], F32).ap()

    a0 = ctx.cnt["ald"]
    with nc.Block() as blk:
        @blk.sync
        def _(sy):
            n = 0
            for i in range(8):
                sy.dma_start(xtm[i][:, :], io["xmf"][0:1024, i * 128:(i + 1) * 128], transpose=True).then_inc(ctx.sem("ald"), 16); n += 1
                sy.dma_start(btm[i][:, :], io["bmf"][0:1024, i * 128:(i + 1) * 128], transpose=True).then_inc(ctx.sem("ald"), 16); n += 1
            for ap, name in [(dtr_, "dtr"), (dtb, "dtb"), (alog, "alog"), (dpc, "dpc64"),
                             (caus, "causal"), (sel64, "sel64")]:
                sy.dma_start(ap[:, :], io[name][:, :]).then_inc(ctx.sem("ald"), 16); n += 1
            sy.wait_ge(ctx.sem("ald"), a0 + 16 * n)
            ctx.cnt["ald"] = a0 + 16 * n
        @blk.vector
        def _(v):
            v.memset(zt[:, :], 0.0)
            v.memset(Ecol[:, :], 0.0)
            v.memset(Sf[:, :], 0.0)
            v.memset(Sb[:, :], 0.0)

    steps = []
    def S(eng, fn):
        steps.append((eng, fn))
    S("s", lambda e: e.activation(dts[:, :], dtr_[:, :], AF.Identity, bias=dtb[:, 0:1]))
    S("s", lambda e: e.activation(tmpa[:, :], dts[:, :], AF.Abs))
    S("s", lambda e: e.activation(tmpb[:, :], tmpa[:, :], AF.Exp, scale=-1.0))
    S("s", lambda e: e.activation(tmpa[:, :], tmpb[:, :], AF.Ln, bias=1.0))
    S("v", lambda e: e.tensor_relu(tmpb[:, :], dts[:, :]))
    S("v", lambda e: e.tensor_add(dts[:, :], tmpa[:, :], tmpb[:, :]))
    S("s", lambda e: e.activation(lndt[:, :], dts[:, :], AF.Ln))
    S("s", lambda e: e.activation(Acol[:, 0:1], alog[:, 0:1], AF.Exp))
    S("v", lambda e: e.tensor_scalar(Acol[:, 0:1], Acol[:, 0:1], -1.0, None, ALU.mult))
    S("v", lambda e: e.tensor_scalar(logdA[:, :], dts[:, :], Acol[:, 0:1], None, ALU.mult))
    S("v", lambda e: e.tensor_tensor_scan(Lg[:, :], logdA[:, :], logdA[:, :], 0.0, ALU.add, ALU.bypass))
    S("v", lambda e: e.tensor_copy(Ecol[:, 1:9], Lg[:, 127::128]))
    S("v", lambda e: e.tensor_sub(Rt[:, :], lndt[:, :], Lg[:, :]))
    for c in range(NCH):
        S("v", lambda e, c=c: e.tensor_scalar(tmpa[:, c * 128:(c + 1) * 128], Lg[:, c * 128:(c + 1) * 128], Ecol[:, c:c + 1], None, ALU.subtract))
        S("v", lambda e, c=c: e.tensor_scalar(tmpb[:, c * 128:(c + 1) * 128], Rt[:, c * 128:(c + 1) * 128], Ecol[:, c + 1:c + 2], None, ALU.add))
    S("s", lambda e: e.activation(eloc[:, :], tmpa[:, :], AF.Exp))
    S("s", lambda e: e.activation(eglo[:, :], Lg[:, :], AF.Exp))
    S("s", lambda e: e.activation(wsc[:, :], tmpb[:, :], AF.Exp))
    S("v", lambda e: e.tensor_sub(dE[:, :], Ecol[:, 1:9], Ecol[:, 0:8]))
    S("s", lambda e: e.activation(Dex[:, :], dE[:, :], AF.Exp))
    S("g", lambda e: e.dma_start(dram_tmp[:, :], Dex[:, :]))
    for c in range(NCH):
        S("g", lambda e, c=c: e.dma_start(Dflat[0:1, c * 64:(c + 1) * 64],
                                          dram_tmp[:, c:c + 1].rearrange("h c -> c h"),
                                          allow_slow_non_contiguous=True))
    for c in range(NCH):
        S("t", lambda e, c=c: e.transpose(psA[0][:, 0:64], Lg[:, c * 128:(c + 1) * 128], ctx.eye[0:64, 0:64]))
        S("v", lambda e, c=c: e.tensor_copy(LgT[:, c * 64:(c + 1) * 64], psA[0][:, 0:64]))
        S("t", lambda e, c=c: e.transpose(psA[0][:, 0:64], Rt[:, c * 128:(c + 1) * 128], ctx.eye[0:64, 0:64]))
        S("v", lambda e, c=c: e.tensor_copy(RT[:, c * 64:(c + 1) * 64], psA[0][:, 0:64]))
        S("t", lambda e, c=c: e.transpose(psA[0][:, 0:64], wsc[:, c * 128:(c + 1) * 128], ctx.eye[0:64, 0:64]))
        S("v", lambda e, c=c: e.tensor_copy(wT[:, c * 64:(c + 1) * 64], psA[0][:, 0:64]))

    aux = ctx.sem("aux")
    A0 = ctx.cnt["aux"]
    incs = [16 if en == "g" else 1 for en, _ in steps]
    tgts = [A0 + sum(incs[:i]) for i in range(len(steps))]
    with nc.Block() as blk:
        for eng_name, body in [("v", "vector"), ("s", "scalar"), ("t", "tensor"), ("g", "gpsimd")]:
            def mk(eng_name=eng_name):
                def emit(e):
                    for i, (en, fn) in enumerate(steps):
                        if en != eng_name:
                            continue
                        e.wait_ge(aux, tgts[i])
                        fn(e).then_inc(aux, incs[i])
                return emit
            getattr(blk, body)(mk())
    ctx.cnt["aux"] = A0 + sum(incs)

    mm, ev, evs, mmp, evp = (ctx.sem(s) for s in ["mm", "ev", "evs", "mmp", "evp"])
    al = [ctx.sem("ald0"), ctx.sem("ald1")]
    stot = [ctx.sem(f"sto{r}") for r in range(4)]
    b = {s: ctx.cnt[s] for s in ["mm", "ev", "evs", "mmp", "evp", "ald0", "ald1",
                                 "sto0", "sto1", "sto2", "sto3"]}
    M0, E0, V0, MP0, EP0 = b["mm"], b["evs"], b["ev"], b["mmp"], b["evp"]
    csrc = io["cmf"].rearrange("(a p) t -> p a t", p=64)
    xsrc = io["xmf"].rearrange("(a p) t -> p a t", p=64)
    bsrc = io["bmf"].rearrange("(a p) t -> p a t", p=64)

    def kvg(h):
        return h // 4

    def sfv(buf, h):
        return buf[:, h * 64:(h + 1) * 64]

    def hv(buf, h, w=128):
        return buf[:, h * 128:h * 128 + w]

    def cload(sy, c):
        for d in range(4):
            sy.dma_start(cc[c % 2][:, d * 16 * 128:(d + 1) * 16 * 128].rearrange("p (a t) -> p a t", t=128),
                         csrc[:, d * 16:(d + 1) * 16, c * 128:(c + 1) * 128]).then_inc(al[c % 2], 16)
        sy.dma_start(xfc[c % 2][:, :].rearrange("p (a t) -> p a t", t=128),
                     xsrc[:, :, c * 128:(c + 1) * 128]).then_inc(al[c % 2], 16)
        sy.dma_start(bfc[c % 2][:, :].rearrange("p (a t) -> p a t", t=128),
                     bsrc[:, :, c * 128:(c + 1) * 128]).then_inc(al[c % 2], 16)

    NPRE = 65

    with nc.Block() as blk:
        @blk.sync
        def _(sy):
            for c in range(min(2, NCH)):
                cload(sy, c)
            for c in range(NCH):
                for h in range(NH):
                    g = c * 64 + h
                    sy.wait_ge(ev, V0 + 4 * g + 3)
                    sy.dma_start(io["yloc"][h * 64:(h + 1) * 64, c * 128:(c + 1) * 128],
                                 hv(ylb, h)).then_inc(stot[h % 4], 16)
                if c + 2 < NCH:
                    sy.wait_ge(mm, M0 + 4 * 64 * (c + 1))
                    sy.wait_ge(evp, EP0 + NPRE * (c + 1))
                    cload(sy, c + 2)
            for r in range(4):
                sy.wait_ge(stot[r], b[f"sto{r}"] + 16 * 16 * 8)
            sy.wait_ge(ev, V0 + 4 * 64 * NCH)
            for h in range(NH):
                sy.dma_start(io["sfin"][h * 64:(h + 1) * 64, :], sfv(Sf, h)).then_inc(stot[0], 16)
            sy.dma_start(io["elast"][:, 0:1], Ecol[:, 8:9]).then_inc(stot[0], 16)
            sy.dma_start(io["eglob"][:, :], eglo[:, :]).then_inc(stot[0], 16)
            sy.wait_ge(stot[0], b["sto0"] + 16 * 16 * 8 + 16 * 66)

        @blk.tensor
        def _(te):
            nmm = M0
            for c in range(NCH):
                par = c % 2
                te.wait_ge(al[par], b[f"ald{par}"] + 16 * 6 * (c // 2 + 1))
                if c > 0:
                    te.wait_ge(ev, V0 + 4 * 64 * c)
                for h in range(NH):
                    if h >= 2:
                        te.wait_ge(evp, EP0 + NPRE * c + h - 1)
                    te.matmul(psY[h % 2][:, :], sel64[:, h * 64:(h + 1) * 64],
                              eloc[:, c * 128:(c + 1) * 128],
                              start=True, stop=True).then_inc(mmp, 1)
                te.wait_ge(evp, EP0 + NPRE * c + 64)
                te.matmul(psS[0][:, :], ctx.onec[0:1, 0:64], Dflat[0:1, c * 64:(c + 1) * 64],
                          start=True, stop=True).then_inc(mmp, 1)
                for h in range(NH):
                    g = c * 64 + h
                    if h == 0:
                        te.wait_ge(evp, EP0 + NPRE * (c + 1))
                    te.wait_ge(evs, E0 + 5 * g + 1)
                    if g >= 2:
                        te.wait_ge(ev, V0 + 4 * (g - 2) + 2)
                    te.transpose(psA[g % 2][:, :], bct[g % 2][:, :], ctx.eye[:, :]).then_inc(mm, 1)
                    te.matmul(psG[g % 2][:, :], bfc[par][:, kvg(h) * 128:(kvg(h) + 1) * 128],
                              hv(cc[par], h), start=True, stop=True).then_inc(mm, 1)
                    te.wait_ge(ev, V0 + 4 * g + 2)
                    te.matmul(psY[g % 2][:, :], xtm[c][:, kvg(h) * 64:(kvg(h) + 1) * 64],
                              Gm[g % 2][:, :], start=True, stop=False)
                    if c > 0:
                        te.wait_ge(evs, E0 + 5 * (g - 64) + 5)
                    te.matmul(psY[g % 2][:, :], sfv(Sb, h), hv(clb, h),
                              start=False, stop=True).then_inc(mm, 1)
                    te.wait_ge(evs, E0 + 5 * g + 4)
                    te.matmul(psS[g % 2][:, :], bw[g % 2][:, :],
                              xtm[c][:, kvg(h) * 64:(kvg(h) + 1) * 64],
                              start=True, stop=True).then_inc(mm, 1)

        @blk.vector
        def _(v):
            for c in range(NCH):
                if c > 0:
                    v.wait_ge(mm, M0 + 4 * 64 * c)
                for h in range(NH):
                    v.wait_ge(mmp, MP0 + NPRE * c + h + 1)
                    v.tensor_tensor(hv(clb, h), hv(cc[c % 2], h),
                                    psY[h % 2][:, :], ALU.mult).then_inc(evp, 1)
                v.wait_ge(mmp, MP0 + NPRE * c + 65)
                v.tensor_copy(Dbc[:, :], psS[0][:, :]).then_inc(evp, 1)
                for h in range(NH):
                    g = c * 64 + h
                    v.wait_ge(mm, M0 + 4 * g + 1)
                    if g >= 2:
                        v.wait_ge(evs, E0 + 5 * (g - 2) + 2)
                    v.tensor_tensor(msk[g % 2][:, :], psA[g % 2][:, :], caus[:, :], ALU.add).then_inc(ev, 1)
                    v.wait_ge(mm, M0 + 4 * g + 2)
                    v.wait_ge(evs, E0 + 5 * g + 2)
                    v.tensor_tensor(Gm[g % 2][:, :], psG[g % 2][:, :], Mp[g % 2][:, :], ALU.mult).then_inc(ev, 1)
                    v.wait_ge(mm, M0 + 4 * g + 3)
                    v.wait_ge(evs, E0 + 5 * g + 3)
                    if c > 0:
                        v.wait_ge(stot[h % 4], b[f"sto{h % 4}"] + 16 * (16 * (c - 1) + h // 4 + 1))
                    v.tensor_tensor(hv(ylb, h), psY[g % 2][:, :], xsc[g % 2][:, :], ALU.add).then_inc(ev, 1)
                    v.wait_ge(mm, M0 + 4 * g + 4)
                    v.wait_ge(evp, EP0 + NPRE * (c + 1))
                    v.tensor_scalar(st1[:, :], sfv(Sf, h), Dbc[:, h:h + 1], None, ALU.mult)
                    v.tensor_tensor(sfv(Sf, h), st1[:, :], psS[g % 2][:, :], ALU.add).then_inc(ev, 1)

        @blk.scalar
        def _(s):
            for c in range(NCH):
                for h in range(NH):
                    g = c * 64 + h
                    if g >= 2:
                        s.wait_ge(mm, M0 + 4 * (g - 2) + 1)
                    s.activation(bct[g % 2][:, :], zt[:, :], AF.Identity,
                                 bias=LgT[:, c * 64 + h:c * 64 + h + 1]).then_inc(evs, 1)
                    s.wait_ge(ev, V0 + 4 * g + 1)
                    s.activation(Mp[g % 2][:, :], msk[g % 2][:, :], AF.Exp,
                                 bias=RT[:, c * 64 + h:c * 64 + h + 1]).then_inc(evs, 1)
                    if h == 0:
                        s.wait_ge(al[c % 2], b[f"ald{c % 2}"] + 16 * 6 * (c // 2 + 1))
                    if g >= 2:
                        s.wait_ge(ev, V0 + 4 * (g - 2) + 3)
                    s.activation(xsc[g % 2][:, :], xfc[c % 2][:, kvg(h) * 128:(kvg(h) + 1) * 128],
                                 AF.Identity, scale=dpc[:, h:h + 1]).then_inc(evs, 1)
                    if g >= 2:
                        s.wait_ge(mm, M0 + 4 * (g - 2) + 4)
                    s.activation(bw[g % 2][:, :],
                                 btm[c][:, kvg(h) * 64:(kvg(h) + 1) * 64], AF.Identity,
                                 scale=wT[:, c * 64 + h:c * 64 + h + 1]).then_inc(evs, 1)
                    s.wait_ge(ev, V0 + 4 * g + 4)
                    s.copy(sfv(Sb, h), sfv(Sf, h)).then_inc(evs, 1)

    ctx.cnt["mm"] = M0 + 4 * 64 * NCH
    ctx.cnt["ev"] = V0 + 4 * 64 * NCH
    ctx.cnt["evs"] = E0 + 5 * 64 * NCH
    ctx.cnt["mmp"] = MP0 + NPRE * NCH
    ctx.cnt["evp"] = EP0 + NPRE * NCH
    ctx.cnt["ald0"] = b["ald0"] + 16 * 6 * ((NCH + 1) // 2)
    ctx.cnt["ald1"] = b["ald1"] + 16 * 6 * (NCH // 2)
    for r in range(4):
        ctx.cnt[f"sto{r}"] = b[f"sto{r}"] + 16 * 16 * 8 + (16 * 66 if r == 0 else 0)
    es.close()


def emit_p5(ctx):
    nc, io = ctx.nc, ctx.io
    es0 = contextlib.ExitStack()
    esA = contextlib.ExitStack()
    sb = lambda st, n, s, d: st.enter_context(nc.sbuf_tensor(n, s, d))
    esA2 = contextlib.ExitStack()
    ssq = es0.enter_context(nc.sbuf_tensor("p5ssq", [1, T], F32))
    sel64 = sb(esA, "p5s64", [64, 64 * 64], F32)
    Sin = sb(esA, "p5sin", [64, 4096], BF16)
    Esb = sb(esA2, "p5E", [64, 8], F32)
    emk = sb(esA2, "p5em", [64, 8], F32)
    ExpE = sb(esA2, "p5xE", [64, 8], F32)
    Wbc = sb(esA2, "p5W", [64, 64 * 8], F32)
    acc = sb(esA2, "p5acc", [64, 4096], F32)
    skb = sb(esA2, "p5skb", [64, 4096], F32)
    sgg = sb(esA2, "p5sgg", [64, 4096], F32)
    psW = esA2.enter_context(nc.psum_tensor("p5pW", [64, 8], F32))

    a0 = ctx.cnt["ald"]
    with nc.Block() as blk:
        @blk.sync
        def _(sy):
            n = 0
            for k in range(8):
                sy.dma_start(Esb[:, k:k + 1], io["egath"][k * 64:(k + 1) * 64, 0:1]).then_inc(ctx.sem("ald"), 16); n += 1
            sy.dma_start(emk[:, :], io["emask64"][:, :]).then_inc(ctx.sem("ald"), 16); n += 1
            sy.dma_start(sel64[:, :], io["sel64"][:, :]).then_inc(ctx.sem("ald"), 16); n += 1
            sy.wait_ge(ctx.sem("ald"), a0 + 16 * n)
            ctx.cnt["ald"] = a0 + 16 * n
        @blk.vector
        def _(v):
            v.memset(acc[:, :], 0.0)

    steps = []
    def S(eng, fn):
        steps.append((eng, fn))
    S("v", lambda e: e.tensor_mul(Esb[:, :], Esb[:, :], emk[:, :]))
    S("s", lambda e: e.activation(ExpE[:, :], Esb[:, :], AF.Exp))
    for h in range(64):
        S("t", lambda e, h=h: e.matmul(psW[:, :], sel64[:, h * 64:(h + 1) * 64], ExpE[:, :],
                                       start=True, stop=True))
        S("v", lambda e, h=h: e.tensor_copy(Wbc[:, h * 8:(h + 1) * 8], psW[:, :]))
    for k in range(7):
        sgsrc = io["sgath"].rearrange("(c hh n) p -> c n hh p", c=8, n=64)
        for d in range(4):
            S("g", lambda e, k=k, d=d: e.dma_start(
                skb[:, d * 16 * 64:(d + 1) * 16 * 64].rearrange("n (hh p) -> n hh p", p=64),
                sgsrc[k, :, d * 16:(d + 1) * 16, :]))
        for h in range(64):
            S("v", lambda e, k=k, h=h: e.tensor_scalar(
                acc[:, h * 64:(h + 1) * 64], acc[:, h * 64:(h + 1) * 64],
                Wbc[:, h * 8 + k:h * 8 + k + 1], None, ALU.mult))
        S("v", lambda e, k=k: e.tensor_scalar(sgg[:, :], skb[:, :], emk[:, k:k + 1], None, ALU.mult))
        S("v", lambda e, k=k: e.tensor_add(acc[:, :], acc[:, :], sgg[:, :]))
    S("v", lambda e: e.tensor_copy(Sin[:, :], acc[:, :]))
    if "dbg3" in io:
        S("g", lambda e: e.dma_start(io["dbg3"][:, :], acc[:, :]))
        S("g", lambda e: e.dma_start(io["dbg4"][:, :], Wbc[:, :]))

    aux = ctx.sem("aux")
    A0 = ctx.cnt["aux"]
    incs = [16 if en == "g" else 1 for en, _ in steps]
    tgts = [A0 + sum(incs[:i]) for i in range(len(steps))]
    with nc.Block() as blk:
        for eng_name, body in [("v", "vector"), ("s", "scalar"), ("t", "tensor"), ("g", "gpsimd")]:
            def mk(eng_name=eng_name):
                def emit(e):
                    for i, (en, fn) in enumerate(steps):
                        if en != eng_name:
                            continue
                        e.wait_ge(aux, tgts[i])
                        fn(e).then_inc(aux, incs[i])
                return emit
            getattr(blk, body)(mk())
    ctx.cnt["aux"] = A0 + sum(incs)
    esA2.close()

    esB = contextlib.ExitStack()
    cc = [sb(esB, f"p5cc{i}", [64, 64 * 128], BF16) for i in range(2)]
    yl = [sb(esB, f"p5yl{i}", [64, 64 * 128], BF16) for i in range(2)]
    zc = [sb(esB, f"p5zc{i}", [64, 64 * 128], BF16) for i in range(2)]
    yfb = [sb(esB, f"p5yf{i}", [64, 64 * 128], BF16) for i in range(2)]
    eglo = sb(esB, "p5eg", [64, T], F32)
    nw64 = sb(esB, "p5nw", [64, 64], F32)
    CGl = [sb(esB, f"p5cg{i}", [64, 128], BF16) for i in range(2)]
    yt = [sb(esB, f"p5yt{i}", [64, 128], F32) for i in range(2)]
    sz = [sb(esB, f"p5sz{i}", [64, 128], F32) for i in range(2)]
    sq = [sb(esB, f"p5sq{i}", [64, 128], F32) for i in range(2)]
    psA = [esB.enter_context(nc.psum_tensor(f"p5pA{i}", [64, 128], F32)) for i in range(2)]
    psY = [esB.enter_context(nc.psum_tensor(f"p5pY{i}", [64, 128], F32)) for i in range(2)]
    psQ = [esB.enter_context(nc.psum_tensor(f"p5pQ{i}", [1, 128], F32)) for i in range(2)]

    a0 = ctx.cnt["ald"]
    with nc.Block() as blk:
        @blk.sync
        def _(sy):
            sy.dma_start(eglo[:, :], io["eglob"][:, :]).then_inc(ctx.sem("ald"), 16)
            sy.dma_start(nw64[:, :], io["nw64"][:, :]).then_inc(ctx.sem("ald"), 16)
            sy.wait_ge(ctx.sem("ald"), a0 + 32)
            ctx.cnt["ald"] = a0 + 32
        @blk.vector
        def _(v):
            v.memset(ssq[0:1, :], 0.0)
    mm, ev, evs, evp = (ctx.sem(s) for s in ["mm", "ev", "evs", "evp"])
    al = [ctx.sem("ald0"), ctx.sem("ald1")]
    stot = [ctx.sem(f"sto{r}") for r in range(2)]
    b = {s: ctx.cnt[s] for s in ["mm", "ev", "evs", "evp", "ald0", "ald1", "sto0", "sto1"]}
    M0, V0, E0, EP0 = b["mm"], b["ev"], b["evs"], b["evp"]

    def hv(buf, h, w=128):
        return buf[:, h * 128:h * 128 + w]

    def chload(sy, c, tens, dst):
        src = tens.rearrange("(a p) t -> p a t", p=64)
        for d in range(4):
            sy.dma_start(dst[c % 2][:, d * 16 * 128:(d + 1) * 16 * 128].rearrange("p (a t) -> p a t", t=128),
                         src[:, d * 16:(d + 1) * 16, c * 128:(c + 1) * 128]).then_inc(al[c % 2], 16)

    with nc.Block() as blk:
        @blk.sync
        def _(sy):
            for c in range(min(2, NCH)):
                chload(sy, c, io["cmf"], cc)
                chload(sy, c, io["yloc"], yl)
                chload(sy, c, io["zf"], zc)
            for c in range(NCH):
                g9 = c * 64 + 63
                sy.wait_ge(ev, V0 + 5 * g9 + 4)
                ydst = io["yfh"].rearrange("(a p) t -> p a t", p=64)
                for d in range(4):
                    sy.dma_start(ydst[:, d * 16:(d + 1) * 16, c * 128:(c + 1) * 128],
                                 yfb[c % 2][:, d * 16 * 128:(d + 1) * 16 * 128].rearrange("p (a t) -> p a t", t=128)
                                 ).then_inc(stot[c % 2], 16)
                if c + 2 < NCH:
                    sy.wait_ge(ev, V0 + 5 * 64 * (c + 1))
                    sy.wait_ge(evs, E0 + 2 * 64 * (c + 1))
                    chload(sy, c + 2, io["cmf"], cc)
                    chload(sy, c + 2, io["yloc"], yl)
                    chload(sy, c + 2, io["zf"], zc)
            for r in range(2):
                sy.wait_ge(stot[r], b[f"sto{r}"] + 16 * 4 * (4 if r == 0 else 4))

        @blk.tensor
        def _(te):
            for c in range(NCH):
                par = c % 2
                te.wait_ge(al[par], b[f"ald{par}"] + 16 * 12 * (c // 2 + 1))
                for h in range(NH):
                    g = c * 64 + h
                    if g >= 2:
                        te.wait_ge(ev, V0 + 5 * (g - 2) + 1)
                    te.matmul(psA[g % 2][:, :], sel64[:, h * 64:(h + 1) * 64],
                              eglo[:, c * 128:(c + 1) * 128], start=True, stop=True).then_inc(mm, 1)
                    te.wait_ge(ev, V0 + 5 * g + 1)
                    te.matmul(psY[g % 2][:, :], Sin[:, h * 64:(h + 1) * 64],
                              CGl[g % 2][:, :], start=True, stop=True).then_inc(mm, 1)
                    te.wait_ge(evs, E0 + 2 * g + 2)
                    if g >= 2:
                        te.wait_ge(ev, V0 + 5 * (g - 2) + 5)
                    te.matmul(psQ[g % 2][:, :], ctx.ones128[0:64, 0:1], sq[g % 2][:, :],
                              start=True, stop=True).then_inc(mm, 1)

        @blk.vector
        def _(v):
            for c in range(NCH):
                for h in range(NH):
                    g = c * 64 + h
                    v.wait_ge(mm, M0 + 3 * g + 1)
                    if g >= 2:
                        v.wait_ge(mm, M0 + 3 * (g - 2) + 2)
                    v.tensor_tensor(CGl[g % 2][:, :], hv(cc[c % 2], h), psA[g % 2][:, :],
                                    ALU.mult).then_inc(ev, 1)
                    v.wait_ge(mm, M0 + 3 * g + 2)
                    v.tensor_tensor(yt[g % 2][:, :], psY[g % 2][:, :], hv(yl[c % 2], h),
                                    ALU.add).then_inc(ev, 1)
                    v.wait_ge(evs, E0 + 2 * g + 1)
                    if c >= 2 and h == 0:
                        v.wait_ge(stot[c % 2], b[f"sto{c % 2}"] + 16 * 4 * (c // 2))
                    v.tensor_tensor(hv(yfb[c % 2], h), yt[g % 2][:, :], sz[g % 2][:, :],
                                    ALU.mult).then_inc(ev, 1)
                    v.wait_ge(evs, E0 + 2 * g + 2)
                    v.tensor_scalar(hv(yfb[c % 2], h), hv(yfb[c % 2], h),
                                    nw64[:, h:h + 1], None, ALU.mult).then_inc(ev, 1)
                    v.wait_ge(mm, M0 + 3 * g + 3)
                    v.tensor_tensor(ssq[0:1, c * 128:(c + 1) * 128],
                                    ssq[0:1, c * 128:(c + 1) * 128],
                                    psQ[g % 2][0:1, :], ALU.add).then_inc(ev, 1)

        @blk.scalar
        def _(s):
            for c in range(NCH):
                for h in range(NH):
                    g = c * 64 + h
                    if h == 0:
                        s.wait_ge(al[c % 2], b[f"ald{c % 2}"] + 16 * 12 * (c // 2 + 1))
                    if g >= 2:
                        s.wait_ge(ev, V0 + 5 * (g - 2) + 3)
                    s.activation(sz[g % 2][:, :], hv(zc[c % 2], h), AF.Silu).then_inc(evs, 1)
                    s.wait_ge(ev, V0 + 5 * g + 3)
                    s.activation(sq[g % 2][:, :], hv(yfb[c % 2], h), AF.Square).then_inc(evs, 1)

    ctx.cnt["mm"] = M0 + 3 * 64 * NCH
    ctx.cnt["ev"] = V0 + 5 * 64 * NCH
    ctx.cnt["evs"] = E0 + 2 * 64 * NCH
    ctx.cnt["ald0"] = b["ald0"] + 16 * 12 * ((NCH + 1) // 2)
    ctx.cnt["ald1"] = b["ald1"] + 16 * 12 * (NCH // 2)
    ctx.cnt["sto0"] = b["sto0"] + 16 * 4 * 4
    ctx.cnt["sto1"] = b["sto1"] + 16 * 4 * 4
    esB.close()
    esA.close()

    esC = contextlib.ExitStack()
    sbC = lambda n, s, d: esC.enter_context(nc.sbuf_tensor(n, s, d))
    yft = [sbC(f"p5y{k}", [128, T], BF16) for k in range(32)]
    mos = [sbC(f"p5mo{k}", [128, T], F32) for k in range(16)]
    rrow = sbC("p5rr", [1, T], F32)
    rbc = sbC("p5rb", [128, T], F32)
    wb = [sbC(f"p5wb{i}", [128, 32 * 128], BF16) for i in range(2)]
    mt = sbC("p5mt", [128, T], F32)
    xt_ = sbC("p5xt2", [128, T], F32)
    ob = [sbC(f"p5ob{i}", [128, 512], F32) for i in range(4)]
    ps = [esC.enter_context(nc.psum_tensor(f"p5ps{i}", [128, 512], F32)) for i in range(4)]
    psB = esC.enter_context(nc.psum_tensor("p5psB", [128, 512], F32))

    a0 = ctx.cnt["ald"]
    with nc.Block() as blk:
        @blk.sync
        def _(sy):
            for k in range(32):
                sy.dma_start(yft[k][:, :], io["yfh"][k * 128:(k + 1) * 128, :]).then_inc(ctx.sem("ald"), 16)
            sy.wait_ge(ctx.sem("ald"), a0 + 16 * 32)
            ctx.cnt["ald"] = a0 + 16 * 32

    steps = []
    def S(eng, fn):
        steps.append((eng, fn))
    if "dbg1" in io:
        S("g", lambda e: e.dma_start(io["dbg1"][0:1, :], ssq[0:1, :]))
        S("g", lambda e: e.dma_start(io["dbg2"][:, :], Sin[:, :]))

    S("s", lambda e: e.activation(rrow[0:1, :], ssq[0:1, :], AF.Ln, scale=1.0 / DI, bias=EPS0))
    S("s", lambda e: e.activation(rrow[0:1, :], rrow[0:1, :], AF.Exp, scale=-0.5))
    for j in range(2):
        S("t", lambda e, j=j: e.matmul(psB[:, :], ctx.onec[0:1, :], rrow[0:1, j * 512:(j + 1) * 512],
                                       start=True, stop=True))
        S("v", lambda e, j=j: e.tensor_copy(rbc[:, j * 512:(j + 1) * 512], psB[:, :]))
    aux = ctx.sem("aux")
    A0 = ctx.cnt["aux"]
    incsd = [16 if en == "g" else 1 for en, _ in steps]
    tgts = [A0 + sum(incsd[:i]) for i in range(len(steps))]
    with nc.Block() as blk:
        for eng_name, body in [("v", "vector"), ("s", "scalar"), ("t", "tensor"), ("g", "gpsimd")]:
            def mk(eng_name=eng_name):
                def emit(e):
                    for i, (en, fn) in enumerate(steps):
                        if en != eng_name:
                            continue
                        e.wait_ge(aux, tgts[i])
                        fn(e).then_inc(aux, incsd[i])
                return emit
            getattr(blk, body)(mk())
    ctx.cnt["aux"] = A0 + sum(incsd)

    def evop(ve, ji, ti, p, ot, stile):
        t0 = ti * 512
        return ve.tensor_tensor(mos[ji][:, t0:t0 + 512], p[:, :], rbc[:, t0:t0 + 512], ALU.mult)

    linear(ctx, dict(
        name="outproj", wt=io["wop"], K=DI, N=D, T=T, tw=512,
        x_tile=lambda kc: yft[kc],
        psums=ps, wbufs=wb, evac_v=evop, out_ap=None))

    steps = []
    for kc in range(16):
        S("g", lambda e, kc=kc: e.dma_start(mt[:, :], io["mlp"][kc * 128:(kc + 1) * 128, :]))
        S("g", lambda e, kc=kc: e.dma_start(xt_[:, :], io["xnfm"][kc * 128:(kc + 1) * 128, :]))
        S("v", lambda e, kc=kc: e.tensor_add(mos[kc][:, :], mos[kc][:, :], mt[:, :]))
        S("v", lambda e, kc=kc: e.tensor_add(mos[kc][:, :], mos[kc][:, :], xt_[:, :]))
        S("g", lambda e, kc=kc: e.dma_start(io["sfm"][kc * 128:(kc + 1) * 128, :], mos[kc][:, :]))
    A0 = ctx.cnt["aux"]
    incs = [16 if en == "g" else 1 for en, _ in steps]
    tgts = [A0 + sum(incs[:i]) for i in range(len(steps))]
    with nc.Block() as blk:
        for eng_name, body in [("v", "vector"), ("g", "gpsimd")]:
            def mk(eng_name=eng_name):
                def emit(e):
                    for i, (en, fn) in enumerate(steps):
                        if en != eng_name:
                            continue
                        e.wait_ge(aux, tgts[i])
                        fn(e).then_inc(aux, incs[i])
                return emit
            getattr(blk, body)(mk())
        @blk.sync
        def _(sy):
            sy.wait_ge(aux, A0 + sum(incs))
    ctx.cnt["aux"] = A0 + sum(incs)
    esC.close()
    es0.close()


def emit_p6(ctx):
    nc, io = ctx.nc, ctx.io
    esA = contextlib.ExitStack()
    sbA = lambda n, s, d: esA.enter_context(nc.sbuf_tensor(n, s, d))
    xnf = [sbA(f"p6x{k}", [128, T], BF16) for k in range(16)]
    wgb = [sbA(f"p6wg{i}", [128, 16 * 512], BF16) for i in range(2)]
    wub = [sbA(f"p6wu{i}", [128, 16 * 512], BF16) for i in range(2)]
    RNG = 3
    sg = [sbA(f"p6sg{i}", [128, 512], F32) for i in range(RNG)]
    hh = [sbA(f"p6hh{i}", [128, 512], F32) for i in range(RNG)]
    sqv = [sbA(f"p6sq{i}", [128, 512], F32) for i in range(RNG)]
    cb = sbA("p6cb", [128, 6 * 128], F32)
    tb = sbA("p6tb", [128, 2 * 128], F32)
    kp = sbA("p6kp", [128, 128], F32)
    hspb = [sbA(f"p6hs{i}", [128, 512], BF16) for i in range(RNG)]
    psG = [esA.enter_context(nc.psum_tensor(f"p6pG{i}", [128, 512], F32)) for i in range(RNG)]
    psU = [esA.enter_context(nc.psum_tensor(f"p6pU{i}", [128, 512], F32)) for i in range(RNG)]
    load_fm_tiles(ctx, io["xntm"], D, T, xnf)

    mm, ev, evs = (ctx.sem(s) for s in ["mm", "ev", "evs"])
    wld = [ctx.sem("wld0"), ctx.sem("wld1")]
    stot = [ctx.sem(f"sto{r}") for r in range(3)]
    b = {s: ctx.cnt[s] for s in ["mm", "ev", "evs", "wld0", "wld1", "sto0", "sto1", "sto2"]}
    M0, V0, E0 = b["mm"], b["ev"], b["evs"]
    wgsrc = io["wg"].rearrange("(a p) n -> p a n", p=128)
    wusrc = io["wu"].rearrange("(a p) n -> p a n", p=128)
    NIB, NTP = 16, 8

    with nc.Block() as blk:
        @blk.gpsimd
        def _(g):
            for ib in range(NIB):
                par = ib % 2
                if ib >= 2:
                    g.wait_ge(ev, V0 + 2 * 8 * (ib - 1))
                for d in range(2):
                    g.dma_start(wgb[par][:, d * 8 * 512:(d + 1) * 8 * 512].rearrange("p (a n) -> p a n", n=512),
                                wgsrc[:, d * 8:(d + 1) * 8, ib * 512:(ib + 1) * 512]).then_inc(wld[par], 16)
                    g.dma_start(wub[par][:, d * 8 * 512:(d + 1) * 8 * 512].rearrange("p (a n) -> p a n", n=512),
                                wusrc[:, d * 8:(d + 1) * 8, ib * 512:(ib + 1) * 512]).then_inc(wld[par], 16)

        @blk.tensor
        def _(te):
            for ib in range(NIB):
                par = ib % 2
                te.wait_ge(wld[par], b[f"wld{par}"] + 16 * 4 * (ib // 2 + 1))
                for tp in range(NTP):
                    idx = ib * 8 + tp
                    if idx >= 1:
                        te.wait_ge(ev, V0 + 2 * (idx - 1) + 2)
                    if idx >= 3:
                        te.wait_ge(evs, E0 + (idx - 3) + 1)
                    ins = None
                    for kc in range(16):
                        ins = te.matmul(psG[idx % 3][:, :], xnf[kc][:, tp * 128:(tp + 1) * 128],
                                        wgb[par][:, kc * 512:(kc + 1) * 512],
                                        start=(kc == 0), stop=(kc == 15))
                    ins.then_inc(mm, 1)
                    for kc in range(16):
                        ins = te.matmul(psU[idx % 3][:, :], xnf[kc][:, tp * 128:(tp + 1) * 128],
                                        wub[par][:, kc * 512:(kc + 1) * 512],
                                        start=(kc == 0), stop=(kc == 15))
                    ins.then_inc(mm, 1)

        @blk.scalar
        def _(s):
            for idx in range(NIB * NTP):
                s.wait_ge(mm, M0 + 2 * idx + 1)
                if idx >= 3:
                    s.wait_ge(ev, V0 + 2 * (idx - 3) + 1)
                s.activation(sg[idx % 3][:, :], psG[idx % 3][:, :], AF.Silu).then_inc(evs, 1)

        @blk.vector
        def _(v):
            pairs = [(0, 1), (0, 2), (0, 3), (1, 2), (1, 3), (2, 3)]
            for idx in range(NIB * NTP):
                v.wait_ge(evs, E0 + idx + 1)
                v.wait_ge(mm, M0 + 2 * idx + 2)
                v.tensor_tensor(hh[idx % 3][:, :], sg[idx % 3][:, :], psU[idx % 3][:, :],
                                ALU.mult).then_inc(ev, 1)
                v.tensor_mul(sqv[idx % 3][:, :], hh[idx % 3][:, :], hh[idx % 3][:, :])
                if idx >= 3:
                    v.wait_ge(stot[idx % 3], b[f"sto{idx % 3}"] + 16 * (idx // 3))
                sq_ = sqv[idx % 3]
                for j, (a, bb_) in enumerate(pairs):
                    v.tensor_tensor(cb[:, j * 128:(j + 1) * 128], sq_[:, a::4], sq_[:, bb_::4], ALU.is_gt)
                h_ = hh[idx % 3]
                o_ = hspb[idx % 3]
                v.tensor_add(tb[:, 0:128], cb[:, 0:128], cb[:, 128:256])
                v.tensor_add(tb[:, 0:128], tb[:, 0:128], cb[:, 256:384])
                v.tensor_single_scalar(kp[:, :], tb[:, 0:128], 1.5, ALU.is_gt)
                v.tensor_tensor(o_[:, 0::4], h_[:, 0::4], kp[:, :], ALU.mult)
                v.tensor_add(tb[:, 0:128], cb[:, 384:512], cb[:, 512:640])
                v.tensor_sub(tb[:, 0:128], tb[:, 0:128], cb[:, 0:128])
                v.tensor_single_scalar(kp[:, :], tb[:, 0:128], 0.5, ALU.is_gt)
                v.tensor_tensor(o_[:, 1::4], h_[:, 1::4], kp[:, :], ALU.mult)
                v.tensor_sub(tb[:, 0:128], cb[:, 640:768], cb[:, 128:256])
                v.tensor_sub(tb[:, 0:128], tb[:, 0:128], cb[:, 384:512])
                v.tensor_single_scalar(kp[:, :], tb[:, 0:128], -0.5, ALU.is_gt)
                v.tensor_tensor(o_[:, 2::4], h_[:, 2::4], kp[:, :], ALU.mult)
                v.tensor_add(tb[:, 0:128], cb[:, 256:384], cb[:, 512:640])
                v.tensor_add(tb[:, 0:128], tb[:, 0:128], cb[:, 640:768])
                v.tensor_single_scalar(kp[:, :], tb[:, 0:128], 1.5, ALU.is_lt)
                v.tensor_tensor(o_[:, 3::4], h_[:, 3::4], kp[:, :], ALU.mult).then_inc(ev, 1)

        @blk.sync
        def _(sy):
            for idx in range(NIB * NTP):
                ib, tp = idx // 8, idx % 8
                sy.wait_ge(ev, V0 + 2 * idx + 2)
                sy.dma_start(io["hsptm"][tp * 128:(tp + 1) * 128, ib * 512:(ib + 1) * 512],
                             hspb[idx % 3][:, :]).then_inc(stot[idx % 3], 16)
            for r in range(3):
                nr = sum(1 for i2 in range(NIB * NTP) if i2 % 3 == r)
                sy.wait_ge(stot[r], b[f"sto{r}"] + 16 * nr)

    ctx.cnt["mm"] = M0 + 2 * NIB * NTP
    ctx.cnt["ev"] = V0 + 2 * NIB * NTP
    ctx.cnt["evs"] = E0 + NIB * NTP
    ctx.cnt["wld0"] = b["wld0"] + 16 * 4 * 8
    ctx.cnt["wld1"] = b["wld1"] + 16 * 4 * 8
    for r in range(3):
        ctx.cnt[f"sto{r}"] = b[f"sto{r}"] + 16 * sum(1 for i2 in range(NIB * NTP) if i2 % 3 == r)
    esA.close()

    for th in range(2):
        t0 = th * 512
        esB = contextlib.ExitStack()
        sbB = lambda n, s, d: esB.enter_context(nc.sbuf_tensor(n, s, d))
        hspf = [sbB(f"p6hf{th}_{k}", [128, 512], BF16) for k in range(64)]
        sT = [sbB(f"p6sT{th}_{k}", [128, 512], F32) for k in range(16)]
        xT = [sbB(f"p6xT{th}_{k}", [128, 512], F32) for k in range(16)]
        wb = [sbB(f"p6wb{th}_{i}", [128, 64 * 128], BF16) for i in range(2)]
        gsc = [sbB(f"p6gs{th}_{i}", [128, 512], F32) for i in range(2)]
        ob = [sbB(f"p6ob{th}_{i}", [128, 512], F32) for i in range(4)]
        ps = [esB.enter_context(nc.psum_tensor(f"p6ps{th}_{i}", [128, 512], F32)) for i in range(4)]
        a0 = ctx.cnt["ald"]
        with nc.Block() as blk:
            @blk.sync
            def _(sy):
                n = 0
                for k in range(64):
                    sy.dma_start(hspf[k][:, :], io["hsptm"][t0:t0 + 512, k * 128:(k + 1) * 128],
                                 transpose=True).then_inc(ctx.sem("ald"), 16); n += 1
                for k in range(16):
                    sy.dma_start(sT[k][:, :], io["sfm"][k * 128:(k + 1) * 128, t0:t0 + 512]).then_inc(ctx.sem("ald"), 16); n += 1
                    sy.dma_start(xT[k][:, :], io["xfm"][k * 128:(k + 1) * 128, t0:t0 + 512]).then_inc(ctx.sem("ald"), 16); n += 1
                sy.wait_ge(ctx.sem("ald"), a0 + 16 * n)
                ctx.cnt["ald"] = a0 + 16 * n

        def evs_(se, ji, ti, p, ot, stile):
            return se.activation(stile[:, :], p[:, :], AF.Silu)

        def evv_(ve, ji, ti, p, ot, stile):
            ve.tensor_tensor(ot[:, :], stile[:, :], sT[ji][:, :], ALU.mult)
            return ve.tensor_tensor(ot[:, :], ot[:, :], xT[ji][:, :], ALU.add)

        linear(ctx, dict(
            name=f"down{th}", wt=io["wd"], K=INTER, N=D, T=512, tw=512,
            x_tile=lambda kc: hspf[kc],
            psums=ps, wbufs=wb, obufs=ob, sbufs=gsc,
            evac_s=evs_, evac_v=evv_,
            out_ap=lambda ji, ti, t0=t0: io["x1fm"][ji * 128:(ji + 1) * 128, t0:t0 + 512]))
        esB.close()


def emit_p7(ctx):
    nc, io = ctx.nc, ctx.io
    es = contextlib.ExitStack()
    sb = lambda n, s, d: es.enter_context(nc.sbuf_tensor(n, s, d))
    xr = sb("p7xr", [128, 16 * 128], F32)
    xt = sb("p7xt", [128, D], F32)
    xc = sb("p7xc", [128, D], F32)
    sq = sb("p7sq", [128, D], F32)
    xnb = sb("p7xnb", [128, D], BF16)
    st_ = sb("p7st", [128, 8], F32)
    psT = es.enter_context(nc.psum_tensor("p7ps", [128, 128], F32))
    xsrc = io["x1fm"].rearrange("(a p) t -> p a t", p=128)

    steps = []
    def S(eng, fn):
        steps.append((eng, fn))
    for i in range(8):
        S("g", lambda e, i=i: e.dma_start(xr[:, :].rearrange("p (a t) -> p a t", t=128),
                                          xsrc[:, :, i * 128:(i + 1) * 128]))
        for kc in range(16):
            S("t", lambda e, kc=kc: e.transpose(psT[:, :], xr[:, kc * 128:(kc + 1) * 128], ctx.eye[:, :]))
            S("v", lambda e, kc=kc: e.tensor_copy(xt[:, kc * 128:(kc + 1) * 128], psT[:, :]))
        S("v", lambda e: e.reduce_sum(st_[:, 0:1], xt[:, :], axis=AX))
        S("v", lambda e: e.tensor_scalar(st_[:, 1:2], st_[:, 0:1], 1.0 / D, None, ALU.mult))
        S("v", lambda e: e.tensor_scalar(xc[:, :], xt[:, :], st_[:, 1:2], None, ALU.subtract))
        S("s", lambda e: e.activation(sq[:, :], xc[:, :], AF.Square))
        S("v", lambda e: e.reduce_sum(st_[:, 2:3], sq[:, :], axis=AX))
        S("s", lambda e: e.activation(st_[:, 3:4], st_[:, 2:3], AF.Ln, scale=1.0 / D, bias=EPS0))
        S("s", lambda e: e.activation(st_[:, 4:5], st_[:, 3:4], AF.Exp, scale=-0.5))
        S("v", lambda e: e.tensor_scalar(xnb[:, :], xc[:, :], st_[:, 4:5], None, ALU.mult))
        S("g", lambda e, i=i: e.dma_start(io["xn1tm"][i * 128:(i + 1) * 128, :], xnb[:, :]))
    aux = ctx.sem("aux")
    A0 = ctx.cnt["aux"]
    incs = [16 if en == "g" else 1 for en, _ in steps]
    tgts = [A0 + sum(incs[:i]) for i in range(len(steps))]
    with nc.Block() as blk:
        for eng_name, body in [("v", "vector"), ("s", "scalar"), ("t", "tensor"), ("g", "gpsimd")]:
            def mk(eng_name=eng_name):
                def emit(e):
                    for i, (en, fn) in enumerate(steps):
                        if en != eng_name:
                            continue
                        e.wait_ge(aux, tgts[i])
                        fn(e).then_inc(aux, incs[i])
                return emit
            getattr(blk, body)(mk())
        @blk.sync
        def _(sy):
            sy.wait_ge(aux, A0 + sum(incs))
    ctx.cnt["aux"] = A0 + sum(incs)
    es.close()



def emit_p8(ctx):
    nc, io = ctx.nc, ctx.io
    es = contextlib.ExitStack()
    xnf = [es.enter_context(nc.sbuf_tensor(f"p8x{k}", [128, T], BF16)) for k in range(16)]
    wb = [es.enter_context(nc.sbuf_tensor(f"p8w{i}", [128, 16 * 128], BF16)) for i in range(3)]
    ob = [es.enter_context(nc.sbuf_tensor(f"p8o{i}", [128, 512], BF16)) for i in range(4)]
    ps = [es.enter_context(nc.psum_tensor(f"p8p{i}", [128, 512], F32)) for i in range(4)]
    load_fm_tiles(ctx, io["xn1tm"], D, T, xnf)

    def out_ap(ji, ti):
        h, part = ji // 3, ji % 3
        dst = [io["qf"], io["kf"], io["vf"]][part]
        return dst[h * 128:(h + 1) * 128, ti * 512:(ti + 1) * 512]

    def ev(se, ji, ti, p, ot, stile):
        return se.activation(ot[:, :], p[:, :], AF.Silu)

    linear(ctx, dict(
        name="qkv", wt=io["wqkv"], K=D, N=48 * 128, T=T, tw=512,
        x_tile=lambda kc: xnf[kc],
        psums=ps, wbufs=wb, obufs=ob, evac_s=ev, out_ap=out_ap))
    es.close()


def emit_p9(ctx):
    nc, io = ctx.nc, ctx.io
    es = contextlib.ExitStack()
    sb = lambda n, s, d: es.enter_context(nc.sbuf_tensor(n, s, d))
    qf = [sb(f"p9q{k}", [128, T], BF16) for k in range(16)]
    kf = [sb(f"p9k{k}", [128, T], BF16) for k in range(16)]
    vf = [sb(f"p9v{k}", [128, T], BF16) for k in range(16)]
    qdech = sb("p9qd", [128, BLK], F32)
    kdech = sb("p9kd", [128, BLK], F32)
    ddech = sb("p9dd", [128, 2 * BLK], F32)
    kv = sb("p9kv", [128, 128], F32)
    kvt = sb("p9kvt", [128, 128], F32)
    kvb = sb("p9kvb", [128, 128], BF16)
    qd = sb("p9qdt", [128, BLK], BF16)
    qkm = [sb(f"p9qk{i}", [128, BLK], BF16) for i in range(2)]
    kd = sb("p9kdt", [128, BLK], BF16)
    vtm = [sb(f"p9vt{i}", [128, 128], BF16) for i in range(2)]
    ktm = [sb(f"p9kt{i}", [128, 128], BF16) for i in range(2)]
    yb = sb("p9yb", [128, BLK], BF16)
    psY = es.enter_context(nc.psum_tensor("p9pY", [128, BLK], F32))
    psK = [es.enter_context(nc.psum_tensor(f"p9pK{i}", [128, BLK], F32)) for i in range(2)]
    psKV = es.enter_context(nc.psum_tensor("p9pKV", [128, 128], F32))
    psT = es.enter_context(nc.psum_tensor("p9pT", [128, 128], BF16))

    a0 = ctx.cnt["ald"]
    with nc.Block() as blk:
        @blk.sync
        def _(sy):
            n = 0
            for k in range(16):
                sy.dma_start(qf[k][:, :], io["qf"][k * 128:(k + 1) * 128, :]).then_inc(ctx.sem("ald"), 16); n += 1
                sy.dma_start(kf[k][:, :], io["kf"][k * 128:(k + 1) * 128, :]).then_inc(ctx.sem("ald"), 16); n += 1
                sy.dma_start(vf[k][:, :], io["vf"][k * 128:(k + 1) * 128, :]).then_inc(ctx.sem("ald"), 16); n += 1
            sy.wait_ge(ctx.sem("ald"), a0 + 16 * n)
            ctx.cnt["ald"] = a0 + 16 * n

    ds = DepSteps(ctx)
    S = ds.S
    P = 2
    qdechs = [qdech, kdech]
    kdechs = [sb(f"p9kd2{i}", [128, BLK], F32) for i in range(P)]
    ddechs = [ddech] + [sb("p9dd2", [128, 2 * BLK], F32)]
    kvs_ = [kv, sb("p9kv2", [128, 128], F32)]
    kvts = [kvt, sb("p9kvt2", [128, 128], F32)]
    kvbs = [kvb, sb("p9kvb2", [128, 128], BF16)]
    qds = [qd, sb("p9qd2", [128, BLK], BF16)]
    kds = [kd, sb("p9kd3", [128, BLK], BF16)]
    qkms = [qkm, [sb(f"p9qk2{i}", [128, BLK], BF16) for i in range(2)]]
    vtms = [vtm, [sb(f"p9vt2{i}", [128, 128], BF16) for i in range(2)]]
    ktms = [ktm, [sb(f"p9kt2{i}", [128, 128], BF16) for i in range(2)]]
    ybs = [yb, sb("p9yb2", [128, BLK], BF16)]
    psYs = [psY, es.enter_context(nc.psum_tensor("p9pY2", [128, BLK], F32))]
    psKVs = [psKV, es.enter_context(nc.psum_tensor("p9pKV2", [128, 128], F32))]
    last = {}
    for h in range(NH1):
        p_ = h % 2
        bd = float(BDECAY[h])
        ql, kl, dl = qdechs[p_], kdechs[p_], ddechs[p_]
        i_ql = S("g", lambda e, h=h, ql=ql: e.dma_start(ql[:, :], io["qdec"][h * 128:(h + 1) * 128, :]),
                 deps=[last.get(("qluse", p_))])
        i_kl = S("g", lambda e, h=h, kl=kl: e.dma_start(kl[:, :], io["kdec"][h * 128:(h + 1) * 128, :]),
                 deps=[last.get(("kluse", p_))])
        i_dl = S("g", lambda e, h=h, dl=dl: e.dma_start(dl[:, :].rearrange("p (a t) -> p a t", t=BLK),
                 io["ddecT"].rearrange("(a p) t -> p a t", p=128)[:, 2 * h:2 * h + 2, :]),
                 deps=[last.get(("dluse", p_))])
        i_z1 = S("v", lambda e, p_=p_: e.memset(kvs_[p_][:, :], 0.0), deps=[last.get(("kvuse", p_))])
        i_z2 = S("v", lambda e, p_=p_: e.memset(kvbs[p_][:, :], 0.0), deps=[last.get(("kvbuse", p_))])
        for b in range(NBLK):
            t0 = b * BLK
            i_qd = S("v", lambda e, h=h, t0=t0, p_=p_: e.tensor_tensor(
                qds[p_][:, :], qf[h][:, t0:t0 + BLK], qdechs[p_][:, :], ALU.mult),
                deps=[i_ql, last.get(("qduse", p_))])
            qk_is = []
            vt_is = []
            for sh_ in range(2):
                i_qk = S("t", lambda e, h=h, t0=t0, sh_=sh_: e.matmul(
                    psK[sh_][:, :], kf[h][:, t0 + sh_ * 128:t0 + sh_ * 128 + 128],
                    qf[h][:, t0:t0 + BLK], start=True, stop=True),
                    deps=[last.get(("psK", sh_))])
                i_qkm = S("v", lambda e, sh_=sh_, p_=p_: e.tensor_tensor(
                    qkms[p_][sh_][:, :], psK[sh_][:, :],
                    ddechs[p_][:, sh_ * BLK:(sh_ + 1) * BLK], ALU.mult),
                    deps=[i_qk, i_dl, last.get(("qkmuse", p_, sh_))])
                last[("psK", sh_)] = i_qkm
                i_vt = S("t", lambda e, h=h, t0=t0, sh_=sh_: e.transpose(
                    psT[:, :], vf[h][:, t0 + sh_ * 128:t0 + sh_ * 128 + 128], ctx.eyeb[:, :]),
                    deps=[last.get("psT")])
                i_vte = S("v", lambda e, sh_=sh_, p_=p_: e.tensor_copy(vtms[p_][sh_][:, :], psT[:, :]),
                          deps=[i_vt, last.get(("vtmuse", p_, sh_))])
                last["psT"] = i_vte
                qk_is.append(i_qkm)
                vt_is.append(i_vte)
            i_y0 = S("t", lambda e, p_=p_: e.matmul(psYs[p_][:, :], kvbs[p_][:, :], qds[p_][:, :],
                                                    start=True, stop=False),
                     deps=[i_qd, i_z2 if b == 0 else last[("kvb", p_)], last.get(("psY", p_))])
            last[("qduse", p_)] = i_y0
            i_y1 = S("t", lambda e, p_=p_: e.matmul(psYs[p_][:, :], vtms[p_][0][:, :], qkms[p_][0][:, :],
                                                    start=False, stop=False),
                     deps=[qk_is[0], vt_is[0]])
            i_y2 = S("t", lambda e, p_=p_: e.matmul(psYs[p_][:, :], vtms[p_][1][:, :], qkms[p_][1][:, :],
                                                    start=False, stop=True),
                     deps=[qk_is[1], vt_is[1]])
            last[("qkmuse", p_, 0)] = i_y1
            last[("qkmuse", p_, 1)] = i_y2
            i_ye = S("v", lambda e, p_=p_: e.tensor_copy(ybs[p_][:, :], psYs[p_][:, :]),
                     deps=[i_y2, last.get(("ybuse", p_))])
            last[("psY", p_)] = i_ye
            i_yst = S("g", lambda e, h=h, t0=t0, p_=p_: e.dma_start(
                io["y1loc"][h * 128:(h + 1) * 128, t0:t0 + BLK], ybs[p_][:, :]), deps=[i_ye])
            last[("ybuse", p_)] = i_yst
            i_kd = S("v", lambda e, h=h, t0=t0, p_=p_: e.tensor_tensor(
                kds[p_][:, :], kf[h][:, t0:t0 + BLK], kdechs[p_][:, :], ALU.mult),
                deps=[i_kl, last.get(("kduse", p_))])
            kt_is = []
            for sh_ in range(2):
                i_kt = S("t", lambda e, sh_=sh_, p_=p_: e.transpose(
                    psT[:, :], kds[p_][:, sh_ * 128:(sh_ + 1) * 128], ctx.eyeb[:, :]),
                    deps=[i_kd, last.get("psT")])
                i_kte = S("v", lambda e, sh_=sh_, p_=p_: e.tensor_copy(ktms[p_][sh_][:, :], psT[:, :]),
                          deps=[i_kt, last.get(("ktmuse", p_, sh_))])
                last["psT"] = i_kte
                kt_is.append(i_kte)
            last[("kduse", p_)] = kt_is[1]
            i_kv1 = S("t", lambda e, p_=p_: e.matmul(psKVs[p_][:, :], ktms[p_][0][:, :], vtms[p_][0][:, :],
                                                     start=True, stop=False),
                      deps=[kt_is[0], vt_is[0], last.get(("psKV", p_))])
            i_kv2 = S("t", lambda e, p_=p_: e.matmul(psKVs[p_][:, :], ktms[p_][1][:, :], vtms[p_][1][:, :],
                                                     start=False, stop=True),
                      deps=[kt_is[1], vt_is[1]])
            last[("ktmuse", p_, 0)] = i_kv1
            last[("ktmuse", p_, 1)] = i_kv2
            last[("vtmuse", p_, 0)] = i_kv1
            last[("vtmuse", p_, 1)] = i_kv2
            i_ts = S("v", lambda e, bd=bd, p_=p_: e.tensor_scalar(kvts[p_][:, :], kvs_[p_][:, :],
                                                                  bd, None, ALU.mult),
                     deps=[i_z1 if b == 0 else last[("kv", p_)]])
            i_tt = S("v", lambda e, p_=p_: e.tensor_tensor(kvs_[p_][:, :], kvts[p_][:, :],
                                                           psKVs[p_][:, :], ALU.add),
                     deps=[i_kv2])
            last[("psKV", p_)] = i_tt
            last[("kv", p_)] = i_tt
            i_kb = S("v", lambda e, p_=p_: e.tensor_copy(kvbs[p_][:, :], kvs_[p_][:, :]), deps=[i_tt])
            last[("kvb", p_)] = i_kb
        i_fin = S("g", lambda e, h=h, p_=p_: e.dma_start(io["kvfin"][h * 128:(h + 1) * 128, :],
                                                         kvs_[p_][:, :]), deps=[last[("kv", p_)]])
        last[("kvuse", p_)] = i_fin
        last[("kvbuse", p_)] = last[("kvb", p_)]
        last[("qluse", p_)] = last[("qduse", p_)]
        last[("kluse", p_)] = last[("kduse", p_)]
        last[("dluse", p_)] = last[("qkmuse", p_, 1)]
    ds.emit()
    es.close()



def emit_p10(ctx):
    nc, io = ctx.nc, ctx.io
    es = contextlib.ExitStack()
    sb = lambda n, s, d: es.enter_context(nc.sbuf_tensor(n, s, d))
    qf = [sb(f"pAq{k}", [128, T], BF16) for k in range(16)]
    y1l = sb("pAy", [128, T], BF16)
    accs = sb("pAacc", [128, 16 * 128], F32)
    kvk = sb("pAkvk", [128, 16 * 128], F32)
    l1w = sb("pAl1w", [128, NH1 * 8], F32)
    emk = sb("pAem", [128, 8], F32)
    qdech = sb("pAqd", [128, BLK], F32)
    kvs = sb("pAkvs", [128, 128], F32)
    kvsb = sb("pAkvsb", [128, 128], BF16)
    qd = sb("pAqdt", [128, BLK], BF16)
    ot = sb("pAot", [128, BLK], F32)
    sqo = sb("pAsq", [128, BLK], F32)
    ssq1 = sb("pAssq", [1, T], F32)
    obuf = [sb(f"pAo{k}", [128, T], BF16) for k in range(16)]
    esP = contextlib.ExitStack()
    psY = esP.enter_context(nc.psum_tensor("pApY", [128, BLK], F32))
    psQ = esP.enter_context(nc.psum_tensor("pApQ", [1, BLK], F32))

    a0 = ctx.cnt["ald"]
    with nc.Block() as blk:
        @blk.sync
        def _(sy):
            n = 0
            for k in range(16):
                sy.dma_start(qf[k][:, :], io["qf"][k * 128:(k + 1) * 128, :]).then_inc(ctx.sem("ald"), 16); n += 1
            sy.dma_start(l1w[:, :], io["l1w"][:, :]).then_inc(ctx.sem("ald"), 16); n += 1
            sy.dma_start(emk[:, :], io["emask128"][:, :]).then_inc(ctx.sem("ald"), 16); n += 1
            sy.wait_ge(ctx.sem("ald"), a0 + 16 * n)
            ctx.cnt["ald"] = a0 + 16 * n
        @blk.vector
        def _(v):
            v.memset(accs[:, :], 0.0)
            v.memset(ssq1[0:1, :], 0.0)

    ds = DepSteps(ctx)
    S = ds.S
    last = {}
    for k in range(7):
        i_ld = S("g", lambda e, k=k: e.dma_start(
            kvk[:, :].rearrange("p (a t) -> p a t", t=128),
            io["kvgath"].rearrange("(a p) t -> p a t", p=128)[:, k * 16:(k + 1) * 16, :]),
            deps=[last.get(("kvkuse", h2)) for h2 in range(NH1)])
        for h in range(NH1):
            i_m = S("v", lambda e, h=h, k=k: e.tensor_scalar(
                accs[:, h * 128:(h + 1) * 128], accs[:, h * 128:(h + 1) * 128],
                l1w[:, h * 8 + k:h * 8 + k + 1], None, ALU.mult))
            i_g = S("v", lambda e, h=h, k=k: e.tensor_scalar(
                kvk[:, h * 128:(h + 1) * 128], kvk[:, h * 128:(h + 1) * 128],
                emk[:, k:k + 1], None, ALU.mult), deps=[i_ld])
            i_a = S("v", lambda e, h=h: e.tensor_add(
                accs[:, h * 128:(h + 1) * 128], accs[:, h * 128:(h + 1) * 128],
                kvk[:, h * 128:(h + 1) * 128]))
            last[("kvkuse", h)] = i_a
    qdechs = [qdech, sb("pAqd2", [128, BLK], F32)]
    kvss = [kvs, sb("pAkvs2", [128, 128], F32)]
    kvsbs = [kvsb, sb("pAkvsb2", [128, 128], BF16)]
    qds = [qd, sb("pAqdt2", [128, BLK], BF16)]
    ots = [ot, sb("pAot2", [128, BLK], F32)]
    sqos = [sqo, sb("pAsq2", [128, BLK], F32)]
    y1ls = [y1l, sb("pAy2", [128, T], BF16)]
    psYs = [psY, esP.enter_context(nc.psum_tensor("pApY2", [128, BLK], F32))]
    psQs = [psQ, esP.enter_context(nc.psum_tensor("pApQ2", [1, BLK], F32))]
    i_ssqlast = None
    for h in range(NH1):
        p_ = h % 2
        i_ql = S("g", lambda e, h=h, p_=p_: e.dma_start(qdechs[p_][:, :], io["qdec"][h * 128:(h + 1) * 128, :]),
                 deps=[last.get(("qluse", p_))])
        i_yl = S("g", lambda e, h=h, p_=p_: e.dma_start(y1ls[p_][:, :], io["y1loc"][h * 128:(h + 1) * 128, :]),
                 deps=[last.get(("yluse", p_))])
        for b in range(NBLK):
            t0 = b * BLK
            bdb = float(BDECAY[h] ** b)
            i_kvs = S("v", lambda e, h=h, bdb=bdb, p_=p_: e.tensor_scalar(
                kvss[p_][:, :], accs[:, h * 128:(h + 1) * 128], bdb, None, ALU.mult),
                deps=[last[("kvkuse", h)], last.get(("kvsuse", p_))])
            i_kvb = S("v", lambda e, p_=p_: e.tensor_copy(kvsbs[p_][:, :], kvss[p_][:, :]),
                      deps=[i_kvs, last.get(("kvsbuse", p_))])
            i_qd = S("v", lambda e, h=h, t0=t0, p_=p_: e.tensor_tensor(
                qds[p_][:, :], qf[h][:, t0:t0 + BLK], qdechs[p_][:, :], ALU.mult),
                deps=[i_ql, last.get(("qduse", p_))])
            i_mm = S("t", lambda e, p_=p_: e.matmul(psYs[p_][:, :], kvsbs[p_][:, :], qds[p_][:, :],
                                                    start=True, stop=True),
                     deps=[i_kvb, i_qd, last.get(("psY", p_))])
            last[("kvsuse", p_)] = i_mm
            last[("kvsbuse", p_)] = i_mm
            last[("qduse", p_)] = i_mm
            i_ot = S("v", lambda e, t0=t0, p_=p_: e.tensor_tensor(
                ots[p_][:, :], psYs[p_][:, :], y1ls[p_][:, t0:t0 + BLK], ALU.add),
                deps=[i_mm, i_yl, last.get(("otuse", p_))])
            last[("psY", p_)] = i_ot
            i_sq = S("v", lambda e, p_=p_: e.tensor_mul(sqos[p_][:, :], ots[p_][:, :], ots[p_][:, :]),
                     deps=[last.get(("squse", p_))])
            i_mq = S("t", lambda e, p_=p_: e.matmul(psQs[p_][:, :], ctx.ones128[:, 0:1], sqos[p_][:, :],
                                                    start=True, stop=True),
                     deps=[i_sq, last.get(("psQ", p_))])
            last[("squse", p_)] = i_mq
            i_sa = S("v", lambda e, t0=t0, p_=p_: e.tensor_tensor(
                ssq1[0:1, t0:t0 + BLK], ssq1[0:1, t0:t0 + BLK], psQs[p_][0:1, :], ALU.add),
                deps=[i_mq, i_ssqlast])
            i_ssqlast = i_sa
            last[("psQ", p_)] = i_sa
            i_oc = S("v", lambda e, h=h, t0=t0, p_=p_: e.tensor_copy(obuf[h][:, t0:t0 + BLK], ots[p_][:, :]))
            last[("otuse", p_)] = i_oc
        last[("qluse", p_)] = last[("qduse", p_)]
        last[("yluse", p_)] = last[("otuse", p_)]
    ds.emit()
    esP.close()

    rrow = sb("pArr", [1, T], F32)
    rb1 = sb("pArb", [128, T], F32)
    nw1 = sb("pAnw", [128, 16], F32)
    psB = es.enter_context(nc.psum_tensor("pApB", [128, 512], F32))
    a0 = ctx.cnt["ald"]
    with nc.Block() as blk:
        @blk.sync
        def _(sy):
            sy.dma_start(nw1[:, :], io["nw1cols"][:, :]).then_inc(ctx.sem("ald"), 16)
            sy.wait_ge(ctx.sem("ald"), a0 + 16)
            ctx.cnt["ald"] = a0 + 16
    steps2 = []
    def S2(eng, fn):
        steps2.append((eng, fn))
    S2("s", lambda e: e.activation(rrow[0:1, :], ssq1[0:1, :], AF.Ln, scale=1.0 / D, bias=EPS1))
    S2("s", lambda e: e.activation(rrow[0:1, :], rrow[0:1, :], AF.Exp, scale=-0.5))
    for j in range(2):
        S2("t", lambda e, j=j: e.matmul(psB[:, :], ctx.onec[0:1, :], rrow[0:1, j * 512:(j + 1) * 512],
                                        start=True, stop=True))
        S2("v", lambda e, j=j: e.tensor_copy(rb1[:, j * 512:(j + 1) * 512], psB[:, :]))
    aux = ctx.sem("aux")
    A0 = ctx.cnt["aux"]
    tgts = [A0 + i for i in range(len(steps2))]
    with nc.Block() as blk:
        for eng_name, body in [("v", "vector"), ("s", "scalar"), ("t", "tensor")]:
            def mk(eng_name=eng_name):
                def emit(e):
                    for i, (en, fn) in enumerate(steps2):
                        if en != eng_name:
                            continue
                        e.wait_ge(aux, tgts[i])
                        fn(e).then_inc(aux, 1)
                return emit
            getattr(blk, body)(mk())
    ctx.cnt["aux"] = A0 + len(steps2)

    esD = contextlib.ExitStack()
    xnf = [esD.enter_context(nc.sbuf_tensor(f"pAx{k}", [128, T], BF16)) for k in range(16)]
    wb = [esD.enter_context(nc.sbuf_tensor(f"pAw{i}", [128, 16 * 128], BF16)) for i in range(3)]
    sgt = [esD.enter_context(nc.sbuf_tensor(f"pAsg{i}", [128, 512], F32)) for i in range(2)]
    ofb = [esD.enter_context(nc.sbuf_tensor(f"pAof{k}", [128, T], BF16)) for k in range(16)]
    ob = [esD.enter_context(nc.sbuf_tensor(f"pAob{i}", [128, 512], F32)) for i in range(4)]
    ps = [esD.enter_context(nc.psum_tensor(f"pAp{i}", [128, 512], F32)) for i in range(4)]
    load_fm_tiles(ctx, io["xn1tm"], D, T, xnf)

    def evs_(se, ji, ti, p, ot_, stile):
        return se.activation(stile[:, :], p[:, :], AF.Sigmoid)

    def evv_(ve, ji, ti, p, ot_, stile):
        t0 = ti * 512
        ve.scalar_tensor_tensor(ofb[ji][:, t0:t0 + 512], obuf[ji][:, t0:t0 + 512],
                                nw1[:, ji:ji + 1], stile[:, :], ALU.mult, ALU.mult)
        return ve.tensor_tensor(ofb[ji][:, t0:t0 + 512], ofb[ji][:, t0:t0 + 512],
                                rb1[:, t0:t0 + 512], ALU.mult)

    linear(ctx, dict(
        name="wg1", wt=io["wgt"], K=D, N=D, T=T, tw=512,
        x_tile=lambda kc: xnf[kc],
        psums=ps, wbufs=wb, sbufs=sgt, evac_s=evs_, evac_v=evv_, out_ap=None))

    def evo_(ve, ji, ti, p, ot_, stile):
        return ve.tensor_copy(ot_[:, :], p[:, :])

    linear(ctx, dict(
        name="wo1", wt=io["wo1"], K=D, N=D, T=T, tw=512,
        x_tile=lambda kc: ofb[kc],
        psums=ps, wbufs=wb, obufs=ob, evac_v=evo_,
        out_ap=lambda ji, ti: io["s1fm"][ji * 128:(ji + 1) * 128, ti * 512:(ti + 1) * 512]))
    esD.close()
    es.close()


W2ROWS = HID // 8

def emit_qa(ctx):
    nc, io = ctx.nc, ctx.io
    es = contextlib.ExitStack()
    sb = lambda n, s, d: es.enter_context(nc.sbuf_tensor(n, s, d))
    wbuf = sb("qaw", [128, HID], F32)
    cs = sb("qacs", [128, 1], F32)
    row = sb("qarow", [1, 4], F32)
    ps1 = es.enter_context(nc.psum_tensor("qap", [1, 1], F32))
    steps = []
    def S(eng, fn):
        steps.append((eng, fn))
    jobs = [("w1s", [(0, 128), (128, 128)], 0),
            ("w3s", [(0, 128), (128, 128)], 1),
            ("w2s", [(128 * j, 128) for j in range(5)] + [(640, 64)], 2)]
    for name, slices, col in jobs:
        width = HID if name != "w2s" else D
        nsl = len(slices)
        for si, (r0, nr) in enumerate(slices):
            S("g", lambda e, name=name, r0=r0, nr=nr, width=width: e.dma_start(
                wbuf[0:nr, 0:width], io[name][r0:r0 + nr, 0:width]))
            S("v", lambda e, nr=nr, width=width: e.reduce_sum(
                cs[0:nr, 0:1], wbuf[0:nr, 0:width], axis=AX, apply_absolute_value=True))
            S("t", lambda e, nr=nr, si=si, nsl=nsl: e.matmul(
                ps1[0:1, 0:1], cs[0:nr, 0:1], ctx.ones128[0:nr, 0:1],
                start=(si == 0), stop=(si == nsl - 1)))
        S("v", lambda e, col=col: e.tensor_copy(row[0:1, col:col + 1], ps1[0:1, 0:1]))
    S("g", lambda e: e.dma_start(io["absq"][0:1, :], row[0:1, :]))
    aux = ctx.sem("aux")
    A0 = ctx.cnt["aux"]
    incs = [16 if en == "g" else 1 for en, _ in steps]
    tgts = [A0 + sum(incs[:i]) for i in range(len(steps))]
    with nc.Block() as blk:
        for eng_name, body in [("v", "vector"), ("t", "tensor"), ("g", "gpsimd")]:
            def mk(eng_name=eng_name):
                def emit(e):
                    for i, (en, fn) in enumerate(steps):
                        if en != eng_name:
                            continue
                        e.wait_ge(aux, tgts[i])
                        fn(e).then_inc(aux, incs[i])
                return emit
            getattr(blk, body)(mk())
        @blk.sync
        def _(sy):
            sy.wait_ge(aux, A0 + sum(incs))
    ctx.cnt["aux"] = A0 + sum(incs)
    es.close()


def emit_qb(ctx):
    nc, io = ctx.nc, ctx.io
    es = contextlib.ExitStack()
    sb = lambda n, s, d: es.enter_context(nc.sbuf_tensor(n, s, d))
    flat = sb("qbf", [1, 32], F32)
    acc = sb("qbacc", [1, 4], F32)
    scr = sb("qbsc", [1, 4], F32)
    thbc = sb("qbth", [128, 4], F32)
    wbuf = sb("qbw", [128, HID], F32)
    sg = sb("qbsg", [128, HID], F32)
    ab = sb("qbab", [128, HID], F32)
    tq = sb("qbtq", [128, HID], BF16)
    psB = es.enter_context(nc.psum_tensor("qbp", [128, 4], F32))
    NORM = 1.0 / (HID0 * D)
    steps = []
    def S(eng, fn):
        steps.append((eng, fn))
    S("g", lambda e: e.dma_start(flat[0:1, :], io["absg"].rearrange("c v -> (c v)").rearrange("(a f) -> a f", a=1)))
    S("v", lambda e: e.tensor_copy(acc[0:1, :], flat[0:1, 0:4]))
    for k in range(1, 8):
        S("v", lambda e, k=k: e.tensor_add(acc[0:1, :], acc[0:1, :], flat[0:1, 4 * k:4 * k + 4]))
    S("v", lambda e: e.tensor_scalar(scr[0:1, :], acc[0:1, :], NORM, None, ALU.mult))
    S("g", lambda e: e.dma_start(io["sc"][0:1, :], scr[0:1, :]))
    S("t", lambda e: e.matmul(psB[:, :], ctx.onec[0:1, :], scr[0:1, :], start=True, stop=True))
    S("v", lambda e: e.tensor_scalar(thbc[:, :], psB[:, :], 0.5, None, ALU.mult))
    jobs = [("w1s", "tq1s", [(128 * j, 128, 128 * j) for j in range(2)], 0),
            ("w3s", "tq3s", [(128 * j, 128, 128 * j) for j in range(2)], 1),
            ("w2s", "tq2s", [(128 * j, 128 if j < 5 else 64, 128 * j) for j in range(6)], 2)]
    for name, oname, slices, col in jobs:
        width = HID if name != "w2s" else D
        for (r0, nr, o0) in slices:
            S("g", lambda e, name=name, r0=r0, nr=nr, width=width: e.dma_start(
                wbuf[0:nr, 0:width], io[name][r0:r0 + nr, 0:width]))
            S("s", lambda e, nr=nr, width=width: e.activation(sg[0:nr, 0:width], wbuf[0:nr, 0:width], AF.Sign))
            S("s", lambda e, nr=nr, width=width: e.activation(ab[0:nr, 0:width], wbuf[0:nr, 0:width], AF.Abs))
            S("v", lambda e, nr=nr, width=width, col=col: e.tensor_scalar(
                ab[0:nr, 0:width], ab[0:nr, 0:width], thbc[0:nr, col:col + 1], None, ALU.is_gt))
            S("v", lambda e, nr=nr, width=width: e.tensor_tensor(
                tq[0:nr, 0:width], sg[0:nr, 0:width], ab[0:nr, 0:width], ALU.mult))
            S("g", lambda e, oname=oname, o0=o0, nr=nr, width=width: e.dma_start(
                io[oname][o0:o0 + nr, 0:width], tq[0:nr, 0:width]))
    aux = ctx.sem("aux")
    A0 = ctx.cnt["aux"]
    incs = [16 if en == "g" else 1 for en, _ in steps]
    tgts = [A0 + sum(incs[:i]) for i in range(len(steps))]
    with nc.Block() as blk:
        for eng_name, body in [("v", "vector"), ("s", "scalar"), ("t", "tensor"), ("g", "gpsimd")]:
            def mk(eng_name=eng_name):
                def emit(e):
                    for i, (en, fn) in enumerate(steps):
                        if en != eng_name:
                            continue
                        e.wait_ge(aux, tgts[i])
                        fn(e).then_inc(aux, incs[i])
                return emit
            getattr(blk, body)(mk())
        @blk.sync
        def _(sy):
            sy.wait_ge(aux, A0 + sum(incs))
    ctx.cnt["aux"] = A0 + sum(incs)
    es.close()



def emit_p12(ctx):
    nc, io = ctx.nc, ctx.io
    es0 = contextlib.ExitStack()
    sb0 = lambda n, s, d: es0.enter_context(nc.sbuf_tensor(n, s, d))
    scb = sb0("pCsc", [128, 4], F32)
    scrow = sb0("pCsr", [1, 4], F32)
    psB = es0.enter_context(nc.psum_tensor("pCpB", [128, 4], F32))
    a0 = ctx.cnt["ald"]
    with nc.Block() as blk:
        @blk.sync
        def _(sy):
            sy.dma_start(scrow[0:1, :], io["sc"][0:1, :]).then_inc(ctx.sem("ald"), 16)
            sy.wait_ge(ctx.sem("ald"), a0 + 16)
            ctx.cnt["ald"] = a0 + 16
    steps = []
    def S(eng, fn):
        steps.append((eng, fn))
    S("t", lambda e: e.matmul(psB[:, :], ctx.onec[0:1, :], scrow[0:1, :], start=True, stop=True))
    S("v", lambda e: e.tensor_copy(scb[:, :], psB[:, :]))
    aux = ctx.sem("aux")
    A0 = ctx.cnt["aux"]
    tgts = [A0 + i for i in range(len(steps))]
    with nc.Block() as blk:
        for eng_name, body in [("v", "vector"), ("t", "tensor")]:
            def mk(eng_name=eng_name):
                def emit(e):
                    for i, (en, fn) in enumerate(steps):
                        if en != eng_name:
                            continue
                        e.wait_ge(aux, tgts[i])
                        fn(e).then_inc(aux, 1)
                return emit
            getattr(blk, body)(mk())
    ctx.cnt["aux"] = A0 + len(steps)

    for th in range(2):
        t0 = th * 512
        esT = contextlib.ExitStack()
        sbT = lambda n, s, d: esT.enter_context(nc.sbuf_tensor(n, s, d))
        g1 = [sbT(f"pCg{th}_{k}", [128, 512], BF16) for k in range(44)]
        xnf = [sbT(f"pCxn{th}_{k}", [128, 512], BF16) for k in range(16)]
        s1T = [sbT(f"pCs1{th}_{k}", [128, 512], F32) for k in range(16)]
        x1T = [sbT(f"pCx1{th}_{k}", [128, 512], F32) for k in range(16)]
        wb = [sbT(f"pCwb{th}_{i}", [128, 44 * 128], BF16) for i in range(3)]
        gsc = [sbT(f"pCgs{th}_{i}", [128, 512], F32) for i in range(2)]
        ps = [esT.enter_context(nc.psum_tensor(f"pCp{th}_{i}", [128, 512], F32)) for i in range(4)]
        a0 = ctx.cnt["ald"]
        with nc.Block() as blk:
            @blk.sync
            def _(sy):
                n = 0
                for k in range(16):
                    sy.dma_start(xnf[k][:, :], io["xn1tm"][t0:t0 + 512, k * 128:(k + 1) * 128],
                                 transpose=True).then_inc(ctx.sem("ald"), 16); n += 1
                    sy.dma_start(s1T[k][:, :], io["s1fm"][k * 128:(k + 1) * 128, t0:t0 + 512]).then_inc(ctx.sem("ald"), 16); n += 1
                    sy.dma_start(x1T[k][:, :], io["x1fm"][k * 128:(k + 1) * 128, t0:t0 + 512]).then_inc(ctx.sem("ald"), 16); n += 1
                sy.wait_ge(ctx.sem("ald"), a0 + 16 * n)
                ctx.cnt["ald"] = a0 + 16 * n
        def ev3(ve, ji, ti, p, ot_, stile):
            return ve.tensor_scalar(g1[ji][:, :], p[:, :], scb[:, 1:2], None, ALU.mult)

        linear(ctx, dict(
            name=f"h3_{th}", wt=io["tq3"], K=D, N=HID, T=512, tw=512,
            x_tile=lambda kc: xnf[kc],
            psums=ps, wbufs=wb, evac_v=ev3, out_ap=None))

        def ev1s(se, ji, ti, p, ot_, stile):
            return se.activation(stile[:, :], p[:, :], AF.Silu, scale=scb[:, 0:1])

        def ev1v(ve, ji, ti, p, ot_, stile):
            return ve.tensor_tensor(g1[ji][:, :], g1[ji][:, :], stile[:, :], ALU.mult)

        linear(ctx, dict(
            name=f"h1_{th}", wt=io["tq1"], K=D, N=HID, T=512, tw=512,
            x_tile=lambda kc: xnf[kc],
            psums=ps, wbufs=wb, sbufs=gsc, evac_s=ev1s, evac_v=ev1v, out_ap=None))

        wb2 = [esT.enter_context(nc.sbuf_tensor(f"pCw2{th}_{i}", [128, 44 * 128], BF16)) for i in range(2)]
        ob = [esT.enter_context(nc.sbuf_tensor(f"pCob{th}_{i}", [128, 512], F32)) for i in range(4)]

        def evg(se, ji, ti, p, ot_, stile):
            return se.activation(stile[:, :], p[:, :], AF.Silu, scale=scb[:, 2:3])

        def evx(ve, ji, ti, p, ot_, stile):
            ve.tensor_tensor(ot_[:, :], stile[:, :], s1T[ji][:, :], ALU.mult)
            return ve.tensor_tensor(ot_[:, :], ot_[:, :], x1T[ji][:, :], ALU.add)

        linear(ctx, dict(
            name=f"w2_{th}", wt=io["tq2"], K=HID, N=D, T=512, tw=512,
            x_tile=lambda kc: g1[kc],
            psums=ps, wbufs=wb2, sbufs=gsc, obufs=ob, evac_s=evg, evac_v=evx,
            out_ap=lambda ji, ti, t0=t0: io["x2fm"][ji * 128:(ji + 1) * 128, t0:t0 + 512]))
        esT.close()

    esF = contextlib.ExitStack()
    xr = esF.enter_context(nc.sbuf_tensor("pCxr", [128, 16 * 128], F32))
    otile = esF.enter_context(nc.sbuf_tensor("pCot", [128, D], F32))
    psT = esF.enter_context(nc.psum_tensor("pCpT", [128, 128], F32))
    x2src = io["x2fm"].rearrange("(a p) t -> p a t", p=128)
    steps = []
    for i in range(8):
        S("g", lambda e, i=i: e.dma_start(xr[:, :].rearrange("p (a t) -> p a t", t=128),
                                          x2src[:, :, i * 128:(i + 1) * 128]))
        for kc in range(16):
            S("t", lambda e, kc=kc: e.transpose(psT[:, :], xr[:, kc * 128:(kc + 1) * 128],
                                                ctx.eye[:, :]))
            S("v", lambda e, kc=kc: e.tensor_copy(otile[:, kc * 128:(kc + 1) * 128], psT[:, :]))
        S("g", lambda e, i=i: e.dma_start(io["ys"][i * 128:(i + 1) * 128, :], otile[:, :]))
    A0 = ctx.cnt["aux"]
    incs = [16 if en == "g" else 1 for en, _ in steps]
    tgts = [A0 + sum(incs[:i]) for i in range(len(steps))]
    with nc.Block() as blk:
        for eng_name, body in [("v", "vector"), ("t", "tensor"), ("g", "gpsimd")]:
            def mk(eng_name=eng_name):
                def emit(e):
                    for i, (en, fn) in enumerate(steps):
                        if en != eng_name:
                            continue
                        e.wait_ge(aux, tgts[i])
                        fn(e).then_inc(aux, incs[i])
                return emit
            getattr(blk, body)(mk())
        @blk.sync
        def _(sy):
            sy.wait_ge(aux, A0 + sum(incs))
    ctx.cnt["aux"] = A0 + sum(incs)
    esF.close()
    es0.close()


class DepSteps:

    ENGS = [("v", "vector"), ("s", "scalar"), ("t", "tensor"), ("g", "gpsimd"), ("p", "gpsimd")]

    def __init__(self, ctx):
        self.ctx = ctx
        self.steps = []

    def S(self, eng, fn, deps=()):
        self.steps.append((eng, fn, [d for d in deps if d is not None]))
        return len(self.steps) - 1

    def emit(self):
        ctx = self.ctx
        nc = ctx.nc
        sems = {e: ctx.sem("x" + e) for e, _ in self.ENGS}
        base = {e: ctx.cnt["x" + e] for e, _ in self.ENGS}
        cum = {}
        run = dict(base)
        for i, (en, fn, deps) in enumerate(self.steps):
            run[en] += 16 if en == "g" else 1
            cum[i] = run[en]
        gp_classes = [e for e in ("g", "p") if any(en == e for en, _, _ in self.steps)]
        with nc.Block() as blk:
            for eng, body in [("v", "vector"), ("s", "scalar"), ("t", "tensor")]:
                if not any(en == eng for en, _, _ in self.steps):
                    continue

                def mk(eng=eng):
                    def emit_body(e):
                        water = {o: base[o] for o, _ in self.ENGS}
                        for i, (en, fn, deps) in enumerate(self.steps):
                            if en != eng:
                                continue
                            for d in deps:
                                den = self.steps[d][0]
                                if den == eng:
                                    continue
                                tgt = cum[d]
                                if tgt > water[den]:
                                    e.wait_ge(sems[den], tgt)
                                    water[den] = tgt
                            fn(e).then_inc(sems[eng], 1)
                    return emit_body
                getattr(blk, body)(mk())
            if gp_classes:
                @blk.gpsimd
                def _(e):
                    water = {o: base[o] for o, _ in self.ENGS}
                    for i, (en, fn, deps) in enumerate(self.steps):
                        if en not in ("g", "p"):
                            continue
                        for d in deps:
                            den = self.steps[d][0]
                            if den in ("g", "p") and den == en:
                                continue
                            if den in ("g", "p") and den != en:
                                pass
                            elif den == en:
                                continue
                            tgt = cum[d]
                            if self.steps[d][0] in ("g", "p"):
                                continue
                            if tgt > water[den]:
                                e.wait_ge(sems[den], tgt)
                                water[den] = tgt
                        fn(e).then_inc(sems[en], 16 if en == "g" else 1)
                        if en == "g":
                            e.wait_ge(sems[en], cum[i])
            @blk.sync
            def _(sy):
                for eng in set(en for en, _, _ in self.steps):
                    if run[eng] > base[eng]:
                        sy.wait_ge(sems[eng], run[eng])
        for eng, _ in self.ENGS:
            ctx.cnt["x" + eng] = run[eng]


def emit_p45(ctx, gather_cb):
    nc, io = ctx.nc, ctx.io
    ctx.p45_es0 = contextlib.ExitStack()
    ssq = ctx.p45_es0.enter_context(nc.sbuf_tensor("fssq", [1, T], F32))
    esP = contextlib.ExitStack()
    sbP = lambda n, s, d: esP.enter_context(nc.sbuf_tensor(n, s, d))
    eloc = sbP("fEl", [64, T], F32)
    eglo = sbP("fEg", [64, T], F32)
    LgT = sbP("fLgT", [128, 8 * 64], F32)
    RT = sbP("fRT", [128, 8 * 64], F32)
    wT = sbP("fwT", [128, 8 * 64], F32)
    Ecol = sbP("fE", [64, 16], F32)
    Dflat = sbP("fDf", [1, 512], F32)
    Eflat = sbP("fEf", [1, 576], F32)
    dpc = sbP("fdp", [64, 64], F32)
    caus = sbP("fca", [128, 128], F32)
    zt = sbP("fz", [128, 128], F32)
    elocT = sbP("felT", [128, 8 * 64], F32)
    nw64 = sbP("fnw", [64, 64], F32)
    Sf = sbP("fS", [64, 4096], F32)
    Sin = sbP("fsin", [64, 4096], BF16)
    xtm = [sbP(f"fxt{i}", [128, 1024], BF16) for i in range(8)]
    btm = [sbP(f"fbt{i}", [128, 1024], BF16) for i in range(8)]
    ctx.p45 = {"ssq": ssq}

    esQ = contextlib.ExitStack()
    sbQ = lambda n, s, d: esQ.enter_context(nc.sbuf_tensor(n, s, d))
    dtr_ = sbQ("fdtr", [64, T], F32)
    dts = sbQ("fdts", [64, T], F32)
    tmpa = sbQ("ftma", [64, T], F32)
    tmpb = sbQ("ftmb", [64, T], F32)
    lndt = sbQ("fln", [64, T], F32)
    Lg = sbQ("flg", [64, T], F32)
    Rt = sbQ("fr", [64, T], F32)
    wsc = sbQ("fw", [64, T], F32)
    dE = sbQ("fdE", [64, 9], F32)
    dtb = sbQ("fdtb", [64, 1], F32)
    alog = sbQ("fal", [64, 1], F32)
    Acol = sbQ("fA", [64, 1], F32)
    psP = esQ.enter_context(nc.psum_tensor("fpP", [128, 128], F32))
    dram_tmp = nc.dram_tensor("f45dram", [64, 9], F32).ap()

    a0 = ctx.cnt["ald"]
    with nc.Block() as blk:
        @blk.sync
        def _(sy):
            n = 0
            for i in range(8):
                sy.dma_start(xtm[i][:, :], io["xmf"][0:1024, i * 128:(i + 1) * 128], transpose=True).then_inc(ctx.sem("ald"), 16); n += 1
                sy.dma_start(btm[i][:, :], io["bmf"][0:1024, i * 128:(i + 1) * 128], transpose=True).then_inc(ctx.sem("ald"), 16); n += 1
            for ap, name in [(dtr_, "dtr"), (dtb, "dtb"), (alog, "alog"), (dpc, "dpc64"),
                             (caus, "causT"), (nw64, "nw64")]:
                sy.dma_start(ap[:, :], io[name][:, :]).then_inc(ctx.sem("ald"), 16); n += 1
            sy.wait_ge(ctx.sem("ald"), a0 + 16 * n)
            ctx.cnt["ald"] = a0 + 16 * n
        @blk.vector
        def _(v):
            v.memset(zt[:, :], 0.0)
            v.memset(Ecol[:, :], 0.0)
            v.memset(Sf[:, :], 0.0)
            v.memset(ssq[0:1, :], 0.0)

    steps = []
    def S(eng, fn):
        steps.append((eng, fn))
    S("s", lambda e: e.activation(dts[:, :], dtr_[:, :], AF.Identity, bias=dtb[:, 0:1]))
    S("s", lambda e: e.activation(tmpa[:, :], dts[:, :], AF.Abs))
    S("s", lambda e: e.activation(tmpb[:, :], tmpa[:, :], AF.Exp, scale=-1.0))
    S("s", lambda e: e.activation(tmpa[:, :], tmpb[:, :], AF.Ln, bias=1.0))
    S("v", lambda e: e.tensor_relu(tmpb[:, :], dts[:, :]))
    S("v", lambda e: e.tensor_add(dts[:, :], tmpa[:, :], tmpb[:, :]))
    S("s", lambda e: e.activation(lndt[:, :], dts[:, :], AF.Ln))
    S("s", lambda e: e.activation(Acol[:, 0:1], alog[:, 0:1], AF.Exp))
    S("v", lambda e: e.tensor_scalar(Acol[:, 0:1], Acol[:, 0:1], -1.0, None, ALU.mult))
    S("v", lambda e: e.tensor_scalar(tmpa[:, :], dts[:, :], Acol[:, 0:1], None, ALU.mult))
    S("v", lambda e: e.tensor_tensor_scan(Lg[:, :], tmpa[:, :], tmpa[:, :], 0.0, ALU.add, ALU.bypass))
    S("v", lambda e: e.tensor_copy(Ecol[:, 1:9], Lg[:, 127::128]))
    S("v", lambda e: e.tensor_sub(Rt[:, :], lndt[:, :], Lg[:, :]))
    for c in range(NCH):
        S("v", lambda e, c=c: e.tensor_scalar(tmpa[:, c * 128:(c + 1) * 128], Lg[:, c * 128:(c + 1) * 128], Ecol[:, c:c + 1], None, ALU.subtract))
        S("v", lambda e, c=c: e.tensor_scalar(tmpb[:, c * 128:(c + 1) * 128], Rt[:, c * 128:(c + 1) * 128], Ecol[:, c + 1:c + 2], None, ALU.add))
    S("s", lambda e: e.activation(eloc[:, :], tmpa[:, :], AF.Exp))
    S("s", lambda e: e.activation(eglo[:, :], Lg[:, :], AF.Exp))
    S("s", lambda e: e.activation(wsc[:, :], tmpb[:, :], AF.Exp))
    S("v", lambda e: e.tensor_sub(dE[:, 0:8], Ecol[:, 1:9], Ecol[:, 0:8]))
    S("s", lambda e: e.activation(dE[:, 0:8], dE[:, 0:8], AF.Exp))
    S("s", lambda e: e.activation(dE[:, 8:9], Ecol[:, 0:8][:, 0:1], AF.Identity))
    S("g", lambda e: e.dma_start(dram_tmp[:, 0:8], dE[:, 0:8]))
    for c in range(NCH):
        S("g", lambda e, c=c: e.dma_start(Dflat[0:1, c * 64:(c + 1) * 64],
                                          dram_tmp[:, c:c + 1].rearrange("h c -> c h"),
                                          allow_slow_non_contiguous=True))
    S("s", lambda e: e.activation(dE[:, :], Ecol[:, 0:9], AF.Exp))
    S("g", lambda e: e.dma_start(dram_tmp[:, 0:9], dE[:, 0:9]))
    for c in range(9):
        S("g", lambda e, c=c: e.dma_start(Eflat[0:1, c * 64:(c + 1) * 64],
                                          dram_tmp[:, c:c + 1].rearrange("h c -> c h"),
                                          allow_slow_non_contiguous=True))
    for c in range(NCH):
        S("t", lambda e, c=c: e.transpose(psP[:, 0:64], Lg[:, c * 128:(c + 1) * 128], ctx.eye[0:64, 0:64]))
        S("v", lambda e, c=c: e.tensor_copy(LgT[:, c * 64:(c + 1) * 64], psP[:, 0:64]))
        S("t", lambda e, c=c: e.transpose(psP[:, 0:64], Rt[:, c * 128:(c + 1) * 128], ctx.eye[0:64, 0:64]))
        S("v", lambda e, c=c: e.tensor_copy(RT[:, c * 64:(c + 1) * 64], psP[:, 0:64]))
        S("t", lambda e, c=c: e.transpose(psP[:, 0:64], wsc[:, c * 128:(c + 1) * 128], ctx.eye[0:64, 0:64]))
        S("v", lambda e, c=c: e.tensor_copy(wT[:, c * 64:(c + 1) * 64], psP[:, 0:64]))
        S("t", lambda e, c=c: e.transpose(psP[:, 0:64], eloc[:, c * 128:(c + 1) * 128], ctx.eye[0:64, 0:64]))
        S("v", lambda e, c=c: e.tensor_copy(elocT[:, c * 64:(c + 1) * 64], psP[:, 0:64]))
    aux = ctx.sem("aux")
    A0 = ctx.cnt["aux"]
    incs = [16 if en == "g" else 1 for en, _ in steps]
    tgts = [A0 + sum(incs[:i]) for i in range(len(steps))]
    with nc.Block() as blk:
        for eng_name, body in [("v", "vector"), ("s", "scalar"), ("t", "tensor"), ("g", "gpsimd")]:
            def mk(eng_name=eng_name):
                def emit(e):
                    for i, (en, fn) in enumerate(steps):
                        if en != eng_name:
                            continue
                        e.wait_ge(aux, tgts[i])
                        fn(e).then_inc(aux, incs[i])
                return emit
            getattr(blk, body)(mk())
    ctx.cnt["aux"] = A0 + sum(incs)
    esQ.close()

    esS = contextlib.ExitStack()
    sbS = lambda n, s, d: esS.enter_context(nc.sbuf_tensor(n, s, d))
    bw = [sbS(f"fbw{i}", [128, 64], BF16) for i in range(2)]
    st1 = [sbS(f"fst{i}", [64, 64], F32) for i in range(2)]
    Dbc = sbS("fDbc", [64, 64], F32)
    Dbt = sbS("fDbt", [64, 64], F32)
    psS = [esS.enter_context(nc.psum_tensor(f"fpS{i}", [64, 64], F32)) for i in range(2)]
    psD = esS.enter_context(nc.psum_tensor("fpD", [64, 64], F32))
    ds = DepSteps(ctx)
    Sd = ds.S
    last = {}
    for c in range(NCH):
        i_dm = Sd("t", lambda e, c=c: e.matmul(psD[:, :], ctx.onec[0:1, 0:64],
                                               Dflat[0:1, c * 64:(c + 1) * 64], start=True, stop=True),
                  deps=[last.get("Dbcuse")])
        i_db0 = Sd("v", lambda e: e.tensor_copy(Dbt[:, :], psD[:, :]), deps=[i_dm, last.get("Dbtuse")])
        i_db = Sd("v", lambda e: e.tensor_copy(Dbc[:, :], Dbt[:, :]), deps=[last.get("Dbcuse")])
        last["Dbtuse"] = i_db
        lastdb = i_db
        for h in range(NH):
            p_ = h % 2
            i_bw = Sd("s", lambda e, c=c, h=h, p_=p_: e.activation(
                bw[p_][:, :], btm[c][:, (h // 4) * 64:(h // 4) * 64 + 64], AF.Identity,
                scale=wT[:, c * 64 + h:c * 64 + h + 1]), deps=[last.get(("bwuse", p_))])
            i_ds = Sd("t", lambda e, c=c, h=h, p_=p_: e.matmul(
                psS[p_][:, :], bw[p_][:, :], xtm[c][:, (h // 4) * 64:(h // 4) * 64 + 64],
                start=True, stop=True), deps=[i_bw, last.get(("psS", p_))])
            last[("bwuse", p_)] = i_ds
            i_t1 = Sd("v", lambda e, h=h, p_=p_: e.tensor_scalar(
                st1[p_][:, :], Sf[:, h * 64:(h + 1) * 64], Dbc[:, h:h + 1], None, ALU.mult),
                deps=[i_db, last.get(("Sup", h)), last.get(("st1use", p_))])
            i_t2 = Sd("v", lambda e, h=h, p_=p_: e.tensor_tensor(
                Sf[:, h * 64:(h + 1) * 64], st1[p_][:, :], psS[p_][:, :], ALU.add),
                deps=[i_ds, i_t1])
            last[("psS", p_)] = i_t2
            last[("st1use", p_)] = i_t2
            last[("Sup", h)] = i_t2
        last["Dbcuse"] = last[("Sup", 63)]
    prev = None
    for h in range(NH):
        prev = Sd("g", lambda e, h=h: e.dma_start(io["sfin"][h * 64:(h + 1) * 64, :],
                                                  Sf[:, h * 64:(h + 1) * 64]),
                  deps=[last[("Sup", h)], prev])
    Sd("g", lambda e: e.dma_start(io["elast"][:, 0:1], Ecol[:, 8:9]), deps=[prev])
    ds.emit()
    esS.close()

    gather_cb(ctx)

    esH = contextlib.ExitStack()
    sbH = lambda n, s, d: esH.enter_context(nc.sbuf_tensor(n, s, d))
    sel64 = sbH("fhs64", [64, 64 * 64], F32)
    Esb = sbH("fhE", [64, 8], F32)
    emk = sbH("fhem", [64, 8], F32)
    Wbc = sbH("fhW", [64, 64 * 8], F32)
    acc = sbH("fhacc", [64, 4096], F32)
    skb = sbH("fhskb", [64, 4096], F32)
    sgg = sbH("fhsgg", [64, 4096], F32)
    psW = esH.enter_context(nc.psum_tensor("fhpW", [64, 8], F32))
    a0 = ctx.cnt["ald"]
    with nc.Block() as blk:
        @blk.sync
        def _(sy):
            n = 0
            for k in range(8):
                sy.dma_start(Esb[:, k:k + 1], io["egath"][k * 64:(k + 1) * 64, 0:1]).then_inc(ctx.sem("ald"), 16); n += 1
            sy.dma_start(emk[:, :], io["emask64"][:, :]).then_inc(ctx.sem("ald"), 16); n += 1
            sy.dma_start(sel64[:, :], io["sel64"][:, :]).then_inc(ctx.sem("ald"), 16); n += 1
            sy.wait_ge(ctx.sem("ald"), a0 + 16 * n)
            ctx.cnt["ald"] = a0 + 16 * n
        @blk.vector
        def _(v):
            v.memset(acc[:, :], 0.0)
    steps = []
    S("v", lambda e: e.tensor_mul(Esb[:, :], Esb[:, :], emk[:, :]))
    S("s", lambda e: e.activation(Esb[:, :], Esb[:, :], AF.Exp))
    for h in range(64):
        S("t", lambda e, h=h: e.matmul(psW[:, :], sel64[:, h * 64:(h + 1) * 64], Esb[:, :],
                                       start=True, stop=True))
        S("v", lambda e, h=h: e.tensor_copy(Wbc[:, h * 8:(h + 1) * 8], psW[:, :]))
    for k in range(7):
        sgsrc = io["sgath"].rearrange("(c hh n) p -> c n hh p", c=8, n=64)
        for d in range(4):
            S("g", lambda e, k=k, d=d: e.dma_start(
                skb[:, d * 16 * 64:(d + 1) * 16 * 64].rearrange("n (hh p) -> n hh p", p=64),
                sgsrc[k, :, d * 16:(d + 1) * 16, :]))
        for h in range(64):
            S("v", lambda e, k=k, h=h: e.tensor_scalar(
                acc[:, h * 64:(h + 1) * 64], acc[:, h * 64:(h + 1) * 64],
                Wbc[:, h * 8 + k:h * 8 + k + 1], None, ALU.mult))
        S("v", lambda e, k=k: e.tensor_scalar(sgg[:, :], skb[:, :], emk[:, k:k + 1], None, ALU.mult))
        S("v", lambda e: e.tensor_add(acc[:, :], acc[:, :], sgg[:, :]))
    S("v", lambda e: e.tensor_copy(Sin[:, :], acc[:, :]))
    if "fdbg1" in io:
        S("g", lambda e: e.dma_start(io["fdbg1"][0:1, :], Eflat[0:1, :]))
        S("g", lambda e: e.dma_start(io["fdbg2"][:, :], Sin[:, :]))
    A0 = ctx.cnt["aux"]
    incs = [16 if en == "g" else 1 for en, _ in steps]
    tgts = [A0 + sum(incs[:i]) for i in range(len(steps))]
    with nc.Block() as blk:
        for eng_name, body in [("v", "vector"), ("s", "scalar"), ("t", "tensor"), ("g", "gpsimd")]:
            def mk(eng_name=eng_name):
                def emit(e):
                    for i, (en, fn) in enumerate(steps):
                        if en != eng_name:
                            continue
                        e.wait_ge(aux, tgts[i])
                        fn(e).then_inc(aux, incs[i])
                return emit
            getattr(blk, body)(mk())
    ctx.cnt["aux"] = A0 + sum(incs)
    esH.close()

    esY = contextlib.ExitStack()
    sbY = lambda n, s, d: esY.enter_context(nc.sbuf_tensor(n, s, d))
    cc = [sbY(f"fcc{i}", [64, 64 * 128], BF16) for i in range(2)]
    zc = sbY("fzc", [64, 64 * 128], BF16)
    clb = sbY("fcl", [64, 64 * 128], BF16)
    yfb = [sbY(f"fyf{i}", [64, 64 * 128], BF16) for i in range(2)]
    xfc0 = sbY("fxf", [64, 16 * 128], BF16)
    xfc = [xfc0, xfc0]
    bfc0 = sbY("fbf", [64, 16 * 128], BF16)
    bfc = [bfc0, bfc0]
    Sb = sbY("fSb", [64, 4096], BF16)
    Sgb = [sbY(f"fSg{i}", [64, 64], BF16) for i in range(2)]
    Sgs = [sbY(f"fSgs{i}", [64, 64], F32) for i in range(2)]
    Ebc = sbY("fEbc", [64, 64], F32)
    Dbc2 = sbY("fDbc2", [64, 64], F32)
    Ebt2 = sbY("fEbt2", [64, 64], F32)
    Dbt2 = sbY("fDbt2", [64, 64], F32)
    bct = [sbY(f"fbc{i}", [128, 128], F32) for i in range(2)]
    msk = [sbY(f"fmk{i}", [128, 128], F32) for i in range(2)]
    Mp = [sbY(f"fMp{i}", [128, 128], F32) for i in range(2)]
    Gm = [sbY(f"fGm{i}", [128, 128], BF16) for i in range(2)]
    Gm2 = [sbY(f"fGm2{i}", [128, 128], BF16) for i in range(2)]
    cltmp = [sbY(f"fclt{i}", [64, 128], BF16) for i in range(2)]
    xsc = [sbY(f"fxs{i}", [64, 128], F32) for i in range(2)]
    szb = [sbY(f"fsz{i}", [64, 128], F32) for i in range(2)]
    sqb = [sbY(f"fsq{i}", [64, 128], F32) for i in range(2)]
    ytb = [sbY(f"fyt{i}", [64, 128], F32) for i in range(2)]
    bw2 = [sbY(f"fbw2{i}", [128, 64], BF16) for i in range(2)]
    st2 = [sbY(f"fst2{i}", [64, 64], F32) for i in range(2)]
    psA = [esY.enter_context(nc.psum_tensor(f"fpA{i}", [128, 128], F32)) for i in range(2)]
    psG = [esY.enter_context(nc.psum_tensor(f"fpG{i}", [128, 128], F32)) for i in range(2)]
    psY = [esY.enter_context(nc.psum_tensor(f"fpY{i}", [64, 128], F32)) for i in range(2)]
    psS2 = [esY.enter_context(nc.psum_tensor(f"fpS2{i}", [64, 128], F32)) for i in range(2)]
    csrc = io["cmf"].rearrange("(a p) t -> p a t", p=64)
    zsrc = io["zf"].rearrange("(a p) t -> p a t", p=64)
    xsrc = io["xmf"].rearrange("(a p) t -> p a t", p=64)
    bsrc = io["bmf"].rearrange("(a p) t -> p a t", p=64)
    ydst = io["yfh"].rearrange("(a p) t -> p a t", p=64)

    def hv(buf, h, w=128):
        return buf[:, h * 128:h * 128 + w]

    ds = DepSteps(ctx)
    Sd = ds.S
    last = {}
    with nc.Block() as blk0:
        @blk0.vector
        def _(v):
            v.memset(Sf[:, :], 0.0)
            v.memset(Sb[:, :], 0.0)
    for c in range(NCH):
        p_ = c % 2
        ld_c = []
        for dd in range(4):
            ld_c.append(Sd("g", lambda e, c=c, dd=dd, p_=p_: e.dma_start(
                cc[p_][:, dd * 16 * 128:(dd + 1) * 16 * 128].rearrange("p (a t) -> p a t", t=128),
                csrc[:, dd * 16:(dd + 1) * 16, c * 128:(c + 1) * 128]),
                deps=[last.get(("ccuse", p_))]))
        ld_z = []
        for dd in range(4):
            ld_z.append(Sd("g", lambda e, c=c, dd=dd: e.dma_start(
                zc[:, dd * 16 * 128:(dd + 1) * 16 * 128].rearrange("p (a t) -> p a t", t=128),
                zsrc[:, dd * 16:(dd + 1) * 16, c * 128:(c + 1) * 128]),
                deps=[last.get("zcuse")]))
        i_lx = Sd("g", lambda e, c=c, p_=p_: e.dma_start(
            xfc[p_][:, :].rearrange("p (a t) -> p a t", t=128),
            xsrc[:, :, c * 128:(c + 1) * 128]), deps=[last.get(("xfcuse", 0)), last.get(("xfcuse", 1))])
        i_lb = Sd("g", lambda e, c=c, p_=p_: e.dma_start(
            bfc[p_][:, :].rearrange("p (a t) -> p a t", t=128),
            bsrc[:, :, c * 128:(c + 1) * 128]), deps=[last.get(("bfcuse", 0)), last.get(("bfcuse", 1))])
        i_dm = Sd("t", lambda e, c=c: e.matmul(psS2[0][:, 0:64], ctx.onec[0:1, 0:64],
                                               Dflat[0:1, c * 64:(c + 1) * 64], start=True, stop=True),
                  deps=[last.get(("psS2", 0))])
        i_db0 = Sd("v", lambda e: e.tensor_copy(Dbt2[:, :], psS2[0][:, 0:64]),
                   deps=[i_dm, last.get("Dbt2use")])
        i_db = Sd("v", lambda e: e.tensor_copy(Dbc2[:, :], Dbt2[:, :]),
                  deps=[last.get("Dbc2use")])
        i_em = Sd("t", lambda e, c=c: e.matmul(psS2[1][:, 0:64], ctx.onec[0:1, 0:64],
                                               Eflat[0:1, c * 64:(c + 1) * 64], start=True, stop=True),
                  deps=[last.get(("psS2", 1))])
        i_eb0 = Sd("v", lambda e: e.tensor_copy(Ebt2[:, :], psS2[1][:, 0:64]),
                   deps=[i_em, last.get("Ebt2use")])
        i_eb = Sd("v", lambda e: e.tensor_copy(Ebc[:, :], Ebt2[:, :]),
                  deps=[last.get("Ebcuse")])
        last["Dbt2use"] = i_db
        last["Ebt2use"] = i_eb
        last[("psS2", 0)] = i_db
        last[("psS2", 1)] = i_eb
        cl_done = []
        for h in range(NH):
            hp = h % 2
            i_b2 = Sd("s", lambda e, c=c, h=h, hp=hp: e.activation(
                bct[hp][:, 0:64], zt[:, 0:64], AF.Identity,
                bias=elocT[:, c * 64 + h:c * 64 + h + 1]), deps=[last.get(("bctuse", hp))])
            i_sc = Sd("t", lambda e, hp=hp: e.transpose(
                psA[hp][0:64, :], bct[hp][:, 0:64], ctx.eye[:, :]),
                deps=[i_b2, last.get(("psA", hp))])
            last[("bctuse", hp)] = i_sc
            i_cl0 = Sd("v", lambda e, h=h, hp=hp, p_=p_: e.tensor_tensor(
                cltmp[hp][:, :], hv(cc[p_], h), psA[hp][0:64, :], ALU.mult),
                deps=[i_sc, ld_c[h // 16], last.get(("cltmpuse", hp))])
            last[("psA", hp)] = i_cl0
            i_cl = Sd("v", lambda e, h=h, hp=hp: e.tensor_copy(hv(clb, h), cltmp[hp][:, :]),
                      deps=[i_cl0, last.get(("clbuse", h))])
            last[("cltmpuse", hp)] = i_cl
            cl_done.append(i_cl)
        for h in range(NH):
            hp = h % 2
            kv = h // 4
            g = c * 64 + h
            i_b = Sd("s", lambda e, c=c, h=h, hp=hp: e.activation(
                bct[hp][:, :], caus[:, :], AF.Identity, bias=LgT[:, c * 64 + h:c * 64 + h + 1]),
                deps=[last.get(("bctuse", hp))])
            i_tr = Sd("t", lambda e, hp=hp: e.transpose(psA[hp][:, :], bct[hp][:, :], ctx.eye[:, :]),
                      deps=[i_b, last.get(("psA", hp))])
            last[("bctuse", hp)] = i_tr
            i_mp = Sd("s", lambda e, c=c, h=h, hp=hp: e.activation(
                Mp[hp][:, :], psA[hp][:, :], AF.Exp, bias=RT[:, c * 64 + h:c * 64 + h + 1]),
                deps=[i_tr, last.get(("Mpuse", hp))])
            last[("psA", hp)] = i_mp
            i_gp = Sd("t", lambda e, c=c, h=h, hp=hp, p_=p_: e.matmul(
                psG[hp][:, :], bfc[p_][0:64, (h // 4) * 128:(h // 4) * 128 + 128],
                hv(cc[p_], h), start=True, stop=True),
                deps=[i_lb, ld_c[h // 16], last.get(("psG", hp))])
            i_gm0 = Sd("v", lambda e, hp=hp: e.tensor_tensor(Gm[hp][:, :], psG[hp][:, :], Mp[hp][:, :], ALU.mult),
                       deps=[i_gp, i_mp, last.get(("Gmuse", hp))])
            i_gm = Sd("v", lambda e, hp=hp: e.tensor_copy(Gm2[hp][:, :], Gm[hp][:, :]),
                      deps=[i_gm0, last.get(("Gm2use", hp))])
            last[("psG", hp)] = i_gm0
            last[("Mpuse", hp)] = i_gm0
            last[("Gmuse", hp)] = i_gm
            i_sg = Sd("v", lambda e, h=h, hp=hp: e.tensor_scalar(
                Sgs[hp][:, :], Sin[:, h * 64:(h + 1) * 64], Ebc[:, h:h + 1], None, ALU.mult),
                deps=[i_eb, last.get(("Sgsuse", hp))])
            i_sgb = Sd("v", lambda e, hp=hp: e.tensor_copy(Sgb[hp][:, :], Sgs[hp][:, :]),
                       deps=[i_sg, last.get(("Sgbuse", hp))])
            last[("Sgsuse", hp)] = i_sgb
            if c == 0 and h == 6 and "fdbg3" in io:
                Sd("g", lambda e, hp=hp: e.dma_start(io["fdbg3"][:, :], Sgs[hp][:, :]), deps=[i_sg])
                Sd("g", lambda e: e.dma_start(io["fdbg4"][:, :], Ebc[:, :]), deps=[i_eb])
                Sd("g", lambda e, h=h: e.dma_start(io["fdbg5"][:, :], hv(clb, h)), deps=[cl_done[h]])
            i_y1 = Sd("t", lambda e, c=c, h=h, hp=hp: e.matmul(
                psY[hp][:, :], xtm[c][:, (h // 4) * 64:(h // 4) * 64 + 64], Gm2[hp][:, :],
                start=True, stop=False), deps=[i_gm, last.get(("psY", hp))])
            last[("Gm2use", hp)] = i_y1
            i_y2 = Sd("t", lambda e, h=h, hp=hp: e.matmul(
                psY[hp][:, :], Sb[:, h * 64:(h + 1) * 64], hv(clb, h),
                start=False, stop=False), deps=[cl_done[h], last.get(("Sbup", h))])
            i_y3 = Sd("t", lambda e, h=h, hp=hp: e.matmul(
                psY[hp][:, :], Sgb[hp][:, :], hv(clb, h), start=False, stop=True),
                deps=[i_sgb])
            last[("Sgbuse", hp)] = i_y3
            i_xs = Sd("s", lambda e, c=c, h=h, hp=hp, p_=p_: e.activation(
                xsc[hp][:, :], xfc[p_][0:64, (h // 4) * 128:(h // 4) * 128 + 128],
                AF.Identity, scale=dpc[:, h:h + 1]), deps=[i_lx, last.get(("xscuse", hp))])
            i_yt = Sd("v", lambda e, hp=hp: e.tensor_tensor(ytb[hp][:, :], psY[hp][:, :], xsc[hp][:, :], ALU.add),
                      deps=[i_y3, i_xs, last.get(("ytuse", hp))])
            last[("psY", hp)] = i_yt
            last[("xscuse", hp)] = i_yt
            i_sz = Sd("s", lambda e, h=h, hp=hp: e.activation(szb[hp][:, :], hv(zc, h), AF.Silu),
                      deps=[ld_z[h // 16], last.get(("szuse", hp))])
            i_yf = Sd("v", lambda e, h=h, hp=hp, p_=p_: e.tensor_tensor(
                hv(yfb[p_], h), ytb[hp][:, :], szb[hp][:, :], ALU.mult),
                deps=[i_sz, i_yt, last.get(("yfbuse", p_, h))])
            last[("ytuse", hp)] = i_yf
            last[("szuse", hp)] = i_yf
            i_sq = Sd("v", lambda e, h=h, hp=hp, p_=p_: e.tensor_mul(sqb[hp][:, :], hv(yfb[p_], h), hv(yfb[p_], h)),
                      deps=[last.get(("squse", hp))])
            i_nw = Sd("v", lambda e, h=h, p_=p_: e.tensor_scalar(
                hv(yfb[p_], h), hv(yfb[p_], h), nw64[:, h:h + 1], None, ALU.mult))
            i_mq = Sd("t", lambda e, hp=hp: e.matmul(psS2[hp][0:1, 0:128], ctx.ones128[0:64, 0:1], sqb[hp][:, :],
                                                     start=True, stop=True),
                      deps=[i_sq, last.get(("psS2", hp))])
            last[("squse", hp)] = i_mq
            i_sa = Sd("v", lambda e, c=c, hp=hp: e.tensor_tensor(
                ssq[0:1, c * 128:(c + 1) * 128], ssq[0:1, c * 128:(c + 1) * 128],
                psS2[hp][0:1, 0:128], ALU.add), deps=[i_mq, last.get("ssqlast")])
            last[("psS2", hp)] = i_sa
            last["ssqlast"] = i_sa
            last[("yfbuse2", p_, h)] = i_nw
            i_bw = Sd("s", lambda e, c=c, h=h, hp=hp: e.activation(
                bw2[hp][:, :], btm[c][:, (h // 4) * 64:(h // 4) * 64 + 64], AF.Identity,
                scale=wT[:, c * 64 + h:c * 64 + h + 1]), deps=[last.get(("bw2use", hp))])
            i_ds2 = Sd("t", lambda e, c=c, h=h, hp=hp: e.matmul(
                psS2[hp][:, 0:64], bw2[hp][:, :], xtm[c][:, (h // 4) * 64:(h // 4) * 64 + 64],
                start=True, stop=True), deps=[i_bw, last.get(("psS2", hp))])
            last[("bw2use", hp)] = i_ds2
            i_t1 = Sd("v", lambda e, h=h, hp=hp: e.tensor_scalar(
                st2[hp][:, :], Sf[:, h * 64:(h + 1) * 64], Dbc2[:, h:h + 1], None, ALU.mult),
                deps=[i_db, last.get(("Sup2", h)), last.get(("st2use", hp))])
            i_t2 = Sd("v", lambda e, h=h, hp=hp: e.tensor_tensor(
                Sf[:, h * 64:(h + 1) * 64], st2[hp][:, :], psS2[hp][:, 0:64], ALU.add),
                deps=[i_ds2, i_t1])
            last[("psS2", hp)] = i_t2
            last[("st2use", hp)] = i_t2
            last[("Sup2", h)] = i_t2
            i_sb = Sd("v", lambda e, h=h: e.tensor_copy(Sb[:, h * 64:(h + 1) * 64], Sf[:, h * 64:(h + 1) * 64]),
                      deps=[i_y2, i_t2])
            last[("Sbup", h)] = i_sb
            last[("clbuse", h)] = i_y3
        last["Dbc2use"] = last[("Sup2", 63)]
        last["Ebcuse"] = last[("Sgsuse", 1)]
        st_c = None
        for dd in range(4):
            st_c = Sd("g", lambda e, c=c, dd=dd, p_=p_: e.dma_start(
                ydst[:, dd * 16:(dd + 1) * 16, c * 128:(c + 1) * 128],
                yfb[p_][:, dd * 16 * 128:(dd + 1) * 16 * 128].rearrange("p (a t) -> p a t", t=128)),
                deps=[last[("yfbuse2", p_, 16 * dd + 15)], last["ssqlast"], st_c])
        for h in range(NH):
            last[("yfbuse", p_, h)] = st_c
        last[("ccuse", p_)] = last[("psG", 1)]
        last["zcuse"] = last[("szuse", 1)]
        last[("xfcuse", p_)] = last[("xscuse", 1)]
        last[("bfcuse", p_)] = last[("psG", 1)]
    ds.emit()
    esY.close()
    esP.close()

def emit_p5c(ctx):
    nc, io = ctx.nc, ctx.io
    ssq = ctx.p45["ssq"]
    esC = contextlib.ExitStack()
    sbC = lambda n, s, d: esC.enter_context(nc.sbuf_tensor(n, s, d))
    yft = [sbC(f"c5y{k}", [128, T], BF16) for k in range(32)]
    mos = [sbC(f"c5mo{k}", [128, T], F32) for k in range(16)]
    rrow = sbC("c5rr", [1, T], F32)
    rbc = sbC("c5rb", [128, T], F32)
    wb = [sbC(f"c5wb{i}", [128, 32 * 128], BF16) for i in range(3)]
    mt = sbC("c5mt", [128, T], F32)
    xt_ = sbC("c5xt2", [128, T], F32)
    ps = [esC.enter_context(nc.psum_tensor(f"c5ps{i}", [128, 512], F32)) for i in range(4)]
    psB = esC.enter_context(nc.psum_tensor("c5psB", [128, 512], F32))
    a0 = ctx.cnt["ald"]
    with nc.Block() as blk:
        @blk.sync
        def _(sy):
            for k in range(32):
                sy.dma_start(yft[k][:, :], io["yfh"][k * 128:(k + 1) * 128, :]).then_inc(ctx.sem("ald"), 16)
            sy.wait_ge(ctx.sem("ald"), a0 + 16 * 32)
            ctx.cnt["ald"] = a0 + 16 * 32
    steps = []
    def S(eng, fn):
        steps.append((eng, fn))
    S("s", lambda e: e.activation(rrow[0:1, :], ssq[0:1, :], AF.Ln, scale=1.0 / DI, bias=EPS0))
    S("s", lambda e: e.activation(rrow[0:1, :], rrow[0:1, :], AF.Exp, scale=-0.5))
    for j in range(2):
        S("t", lambda e, j=j: e.matmul(psB[:, :], ctx.onec[0:1, :], rrow[0:1, j * 512:(j + 1) * 512],
                                       start=True, stop=True))
        S("v", lambda e, j=j: e.tensor_copy(rbc[:, j * 512:(j + 1) * 512], psB[:, :]))
    aux = ctx.sem("aux")
    A0 = ctx.cnt["aux"]
    tgts = [A0 + i for i in range(len(steps))]
    with nc.Block() as blk:
        for eng_name, body in [("v", "vector"), ("s", "scalar"), ("t", "tensor")]:
            def mk(eng_name=eng_name):
                def emit(e):
                    for i, (en, fn) in enumerate(steps):
                        if en != eng_name:
                            continue
                        e.wait_ge(aux, tgts[i])
                        fn(e).then_inc(aux, 1)
                return emit
            getattr(blk, body)(mk())
    ctx.cnt["aux"] = A0 + len(steps)

    def evop(ve, ji, ti, p, ot, stile):
        t0 = ti * 512
        return ve.tensor_tensor(mos[ji][:, t0:t0 + 512], p[:, :], rbc[:, t0:t0 + 512], ALU.mult)

    linear(ctx, dict(
        name="outproj", wt=io["wop"], K=DI, N=D, T=T, tw=512,
        x_tile=lambda kc: yft[kc],
        psums=ps, wbufs=wb, evac_v=evop, out_ap=None))

    steps = []
    for kc in range(16):
        S("g", lambda e, kc=kc: e.dma_start(mt[:, :], io["mlp"][kc * 128:(kc + 1) * 128, :]))
        S("g", lambda e, kc=kc: e.dma_start(xt_[:, :], io["xnfm"][kc * 128:(kc + 1) * 128, :]))
        S("v", lambda e, kc=kc: e.tensor_add(mos[kc][:, :], mos[kc][:, :], mt[:, :]))
        S("v", lambda e, kc=kc: e.tensor_add(mos[kc][:, :], mos[kc][:, :], xt_[:, :]))
        S("g", lambda e, kc=kc: e.dma_start(io["sfm"][kc * 128:(kc + 1) * 128, :], mos[kc][:, :]))
    A0 = ctx.cnt["aux"]
    incs = [16 if en == "g" else 1 for en, _ in steps]
    tgts = [A0 + sum(incs[:i]) for i in range(len(steps))]
    with nc.Block() as blk:
        for eng_name, body in [("v", "vector"), ("g", "gpsimd")]:
            def mk(eng_name=eng_name):
                def emit(e):
                    for i, (en, fn) in enumerate(steps):
                        if en != eng_name:
                            continue
                        e.wait_ge(aux, tgts[i])
                        fn(e).then_inc(aux, incs[i])
                return emit
            getattr(blk, body)(mk())
        @blk.sync
        def _(sy):
            sy.wait_ge(aux, A0 + sum(incs))
    ctx.cnt["aux"] = A0 + sum(incs)
    esC.close()
    ctx.p45_es0.close()



from concourse.bass_utils import run_bass_kernel_spmd

_BF = ml_dtypes.bfloat16
_DT = {F32: np.float32, BF16: _BF}

CONSTS_SPEC = [("eye", (128, 128), F32), ("eyeb", (128, 128), BF16),
               ("onescol", (1, 128), F32), ("ones128", (128, 1), F32)]

INTERNAL = {
    "absq": ((1, 4), F32), "sc": ((1, 4), F32),
    "tq1s": ((256, HID), BF16), "tq3s": ((256, HID), BF16), "tq2s": ((W2ROWS, D), BF16),
    "xfm": ((D, T), F32), "xnfm": ((D, T), F32), "xntm": ((T, D), BF16), "htm": ((T, D), BF16),
    "mlp": ((D, T), F32),
    "zf": ((DI, T), BF16), "xmf": ((DXB, T), BF16), "bmf": ((DXB, T), BF16),
    "cmf": ((DI, T), BF16), "dtr": ((64, T), F32),
    "yloc": ((DI, T), BF16), "eglob": ((64, T), F32), "sfin": ((DI, 64), F32), "elast": ((64, 1), F32),
    "sfm": ((D, T), F32), "yfh": ((DI, T), BF16),
    "x1fm": ((D, T), F32), "hsptm": ((T, INTER), BF16), "xn1tm": ((T, D), BF16),
    "qf": ((2048, T), BF16), "kf": ((2048, T), BF16), "vf": ((2048, T), BF16),
    "y1loc": ((2048, T), BF16), "kvfin": ((2048, 128), F32),
    "s1fm": ((D, T), F32), "x2fm": ((D, T), F32),
    "absg": ((8, 4), F32), "sgath": ((8 * DI, 64), F32), "egath": ((512, 1), F32),
    "kvgath": ((8 * 2048, 128), F32),
    "tq1": ((D, HID), BF16), "tq3": ((D, HID), BF16), "tq2": ((HID, D), BF16),
}
WSHAPES = {
    "wip": (D, DINP), "wop": (DI, D), "wfc1": (D, INTER), "wfc2": (INTER, D),
    "wg": (D, INTER), "wu": (D, INTER), "wd": (INTER, D),
    "wqkv": (D, 6144), "wgt": (D, D), "wo1": (D, D),
    "ln2w": (1, D), "ln2b": (1, D), "dtb": (64, 1), "alog": (64, 1),
    "dpc64": (64, 64), "sel64": (64, 64 * 64), "nw64": (64, 64),
    "causal": (128, 128), "causT": (128, 128), "qdec": (2048, 256), "kdec": (2048, 256), "ddecT": (4096, 256),
    "nw1cols": (128, 16), "fc1bc": (128, 64), "fc2bc": (128, 16),
}
PCSHAPES = {"xs": ((T, D), F32), "emask64": ((64, 8), F32), "emask128": ((128, 8), F32),
            "l1w": ((128, NH1 * 8), F32),
            "w1s": ((256, HID), F32), "w3s": ((256, HID), F32), "w2s": ((W2ROWS, D), F32)}


def emit_gather(ctx, pairs, wait=True):
    nc = ctx.nc
    cc = ctx.sem("cc")
    c0 = ctx.cnt["cc"]
    with nc.Block() as blk:
        @blk.gpsimd
        def _(g):
            n = 0
            for src, dst in pairs:
                g.collective_compute(
                    "AllGather", ALU.bypass,
                    replica_groups=[list(range(NCORES))],
                    ins=[src.ap().opt()],
                    outs=[dst.ap().opt()],
                ).then_inc(cc, 1)
                n += 1
            if wait:
                g.wait_ge(cc, c0 + n)
    ctx.cnt["cc"] = c0 + len(pairs)


def emit_gather_wait(ctx):
    nc = ctx.nc
    cc = ctx.sem("cc")
    tgt = ctx.cnt["cc"]
    with nc.Block() as blk:
        @blk.gpsimd
        def _(g):
            g.wait_ge(cc, tgt)


def kernel(**inputs):
    sh, per_core = host_prep(inputs)
    for i in range(NCORES):
        per_core[i]["w1s"] = np.ascontiguousarray(sh["w1t"][256 * i:256 * (i + 1)])
        per_core[i]["w3s"] = np.ascontiguousarray(sh["w3t"][256 * i:256 * (i + 1)])
        per_core[i]["w2s"] = np.ascontiguousarray(sh["w2t"][W2ROWS * i:W2ROWS * (i + 1)])

    nc = bass.Bass("TRN2", target_bir_lowering=False, detect_race_conditions=False)
    io = {}
    handles = {}
    ins_spec = list(CONSTS_SPEC)
    for name, shape in WSHAPES.items():
        ins_spec.append((name, shape, F32))
    for name, (shape, dt) in PCSHAPES.items():
        ins_spec.append((name, shape, dt))
    for name, shape, dt in ins_spec:
        io[name] = nc.dram_tensor(name, list(shape), dt, kind="ExternalInput").ap()
    for name, (shape, dt) in INTERNAL.items():
        h = nc.dram_tensor(name, list(shape), dt, kind="Internal")
        handles[name] = h
        io[name] = h.ap()
    ysh = nc.dram_tensor("ys", [T, D], F32, kind="ExternalOutput")
    io["ys"] = ysh.ap()

    ctx = Ctx(nc, io)
    phases = [
        ("setup", lambda: emit_setup(ctx)),
        ("qa", lambda: emit_qa(ctx)),
        ("cc1", lambda: emit_gather(ctx, [(handles["absq"], handles["absg"])])),
        ("qb", lambda: emit_qb(ctx)),
        ("p1", lambda: emit_p1(ctx)),
        ("cctq", lambda: emit_gather(ctx, [(handles["tq1s"], handles["tq1"]),
                                           (handles["tq3s"], handles["tq3"]),
                                           (handles["tq2s"], handles["tq2"])], wait=False)),
        ("p2", lambda: emit_p2(ctx)),
        ("p3", lambda: emit_p3(ctx)),
        ("p45", lambda: emit_p45(ctx, lambda c2: emit_gather(
            c2, [(handles["sfin"], handles["sgath"]), (handles["elast"], handles["egath"])]))),
        ("p5c", lambda: emit_p5c(ctx)),
        ("p6", lambda: emit_p6(ctx)),
        ("p7", lambda: emit_p7(ctx)),
        ("p8", lambda: emit_p8(ctx)),
        ("p9", lambda: emit_p9(ctx)),
        ("cc3", lambda: emit_gather(ctx, [(handles["kvfin"], handles["kvgath"])])),
        ("p10", lambda: emit_p10(ctx)),
        ("cctqw", lambda: emit_gather_wait(ctx)),
        ("p12", lambda: emit_p12(ctx)),
    ]
    for name, fn in phases:
        with nc.named_scope(name):
            fn()
    ctx.close()

    in_maps = []
    for c in range(NCORES):
        m = {}
        for name, shape, dt in ins_spec:
            a = per_core[c][name] if name in per_core[c] else sh[name]
            m[name] = np.ascontiguousarray(np.asarray(a, dtype=_DT[dt]))
        in_maps.append(m)
    res = run_bass_kernel_spmd(nc, in_maps, core_ids=list(range(NCORES)))
    out = np.concatenate([np.asarray(r["ys"], np.float32) for r in res.results], axis=0)
    return out[None]


# version 09 (final)
# speedup vs baseline: 1.0966x; 1.0627x over previous
import contextlib
import math
from collections import defaultdict

import numpy as np
import ml_dtypes

import concourse.bass as bass
import concourse.mybir as mybir

F32 = mybir.dt.float32
BF16 = mybir.dt.bfloat16
AF = mybir.ActivationFunctionType
ALU = mybir.AluOpType
AX = mybir.AxisListType.X

T = 1024
D = 2048
DI = 4096
DXB = 1024
HD = 64
DS = 64
NH = 64
NKV = 16
INTER = 8192
HID = 5632
HID0 = 5461
NH1 = 16
HD1 = 128
BLK = 256
NBLK = T // BLK
EPS0 = 1e-5
EPS1 = 1e-6
Q = 128
NCH = T // Q
NCORES = 8

DINP = 10304


def _get_slopes(n):
    def p2(n):
        start = 2 ** (-(2 ** (-(math.log2(n) - 3))))
        return [start * start**i for i in range(n)]
    if math.log2(n).is_integer():
        return p2(n)
    c = 2 ** math.floor(math.log2(n))
    return p2(c) + _get_slopes(2 * c)[0::2][: n - c]


SLOPES = np.asarray(_get_slopes(NH1), dtype=np.float32)
BDECAY = np.exp(-SLOPES * BLK).astype(np.float64)
CORE_DECAY = np.exp(-SLOPES.astype(np.float64) * T)


def host_consts():
    c = {}
    c["eye"] = np.eye(128, dtype=np.float32)
    c["eyeb"] = np.eye(128, dtype=ml_dtypes.bfloat16)
    c["onescol"] = np.ones((1, 128), dtype=np.float32)
    c["ones128"] = np.ones((128, 1), dtype=np.float32)
    s, t = np.meshgrid(np.arange(128), np.arange(128), indexing="ij")
    c["causal"] = np.where(t >= s, 0.0, -1e30).astype(np.float32)
    c["causT"] = np.ascontiguousarray(c["causal"].T)
    sel = np.zeros((64, 64 * 128), dtype=np.float32)
    for h in range(64):
        sel[h, h * 128:(h + 1) * 128] = 1.0
    c["sel"] = sel
    sel2 = np.zeros((64, 32 * 128), dtype=np.float32)
    for pc in range(32):
        sel2[2 * pc, pc * 128:pc * 128 + 64] = 1.0
        sel2[2 * pc + 1, pc * 128 + 64:(pc + 1) * 128] = 1.0
    c["sel2"] = sel2
    arr = np.arange(BLK, dtype=np.float64) + 1.0
    qdec = np.zeros((NH1 * HD1, BLK), dtype=np.float32)
    kdec = np.zeros((NH1 * HD1, BLK), dtype=np.float32)
    ddecT = np.zeros((NH1 * BLK, BLK), dtype=np.float32)
    for h in range(NH1):
        sl = float(SLOPES[h])
        qdec[h * HD1:(h + 1) * HD1, :] = np.exp(-sl * arr)[None, :]
        kdec[h * HD1:(h + 1) * HD1, :] = np.exp(-sl * (BLK - arr))[None, :]
        idx = arr[None, :] - arr[:, None]
        ddecT[h * BLK:(h + 1) * BLK, :] = np.where(idx >= 0, np.exp(-sl * idx), 0.0)
    c["qdec"] = qdec
    c["kdec"] = kdec
    c["ddecT"] = ddecT
    return c


def host_prep(inputs):
    f = np.ascontiguousarray
    sh = dict(host_consts())
    sh["wip"] = f(inputs["L0_attn_in_proj_w"].T)
    sh["wop"] = f(inputs["L0_attn_out_proj_w"].T)
    sh["wfc1"] = f(inputs["L0_attn_fc1_w"].T)
    sh["wfc2"] = f(inputs["L0_attn_fc2_w"].T)
    sh["wg"] = f(inputs["L0_ffn_w_gate"])
    sh["wu"] = f(inputs["L0_ffn_w_up"])
    sh["wd"] = f(inputs["L0_ffn_w_down"])
    sh["wqkv"] = f(inputs["L1_attn_Wqkv"])
    sh["wgt"] = f(inputs["L1_attn_Wg"])
    sh["wo1"] = f(inputs["L1_attn_Wout"])
    w1t = np.zeros((D, HID), dtype=np.float32)
    w1t[:, :HID0] = inputs["L1_ffn_w1"].T
    sh["w1t"] = w1t
    w3t = np.zeros((D, HID), dtype=np.float32)
    w3t[:, :HID0] = inputs["L1_ffn_w3"].T
    sh["w3t"] = w3t
    w2t = np.zeros((HID, D), dtype=np.float32)
    w2t[:HID0, :] = inputs["L1_ffn_w2"].T
    sh["w2t"] = w2t
    sh["ln2w"] = f(inputs["L0_attn_ln_w"].reshape(1, D))
    sh["ln2b"] = f(inputs["L0_attn_ln_b"].reshape(1, D))
    sh["dtb"] = f(inputs["L0_attn_dt_bias"].reshape(64, 1))
    sh["alog"] = f(inputs["L0_attn_A_log"].reshape(64, 1))
    j = np.arange(DI).reshape(32, 128).T
    sh["dpcols"] = f(inputs["L0_attn_Dp"][j // HD])
    sh["nwcols"] = f(inputs["L0_attn_norm_w"].reshape(32, 128).T)
    sh["dpc64"] = np.ascontiguousarray(np.broadcast_to(
        np.asarray(inputs["L0_attn_Dp"], np.float32)[None, :], (64, 64)))
    sel64 = np.zeros((64, 64 * 64), dtype=np.float32)
    for h in range(64):
        sel64[h, h * 64:(h + 1) * 64] = 1.0
    sh["sel64"] = sel64
    sh["nw64"] = f(inputs["L0_attn_norm_w"].reshape(64, 64).T)
    sh["nw1cols"] = f(inputs["L1_attn_norm_w"].reshape(16, 128).T)
    sh["fc1bc"] = f(inputs["L0_attn_fc1_b"].reshape(64, 128).T)
    sh["fc2bc"] = f(inputs["L0_attn_fc2_b"].reshape(16, 128).T)
    x = np.asarray(inputs["L0_attn_hidden_states"])[0]
    per_core = []
    for i in range(NCORES):
        em = (np.arange(8) < i).astype(np.float32)
        l1w = np.ones((128, NH1 * 8), dtype=np.float32)
        for h in range(NH1):
            for k in range(8):
                if k < i:
                    l1w[:, h * 8 + k] = CORE_DECAY[h]
        per_core.append({
            "xs": f(x[i * T:(i + 1) * T]),
            "emask64": np.ascontiguousarray(np.broadcast_to(em[None, :], (64, 8))),
            "emask128": np.ascontiguousarray(np.broadcast_to(em[None, :], (128, 8))),
            "l1w": l1w,
        })
    return sh, per_core



class Ctx:
    def __init__(self, nc, io):
        self.nc = nc
        self.io = io
        self._sems = {}
        self.cnt = defaultdict(int)
        self.es = contextlib.ExitStack()

    def sem(self, name):
        if name not in self._sems:
            self._sems[name] = self.es.enter_context(self.nc.semaphore(name))
        return self._sems[name]

    def close(self):
        self.es.close()


SEMS = ["wld", "ald", "mm", "ev", "evs", "sto", "wf0", "wf1", "aux", "aux2", "cc"]



def linear(ctx, cfg):
    nc = ctx.nc
    K, N, Tn = cfg["K"], cfg["N"], cfg["T"]
    tw = cfg.get("tw", 512)
    Kc = K // 128
    nw = cfg.get("nw", 128)
    jobs = cfg.get("n0s") or [(n0, min(nw, N - n0)) for n0 in range(0, N, nw)]
    tis = [(t0, min(tw, Tn - t0)) for t0 in range(0, Tn, tw)]
    ntis = len(tis)
    items = [(ji, ti) for ji in range(len(jobs)) for ti in range(ntis)]
    psums = cfg["psums"]
    P = len(psums)
    obufs = cfg.get("obufs")
    O = len(obufs) if obufs else 0
    sbufs = cfg.get("sbufs")
    S = len(sbufs) if sbufs else 0
    store = cfg.get("out_ap") is not None
    two_stage = cfg.get("evac_s") is not None and cfg.get("evac_v") is not None

    mm, ev, evs = (ctx.sem(s) for s in ["mm", "ev", "evs"])
    NW = len(cfg["wbufs"])
    wld = [ctx.sem(f"wld{i}") for i in range(NW)]
    stos = [ctx.sem(f"sto{r}") for r in range(O)] if store else []
    mm0, ev0, evs0 = (ctx.cnt[s] for s in ["mm", "ev", "evs"])
    wld0 = [ctx.cnt[f"wld{i}"] for i in range(NW)]
    sto0 = [ctx.cnt[f"sto{r}"] for r in range(O)] if store else []

    KTILE_PER_DMA = 8
    ndma_job = (Kc + KTILE_PER_DMA - 1) // KTILE_PER_DMA
    wsrc = cfg["wt"].rearrange("(a p) n -> p a n", p=128)

    W = len(cfg["wbufs"])
    with nc.Block() as blk:
        @blk.gpsimd
        def _(g):
            for ji, (n0, nwj) in enumerate(jobs):
                par = ji % W
                if ji >= W:
                    g.wait_ge(ev, ev0 + (ji - W + 1) * ntis)
                wb = cfg["wbufs"][par]
                for di in range(ndma_job):
                    k0 = di * KTILE_PER_DMA
                    k1 = min(Kc, k0 + KTILE_PER_DMA)
                    dst = wb[:, k0 * nwj:k1 * nwj].rearrange("p (a n) -> p a n", n=nwj)
                    g.dma_start(dst, wsrc[:, k0:k1, n0:n0 + nwj]).then_inc(wld[par], 16)


        @blk.tensor
        def _(te):
            for idx, (ji, ti) in enumerate(items):
                n0, nwj = jobs[ji]
                t0, twj = tis[ti]
                wb = cfg["wbufs"][ji % W]
                if ti == 0:
                    te.wait_ge(wld[ji % W], wld0[ji % W] + 16 * ndma_job * (ji // W + 1))
                ps = psums[idx % P]
                if idx >= P:
                    te.wait_ge(ev, ev0 + idx - P + 1)
                ins = None
                for kc in range(Kc):
                    ins = te.matmul(
                        ps[0:nwj, 0:twj],
                        wb[:, kc * nwj:kc * nwj + nwj],
                        cfg["x_tile"](kc)[:, t0:t0 + twj],
                        start=(kc == 0), stop=(kc == Kc - 1))
                ins.then_inc(mm, 1)

        if cfg.get("evac_s") is not None:
            @blk.scalar
            def _(se):
                for idx, (ji, ti) in enumerate(items):
                    ps = psums[idx % P]
                    se.wait_ge(mm, mm0 + idx + 1)
                    ot = obufs[idx % O] if obufs else None
                    stile = sbufs[idx % S] if sbufs else None
                    if two_stage:
                        if idx >= S and S:
                            se.wait_ge(ev, ev0 + idx - S + 1)
                    elif store and idx >= O:
                        se.wait_ge(stos[idx % O], sto0[idx % O] + 16 * (idx // O))
                    ins = cfg["evac_s"](se, ji, ti, ps, ot, stile)
                    ins.then_inc(evs if two_stage else ev, 1)

        if cfg.get("evac_v") is not None:
            @blk.vector
            def _(ve):
                for idx, (ji, ti) in enumerate(items):
                    ps = psums[idx % P]
                    if two_stage:
                        ve.wait_ge(evs, evs0 + idx + 1)
                    else:
                        ve.wait_ge(mm, mm0 + idx + 1)
                    ot = obufs[idx % O] if obufs else None
                    stile = sbufs[idx % S] if sbufs else None
                    if store and idx >= O:
                        ve.wait_ge(stos[idx % O], sto0[idx % O] + 16 * (idx // O))
                    ins = cfg["evac_v"](ve, ji, ti, ps, ot, stile)
                    ins.then_inc(ev, 1)

        if store:
            @blk.sync
            def _(sy):
                for idx, (ji, ti) in enumerate(items):
                    sy.wait_ge(ev, ev0 + idx + 1)
                    ap = cfg["out_ap"](ji, ti)
                    sy.dma_start(ap, obufs[idx % O][0:jobs[ji][1], 0:tis[ti][1]]).then_inc(stos[idx % O], 16)
                for r in range(O):
                    n_r = sum(1 for i2 in range(len(items)) if i2 % O == r)
                    sy.wait_ge(stos[r], sto0[r] + 16 * n_r)

    for i in range(W):
        nji = sum(1 for j2 in range(len(jobs)) if j2 % W == i)
        ctx.cnt[f"wld{i}"] = wld0[i] + 16 * ndma_job * nji
    ctx.cnt["mm"] = mm0 + len(items)
    ctx.cnt["ev"] = ev0 + len(items)
    if two_stage:
        ctx.cnt["evs"] = evs0 + len(items)
    if store:
        for r in range(O):
            ctx.cnt[f"sto{r}"] = sto0[r] + 16 * sum(1 for i2 in range(len(items)) if i2 % O == r)


def load_fm_tiles(ctx, tm_ap, ncol, Tn, bufs, sem="ald"):
    nc = ctx.nc
    s0 = ctx.cnt[sem]
    n = ncol // 128
    with nc.Block() as blk:
        @blk.sync
        def _(sy):
            for kc in range(n):
                sy.dma_start(bufs[kc][:, 0:Tn], tm_ap[0:Tn, kc * 128:(kc + 1) * 128],
                             transpose=True).then_inc(ctx.sem(sem), 16)
            sy.wait_ge(ctx.sem(sem), s0 + 16 * n)
    ctx.cnt[sem] = s0 + 16 * n


def bcast_rows(ctx, blk_engines, row_ap, out_sb, psum, width):
    raise NotImplementedError




CONSTS = [1.0 / 2048, EPS0, -0.5, 1.0 / 4096, EPS1, 1.0 / HID0 / 2048 / 2, 0.0, 1.0]

def emit_setup(ctx):
    nc = ctx.nc
    cp = ctx.es.enter_context(nc.sbuf_tensor("cpool", [128, len(CONSTS)], F32))
    with nc.Block() as blk:
        @blk.vector
        def _(v):
            for j, val in enumerate(CONSTS):
                v.memset(cp[:, j:j + 1], float(val))
    for j, val in enumerate(CONSTS):
        nc.const_aps.register(F32, float(val), cp[:, j:j + 1])
    ctx.cpool = cp
    nc2 = nc
    ctx.eye = ctx.es.enter_context(nc2.sbuf_tensor("c_eye", [128, 128], F32))
    ctx.eyeb = ctx.es.enter_context(nc2.sbuf_tensor("c_eyeb", [128, 128], BF16))
    ctx.onec = ctx.es.enter_context(nc2.sbuf_tensor("c_onec", [1, 128], F32))
    ctx.ones128 = ctx.es.enter_context(nc2.sbuf_tensor("c_ones128", [128, 1], F32))
    s0 = ctx.cnt["ald"]
    with nc.Block() as blk:
        @blk.sync
        def _(sy):
            sy.dma_start(ctx.eye[:, :], ctx.io["eye"][:, :]).then_inc(ctx.sem("ald"), 16)
            sy.dma_start(ctx.eyeb[:, :], ctx.io["eyeb"][:, :]).then_inc(ctx.sem("ald"), 16)
            sy.dma_start(ctx.onec[0:1, :], ctx.io["onescol"][0:1, :]).then_inc(ctx.sem("ald"), 16)
            sy.dma_start(ctx.ones128[:, 0:1], ctx.io["ones128"][:, 0:1]).then_inc(ctx.sem("ald"), 16)
            sy.wait_ge(ctx.sem("ald"), s0 + 64)
    ctx.cnt["ald"] = s0 + 64



def emit_p1(ctx):
    nc, io = ctx.nc, ctx.io
    es = contextlib.ExitStack()
    xt = [es.enter_context(nc.sbuf_tensor(f"p1xt{i}", [128, D], F32)) for i in range(2)]
    xc = es.enter_context(nc.sbuf_tensor("p1xc", [128, D], F32))
    sq = es.enter_context(nc.sbuf_tensor("p1sq", [128, D], F32))
    xn = es.enter_context(nc.sbuf_tensor("p1xn", [128, D], F32))
    xnb = es.enter_context(nc.sbuf_tensor("p1xnb", [128, D], BF16))
    ht = es.enter_context(nc.sbuf_tensor("p1ht", [128, D], BF16))
    st_ = es.enter_context(nc.sbuf_tensor("p1st", [128, 12], F32))
    wbc = es.enter_context(nc.sbuf_tensor("p1wbc", [128, D], F32))
    bbc = es.enter_context(nc.sbuf_tensor("p1bbc", [128, D], F32))
    row = es.enter_context(nc.sbuf_tensor("p1row", [1, D], F32))
    row2 = es.enter_context(nc.sbuf_tensor("p1row2", [1, D], F32))
    onec = es.enter_context(nc.sbuf_tensor("p1one", [1, 128], F32))
    ot = [es.enter_context(nc.sbuf_tensor(f"p1ot{i}", [128, 128], F32)) for i in range(4)]
    pt = [es.enter_context(nc.psum_tensor(f"p1pt{i}", [128, 128], F32)) for i in range(4)]
    pb = es.enter_context(nc.psum_tensor("p1pb", [128, 512], F32))

    ald, aux, evs, mm, ev, sto, aux2 = (ctx.sem(s) for s in
        ["ald", "aux", "evs", "mm", "ev", "sto", "aux2"])
    b = {s: ctx.cnt[s] for s in ["ald", "aux", "evs", "mm", "ev", "sto", "aux2"]}
    NTI = T // 128

    with nc.Block() as blk:
        @blk.sync
        def _(sy):
            sy.dma_start(row[0:1, :], io["ln2w"][0:1, :]).then_inc(ald, 16)
            sy.dma_start(row2[0:1, :], io["ln2b"][0:1, :]).then_inc(ald, 16)
            sy.dma_start(onec[0:1, :], io["onescol"][0:1, :]).then_inc(ald, 16)
        @blk.tensor
        def _(te):
            te.wait_ge(ald, b["ald"] + 48)
            for j in range(4):
                if j > 0:
                    te.wait_ge(ev, b["ev"] + 2 * j)
                te.matmul(pb[:, :], onec[0:1, :], row[0:1, j * 512:(j + 1) * 512],
                          start=True, stop=True).then_inc(mm, 1)
                te.wait_ge(ev, b["ev"] + 2 * j + 1)
                te.matmul(pb[:, :], onec[0:1, :], row2[0:1, j * 512:(j + 1) * 512],
                          start=True, stop=True).then_inc(mm, 1)
        @blk.vector
        def _(v):
            for j in range(4):
                v.wait_ge(mm, b["mm"] + 2 * j + 1)
                v.tensor_copy(wbc[:, j * 512:(j + 1) * 512], pb[:, :]).then_inc(ev, 1)
                v.wait_ge(mm, b["mm"] + 2 * j + 2)
                v.tensor_copy(bbc[:, j * 512:(j + 1) * 512], pb[:, :]).then_inc(ev, 1)
    ctx.cnt["ald"] += 48
    ctx.cnt["mm"] += 8
    ctx.cnt["ev"] += 8
    b = {s: ctx.cnt[s] for s in ["ald", "aux", "evs", "mm", "ev", "sto", "aux2"]}

    al = [ctx.sem("ald0"), ctx.sem("ald1")]
    stoa, stob = ctx.sem("stoa"), ctx.sem("stob")
    stot = [ctx.sem(f"sto{r}") for r in range(4)]
    b2 = {s: ctx.cnt[s] for s in
          ["ald0", "ald1", "stoa", "stob", "sto0", "sto1", "sto2", "sto3",
           "aux", "aux2", "evs", "mm", "ev"]}
    A, E = b2["aux"], b2["evs"]

    def altgt(i):
        return b2[f"ald{i % 2}"] + 16 * (i // 2 + 1)

    def stotgt(gi):
        return b2[f"sto{gi % 4}"] + 16 * (gi // 4 + 1)

    with nc.Block() as blk:
        @blk.sync
        def _(sy):
            sy.dma_start(xt[0][:, :], io["xs"][0:128, :]).then_inc(al[0], 16)
            if NTI > 1:
                sy.dma_start(xt[1][:, :], io["xs"][128:256, :]).then_inc(al[1], 16)
            for i in range(NTI):
                sy.wait_ge(evs, E + 4 * i + 4)
                sy.dma_start(io["xntm"][i * 128:(i + 1) * 128, :], xnb[:, :]).then_inc(stoa, 16)
                sy.wait_ge(aux2, b2["aux2"] + i + 1)
                sy.dma_start(io["htm"][i * 128:(i + 1) * 128, :], ht[:, :]).then_inc(stob, 16)
                for j in range(32):
                    gi = 32 * i + j
                    sy.wait_ge(ev, b2["ev"] + gi + 1)
                    kc = j % 16
                    dst = io["xfm"] if j < 16 else io["xnfm"]
                    sy.dma_start(dst[kc * 128:(kc + 1) * 128, i * 128:(i + 1) * 128],
                                 ot[gi % 4][:, :]).then_inc(stot[gi % 4], 16)
                if i + 2 < NTI:
                    sy.wait_ge(aux, A + 8 * i + 3)
                    sy.wait_ge(mm, b2["mm"] + 32 * i + 16)
                    sy.dma_start(xt[i % 2][:, :],
                                 io["xs"][(i + 2) * 128:(i + 3) * 128, :]).then_inc(al[i % 2], 16)
            sy.wait_ge(stoa, b2["stoa"] + 16 * NTI)
            sy.wait_ge(stob, b2["stob"] + 16 * NTI)
            for r in range(4):
                sy.wait_ge(stot[r], b2[f"sto{r}"] + 16 * 8 * NTI)

        @blk.vector
        def _(v):
            for i in range(NTI):
                x_ = xt[i % 2]
                v.wait_ge(al[i % 2], altgt(i))
                if i > 0:
                    v.wait_ge(evs, E + 4 * i - 1)
                v.reduce_sum(st_[:, 0:1], x_[:, :], axis=AX).then_inc(aux, 1)
                v.wait_ge(aux, A + 8 * i + 1)
                v.tensor_scalar(st_[:, 1:2], st_[:, 0:1], 1.0 / D, None,
                                ALU.mult).then_inc(aux, 1)
                v.wait_ge(aux, A + 8 * i + 2)
                v.tensor_scalar(xc[:, :], x_[:, :], st_[:, 1:2], None,
                                ALU.subtract).then_inc(aux, 1)
                v.wait_ge(evs, E + 4 * i + 1)
                v.reduce_sum(st_[:, 2:3], sq[:, :], axis=AX).then_inc(aux, 1)
                v.wait_ge(evs, E + 4 * i + 2)
                if i > 0:
                    v.wait_ge(mm, b2["mm"] + 32 * i)
                v.tensor_scalar(xn[:, :], xc[:, :], st_[:, 4:5], None, ALU.mult)
                v.reduce_sum(st_[:, 5:6], xn[:, :], axis=AX).then_inc(aux, 1)
                v.wait_ge(aux, A + 8 * i + 5)
                v.tensor_scalar(st_[:, 6:7], st_[:, 5:6], 1.0 / D, None,
                                ALU.mult).then_inc(aux, 1)
                v.wait_ge(aux, A + 8 * i + 6)
                v.tensor_scalar(xc[:, :], xn[:, :], st_[:, 6:7], None,
                                ALU.subtract).then_inc(aux, 1)
                v.wait_ge(evs, E + 4 * i + 3)
                v.reduce_sum(st_[:, 7:8], sq[:, :], axis=AX).then_inc(aux, 1)
                v.wait_ge(evs, E + 4 * i + 4)
                if i > 0:
                    v.wait_ge(stob, b2["stob"] + 16 * i)
                v.scalar_tensor_tensor(sq[:, :], xc[:, :], st_[:, 9:10], wbc[:, :],
                                       ALU.mult, ALU.mult)
                v.tensor_tensor(ht[:, :], sq[:, :], bbc[:, :], ALU.add).then_inc(aux2, 1)
                for j in range(32):
                    gi = 32 * i + j
                    v.wait_ge(mm, b2["mm"] + gi + 1)
                    if gi >= 4:
                        v.wait_ge(stot[gi % 4], stotgt(gi - 4))
                    v.tensor_copy(ot[gi % 4][:, :], pt[gi % 4][:, :]).then_inc(ev, 1)

        @blk.scalar
        def _(s):
            for i in range(NTI):
                s.wait_ge(aux, A + 8 * i + 3)
                s.activation(sq[:, :], xc[:, :], AF.Square).then_inc(evs, 1)
                s.wait_ge(aux, A + 8 * i + 4)
                s.activation(st_[:, 3:4], st_[:, 2:3], AF.Ln, scale=1.0 / D, bias=EPS0)
                s.activation(st_[:, 4:5], st_[:, 3:4], AF.Exp, scale=-0.5).then_inc(evs, 1)
                s.wait_ge(aux, A + 8 * i + 7)
                s.activation(sq[:, :], xc[:, :], AF.Square).then_inc(evs, 1)
                s.wait_ge(aux, A + 8 * i + 8)
                s.activation(st_[:, 8:9], st_[:, 7:8], AF.Ln, scale=1.0 / D, bias=EPS0)
                s.activation(st_[:, 9:10], st_[:, 8:9], AF.Exp, scale=-0.5)
                if i > 0:
                    s.wait_ge(stoa, b2["stoa"] + 16 * i)
                s.copy(xnb[:, :], xn[:, :]).then_inc(evs, 1)

        @blk.tensor
        def _(te):
            eye = ctx.eye
            for i in range(NTI):
                te.wait_ge(al[i % 2], altgt(i))
                for j in range(32):
                    gi = 32 * i + j
                    if j == 16:
                        te.wait_ge(aux, A + 8 * i + 5)
                    if gi >= 4:
                        te.wait_ge(ev, b2["ev"] + gi - 3)
                    src = xt[i % 2] if j < 16 else xn
                    kc = j % 16
                    te.transpose(pt[gi % 4][:, :], src[:, kc * 128:(kc + 1) * 128],
                                 eye[:, :]).then_inc(mm, 1)

    ctx.cnt["ald0"] = b2["ald0"] + 16 * ((NTI + 1) // 2)
    ctx.cnt["ald1"] = b2["ald1"] + 16 * (NTI // 2)
    ctx.cnt["aux"] = A + 8 * NTI
    ctx.cnt["aux2"] = b2["aux2"] + NTI
    ctx.cnt["evs"] = E + 4 * NTI
    ctx.cnt["mm"] = b2["mm"] + 32 * NTI
    ctx.cnt["ev"] = b2["ev"] + 32 * NTI
    ctx.cnt["stoa"] = b2["stoa"] + 16 * NTI
    ctx.cnt["stob"] = b2["stob"] + 16 * NTI
    for r in range(4):
        ctx.cnt[f"sto{r}"] = b2[f"sto{r}"] + 16 * 8 * NTI
    es.close()


def emit_p2(ctx):
    nc, io = ctx.nc, ctx.io
    es = contextlib.ExitStack()
    hfm = [es.enter_context(nc.sbuf_tensor(f"p2h{k}", [128, T], BF16)) for k in range(16)]
    gu = es.enter_context(nc.sbuf_tensor("p2gu", [128, 64 * 512], BF16))
    wb1 = [es.enter_context(nc.sbuf_tensor(f"p2w1{i}", [128, 16 * 128], BF16)) for i in range(3)]
    wb2 = [es.enter_context(nc.sbuf_tensor(f"p2w2{i}", [128, 64 * 128], BF16)) for i in range(3)]
    b1c = es.enter_context(nc.sbuf_tensor("p2b1", [128, 64], F32))
    b2c = es.enter_context(nc.sbuf_tensor("p2b2", [128, 16], F32))
    ob = [es.enter_context(nc.sbuf_tensor(f"p2ob{i}", [128, 512], F32)) for i in range(4)]
    ps = [es.enter_context(nc.psum_tensor(f"p2ps{i}", [128, 512], F32)) for i in range(4)]

    s0 = ctx.cnt["ald"]
    with nc.Block() as blk:
        @blk.sync
        def _(sy):
            sy.dma_start(b1c[:, :], io["fc1bc"][:, :]).then_inc(ctx.sem("ald"), 16)
            sy.dma_start(b2c[:, :], io["fc2bc"][:, :]).then_inc(ctx.sem("ald"), 16)
            sy.wait_ge(ctx.sem("ald"), s0 + 32)
    ctx.cnt["ald"] = s0 + 32
    load_fm_tiles(ctx, io["htm"], D, T, hfm)

    for th in range(2):
        t0 = th * 512

        def ev1(se, ji, ti, p, ot, stile):
            return se.activation(gu[:, ji * 512:(ji + 1) * 512], p[:, :],
                                 AF.Gelu_apprx_tanh, bias=b1c[:, ji:ji + 1])

        linear(ctx, dict(
            name="fc1", wt=io["wfc1"], K=D, N=INTER, T=512, tw=512,
            x_tile=lambda kc, t0=t0: hfm[kc][:, t0:t0 + 512],
            psums=ps, wbufs=wb1, evac_s=ev1, evac_v=None, out_ap=None))

        def ev2(se, ji, ti, p, ot, stile):
            return se.activation(ot[:, :], p[:, :], AF.Identity, bias=b2c[:, ji:ji + 1])

        linear(ctx, dict(
            name="fc2", wt=io["wfc2"], K=INTER, N=D, T=512, tw=512,
            x_tile=lambda kc: gu[:, kc * 512:(kc + 1) * 512],
            psums=ps, wbufs=wb2, obufs=ob, evac_s=ev2, evac_v=None,
            out_ap=lambda ji, ti, t0=t0: io["mlp"][ji * 128:(ji + 1) * 128, t0:t0 + 512]))
    es.close()



def emit_p3(ctx):
    nc, io = ctx.nc, ctx.io
    es = contextlib.ExitStack()
    hfm = [es.enter_context(nc.sbuf_tensor(f"p3h{k}", [128, T], BF16)) for k in range(16)]
    wb = [es.enter_context(nc.sbuf_tensor(f"p3w{i}", [128, 16 * 128], BF16)) for i in range(3)]
    ob = [es.enter_context(nc.sbuf_tensor(f"p3ob{i}", [128, 512], BF16)) for i in range(4)]
    obf = [es.enter_context(nc.sbuf_tensor(f"p3of{i}", [64, 512], F32)) for i in range(2)]
    ps = [es.enter_context(nc.psum_tensor(f"p3ps{i}", [128, 512], F32)) for i in range(4)]
    load_fm_tiles(ctx, io["htm"], D, T, hfm)

    def out_ap(ji, ti):
        t0 = ti * 512
        if ji < 32:
            return io["zf"][ji * 128:(ji + 1) * 128, t0:t0 + 512]
        if ji < 40:
            return io["xmf"][(ji - 32) * 128:(ji - 31) * 128, t0:t0 + 512]
        if ji < 48:
            return io["bmf"][(ji - 40) * 128:(ji - 39) * 128, t0:t0 + 512]
        return io["cmf"][(ji - 48) * 128:(ji - 47) * 128, t0:t0 + 512]

    def ev(ve, ji, ti, p, ot, stile):
        return ve.tensor_copy(ot[:, :], p[:, :])

    linear(ctx, dict(
        name="inproj", wt=io["wip"], K=D, N=80 * 128, T=T, tw=512,
        x_tile=lambda kc: hfm[kc],
        psums=ps, wbufs=wb, obufs=ob, evac_v=ev, out_ap=out_ap))

    def evdt(ve, ji, ti, p, ot, stile):
        return ve.tensor_copy(ot[0:64, :], p[0:64, :])

    linear(ctx, dict(
        name="dt", wt=io["wip"], K=D, N=64, T=T, tw=512,
        n0s=[(10240, 64)],
        x_tile=lambda kc: hfm[kc],
        psums=ps, wbufs=wb, obufs=obf, evac_v=evdt,
        out_ap=lambda ji, ti: io["dtr"][0:64, ti * 512:(ti + 1) * 512]))
    es.close()



def emit_p4(ctx):
    nc, io = ctx.nc, ctx.io
    es = contextlib.ExitStack()
    sb = lambda n, s, d: es.enter_context(nc.sbuf_tensor(n, s, d))
    xfc = [sb(f"p4xf{i}", [64, 16 * 128], BF16) for i in range(2)]
    bfc = [sb(f"p4bf{i}", [64, 16 * 128], BF16) for i in range(2)]
    xtm = [sb(f"p4xt{i}", [128, 1024], BF16) for i in range(8)]
    btm = [sb(f"p4bt{i}", [128, 1024], BF16) for i in range(8)]
    cc = [sb(f"p4cc{i}", [64, 64 * 128], BF16) for i in range(2)]
    clb = sb("p4cl", [64, 64 * 128], BF16)
    ylb = sb("p4yl", [64, 64 * 128], BF16)
    Sf = sb("p4S", [64, 4096], F32)
    Sb = sb("p4Sb", [64, 4096], BF16)
    dtr_ = sb("p4dtr", [64, T], F32)
    dts = sb("p4dts", [64, T], F32)
    tmpa = sb("p4tma", [64, T], F32)
    tmpb = sb("p4tmb", [64, T], F32)
    lndt = sb("p4ln", [64, T], F32)
    logdA = sb("p4da", [64, T], F32)
    Lg = sb("p4lg", [64, T], F32)
    Rt = sb("p4r", [64, T], F32)
    eloc = dtr_
    eglo = logdA
    wsc = sb("p4w", [64, T], F32)
    Ecol = sb("p4E", [64, 16], F32)
    dE = sb("p4dE", [64, 8], F32)
    Dex = sb("p4Dx", [64, 8], F32)
    Dflat = sb("p4Df", [1, 512], F32)
    Dbc = sb("p4Dbc", [64, 64], F32)
    LgT = sb("p4lgt", [128, 8 * 64], F32)
    RT = sb("p4rt", [128, 8 * 64], F32)
    wT = sb("p4wt", [128, 8 * 64], F32)
    dtb = sb("p4dtb", [64, 1], F32)
    alog = sb("p4al", [64, 1], F32)
    Acol = sb("p4A", [64, 1], F32)
    dpc = sb("p4dp", [64, 64], F32)
    sel64 = sb("p4s64", [64, 64 * 64], F32)
    caus = sb("p4ca", [128, 128], F32)
    zt = sb("p4z", [128, 128], F32)
    bct = [sb(f"p4bc{i}", [128, 128], F32) for i in range(2)]
    msk = [sb(f"p4mk{i}", [128, 128], F32) for i in range(2)]
    Mp = [sb(f"p4Mp{i}", [128, 128], F32) for i in range(2)]
    Gm = [sb(f"p4Gm{i}", [128, 128], BF16) for i in range(2)]
    xsc = [sb(f"p4xs{i}", [64, 128], F32) for i in range(2)]
    bw = [sb(f"p4bw{i}", [128, 64], BF16) for i in range(2)]
    st1 = sb("p4s1", [64, 64], F32)
    psA = [es.enter_context(nc.psum_tensor(f"p4pA{i}", [128, 128], F32)) for i in range(2)]
    psG = [es.enter_context(nc.psum_tensor(f"p4pG{i}", [128, 128], F32)) for i in range(2)]
    psY = [es.enter_context(nc.psum_tensor(f"p4pY{i}", [64, 128], F32)) for i in range(2)]
    psS = [es.enter_context(nc.psum_tensor(f"p4pS{i}", [64, 64], F32)) for i in range(2)]
    dram_tmp = nc.dram_tensor("p4dram", [64, 8], F32).ap()

    a0 = ctx.cnt["ald"]
    with nc.Block() as blk:
        @blk.sync
        def _(sy):
            n = 0
            for i in range(8):
                sy.dma_start(xtm[i][:, :], io["xmf"][0:1024, i * 128:(i + 1) * 128], transpose=True).then_inc(ctx.sem("ald"), 16); n += 1
                sy.dma_start(btm[i][:, :], io["bmf"][0:1024, i * 128:(i + 1) * 128], transpose=True).then_inc(ctx.sem("ald"), 16); n += 1
            for ap, name in [(dtr_, "dtr"), (dtb, "dtb"), (alog, "alog"), (dpc, "dpc64"),
                             (caus, "causal"), (sel64, "sel64")]:
                sy.dma_start(ap[:, :], io[name][:, :]).then_inc(ctx.sem("ald"), 16); n += 1
            sy.wait_ge(ctx.sem("ald"), a0 + 16 * n)
            ctx.cnt["ald"] = a0 + 16 * n
        @blk.vector
        def _(v):
            v.memset(zt[:, :], 0.0)
            v.memset(Ecol[:, :], 0.0)
            v.memset(Sf[:, :], 0.0)
            v.memset(Sb[:, :], 0.0)

    steps = []
    def S(eng, fn):
        steps.append((eng, fn))
    S("s", lambda e: e.activation(dts[:, :], dtr_[:, :], AF.Identity, bias=dtb[:, 0:1]))
    S("s", lambda e: e.activation(tmpa[:, :], dts[:, :], AF.Abs))
    S("s", lambda e: e.activation(tmpb[:, :], tmpa[:, :], AF.Exp, scale=-1.0))
    S("s", lambda e: e.activation(tmpa[:, :], tmpb[:, :], AF.Ln, bias=1.0))
    S("v", lambda e: e.tensor_relu(tmpb[:, :], dts[:, :]))
    S("v", lambda e: e.tensor_add(dts[:, :], tmpa[:, :], tmpb[:, :]))
    S("s", lambda e: e.activation(lndt[:, :], dts[:, :], AF.Ln))
    S("s", lambda e: e.activation(Acol[:, 0:1], alog[:, 0:1], AF.Exp))
    S("v", lambda e: e.tensor_scalar(Acol[:, 0:1], Acol[:, 0:1], -1.0, None, ALU.mult))
    S("v", lambda e: e.tensor_scalar(logdA[:, :], dts[:, :], Acol[:, 0:1], None, ALU.mult))
    S("v", lambda e: e.tensor_tensor_scan(Lg[:, :], logdA[:, :], logdA[:, :], 0.0, ALU.add, ALU.bypass))
    S("v", lambda e: e.tensor_copy(Ecol[:, 1:9], Lg[:, 127::128]))
    S("v", lambda e: e.tensor_sub(Rt[:, :], lndt[:, :], Lg[:, :]))
    for c in range(NCH):
        S("v", lambda e, c=c: e.tensor_scalar(tmpa[:, c * 128:(c + 1) * 128], Lg[:, c * 128:(c + 1) * 128], Ecol[:, c:c + 1], None, ALU.subtract))
        S("v", lambda e, c=c: e.tensor_scalar(tmpb[:, c * 128:(c + 1) * 128], Rt[:, c * 128:(c + 1) * 128], Ecol[:, c + 1:c + 2], None, ALU.add))
    S("s", lambda e: e.activation(eloc[:, :], tmpa[:, :], AF.Exp))
    S("s", lambda e: e.activation(eglo[:, :], Lg[:, :], AF.Exp))
    S("s", lambda e: e.activation(wsc[:, :], tmpb[:, :], AF.Exp))
    S("v", lambda e: e.tensor_sub(dE[:, :], Ecol[:, 1:9], Ecol[:, 0:8]))
    S("s", lambda e: e.activation(Dex[:, :], dE[:, :], AF.Exp))
    S("g", lambda e: e.dma_start(dram_tmp[:, :], Dex[:, :]))
    for c in range(NCH):
        S("g", lambda e, c=c: e.dma_start(Dflat[0:1, c * 64:(c + 1) * 64],
                                          dram_tmp[:, c:c + 1].rearrange("h c -> c h"),
                                          allow_slow_non_contiguous=True))
    for c in range(NCH):
        S("t", lambda e, c=c: e.transpose(psA[0][:, 0:64], Lg[:, c * 128:(c + 1) * 128], ctx.eye[0:64, 0:64]))
        S("v", lambda e, c=c: e.tensor_copy(LgT[:, c * 64:(c + 1) * 64], psA[0][:, 0:64]))
        S("t", lambda e, c=c: e.transpose(psA[0][:, 0:64], Rt[:, c * 128:(c + 1) * 128], ctx.eye[0:64, 0:64]))
        S("v", lambda e, c=c: e.tensor_copy(RT[:, c * 64:(c + 1) * 64], psA[0][:, 0:64]))
        S("t", lambda e, c=c: e.transpose(psA[0][:, 0:64], wsc[:, c * 128:(c + 1) * 128], ctx.eye[0:64, 0:64]))
        S("v", lambda e, c=c: e.tensor_copy(wT[:, c * 64:(c + 1) * 64], psA[0][:, 0:64]))

    aux = ctx.sem("aux")
    A0 = ctx.cnt["aux"]
    incs = [16 if en == "g" else 1 for en, _ in steps]
    tgts = [A0 + sum(incs[:i]) for i in range(len(steps))]
    with nc.Block() as blk:
        for eng_name, body in [("v", "vector"), ("s", "scalar"), ("t", "tensor"), ("g", "gpsimd")]:
            def mk(eng_name=eng_name):
                def emit(e):
                    for i, (en, fn) in enumerate(steps):
                        if en != eng_name:
                            continue
                        e.wait_ge(aux, tgts[i])
                        fn(e).then_inc(aux, incs[i])
                return emit
            getattr(blk, body)(mk())
    ctx.cnt["aux"] = A0 + sum(incs)

    mm, ev, evs, mmp, evp = (ctx.sem(s) for s in ["mm", "ev", "evs", "mmp", "evp"])
    al = [ctx.sem("ald0"), ctx.sem("ald1")]
    stot = [ctx.sem(f"sto{r}") for r in range(4)]
    b = {s: ctx.cnt[s] for s in ["mm", "ev", "evs", "mmp", "evp", "ald0", "ald1",
                                 "sto0", "sto1", "sto2", "sto3"]}
    M0, E0, V0, MP0, EP0 = b["mm"], b["evs"], b["ev"], b["mmp"], b["evp"]
    csrc = io["cmf"].rearrange("(a p) t -> p a t", p=64)
    xsrc = io["xmf"].rearrange("(a p) t -> p a t", p=64)
    bsrc = io["bmf"].rearrange("(a p) t -> p a t", p=64)

    def kvg(h):
        return h // 4

    def sfv(buf, h):
        return buf[:, h * 64:(h + 1) * 64]

    def hv(buf, h, w=128):
        return buf[:, h * 128:h * 128 + w]

    def cload(sy, c):
        for d in range(4):
            sy.dma_start(cc[c % 2][:, d * 16 * 128:(d + 1) * 16 * 128].rearrange("p (a t) -> p a t", t=128),
                         csrc[:, d * 16:(d + 1) * 16, c * 128:(c + 1) * 128]).then_inc(al[c % 2], 16)
        sy.dma_start(xfc[c % 2][:, :].rearrange("p (a t) -> p a t", t=128),
                     xsrc[:, :, c * 128:(c + 1) * 128]).then_inc(al[c % 2], 16)
        sy.dma_start(bfc[c % 2][:, :].rearrange("p (a t) -> p a t", t=128),
                     bsrc[:, :, c * 128:(c + 1) * 128]).then_inc(al[c % 2], 16)

    NPRE = 65

    with nc.Block() as blk:
        @blk.sync
        def _(sy):
            for c in range(min(2, NCH)):
                cload(sy, c)
            for c in range(NCH):
                for h in range(NH):
                    g = c * 64 + h
                    sy.wait_ge(ev, V0 + 4 * g + 3)
                    sy.dma_start(io["yloc"][h * 64:(h + 1) * 64, c * 128:(c + 1) * 128],
                                 hv(ylb, h)).then_inc(stot[h % 4], 16)
                if c + 2 < NCH:
                    sy.wait_ge(mm, M0 + 4 * 64 * (c + 1))
                    sy.wait_ge(evp, EP0 + NPRE * (c + 1))
                    cload(sy, c + 2)
            for r in range(4):
                sy.wait_ge(stot[r], b[f"sto{r}"] + 16 * 16 * 8)
            sy.wait_ge(ev, V0 + 4 * 64 * NCH)
            for h in range(NH):
                sy.dma_start(io["sfin"][h * 64:(h + 1) * 64, :], sfv(Sf, h)).then_inc(stot[0], 16)
            sy.dma_start(io["elast"][:, 0:1], Ecol[:, 8:9]).then_inc(stot[0], 16)
            sy.dma_start(io["eglob"][:, :], eglo[:, :]).then_inc(stot[0], 16)
            sy.wait_ge(stot[0], b["sto0"] + 16 * 16 * 8 + 16 * 66)

        @blk.tensor
        def _(te):
            nmm = M0
            for c in range(NCH):
                par = c % 2
                te.wait_ge(al[par], b[f"ald{par}"] + 16 * 6 * (c // 2 + 1))
                if c > 0:
                    te.wait_ge(ev, V0 + 4 * 64 * c)
                for h in range(NH):
                    if h >= 2:
                        te.wait_ge(evp, EP0 + NPRE * c + h - 1)
                    te.matmul(psY[h % 2][:, :], sel64[:, h * 64:(h + 1) * 64],
                              eloc[:, c * 128:(c + 1) * 128],
                              start=True, stop=True).then_inc(mmp, 1)
                te.wait_ge(evp, EP0 + NPRE * c + 64)
                te.matmul(psS[0][:, :], ctx.onec[0:1, 0:64], Dflat[0:1, c * 64:(c + 1) * 64],
                          start=True, stop=True).then_inc(mmp, 1)
                for h in range(NH):
                    g = c * 64 + h
                    if h == 0:
                        te.wait_ge(evp, EP0 + NPRE * (c + 1))
                    te.wait_ge(evs, E0 + 5 * g + 1)
                    if g >= 2:
                        te.wait_ge(ev, V0 + 4 * (g - 2) + 2)
                    te.transpose(psA[g % 2][:, :], bct[g % 2][:, :], ctx.eye[:, :]).then_inc(mm, 1)
                    te.matmul(psG[g % 2][:, :], bfc[par][:, kvg(h) * 128:(kvg(h) + 1) * 128],
                              hv(cc[par], h), start=True, stop=True).then_inc(mm, 1)
                    te.wait_ge(ev, V0 + 4 * g + 2)
                    te.matmul(psY[g % 2][:, :], xtm[c][:, kvg(h) * 64:(kvg(h) + 1) * 64],
                              Gm[g % 2][:, :], start=True, stop=False)
                    if c > 0:
                        te.wait_ge(evs, E0 + 5 * (g - 64) + 5)
                    te.matmul(psY[g % 2][:, :], sfv(Sb, h), hv(clb, h),
                              start=False, stop=True).then_inc(mm, 1)
                    te.wait_ge(evs, E0 + 5 * g + 4)
                    te.matmul(psS[g % 2][:, :], bw[g % 2][:, :],
                              xtm[c][:, kvg(h) * 64:(kvg(h) + 1) * 64],
                              start=True, stop=True).then_inc(mm, 1)

        @blk.vector
        def _(v):
            for c in range(NCH):
                if c > 0:
                    v.wait_ge(mm, M0 + 4 * 64 * c)
                for h in range(NH):
                    v.wait_ge(mmp, MP0 + NPRE * c + h + 1)
                    v.tensor_tensor(hv(clb, h), hv(cc[c % 2], h),
                                    psY[h % 2][:, :], ALU.mult).then_inc(evp, 1)
                v.wait_ge(mmp, MP0 + NPRE * c + 65)
                v.tensor_copy(Dbc[:, :], psS[0][:, :]).then_inc(evp, 1)
                for h in range(NH):
                    g = c * 64 + h
                    v.wait_ge(mm, M0 + 4 * g + 1)
                    if g >= 2:
                        v.wait_ge(evs, E0 + 5 * (g - 2) + 2)
                    v.tensor_tensor(msk[g % 2][:, :], psA[g % 2][:, :], caus[:, :], ALU.add).then_inc(ev, 1)
                    v.wait_ge(mm, M0 + 4 * g + 2)
                    v.wait_ge(evs, E0 + 5 * g + 2)
                    v.tensor_tensor(Gm[g % 2][:, :], psG[g % 2][:, :], Mp[g % 2][:, :], ALU.mult).then_inc(ev, 1)
                    v.wait_ge(mm, M0 + 4 * g + 3)
                    v.wait_ge(evs, E0 + 5 * g + 3)
                    if c > 0:
                        v.wait_ge(stot[h % 4], b[f"sto{h % 4}"] + 16 * (16 * (c - 1) + h // 4 + 1))
                    v.tensor_tensor(hv(ylb, h), psY[g % 2][:, :], xsc[g % 2][:, :], ALU.add).then_inc(ev, 1)
                    v.wait_ge(mm, M0 + 4 * g + 4)
                    v.wait_ge(evp, EP0 + NPRE * (c + 1))
                    v.tensor_scalar(st1[:, :], sfv(Sf, h), Dbc[:, h:h + 1], None, ALU.mult)
                    v.tensor_tensor(sfv(Sf, h), st1[:, :], psS[g % 2][:, :], ALU.add).then_inc(ev, 1)

        @blk.scalar
        def _(s):
            for c in range(NCH):
                for h in range(NH):
                    g = c * 64 + h
                    if g >= 2:
                        s.wait_ge(mm, M0 + 4 * (g - 2) + 1)
                    s.activation(bct[g % 2][:, :], zt[:, :], AF.Identity,
                                 bias=LgT[:, c * 64 + h:c * 64 + h + 1]).then_inc(evs, 1)
                    s.wait_ge(ev, V0 + 4 * g + 1)
                    s.activation(Mp[g % 2][:, :], msk[g % 2][:, :], AF.Exp,
                                 bias=RT[:, c * 64 + h:c * 64 + h + 1]).then_inc(evs, 1)
                    if h == 0:
                        s.wait_ge(al[c % 2], b[f"ald{c % 2}"] + 16 * 6 * (c // 2 + 1))
                    if g >= 2:
                        s.wait_ge(ev, V0 + 4 * (g - 2) + 3)
                    s.activation(xsc[g % 2][:, :], xfc[c % 2][:, kvg(h) * 128:(kvg(h) + 1) * 128],
                                 AF.Identity, scale=dpc[:, h:h + 1]).then_inc(evs, 1)
                    if g >= 2:
                        s.wait_ge(mm, M0 + 4 * (g - 2) + 4)
                    s.activation(bw[g % 2][:, :],
                                 btm[c][:, kvg(h) * 64:(kvg(h) + 1) * 64], AF.Identity,
                                 scale=wT[:, c * 64 + h:c * 64 + h + 1]).then_inc(evs, 1)
                    s.wait_ge(ev, V0 + 4 * g + 4)
                    s.copy(sfv(Sb, h), sfv(Sf, h)).then_inc(evs, 1)

    ctx.cnt["mm"] = M0 + 4 * 64 * NCH
    ctx.cnt["ev"] = V0 + 4 * 64 * NCH
    ctx.cnt["evs"] = E0 + 5 * 64 * NCH
    ctx.cnt["mmp"] = MP0 + NPRE * NCH
    ctx.cnt["evp"] = EP0 + NPRE * NCH
    ctx.cnt["ald0"] = b["ald0"] + 16 * 6 * ((NCH + 1) // 2)
    ctx.cnt["ald1"] = b["ald1"] + 16 * 6 * (NCH // 2)
    for r in range(4):
        ctx.cnt[f"sto{r}"] = b[f"sto{r}"] + 16 * 16 * 8 + (16 * 66 if r == 0 else 0)
    es.close()


def emit_p5(ctx):
    nc, io = ctx.nc, ctx.io
    es0 = contextlib.ExitStack()
    esA = contextlib.ExitStack()
    sb = lambda st, n, s, d: st.enter_context(nc.sbuf_tensor(n, s, d))
    esA2 = contextlib.ExitStack()
    ssq = es0.enter_context(nc.sbuf_tensor("p5ssq", [1, T], F32))
    sel64 = sb(esA, "p5s64", [64, 64 * 64], F32)
    Sin = sb(esA, "p5sin", [64, 4096], BF16)
    Esb = sb(esA2, "p5E", [64, 8], F32)
    emk = sb(esA2, "p5em", [64, 8], F32)
    ExpE = sb(esA2, "p5xE", [64, 8], F32)
    Wbc = sb(esA2, "p5W", [64, 64 * 8], F32)
    acc = sb(esA2, "p5acc", [64, 4096], F32)
    skb = sb(esA2, "p5skb", [64, 4096], F32)
    sgg = sb(esA2, "p5sgg", [64, 4096], F32)
    psW = esA2.enter_context(nc.psum_tensor("p5pW", [64, 8], F32))

    a0 = ctx.cnt["ald"]
    with nc.Block() as blk:
        @blk.sync
        def _(sy):
            n = 0
            for k in range(8):
                sy.dma_start(Esb[:, k:k + 1], io["egath"][k * 64:(k + 1) * 64, 0:1]).then_inc(ctx.sem("ald"), 16); n += 1
            sy.dma_start(emk[:, :], io["emask64"][:, :]).then_inc(ctx.sem("ald"), 16); n += 1
            sy.dma_start(sel64[:, :], io["sel64"][:, :]).then_inc(ctx.sem("ald"), 16); n += 1
            sy.wait_ge(ctx.sem("ald"), a0 + 16 * n)
            ctx.cnt["ald"] = a0 + 16 * n
        @blk.vector
        def _(v):
            v.memset(acc[:, :], 0.0)

    steps = []
    def S(eng, fn):
        steps.append((eng, fn))
    S("v", lambda e: e.tensor_mul(Esb[:, :], Esb[:, :], emk[:, :]))
    S("s", lambda e: e.activation(ExpE[:, :], Esb[:, :], AF.Exp))
    for h in range(64):
        S("t", lambda e, h=h: e.matmul(psW[:, :], sel64[:, h * 64:(h + 1) * 64], ExpE[:, :],
                                       start=True, stop=True))
        S("v", lambda e, h=h: e.tensor_copy(Wbc[:, h * 8:(h + 1) * 8], psW[:, :]))
    for k in range(7):
        sgsrc = io["sgath"].rearrange("(c hh n) p -> c n hh p", c=8, n=64)
        for d in range(4):
            S("g", lambda e, k=k, d=d: e.dma_start(
                skb[:, d * 16 * 64:(d + 1) * 16 * 64].rearrange("n (hh p) -> n hh p", p=64),
                sgsrc[k, :, d * 16:(d + 1) * 16, :]))
        for h in range(64):
            S("v", lambda e, k=k, h=h: e.tensor_scalar(
                acc[:, h * 64:(h + 1) * 64], acc[:, h * 64:(h + 1) * 64],
                Wbc[:, h * 8 + k:h * 8 + k + 1], None, ALU.mult))
        S("v", lambda e, k=k: e.tensor_scalar(sgg[:, :], skb[:, :], emk[:, k:k + 1], None, ALU.mult))
        S("v", lambda e, k=k: e.tensor_add(acc[:, :], acc[:, :], sgg[:, :]))
    S("v", lambda e: e.tensor_copy(Sin[:, :], acc[:, :]))
    if "dbg3" in io:
        S("g", lambda e: e.dma_start(io["dbg3"][:, :], acc[:, :]))
        S("g", lambda e: e.dma_start(io["dbg4"][:, :], Wbc[:, :]))

    aux = ctx.sem("aux")
    A0 = ctx.cnt["aux"]
    incs = [16 if en == "g" else 1 for en, _ in steps]
    tgts = [A0 + sum(incs[:i]) for i in range(len(steps))]
    with nc.Block() as blk:
        for eng_name, body in [("v", "vector"), ("s", "scalar"), ("t", "tensor"), ("g", "gpsimd")]:
            def mk(eng_name=eng_name):
                def emit(e):
                    for i, (en, fn) in enumerate(steps):
                        if en != eng_name:
                            continue
                        e.wait_ge(aux, tgts[i])
                        fn(e).then_inc(aux, incs[i])
                return emit
            getattr(blk, body)(mk())
    ctx.cnt["aux"] = A0 + sum(incs)
    esA2.close()

    esB = contextlib.ExitStack()
    cc = [sb(esB, f"p5cc{i}", [64, 64 * 128], BF16) for i in range(2)]
    yl = [sb(esB, f"p5yl{i}", [64, 64 * 128], BF16) for i in range(2)]
    zc = [sb(esB, f"p5zc{i}", [64, 64 * 128], BF16) for i in range(2)]
    yfb = [sb(esB, f"p5yf{i}", [64, 64 * 128], BF16) for i in range(2)]
    eglo = sb(esB, "p5eg", [64, T], F32)
    nw64 = sb(esB, "p5nw", [64, 64], F32)
    CGl = [sb(esB, f"p5cg{i}", [64, 128], BF16) for i in range(2)]
    yt = [sb(esB, f"p5yt{i}", [64, 128], F32) for i in range(2)]
    sz = [sb(esB, f"p5sz{i}", [64, 128], F32) for i in range(2)]
    sq = [sb(esB, f"p5sq{i}", [64, 128], F32) for i in range(2)]
    psA = [esB.enter_context(nc.psum_tensor(f"p5pA{i}", [64, 128], F32)) for i in range(2)]
    psY = [esB.enter_context(nc.psum_tensor(f"p5pY{i}", [64, 128], F32)) for i in range(2)]
    psQ = [esB.enter_context(nc.psum_tensor(f"p5pQ{i}", [1, 128], F32)) for i in range(2)]

    a0 = ctx.cnt["ald"]
    with nc.Block() as blk:
        @blk.sync
        def _(sy):
            sy.dma_start(eglo[:, :], io["eglob"][:, :]).then_inc(ctx.sem("ald"), 16)
            sy.dma_start(nw64[:, :], io["nw64"][:, :]).then_inc(ctx.sem("ald"), 16)
            sy.wait_ge(ctx.sem("ald"), a0 + 32)
            ctx.cnt["ald"] = a0 + 32
        @blk.vector
        def _(v):
            v.memset(ssq[0:1, :], 0.0)
    mm, ev, evs, evp = (ctx.sem(s) for s in ["mm", "ev", "evs", "evp"])
    al = [ctx.sem("ald0"), ctx.sem("ald1")]
    stot = [ctx.sem(f"sto{r}") for r in range(2)]
    b = {s: ctx.cnt[s] for s in ["mm", "ev", "evs", "evp", "ald0", "ald1", "sto0", "sto1"]}
    M0, V0, E0, EP0 = b["mm"], b["ev"], b["evs"], b["evp"]

    def hv(buf, h, w=128):
        return buf[:, h * 128:h * 128 + w]

    def chload(sy, c, tens, dst):
        src = tens.rearrange("(a p) t -> p a t", p=64)
        for d in range(4):
            sy.dma_start(dst[c % 2][:, d * 16 * 128:(d + 1) * 16 * 128].rearrange("p (a t) -> p a t", t=128),
                         src[:, d * 16:(d + 1) * 16, c * 128:(c + 1) * 128]).then_inc(al[c % 2], 16)

    with nc.Block() as blk:
        @blk.sync
        def _(sy):
            for c in range(min(2, NCH)):
                chload(sy, c, io["cmf"], cc)
                chload(sy, c, io["yloc"], yl)
                chload(sy, c, io["zf"], zc)
            for c in range(NCH):
                g9 = c * 64 + 63
                sy.wait_ge(ev, V0 + 5 * g9 + 4)
                ydst = io["yfh"].rearrange("(a p) t -> p a t", p=64)
                for d in range(4):
                    sy.dma_start(ydst[:, d * 16:(d + 1) * 16, c * 128:(c + 1) * 128],
                                 yfb[c % 2][:, d * 16 * 128:(d + 1) * 16 * 128].rearrange("p (a t) -> p a t", t=128)
                                 ).then_inc(stot[c % 2], 16)
                if c + 2 < NCH:
                    sy.wait_ge(ev, V0 + 5 * 64 * (c + 1))
                    sy.wait_ge(evs, E0 + 2 * 64 * (c + 1))
                    chload(sy, c + 2, io["cmf"], cc)
                    chload(sy, c + 2, io["yloc"], yl)
                    chload(sy, c + 2, io["zf"], zc)
            for r in range(2):
                sy.wait_ge(stot[r], b[f"sto{r}"] + 16 * 4 * (4 if r == 0 else 4))

        @blk.tensor
        def _(te):
            for c in range(NCH):
                par = c % 2
                te.wait_ge(al[par], b[f"ald{par}"] + 16 * 12 * (c // 2 + 1))
                for h in range(NH):
                    g = c * 64 + h
                    if g >= 2:
                        te.wait_ge(ev, V0 + 5 * (g - 2) + 1)
                    te.matmul(psA[g % 2][:, :], sel64[:, h * 64:(h + 1) * 64],
                              eglo[:, c * 128:(c + 1) * 128], start=True, stop=True).then_inc(mm, 1)
                    te.wait_ge(ev, V0 + 5 * g + 1)
                    te.matmul(psY[g % 2][:, :], Sin[:, h * 64:(h + 1) * 64],
                              CGl[g % 2][:, :], start=True, stop=True).then_inc(mm, 1)
                    te.wait_ge(evs, E0 + 2 * g + 2)
                    if g >= 2:
                        te.wait_ge(ev, V0 + 5 * (g - 2) + 5)
                    te.matmul(psQ[g % 2][:, :], ctx.ones128[0:64, 0:1], sq[g % 2][:, :],
                              start=True, stop=True).then_inc(mm, 1)

        @blk.vector
        def _(v):
            for c in range(NCH):
                for h in range(NH):
                    g = c * 64 + h
                    v.wait_ge(mm, M0 + 3 * g + 1)
                    if g >= 2:
                        v.wait_ge(mm, M0 + 3 * (g - 2) + 2)
                    v.tensor_tensor(CGl[g % 2][:, :], hv(cc[c % 2], h), psA[g % 2][:, :],
                                    ALU.mult).then_inc(ev, 1)
                    v.wait_ge(mm, M0 + 3 * g + 2)
                    v.tensor_tensor(yt[g % 2][:, :], psY[g % 2][:, :], hv(yl[c % 2], h),
                                    ALU.add).then_inc(ev, 1)
                    v.wait_ge(evs, E0 + 2 * g + 1)
                    if c >= 2 and h == 0:
                        v.wait_ge(stot[c % 2], b[f"sto{c % 2}"] + 16 * 4 * (c // 2))
                    v.tensor_tensor(hv(yfb[c % 2], h), yt[g % 2][:, :], sz[g % 2][:, :],
                                    ALU.mult).then_inc(ev, 1)
                    v.wait_ge(evs, E0 + 2 * g + 2)
                    v.tensor_scalar(hv(yfb[c % 2], h), hv(yfb[c % 2], h),
                                    nw64[:, h:h + 1], None, ALU.mult).then_inc(ev, 1)
                    v.wait_ge(mm, M0 + 3 * g + 3)
                    v.tensor_tensor(ssq[0:1, c * 128:(c + 1) * 128],
                                    ssq[0:1, c * 128:(c + 1) * 128],
                                    psQ[g % 2][0:1, :], ALU.add).then_inc(ev, 1)

        @blk.scalar
        def _(s):
            for c in range(NCH):
                for h in range(NH):
                    g = c * 64 + h
                    if h == 0:
                        s.wait_ge(al[c % 2], b[f"ald{c % 2}"] + 16 * 12 * (c // 2 + 1))
                    if g >= 2:
                        s.wait_ge(ev, V0 + 5 * (g - 2) + 3)
                    s.activation(sz[g % 2][:, :], hv(zc[c % 2], h), AF.Silu).then_inc(evs, 1)
                    s.wait_ge(ev, V0 + 5 * g + 3)
                    s.activation(sq[g % 2][:, :], hv(yfb[c % 2], h), AF.Square).then_inc(evs, 1)

    ctx.cnt["mm"] = M0 + 3 * 64 * NCH
    ctx.cnt["ev"] = V0 + 5 * 64 * NCH
    ctx.cnt["evs"] = E0 + 2 * 64 * NCH
    ctx.cnt["ald0"] = b["ald0"] + 16 * 12 * ((NCH + 1) // 2)
    ctx.cnt["ald1"] = b["ald1"] + 16 * 12 * (NCH // 2)
    ctx.cnt["sto0"] = b["sto0"] + 16 * 4 * 4
    ctx.cnt["sto1"] = b["sto1"] + 16 * 4 * 4
    esB.close()
    esA.close()

    esC = contextlib.ExitStack()
    sbC = lambda n, s, d: esC.enter_context(nc.sbuf_tensor(n, s, d))
    yft = [sbC(f"p5y{k}", [128, T], BF16) for k in range(32)]
    mos = [sbC(f"p5mo{k}", [128, T], F32) for k in range(16)]
    rrow = sbC("p5rr", [1, T], F32)
    rbc = sbC("p5rb", [128, T], F32)
    wb = [sbC(f"p5wb{i}", [128, 32 * 128], BF16) for i in range(2)]
    mt = sbC("p5mt", [128, T], F32)
    xt_ = sbC("p5xt2", [128, T], F32)
    ob = [sbC(f"p5ob{i}", [128, 512], F32) for i in range(4)]
    ps = [esC.enter_context(nc.psum_tensor(f"p5ps{i}", [128, 512], F32)) for i in range(4)]
    psB = esC.enter_context(nc.psum_tensor("p5psB", [128, 512], F32))

    a0 = ctx.cnt["ald"]
    with nc.Block() as blk:
        @blk.sync
        def _(sy):
            for k in range(32):
                sy.dma_start(yft[k][:, :], io["yfh"][k * 128:(k + 1) * 128, :]).then_inc(ctx.sem("ald"), 16)
            sy.wait_ge(ctx.sem("ald"), a0 + 16 * 32)
            ctx.cnt["ald"] = a0 + 16 * 32

    steps = []
    def S(eng, fn):
        steps.append((eng, fn))
    if "dbg1" in io:
        S("g", lambda e: e.dma_start(io["dbg1"][0:1, :], ssq[0:1, :]))
        S("g", lambda e: e.dma_start(io["dbg2"][:, :], Sin[:, :]))

    S("s", lambda e: e.activation(rrow[0:1, :], ssq[0:1, :], AF.Ln, scale=1.0 / DI, bias=EPS0))
    S("s", lambda e: e.activation(rrow[0:1, :], rrow[0:1, :], AF.Exp, scale=-0.5))
    for j in range(2):
        S("t", lambda e, j=j: e.matmul(psB[:, :], ctx.onec[0:1, :], rrow[0:1, j * 512:(j + 1) * 512],
                                       start=True, stop=True))
        S("v", lambda e, j=j: e.tensor_copy(rbc[:, j * 512:(j + 1) * 512], psB[:, :]))
    aux = ctx.sem("aux")
    A0 = ctx.cnt["aux"]
    incsd = [16 if en == "g" else 1 for en, _ in steps]
    tgts = [A0 + sum(incsd[:i]) for i in range(len(steps))]
    with nc.Block() as blk:
        for eng_name, body in [("v", "vector"), ("s", "scalar"), ("t", "tensor"), ("g", "gpsimd")]:
            def mk(eng_name=eng_name):
                def emit(e):
                    for i, (en, fn) in enumerate(steps):
                        if en != eng_name:
                            continue
                        e.wait_ge(aux, tgts[i])
                        fn(e).then_inc(aux, incsd[i])
                return emit
            getattr(blk, body)(mk())
    ctx.cnt["aux"] = A0 + sum(incsd)

    def evop(ve, ji, ti, p, ot, stile):
        t0 = ti * 512
        return ve.tensor_tensor(mos[ji][:, t0:t0 + 512], p[:, :], rbc[:, t0:t0 + 512], ALU.mult)

    linear(ctx, dict(
        name="outproj", wt=io["wop"], K=DI, N=D, T=T, tw=512,
        x_tile=lambda kc: yft[kc],
        psums=ps, wbufs=wb, evac_v=evop, out_ap=None))

    steps = []
    for kc in range(16):
        S("g", lambda e, kc=kc: e.dma_start(mt[:, :], io["mlp"][kc * 128:(kc + 1) * 128, :]))
        S("g", lambda e, kc=kc: e.dma_start(xt_[:, :], io["xnfm"][kc * 128:(kc + 1) * 128, :]))
        S("v", lambda e, kc=kc: e.tensor_add(mos[kc][:, :], mos[kc][:, :], mt[:, :]))
        S("v", lambda e, kc=kc: e.tensor_add(mos[kc][:, :], mos[kc][:, :], xt_[:, :]))
        S("g", lambda e, kc=kc: e.dma_start(io["sfm"][kc * 128:(kc + 1) * 128, :], mos[kc][:, :]))
    A0 = ctx.cnt["aux"]
    incs = [16 if en == "g" else 1 for en, _ in steps]
    tgts = [A0 + sum(incs[:i]) for i in range(len(steps))]
    with nc.Block() as blk:
        for eng_name, body in [("v", "vector"), ("g", "gpsimd")]:
            def mk(eng_name=eng_name):
                def emit(e):
                    for i, (en, fn) in enumerate(steps):
                        if en != eng_name:
                            continue
                        e.wait_ge(aux, tgts[i])
                        fn(e).then_inc(aux, incs[i])
                return emit
            getattr(blk, body)(mk())
        @blk.sync
        def _(sy):
            sy.wait_ge(aux, A0 + sum(incs))
    ctx.cnt["aux"] = A0 + sum(incs)
    esC.close()
    es0.close()


def emit_p6(ctx):
    nc, io = ctx.nc, ctx.io
    esA = contextlib.ExitStack()
    sbA = lambda n, s, d: esA.enter_context(nc.sbuf_tensor(n, s, d))
    xnf = [sbA(f"p6x{k}", [128, T], BF16) for k in range(16)]
    wgb = [sbA(f"p6wg{i}", [128, 16 * 512], BF16) for i in range(2)]
    wub = [sbA(f"p6wu{i}", [128, 16 * 512], BF16) for i in range(2)]
    RNG = 3
    sg = [sbA(f"p6sg{i}", [128, 512], F32) for i in range(RNG)]
    hh = [sbA(f"p6hh{i}", [128, 512], F32) for i in range(RNG)]
    sqv = [sbA(f"p6sq{i}", [128, 512], F32) for i in range(RNG)]
    cb = sbA("p6cb", [128, 6 * 128], F32)
    tb = sbA("p6tb", [128, 2 * 128], F32)
    kp = sbA("p6kp", [128, 128], F32)
    hspb = [sbA(f"p6hs{i}", [128, 512], BF16) for i in range(RNG)]
    psG = [esA.enter_context(nc.psum_tensor(f"p6pG{i}", [128, 512], F32)) for i in range(RNG)]
    psU = [esA.enter_context(nc.psum_tensor(f"p6pU{i}", [128, 512], F32)) for i in range(RNG)]
    load_fm_tiles(ctx, io["xntm"], D, T, xnf)

    mm, ev, evs = (ctx.sem(s) for s in ["mm", "ev", "evs"])
    wld = [ctx.sem("wld0"), ctx.sem("wld1")]
    stot = [ctx.sem(f"sto{r}") for r in range(3)]
    b = {s: ctx.cnt[s] for s in ["mm", "ev", "evs", "wld0", "wld1", "sto0", "sto1", "sto2"]}
    M0, V0, E0 = b["mm"], b["ev"], b["evs"]
    wgsrc = io["wg"].rearrange("(a p) n -> p a n", p=128)
    wusrc = io["wu"].rearrange("(a p) n -> p a n", p=128)
    NIB, NTP = 16, 8

    with nc.Block() as blk:
        @blk.gpsimd
        def _(g):
            for ib in range(NIB):
                par = ib % 2
                if ib >= 2:
                    g.wait_ge(ev, V0 + 2 * 8 * (ib - 1))
                for d in range(2):
                    g.dma_start(wgb[par][:, d * 8 * 512:(d + 1) * 8 * 512].rearrange("p (a n) -> p a n", n=512),
                                wgsrc[:, d * 8:(d + 1) * 8, ib * 512:(ib + 1) * 512]).then_inc(wld[par], 16)
                    g.dma_start(wub[par][:, d * 8 * 512:(d + 1) * 8 * 512].rearrange("p (a n) -> p a n", n=512),
                                wusrc[:, d * 8:(d + 1) * 8, ib * 512:(ib + 1) * 512]).then_inc(wld[par], 16)

        @blk.tensor
        def _(te):
            for ib in range(NIB):
                par = ib % 2
                te.wait_ge(wld[par], b[f"wld{par}"] + 16 * 4 * (ib // 2 + 1))
                for tp in range(NTP):
                    idx = ib * 8 + tp
                    if idx >= 1:
                        te.wait_ge(ev, V0 + 2 * (idx - 1) + 1)
                    if idx >= 1:
                        te.wait_ge(evs, E0 + (idx - 1) + 1)
                    ins = None
                    for kc in range(16):
                        ins = te.matmul(psG[idx % 3][:, :], xnf[kc][:, tp * 128:(tp + 1) * 128],
                                        wgb[par][:, kc * 512:(kc + 1) * 512],
                                        start=(kc == 0), stop=(kc == 15))
                    ins.then_inc(mm, 1)
                    for kc in range(16):
                        ins = te.matmul(psU[idx % 3][:, :], xnf[kc][:, tp * 128:(tp + 1) * 128],
                                        wub[par][:, kc * 512:(kc + 1) * 512],
                                        start=(kc == 0), stop=(kc == 15))
                    ins.then_inc(mm, 1)

        @blk.scalar
        def _(s):
            for idx in range(NIB * NTP):
                s.wait_ge(mm, M0 + 2 * idx + 1)
                if idx >= 3:
                    s.wait_ge(ev, V0 + 2 * (idx - 3) + 1)
                s.activation(sg[idx % 3][:, :], psG[idx % 3][:, :], AF.Silu).then_inc(evs, 1)

        @blk.vector
        def _(v):
            pairs = [(0, 1), (0, 2), (0, 3), (1, 2), (1, 3), (2, 3)]
            for idx in range(NIB * NTP):
                v.wait_ge(evs, E0 + idx + 1)
                v.wait_ge(mm, M0 + 2 * idx + 2)
                v.tensor_tensor(hh[idx % 3][:, :], sg[idx % 3][:, :], psU[idx % 3][:, :],
                                ALU.mult).then_inc(ev, 1)
                v.tensor_mul(sqv[idx % 3][:, :], hh[idx % 3][:, :], hh[idx % 3][:, :])
                if idx >= 3:
                    v.wait_ge(stot[idx % 3], b[f"sto{idx % 3}"] + 16 * (idx // 3))
                sq_ = sqv[idx % 3]
                for j, (a, bb_) in enumerate(pairs):
                    v.tensor_tensor(cb[:, j * 128:(j + 1) * 128], sq_[:, a::4], sq_[:, bb_::4], ALU.is_gt)
                h_ = hh[idx % 3]
                o_ = hspb[idx % 3]
                v.tensor_add(tb[:, 0:128], cb[:, 0:128], cb[:, 128:256])
                v.tensor_add(tb[:, 0:128], tb[:, 0:128], cb[:, 256:384])
                v.tensor_single_scalar(kp[:, :], tb[:, 0:128], 1.5, ALU.is_gt)
                v.tensor_tensor(o_[:, 0::4], h_[:, 0::4], kp[:, :], ALU.mult)
                v.tensor_add(tb[:, 0:128], cb[:, 384:512], cb[:, 512:640])
                v.tensor_sub(tb[:, 0:128], tb[:, 0:128], cb[:, 0:128])
                v.tensor_single_scalar(kp[:, :], tb[:, 0:128], 0.5, ALU.is_gt)
                v.tensor_tensor(o_[:, 1::4], h_[:, 1::4], kp[:, :], ALU.mult)
                v.tensor_sub(tb[:, 0:128], cb[:, 640:768], cb[:, 128:256])
                v.tensor_sub(tb[:, 0:128], tb[:, 0:128], cb[:, 384:512])
                v.tensor_single_scalar(kp[:, :], tb[:, 0:128], -0.5, ALU.is_gt)
                v.tensor_tensor(o_[:, 2::4], h_[:, 2::4], kp[:, :], ALU.mult)
                v.tensor_add(tb[:, 0:128], cb[:, 256:384], cb[:, 512:640])
                v.tensor_add(tb[:, 0:128], tb[:, 0:128], cb[:, 640:768])
                v.tensor_single_scalar(kp[:, :], tb[:, 0:128], 1.5, ALU.is_lt)
                v.tensor_tensor(o_[:, 3::4], h_[:, 3::4], kp[:, :], ALU.mult).then_inc(ev, 1)

        @blk.sync
        def _(sy):
            for idx in range(NIB * NTP):
                ib, tp = idx // 8, idx % 8
                sy.wait_ge(ev, V0 + 2 * idx + 2)
                sy.dma_start(io["hsptm"][tp * 128:(tp + 1) * 128, ib * 512:(ib + 1) * 512],
                             hspb[idx % 3][:, :]).then_inc(stot[idx % 3], 16)
            for r in range(3):
                nr = sum(1 for i2 in range(NIB * NTP) if i2 % 3 == r)
                sy.wait_ge(stot[r], b[f"sto{r}"] + 16 * nr)

    ctx.cnt["mm"] = M0 + 2 * NIB * NTP
    ctx.cnt["ev"] = V0 + 2 * NIB * NTP
    ctx.cnt["evs"] = E0 + NIB * NTP
    ctx.cnt["wld0"] = b["wld0"] + 16 * 4 * 8
    ctx.cnt["wld1"] = b["wld1"] + 16 * 4 * 8
    for r in range(3):
        ctx.cnt[f"sto{r}"] = b[f"sto{r}"] + 16 * sum(1 for i2 in range(NIB * NTP) if i2 % 3 == r)
    esA.close()

    for th in range(2):
        t0 = th * 512
        esB = contextlib.ExitStack()
        sbB = lambda n, s, d: esB.enter_context(nc.sbuf_tensor(n, s, d))
        hspf = [sbB(f"p6hf{th}_{k}", [128, 512], BF16) for k in range(64)]
        sT = [sbB(f"p6sT{th}_{k}", [128, 512], F32) for k in range(16)]
        xT = [sbB(f"p6xT{th}_{k}", [128, 512], F32) for k in range(16)]
        wb = [sbB(f"p6wb{th}_{i}", [128, 64 * 128], BF16) for i in range(2)]
        gsc = [sbB(f"p6gs{th}_{i}", [128, 512], F32) for i in range(2)]
        ob = [sbB(f"p6ob{th}_{i}", [128, 512], F32) for i in range(4)]
        ps = [esB.enter_context(nc.psum_tensor(f"p6ps{th}_{i}", [128, 512], F32)) for i in range(4)]
        a0 = ctx.cnt["ald"]
        with nc.Block() as blk:
            @blk.sync
            def _(sy):
                n = 0
                for k in range(64):
                    sy.dma_start(hspf[k][:, :], io["hsptm"][t0:t0 + 512, k * 128:(k + 1) * 128],
                                 transpose=True).then_inc(ctx.sem("ald"), 16); n += 1
                for k in range(16):
                    sy.dma_start(sT[k][:, :], io["sfm"][k * 128:(k + 1) * 128, t0:t0 + 512]).then_inc(ctx.sem("ald"), 16); n += 1
                    sy.dma_start(xT[k][:, :], io["xfm"][k * 128:(k + 1) * 128, t0:t0 + 512]).then_inc(ctx.sem("ald"), 16); n += 1
                sy.wait_ge(ctx.sem("ald"), a0 + 16 * n)
                ctx.cnt["ald"] = a0 + 16 * n

        def evs_(se, ji, ti, p, ot, stile):
            return se.activation(stile[:, :], p[:, :], AF.Silu)

        def evv_(ve, ji, ti, p, ot, stile):
            ve.tensor_tensor(ot[:, :], stile[:, :], sT[ji][:, :], ALU.mult)
            return ve.tensor_tensor(ot[:, :], ot[:, :], xT[ji][:, :], ALU.add)

        linear(ctx, dict(
            name=f"down{th}", wt=io["wd"], K=INTER, N=D, T=512, tw=512,
            x_tile=lambda kc: hspf[kc],
            psums=ps, wbufs=wb, obufs=ob, sbufs=gsc,
            evac_s=evs_, evac_v=evv_,
            out_ap=lambda ji, ti, t0=t0: io["x1fm"][ji * 128:(ji + 1) * 128, t0:t0 + 512]))
        esB.close()


def emit_p7(ctx):
    nc, io = ctx.nc, ctx.io
    es = contextlib.ExitStack()
    sb = lambda n, s, d: es.enter_context(nc.sbuf_tensor(n, s, d))
    xr = sb("p7xr", [128, 16 * 128], F32)
    xt = sb("p7xt", [128, D], F32)
    xc = sb("p7xc", [128, D], F32)
    sq = sb("p7sq", [128, D], F32)
    xnb = sb("p7xnb", [128, D], BF16)
    st_ = sb("p7st", [128, 8], F32)
    psT = es.enter_context(nc.psum_tensor("p7ps", [128, 128], F32))
    xsrc = io["x1fm"].rearrange("(a p) t -> p a t", p=128)

    steps = []
    def S(eng, fn):
        steps.append((eng, fn))
    for i in range(8):
        S("g", lambda e, i=i: e.dma_start(xr[:, :].rearrange("p (a t) -> p a t", t=128),
                                          xsrc[:, :, i * 128:(i + 1) * 128]))
        for kc in range(16):
            S("t", lambda e, kc=kc: e.transpose(psT[:, :], xr[:, kc * 128:(kc + 1) * 128], ctx.eye[:, :]))
            S("v", lambda e, kc=kc: e.tensor_copy(xt[:, kc * 128:(kc + 1) * 128], psT[:, :]))
        S("v", lambda e: e.reduce_sum(st_[:, 0:1], xt[:, :], axis=AX))
        S("v", lambda e: e.tensor_scalar(st_[:, 1:2], st_[:, 0:1], 1.0 / D, None, ALU.mult))
        S("v", lambda e: e.tensor_scalar(xc[:, :], xt[:, :], st_[:, 1:2], None, ALU.subtract))
        S("s", lambda e: e.activation(sq[:, :], xc[:, :], AF.Square))
        S("v", lambda e: e.reduce_sum(st_[:, 2:3], sq[:, :], axis=AX))
        S("s", lambda e: e.activation(st_[:, 3:4], st_[:, 2:3], AF.Ln, scale=1.0 / D, bias=EPS0))
        S("s", lambda e: e.activation(st_[:, 4:5], st_[:, 3:4], AF.Exp, scale=-0.5))
        S("v", lambda e: e.tensor_scalar(xnb[:, :], xc[:, :], st_[:, 4:5], None, ALU.mult))
        S("g", lambda e, i=i: e.dma_start(io["xn1tm"][i * 128:(i + 1) * 128, :], xnb[:, :]))
    aux = ctx.sem("aux")
    A0 = ctx.cnt["aux"]
    incs = [16 if en == "g" else 1 for en, _ in steps]
    tgts = [A0 + sum(incs[:i]) for i in range(len(steps))]
    with nc.Block() as blk:
        for eng_name, body in [("v", "vector"), ("s", "scalar"), ("t", "tensor"), ("g", "gpsimd")]:
            def mk(eng_name=eng_name):
                def emit(e):
                    for i, (en, fn) in enumerate(steps):
                        if en != eng_name:
                            continue
                        e.wait_ge(aux, tgts[i])
                        fn(e).then_inc(aux, incs[i])
                return emit
            getattr(blk, body)(mk())
        @blk.sync
        def _(sy):
            sy.wait_ge(aux, A0 + sum(incs))
    ctx.cnt["aux"] = A0 + sum(incs)
    es.close()



def emit_p8(ctx):
    nc, io = ctx.nc, ctx.io
    es = contextlib.ExitStack()
    xnf = [es.enter_context(nc.sbuf_tensor(f"p8x{k}", [128, T], BF16)) for k in range(16)]
    wb = [es.enter_context(nc.sbuf_tensor(f"p8w{i}", [128, 16 * 128], BF16)) for i in range(3)]
    ob = [es.enter_context(nc.sbuf_tensor(f"p8o{i}", [128, 512], BF16)) for i in range(4)]
    ps = [es.enter_context(nc.psum_tensor(f"p8p{i}", [128, 512], F32)) for i in range(4)]
    load_fm_tiles(ctx, io["xn1tm"], D, T, xnf)

    def out_ap(ji, ti):
        h, part = ji // 3, ji % 3
        dst = [io["qf"], io["kf"], io["vf"]][part]
        return dst[h * 128:(h + 1) * 128, ti * 512:(ti + 1) * 512]

    def ev(se, ji, ti, p, ot, stile):
        return se.activation(ot[:, :], p[:, :], AF.Silu)

    linear(ctx, dict(
        name="qkv", wt=io["wqkv"], K=D, N=48 * 128, T=T, tw=512,
        x_tile=lambda kc: xnf[kc],
        psums=ps, wbufs=wb, obufs=ob, evac_s=ev, out_ap=out_ap))
    es.close()


def emit_p9(ctx):
    nc, io = ctx.nc, ctx.io
    es = contextlib.ExitStack()
    sb = lambda n, s, d: es.enter_context(nc.sbuf_tensor(n, s, d))
    qf = [sb(f"p9q{k}", [128, T], BF16) for k in range(16)]
    kf = [sb(f"p9k{k}", [128, T], BF16) for k in range(16)]
    vf = [sb(f"p9v{k}", [128, T], BF16) for k in range(16)]
    qdech = sb("p9qd", [128, BLK], F32)
    kdech = sb("p9kd", [128, BLK], F32)
    ddech = sb("p9dd", [128, 2 * BLK], F32)
    kv = sb("p9kv", [128, 128], F32)
    kvt = sb("p9kvt", [128, 128], F32)
    kvb = sb("p9kvb", [128, 128], BF16)
    qd = sb("p9qdt", [128, BLK], BF16)
    qkm = [sb(f"p9qk{i}", [128, BLK], BF16) for i in range(2)]
    kd = sb("p9kdt", [128, BLK], BF16)
    vtm = [sb(f"p9vt{i}", [128, 128], BF16) for i in range(2)]
    ktm = [sb(f"p9kt{i}", [128, 128], BF16) for i in range(2)]
    yb = sb("p9yb", [128, BLK], BF16)
    psY = es.enter_context(nc.psum_tensor("p9pY", [128, BLK], F32))
    psK = [es.enter_context(nc.psum_tensor(f"p9pK{i}", [128, BLK], F32)) for i in range(2)]
    psKV = es.enter_context(nc.psum_tensor("p9pKV", [128, 128], F32))
    psT = es.enter_context(nc.psum_tensor("p9pT", [128, 128], BF16))

    a0 = ctx.cnt["ald"]
    with nc.Block() as blk:
        @blk.sync
        def _(sy):
            n = 0
            for k in range(16):
                sy.dma_start(qf[k][:, :], io["qf"][k * 128:(k + 1) * 128, :]).then_inc(ctx.sem("ald"), 16); n += 1
                sy.dma_start(kf[k][:, :], io["kf"][k * 128:(k + 1) * 128, :]).then_inc(ctx.sem("ald"), 16); n += 1
                sy.dma_start(vf[k][:, :], io["vf"][k * 128:(k + 1) * 128, :]).then_inc(ctx.sem("ald"), 16); n += 1
            sy.wait_ge(ctx.sem("ald"), a0 + 16 * n)
            ctx.cnt["ald"] = a0 + 16 * n

    ds = DepSteps(ctx)
    S = ds.S
    P = 2
    qdechs = [qdech, kdech]
    kdechs = [sb(f"p9kd2{i}", [128, BLK], F32) for i in range(P)]
    ddechs = [ddech] + [sb("p9dd2", [128, 2 * BLK], F32)]
    kvs_ = [kv, sb("p9kv2", [128, 128], F32)]
    kvts = [kvt, sb("p9kvt2", [128, 128], F32)]
    kvbs = [kvb, sb("p9kvb2", [128, 128], BF16)]
    qds = [qd, sb("p9qd2", [128, BLK], BF16)]
    kds = [kd, sb("p9kd3", [128, BLK], BF16)]
    qkms = [qkm, [sb(f"p9qk2{i}", [128, BLK], BF16) for i in range(2)]]
    vtms = [vtm, [sb(f"p9vt2{i}", [128, 128], BF16) for i in range(2)]]
    ktms = [ktm, [sb(f"p9kt2{i}", [128, 128], BF16) for i in range(2)]]
    ybs = [yb, sb("p9yb2", [128, BLK], BF16)]
    psYs = [psY, es.enter_context(nc.psum_tensor("p9pY2", [128, BLK], F32))]
    psKVs = [psKV, es.enter_context(nc.psum_tensor("p9pKV2", [128, 128], F32))]
    last = {}
    for h in range(NH1):
        p_ = h % 2
        bd = float(BDECAY[h])
        ql, kl, dl = qdechs[p_], kdechs[p_], ddechs[p_]
        i_ql = S("g", lambda e, h=h, ql=ql: e.dma_start(ql[:, :], io["qdec"][h * 128:(h + 1) * 128, :]),
                 deps=[last.get(("qluse", p_))])
        i_kl = S("g", lambda e, h=h, kl=kl: e.dma_start(kl[:, :], io["kdec"][h * 128:(h + 1) * 128, :]),
                 deps=[last.get(("kluse", p_))])
        i_dl = S("g", lambda e, h=h, dl=dl: e.dma_start(dl[:, :].rearrange("p (a t) -> p a t", t=BLK),
                 io["ddecT"].rearrange("(a p) t -> p a t", p=128)[:, 2 * h:2 * h + 2, :]),
                 deps=[last.get(("dluse", p_))])
        i_z1 = S("v", lambda e, p_=p_: e.memset(kvs_[p_][:, :], 0.0), deps=[last.get(("kvuse", p_))])
        i_z2 = S("v", lambda e, p_=p_: e.memset(kvbs[p_][:, :], 0.0), deps=[last.get(("kvbuse", p_))])
        for b in range(NBLK):
            t0 = b * BLK
            i_qd = S("v", lambda e, h=h, t0=t0, p_=p_: e.tensor_tensor(
                qds[p_][:, :], qf[h][:, t0:t0 + BLK], qdechs[p_][:, :], ALU.mult),
                deps=[i_ql, last.get(("qduse", p_))])
            qk_is = []
            vt_is = []
            for sh_ in range(2):
                i_qk = S("t", lambda e, h=h, t0=t0, sh_=sh_: e.matmul(
                    psK[sh_][:, :], kf[h][:, t0 + sh_ * 128:t0 + sh_ * 128 + 128],
                    qf[h][:, t0:t0 + BLK], start=True, stop=True),
                    deps=[last.get(("psK", sh_))])
                i_qkm = S("v", lambda e, sh_=sh_, p_=p_: e.tensor_tensor(
                    qkms[p_][sh_][:, :], psK[sh_][:, :],
                    ddechs[p_][:, sh_ * BLK:(sh_ + 1) * BLK], ALU.mult),
                    deps=[i_qk, i_dl, last.get(("qkmuse", p_, sh_))])
                last[("psK", sh_)] = i_qkm
                i_vt = S("t", lambda e, h=h, t0=t0, sh_=sh_: e.transpose(
                    psT[:, :], vf[h][:, t0 + sh_ * 128:t0 + sh_ * 128 + 128], ctx.eyeb[:, :]),
                    deps=[last.get("psT")])
                i_vte = S("v", lambda e, sh_=sh_, p_=p_: e.tensor_copy(vtms[p_][sh_][:, :], psT[:, :]),
                          deps=[i_vt, last.get(("vtmuse", p_, sh_))])
                last["psT"] = i_vte
                qk_is.append(i_qkm)
                vt_is.append(i_vte)
            i_y0 = S("t", lambda e, p_=p_: e.matmul(psYs[p_][:, :], kvbs[p_][:, :], qds[p_][:, :],
                                                    start=True, stop=False),
                     deps=[i_qd, i_z2 if b == 0 else last[("kvb", p_)], last.get(("psY", p_))])
            last[("qduse", p_)] = i_y0
            i_y1 = S("t", lambda e, p_=p_: e.matmul(psYs[p_][:, :], vtms[p_][0][:, :], qkms[p_][0][:, :],
                                                    start=False, stop=False),
                     deps=[qk_is[0], vt_is[0]])
            i_y2 = S("t", lambda e, p_=p_: e.matmul(psYs[p_][:, :], vtms[p_][1][:, :], qkms[p_][1][:, :],
                                                    start=False, stop=True),
                     deps=[qk_is[1], vt_is[1]])
            last[("qkmuse", p_, 0)] = i_y1
            last[("qkmuse", p_, 1)] = i_y2
            i_ye = S("v", lambda e, p_=p_: e.tensor_copy(ybs[p_][:, :], psYs[p_][:, :]),
                     deps=[i_y2, last.get(("ybuse", p_))])
            last[("psY", p_)] = i_ye
            i_yst = S("g", lambda e, h=h, t0=t0, p_=p_: e.dma_start(
                io["y1loc"][h * 128:(h + 1) * 128, t0:t0 + BLK], ybs[p_][:, :]), deps=[i_ye])
            last[("ybuse", p_)] = i_yst
            i_kd = S("v", lambda e, h=h, t0=t0, p_=p_: e.tensor_tensor(
                kds[p_][:, :], kf[h][:, t0:t0 + BLK], kdechs[p_][:, :], ALU.mult),
                deps=[i_kl, last.get(("kduse", p_))])
            kt_is = []
            for sh_ in range(2):
                i_kt = S("t", lambda e, sh_=sh_, p_=p_: e.transpose(
                    psT[:, :], kds[p_][:, sh_ * 128:(sh_ + 1) * 128], ctx.eyeb[:, :]),
                    deps=[i_kd, last.get("psT")])
                i_kte = S("v", lambda e, sh_=sh_, p_=p_: e.tensor_copy(ktms[p_][sh_][:, :], psT[:, :]),
                          deps=[i_kt, last.get(("ktmuse", p_, sh_))])
                last["psT"] = i_kte
                kt_is.append(i_kte)
            last[("kduse", p_)] = kt_is[1]
            i_kv1 = S("t", lambda e, p_=p_: e.matmul(psKVs[p_][:, :], ktms[p_][0][:, :], vtms[p_][0][:, :],
                                                     start=True, stop=False),
                      deps=[kt_is[0], vt_is[0], last.get(("psKV", p_))])
            i_kv2 = S("t", lambda e, p_=p_: e.matmul(psKVs[p_][:, :], ktms[p_][1][:, :], vtms[p_][1][:, :],
                                                     start=False, stop=True),
                      deps=[kt_is[1], vt_is[1]])
            last[("ktmuse", p_, 0)] = i_kv1
            last[("ktmuse", p_, 1)] = i_kv2
            last[("vtmuse", p_, 0)] = i_kv1
            last[("vtmuse", p_, 1)] = i_kv2
            i_ts = S("v", lambda e, bd=bd, p_=p_: e.tensor_scalar(kvts[p_][:, :], kvs_[p_][:, :],
                                                                  bd, None, ALU.mult),
                     deps=[i_z1 if b == 0 else last[("kv", p_)]])
            i_tt = S("v", lambda e, p_=p_: e.tensor_tensor(kvs_[p_][:, :], kvts[p_][:, :],
                                                           psKVs[p_][:, :], ALU.add),
                     deps=[i_kv2])
            last[("psKV", p_)] = i_tt
            last[("kv", p_)] = i_tt
            i_kb = S("v", lambda e, p_=p_: e.tensor_copy(kvbs[p_][:, :], kvs_[p_][:, :]), deps=[i_tt])
            last[("kvb", p_)] = i_kb
        i_fin = S("g", lambda e, h=h, p_=p_: e.dma_start(io["kvfin"][h * 128:(h + 1) * 128, :],
                                                         kvs_[p_][:, :]), deps=[last[("kv", p_)]])
        last[("kvuse", p_)] = i_fin
        last[("kvbuse", p_)] = last[("kvb", p_)]
        last[("qluse", p_)] = last[("qduse", p_)]
        last[("kluse", p_)] = last[("kduse", p_)]
        last[("dluse", p_)] = last[("qkmuse", p_, 1)]
    ds.emit()
    es.close()



def emit_p10(ctx):
    nc, io = ctx.nc, ctx.io
    es = contextlib.ExitStack()
    sb = lambda n, s, d: es.enter_context(nc.sbuf_tensor(n, s, d))
    qf = [sb(f"pAq{k}", [128, T], BF16) for k in range(16)]
    y1l = sb("pAy", [128, T], BF16)
    accs = sb("pAacc", [128, 16 * 128], F32)
    kvk = sb("pAkvk", [128, 16 * 128], F32)
    l1w = sb("pAl1w", [128, NH1 * 8], F32)
    emk = sb("pAem", [128, 8], F32)
    qdech = sb("pAqd", [128, BLK], F32)
    kvs = sb("pAkvs", [128, 128], F32)
    kvsb = sb("pAkvsb", [128, 128], BF16)
    qd = sb("pAqdt", [128, BLK], BF16)
    ot = sb("pAot", [128, BLK], F32)
    sqo = sb("pAsq", [128, BLK], F32)
    ssq1 = sb("pAssq", [1, T], F32)
    obuf = [sb(f"pAo{k}", [128, T], BF16) for k in range(16)]
    esP = contextlib.ExitStack()
    psY = esP.enter_context(nc.psum_tensor("pApY", [128, BLK], F32))
    psQ = esP.enter_context(nc.psum_tensor("pApQ", [1, BLK], F32))

    a0 = ctx.cnt["ald"]
    with nc.Block() as blk:
        @blk.sync
        def _(sy):
            n = 0
            for k in range(16):
                sy.dma_start(qf[k][:, :], io["qf"][k * 128:(k + 1) * 128, :]).then_inc(ctx.sem("ald"), 16); n += 1
            sy.dma_start(l1w[:, :], io["l1w"][:, :]).then_inc(ctx.sem("ald"), 16); n += 1
            sy.dma_start(emk[:, :], io["emask128"][:, :]).then_inc(ctx.sem("ald"), 16); n += 1
            sy.wait_ge(ctx.sem("ald"), a0 + 16 * n)
            ctx.cnt["ald"] = a0 + 16 * n
        @blk.vector
        def _(v):
            v.memset(accs[:, :], 0.0)
            v.memset(ssq1[0:1, :], 0.0)

    ds = DepSteps(ctx)
    S = ds.S
    last = {}
    for k in range(7):
        i_ld = S("g", lambda e, k=k: e.dma_start(
            kvk[:, :].rearrange("p (a t) -> p a t", t=128),
            io["kvgath"].rearrange("(a p) t -> p a t", p=128)[:, k * 16:(k + 1) * 16, :]),
            deps=[last.get(("kvkuse", h2)) for h2 in range(NH1)])
        for h in range(NH1):
            i_m = S("v", lambda e, h=h, k=k: e.tensor_scalar(
                accs[:, h * 128:(h + 1) * 128], accs[:, h * 128:(h + 1) * 128],
                l1w[:, h * 8 + k:h * 8 + k + 1], None, ALU.mult))
            i_g = S("v", lambda e, h=h, k=k: e.tensor_scalar(
                kvk[:, h * 128:(h + 1) * 128], kvk[:, h * 128:(h + 1) * 128],
                emk[:, k:k + 1], None, ALU.mult), deps=[i_ld])
            i_a = S("v", lambda e, h=h: e.tensor_add(
                accs[:, h * 128:(h + 1) * 128], accs[:, h * 128:(h + 1) * 128],
                kvk[:, h * 128:(h + 1) * 128]))
            last[("kvkuse", h)] = i_a
    qdechs = [qdech, sb("pAqd2", [128, BLK], F32)]
    kvss = [kvs, sb("pAkvs2", [128, 128], F32)]
    kvsbs = [kvsb, sb("pAkvsb2", [128, 128], BF16)]
    qds = [qd, sb("pAqdt2", [128, BLK], BF16)]
    ots = [ot, sb("pAot2", [128, BLK], F32)]
    sqos = [sqo, sb("pAsq2", [128, BLK], F32)]
    y1ls = [y1l, sb("pAy2", [128, T], BF16)]
    psYs = [psY, esP.enter_context(nc.psum_tensor("pApY2", [128, BLK], F32))]
    psQs = [psQ, esP.enter_context(nc.psum_tensor("pApQ2", [1, BLK], F32))]
    i_ssqlast = None
    for h in range(NH1):
        p_ = h % 2
        i_ql = S("g", lambda e, h=h, p_=p_: e.dma_start(qdechs[p_][:, :], io["qdec"][h * 128:(h + 1) * 128, :]),
                 deps=[last.get(("qluse", p_))])
        i_yl = S("g", lambda e, h=h, p_=p_: e.dma_start(y1ls[p_][:, :], io["y1loc"][h * 128:(h + 1) * 128, :]),
                 deps=[last.get(("yluse", p_))])
        for b in range(NBLK):
            t0 = b * BLK
            bdb = float(BDECAY[h] ** b)
            i_kvs = S("v", lambda e, h=h, bdb=bdb, p_=p_: e.tensor_scalar(
                kvss[p_][:, :], accs[:, h * 128:(h + 1) * 128], bdb, None, ALU.mult),
                deps=[last[("kvkuse", h)], last.get(("kvsuse", p_))])
            i_kvb = S("v", lambda e, p_=p_: e.tensor_copy(kvsbs[p_][:, :], kvss[p_][:, :]),
                      deps=[i_kvs, last.get(("kvsbuse", p_))])
            i_qd = S("v", lambda e, h=h, t0=t0, p_=p_: e.tensor_tensor(
                qds[p_][:, :], qf[h][:, t0:t0 + BLK], qdechs[p_][:, :], ALU.mult),
                deps=[i_ql, last.get(("qduse", p_))])
            i_mm = S("t", lambda e, p_=p_: e.matmul(psYs[p_][:, :], kvsbs[p_][:, :], qds[p_][:, :],
                                                    start=True, stop=True),
                     deps=[i_kvb, i_qd, last.get(("psY", p_))])
            last[("kvsuse", p_)] = i_mm
            last[("kvsbuse", p_)] = i_mm
            last[("qduse", p_)] = i_mm
            i_ot = S("v", lambda e, t0=t0, p_=p_: e.tensor_tensor(
                ots[p_][:, :], psYs[p_][:, :], y1ls[p_][:, t0:t0 + BLK], ALU.add),
                deps=[i_mm, i_yl, last.get(("otuse", p_))])
            last[("psY", p_)] = i_ot
            i_sq = S("v", lambda e, p_=p_: e.tensor_mul(sqos[p_][:, :], ots[p_][:, :], ots[p_][:, :]),
                     deps=[last.get(("squse", p_))])
            i_mq = S("t", lambda e, p_=p_: e.matmul(psQs[p_][:, :], ctx.ones128[:, 0:1], sqos[p_][:, :],
                                                    start=True, stop=True),
                     deps=[i_sq, last.get(("psQ", p_))])
            last[("squse", p_)] = i_mq
            i_sa = S("v", lambda e, t0=t0, p_=p_: e.tensor_tensor(
                ssq1[0:1, t0:t0 + BLK], ssq1[0:1, t0:t0 + BLK], psQs[p_][0:1, :], ALU.add),
                deps=[i_mq, i_ssqlast])
            i_ssqlast = i_sa
            last[("psQ", p_)] = i_sa
            i_oc = S("v", lambda e, h=h, t0=t0, p_=p_: e.tensor_copy(obuf[h][:, t0:t0 + BLK], ots[p_][:, :]))
            last[("otuse", p_)] = i_oc
        last[("qluse", p_)] = last[("qduse", p_)]
        last[("yluse", p_)] = last[("otuse", p_)]
    ds.emit()
    esP.close()

    rrow = sb("pArr", [1, T], F32)
    rb1 = sb("pArb", [128, T], F32)
    nw1 = sb("pAnw", [128, 16], F32)
    psB = es.enter_context(nc.psum_tensor("pApB", [128, 512], F32))
    a0 = ctx.cnt["ald"]
    with nc.Block() as blk:
        @blk.sync
        def _(sy):
            sy.dma_start(nw1[:, :], io["nw1cols"][:, :]).then_inc(ctx.sem("ald"), 16)
            sy.wait_ge(ctx.sem("ald"), a0 + 16)
            ctx.cnt["ald"] = a0 + 16
    steps2 = []
    def S2(eng, fn):
        steps2.append((eng, fn))
    S2("s", lambda e: e.activation(rrow[0:1, :], ssq1[0:1, :], AF.Ln, scale=1.0 / D, bias=EPS1))
    S2("s", lambda e: e.activation(rrow[0:1, :], rrow[0:1, :], AF.Exp, scale=-0.5))
    for j in range(2):
        S2("t", lambda e, j=j: e.matmul(psB[:, :], ctx.onec[0:1, :], rrow[0:1, j * 512:(j + 1) * 512],
                                        start=True, stop=True))
        S2("v", lambda e, j=j: e.tensor_copy(rb1[:, j * 512:(j + 1) * 512], psB[:, :]))
    aux = ctx.sem("aux")
    A0 = ctx.cnt["aux"]
    tgts = [A0 + i for i in range(len(steps2))]
    with nc.Block() as blk:
        for eng_name, body in [("v", "vector"), ("s", "scalar"), ("t", "tensor")]:
            def mk(eng_name=eng_name):
                def emit(e):
                    for i, (en, fn) in enumerate(steps2):
                        if en != eng_name:
                            continue
                        e.wait_ge(aux, tgts[i])
                        fn(e).then_inc(aux, 1)
                return emit
            getattr(blk, body)(mk())
    ctx.cnt["aux"] = A0 + len(steps2)

    esD = contextlib.ExitStack()
    xnf = [esD.enter_context(nc.sbuf_tensor(f"pAx{k}", [128, T], BF16)) for k in range(16)]
    wb = [esD.enter_context(nc.sbuf_tensor(f"pAw{i}", [128, 16 * 128], BF16)) for i in range(3)]
    sgt = [esD.enter_context(nc.sbuf_tensor(f"pAsg{i}", [128, 512], F32)) for i in range(2)]
    ofb = [esD.enter_context(nc.sbuf_tensor(f"pAof{k}", [128, T], BF16)) for k in range(16)]
    ob = [esD.enter_context(nc.sbuf_tensor(f"pAob{i}", [128, 512], F32)) for i in range(4)]
    ps = [esD.enter_context(nc.psum_tensor(f"pAp{i}", [128, 512], F32)) for i in range(4)]
    load_fm_tiles(ctx, io["xn1tm"], D, T, xnf)

    def evs_(se, ji, ti, p, ot_, stile):
        return se.activation(stile[:, :], p[:, :], AF.Sigmoid)

    def evv_(ve, ji, ti, p, ot_, stile):
        t0 = ti * 512
        ve.scalar_tensor_tensor(ofb[ji][:, t0:t0 + 512], obuf[ji][:, t0:t0 + 512],
                                nw1[:, ji:ji + 1], stile[:, :], ALU.mult, ALU.mult)
        return ve.tensor_tensor(ofb[ji][:, t0:t0 + 512], ofb[ji][:, t0:t0 + 512],
                                rb1[:, t0:t0 + 512], ALU.mult)

    linear(ctx, dict(
        name="wg1", wt=io["wgt"], K=D, N=D, T=T, tw=512,
        x_tile=lambda kc: xnf[kc],
        psums=ps, wbufs=wb, sbufs=sgt, evac_s=evs_, evac_v=evv_, out_ap=None))

    def evo_(ve, ji, ti, p, ot_, stile):
        return ve.tensor_copy(ot_[:, :], p[:, :])

    linear(ctx, dict(
        name="wo1", wt=io["wo1"], K=D, N=D, T=T, tw=512,
        x_tile=lambda kc: ofb[kc],
        psums=ps, wbufs=wb, obufs=ob, evac_v=evo_,
        out_ap=lambda ji, ti: io["s1fm"][ji * 128:(ji + 1) * 128, ti * 512:(ti + 1) * 512]))
    esD.close()
    es.close()


W2ROWS = HID // 8

def emit_qa(ctx):
    nc, io = ctx.nc, ctx.io
    es = contextlib.ExitStack()
    sb = lambda n, s, d: es.enter_context(nc.sbuf_tensor(n, s, d))
    wbuf = sb("qaw", [128, HID], F32)
    cs = sb("qacs", [128, 1], F32)
    row = sb("qarow", [1, 4], F32)
    ps1 = es.enter_context(nc.psum_tensor("qap", [1, 1], F32))
    steps = []
    def S(eng, fn):
        steps.append((eng, fn))
    jobs = [("w1s", [(0, 128), (128, 128)], 0),
            ("w3s", [(0, 128), (128, 128)], 1),
            ("w2s", [(128 * j, 128) for j in range(5)] + [(640, 64)], 2)]
    for name, slices, col in jobs:
        width = HID if name != "w2s" else D
        nsl = len(slices)
        for si, (r0, nr) in enumerate(slices):
            S("g", lambda e, name=name, r0=r0, nr=nr, width=width: e.dma_start(
                wbuf[0:nr, 0:width], io[name][r0:r0 + nr, 0:width]))
            S("v", lambda e, nr=nr, width=width: e.reduce_sum(
                cs[0:nr, 0:1], wbuf[0:nr, 0:width], axis=AX, apply_absolute_value=True))
            S("t", lambda e, nr=nr, si=si, nsl=nsl: e.matmul(
                ps1[0:1, 0:1], cs[0:nr, 0:1], ctx.ones128[0:nr, 0:1],
                start=(si == 0), stop=(si == nsl - 1)))
        S("v", lambda e, col=col: e.tensor_copy(row[0:1, col:col + 1], ps1[0:1, 0:1]))
    S("g", lambda e: e.dma_start(io["absq"][0:1, :], row[0:1, :]))
    aux = ctx.sem("aux")
    A0 = ctx.cnt["aux"]
    incs = [16 if en == "g" else 1 for en, _ in steps]
    tgts = [A0 + sum(incs[:i]) for i in range(len(steps))]
    with nc.Block() as blk:
        for eng_name, body in [("v", "vector"), ("t", "tensor"), ("g", "gpsimd")]:
            def mk(eng_name=eng_name):
                def emit(e):
                    for i, (en, fn) in enumerate(steps):
                        if en != eng_name:
                            continue
                        e.wait_ge(aux, tgts[i])
                        fn(e).then_inc(aux, incs[i])
                return emit
            getattr(blk, body)(mk())
        @blk.sync
        def _(sy):
            sy.wait_ge(aux, A0 + sum(incs))
    ctx.cnt["aux"] = A0 + sum(incs)
    es.close()


def emit_qb(ctx):
    nc, io = ctx.nc, ctx.io
    es = contextlib.ExitStack()
    sb = lambda n, s, d: es.enter_context(nc.sbuf_tensor(n, s, d))
    flat = sb("qbf", [1, 32], F32)
    acc = sb("qbacc", [1, 4], F32)
    scr = sb("qbsc", [1, 4], F32)
    thbc = sb("qbth", [128, 4], F32)
    wbuf = sb("qbw", [128, HID], F32)
    sg = sb("qbsg", [128, HID], F32)
    ab = sb("qbab", [128, HID], F32)
    tq = sb("qbtq", [128, HID], BF16)
    psB = es.enter_context(nc.psum_tensor("qbp", [128, 4], F32))
    NORM = 1.0 / (HID0 * D)
    steps = []
    def S(eng, fn):
        steps.append((eng, fn))
    S("g", lambda e: e.dma_start(flat[0:1, :], io["absg"].rearrange("c v -> (c v)").rearrange("(a f) -> a f", a=1)))
    S("v", lambda e: e.tensor_copy(acc[0:1, :], flat[0:1, 0:4]))
    for k in range(1, 8):
        S("v", lambda e, k=k: e.tensor_add(acc[0:1, :], acc[0:1, :], flat[0:1, 4 * k:4 * k + 4]))
    S("v", lambda e: e.tensor_scalar(scr[0:1, :], acc[0:1, :], NORM, None, ALU.mult))
    S("g", lambda e: e.dma_start(io["sc"][0:1, :], scr[0:1, :]))
    S("t", lambda e: e.matmul(psB[:, :], ctx.onec[0:1, :], scr[0:1, :], start=True, stop=True))
    S("v", lambda e: e.tensor_scalar(thbc[:, :], psB[:, :], 0.5, None, ALU.mult))
    jobs = [("w1s", "tq1s", [(128 * j, 128, 128 * j) for j in range(2)], 0),
            ("w3s", "tq3s", [(128 * j, 128, 128 * j) for j in range(2)], 1),
            ("w2s", "tq2s", [(128 * j, 128 if j < 5 else 64, 128 * j) for j in range(6)], 2)]
    for name, oname, slices, col in jobs:
        width = HID if name != "w2s" else D
        for (r0, nr, o0) in slices:
            S("g", lambda e, name=name, r0=r0, nr=nr, width=width: e.dma_start(
                wbuf[0:nr, 0:width], io[name][r0:r0 + nr, 0:width]))
            S("s", lambda e, nr=nr, width=width: e.activation(sg[0:nr, 0:width], wbuf[0:nr, 0:width], AF.Sign))
            S("s", lambda e, nr=nr, width=width: e.activation(ab[0:nr, 0:width], wbuf[0:nr, 0:width], AF.Abs))
            S("v", lambda e, nr=nr, width=width, col=col: e.tensor_scalar(
                ab[0:nr, 0:width], ab[0:nr, 0:width], thbc[0:nr, col:col + 1], None, ALU.is_gt))
            S("v", lambda e, nr=nr, width=width: e.tensor_tensor(
                tq[0:nr, 0:width], sg[0:nr, 0:width], ab[0:nr, 0:width], ALU.mult))
            S("g", lambda e, oname=oname, o0=o0, nr=nr, width=width: e.dma_start(
                io[oname][o0:o0 + nr, 0:width], tq[0:nr, 0:width]))
    aux = ctx.sem("aux")
    A0 = ctx.cnt["aux"]
    incs = [16 if en == "g" else 1 for en, _ in steps]
    tgts = [A0 + sum(incs[:i]) for i in range(len(steps))]
    with nc.Block() as blk:
        for eng_name, body in [("v", "vector"), ("s", "scalar"), ("t", "tensor"), ("g", "gpsimd")]:
            def mk(eng_name=eng_name):
                def emit(e):
                    for i, (en, fn) in enumerate(steps):
                        if en != eng_name:
                            continue
                        e.wait_ge(aux, tgts[i])
                        fn(e).then_inc(aux, incs[i])
                return emit
            getattr(blk, body)(mk())
        @blk.sync
        def _(sy):
            sy.wait_ge(aux, A0 + sum(incs))
    ctx.cnt["aux"] = A0 + sum(incs)
    es.close()



def emit_p12(ctx):
    nc, io = ctx.nc, ctx.io
    es0 = contextlib.ExitStack()
    sb0 = lambda n, s, d: es0.enter_context(nc.sbuf_tensor(n, s, d))
    scb = sb0("pCsc", [128, 4], F32)
    scrow = sb0("pCsr", [1, 4], F32)
    psB = es0.enter_context(nc.psum_tensor("pCpB", [128, 4], F32))
    a0 = ctx.cnt["ald"]
    with nc.Block() as blk:
        @blk.sync
        def _(sy):
            sy.dma_start(scrow[0:1, :], io["sc"][0:1, :]).then_inc(ctx.sem("ald"), 16)
            sy.wait_ge(ctx.sem("ald"), a0 + 16)
            ctx.cnt["ald"] = a0 + 16
    steps = []
    def S(eng, fn):
        steps.append((eng, fn))
    S("t", lambda e: e.matmul(psB[:, :], ctx.onec[0:1, :], scrow[0:1, :], start=True, stop=True))
    S("v", lambda e: e.tensor_copy(scb[:, :], psB[:, :]))
    aux = ctx.sem("aux")
    A0 = ctx.cnt["aux"]
    tgts = [A0 + i for i in range(len(steps))]
    with nc.Block() as blk:
        for eng_name, body in [("v", "vector"), ("t", "tensor")]:
            def mk(eng_name=eng_name):
                def emit(e):
                    for i, (en, fn) in enumerate(steps):
                        if en != eng_name:
                            continue
                        e.wait_ge(aux, tgts[i])
                        fn(e).then_inc(aux, 1)
                return emit
            getattr(blk, body)(mk())
    ctx.cnt["aux"] = A0 + len(steps)

    for th in range(2):
        t0 = th * 512
        esT = contextlib.ExitStack()
        sbT = lambda n, s, d: esT.enter_context(nc.sbuf_tensor(n, s, d))
        g1 = [sbT(f"pCg{th}_{k}", [128, 512], BF16) for k in range(44)]
        xnf = [sbT(f"pCxn{th}_{k}", [128, 512], BF16) for k in range(16)]
        s1T = [sbT(f"pCs1{th}_{k}", [128, 512], F32) for k in range(16)]
        x1T = [sbT(f"pCx1{th}_{k}", [128, 512], F32) for k in range(16)]
        wb = [sbT(f"pCwb{th}_{i}", [128, 44 * 128], BF16) for i in range(3)]
        gsc = [sbT(f"pCgs{th}_{i}", [128, 512], F32) for i in range(2)]
        ps = [esT.enter_context(nc.psum_tensor(f"pCp{th}_{i}", [128, 512], F32)) for i in range(4)]
        a0 = ctx.cnt["ald"]
        with nc.Block() as blk:
            @blk.sync
            def _(sy):
                n = 0
                for k in range(16):
                    sy.dma_start(xnf[k][:, :], io["xn1tm"][t0:t0 + 512, k * 128:(k + 1) * 128],
                                 transpose=True).then_inc(ctx.sem("ald"), 16); n += 1
                    sy.dma_start(s1T[k][:, :], io["s1fm"][k * 128:(k + 1) * 128, t0:t0 + 512]).then_inc(ctx.sem("ald"), 16); n += 1
                    sy.dma_start(x1T[k][:, :], io["x1fm"][k * 128:(k + 1) * 128, t0:t0 + 512]).then_inc(ctx.sem("ald"), 16); n += 1
                sy.wait_ge(ctx.sem("ald"), a0 + 16 * n)
                ctx.cnt["ald"] = a0 + 16 * n
        def ev3(ve, ji, ti, p, ot_, stile):
            return ve.tensor_scalar(g1[ji][:, :], p[:, :], scb[:, 1:2], None, ALU.mult)

        linear(ctx, dict(
            name=f"h3_{th}", wt=io["tq3"], K=D, N=HID, T=512, tw=512,
            x_tile=lambda kc: xnf[kc],
            psums=ps, wbufs=wb, evac_v=ev3, out_ap=None))

        def ev1s(se, ji, ti, p, ot_, stile):
            return se.activation(stile[:, :], p[:, :], AF.Silu, scale=scb[:, 0:1])

        def ev1v(ve, ji, ti, p, ot_, stile):
            return ve.tensor_tensor(g1[ji][:, :], g1[ji][:, :], stile[:, :], ALU.mult)

        linear(ctx, dict(
            name=f"h1_{th}", wt=io["tq1"], K=D, N=HID, T=512, tw=512,
            x_tile=lambda kc: xnf[kc],
            psums=ps, wbufs=wb, sbufs=gsc, evac_s=ev1s, evac_v=ev1v, out_ap=None))

        wb2 = [esT.enter_context(nc.sbuf_tensor(f"pCw2{th}_{i}", [128, 44 * 128], BF16)) for i in range(2)]
        ob = [esT.enter_context(nc.sbuf_tensor(f"pCob{th}_{i}", [128, 512], F32)) for i in range(4)]

        def evg(se, ji, ti, p, ot_, stile):
            return se.activation(stile[:, :], p[:, :], AF.Silu, scale=scb[:, 2:3])

        def evx(ve, ji, ti, p, ot_, stile):
            ve.tensor_tensor(ot_[:, :], stile[:, :], s1T[ji][:, :], ALU.mult)
            return ve.tensor_tensor(ot_[:, :], ot_[:, :], x1T[ji][:, :], ALU.add)

        linear(ctx, dict(
            name=f"w2_{th}", wt=io["tq2"], K=HID, N=D, T=512, tw=512,
            x_tile=lambda kc: g1[kc],
            psums=ps, wbufs=wb2, sbufs=gsc, obufs=ob, evac_s=evg, evac_v=evx,
            out_ap=lambda ji, ti, t0=t0: io["x2fm"][ji * 128:(ji + 1) * 128, t0:t0 + 512]))
        esT.close()

    esF = contextlib.ExitStack()
    xr = esF.enter_context(nc.sbuf_tensor("pCxr", [128, 16 * 128], F32))
    otile = esF.enter_context(nc.sbuf_tensor("pCot", [128, D], F32))
    psT = esF.enter_context(nc.psum_tensor("pCpT", [128, 128], F32))
    x2src = io["x2fm"].rearrange("(a p) t -> p a t", p=128)
    steps = []
    for i in range(8):
        S("g", lambda e, i=i: e.dma_start(xr[:, :].rearrange("p (a t) -> p a t", t=128),
                                          x2src[:, :, i * 128:(i + 1) * 128]))
        for kc in range(16):
            S("t", lambda e, kc=kc: e.transpose(psT[:, :], xr[:, kc * 128:(kc + 1) * 128],
                                                ctx.eye[:, :]))
            S("v", lambda e, kc=kc: e.tensor_copy(otile[:, kc * 128:(kc + 1) * 128], psT[:, :]))
        S("g", lambda e, i=i: e.dma_start(io["ys"][i * 128:(i + 1) * 128, :], otile[:, :]))
    A0 = ctx.cnt["aux"]
    incs = [16 if en == "g" else 1 for en, _ in steps]
    tgts = [A0 + sum(incs[:i]) for i in range(len(steps))]
    with nc.Block() as blk:
        for eng_name, body in [("v", "vector"), ("t", "tensor"), ("g", "gpsimd")]:
            def mk(eng_name=eng_name):
                def emit(e):
                    for i, (en, fn) in enumerate(steps):
                        if en != eng_name:
                            continue
                        e.wait_ge(aux, tgts[i])
                        fn(e).then_inc(aux, incs[i])
                return emit
            getattr(blk, body)(mk())
        @blk.sync
        def _(sy):
            sy.wait_ge(aux, A0 + sum(incs))
    ctx.cnt["aux"] = A0 + sum(incs)
    esF.close()
    es0.close()


class DepSteps:

    ENGS = [("v", "vector"), ("s", "scalar"), ("t", "tensor"), ("g", "gpsimd"), ("p", "gpsimd")]

    def __init__(self, ctx):
        self.ctx = ctx
        self.steps = []

    def S(self, eng, fn, deps=()):
        self.steps.append((eng, fn, [d for d in deps if d is not None]))
        return len(self.steps) - 1

    def emit(self):
        ctx = self.ctx
        nc = ctx.nc
        sems = {e: ctx.sem("x" + e) for e, _ in self.ENGS}
        base = {e: ctx.cnt["x" + e] for e, _ in self.ENGS}
        cum = {}
        run = dict(base)
        for i, (en, fn, deps) in enumerate(self.steps):
            run[en] += 16 if en == "g" else 1
            cum[i] = run[en]
        gp_classes = [e for e in ("g", "p") if any(en == e for en, _, _ in self.steps)]
        with nc.Block() as blk:
            for eng, body in [("v", "vector"), ("s", "scalar"), ("t", "tensor")]:
                if not any(en == eng for en, _, _ in self.steps):
                    continue

                def mk(eng=eng):
                    def emit_body(e):
                        water = {o: base[o] for o, _ in self.ENGS}
                        for i, (en, fn, deps) in enumerate(self.steps):
                            if en != eng:
                                continue
                            for d in deps:
                                den = self.steps[d][0]
                                if den == eng:
                                    continue
                                tgt = cum[d]
                                if tgt > water[den]:
                                    e.wait_ge(sems[den], tgt)
                                    water[den] = tgt
                            fn(e).then_inc(sems[eng], 1)
                    return emit_body
                getattr(blk, body)(mk())
            if gp_classes:
                @blk.gpsimd
                def _(e):
                    water = {o: base[o] for o, _ in self.ENGS}
                    for i, (en, fn, deps) in enumerate(self.steps):
                        if en not in ("g", "p"):
                            continue
                        for d in deps:
                            den = self.steps[d][0]
                            if den in ("g", "p") and den == en:
                                continue
                            if den in ("g", "p") and den != en:
                                pass
                            elif den == en:
                                continue
                            tgt = cum[d]
                            if self.steps[d][0] in ("g", "p"):
                                continue
                            if tgt > water[den]:
                                e.wait_ge(sems[den], tgt)
                                water[den] = tgt
                        fn(e).then_inc(sems[en], 16 if en == "g" else 1)
                        if en == "g":
                            e.wait_ge(sems[en], cum[i])
            @blk.sync
            def _(sy):
                for eng in set(en for en, _, _ in self.steps):
                    if run[eng] > base[eng]:
                        sy.wait_ge(sems[eng], run[eng])
        for eng, _ in self.ENGS:
            ctx.cnt["x" + eng] = run[eng]


def emit_p45(ctx, gather_cb):
    nc, io = ctx.nc, ctx.io
    ctx.p45_es0 = contextlib.ExitStack()
    ssq = ctx.p45_es0.enter_context(nc.sbuf_tensor("fssq", [1, T], F32))
    esP = contextlib.ExitStack()
    sbP = lambda n, s, d: esP.enter_context(nc.sbuf_tensor(n, s, d))
    eloc = sbP("fEl", [64, T], F32)
    eglo = sbP("fEg", [64, T], F32)
    LgT = sbP("fLgT", [128, 8 * 64], F32)
    RT = sbP("fRT", [128, 8 * 64], F32)
    wT = sbP("fwT", [128, 8 * 64], F32)
    Ecol = sbP("fE", [64, 16], F32)
    Dflat = sbP("fDf", [1, 512], F32)
    Eflat = sbP("fEf", [1, 576], F32)
    dpc = sbP("fdp", [64, 64], F32)
    caus = sbP("fca", [128, 128], F32)
    zt = sbP("fz", [128, 128], F32)
    elocT = sbP("felT", [128, 8 * 64], F32)
    nw64 = sbP("fnw", [64, 64], F32)
    Sf = sbP("fS", [64, 4096], F32)
    Sin = sbP("fsin", [64, 4096], BF16)
    xtm = [sbP(f"fxt{i}", [128, 1024], BF16) for i in range(8)]
    btm = [sbP(f"fbt{i}", [128, 1024], BF16) for i in range(8)]
    ctx.p45 = {"ssq": ssq}

    esQ = contextlib.ExitStack()
    sbQ = lambda n, s, d: esQ.enter_context(nc.sbuf_tensor(n, s, d))
    dtr_ = sbQ("fdtr", [64, T], F32)
    dts = sbQ("fdts", [64, T], F32)
    tmpa = sbQ("ftma", [64, T], F32)
    tmpb = sbQ("ftmb", [64, T], F32)
    lndt = sbQ("fln", [64, T], F32)
    Lg = sbQ("flg", [64, T], F32)
    Rt = sbQ("fr", [64, T], F32)
    wsc = sbQ("fw", [64, T], F32)
    dE = sbQ("fdE", [64, 9], F32)
    dtb = sbQ("fdtb", [64, 1], F32)
    alog = sbQ("fal", [64, 1], F32)
    Acol = sbQ("fA", [64, 1], F32)
    psP = esQ.enter_context(nc.psum_tensor("fpP", [128, 128], F32))
    dram_tmp = nc.dram_tensor("f45dram", [64, 9], F32).ap()

    a0 = ctx.cnt["ald"]
    with nc.Block() as blk:
        @blk.sync
        def _(sy):
            n = 0
            for i in range(8):
                sy.dma_start(xtm[i][:, :], io["xmf"][0:1024, i * 128:(i + 1) * 128], transpose=True).then_inc(ctx.sem("ald"), 16); n += 1
                sy.dma_start(btm[i][:, :], io["bmf"][0:1024, i * 128:(i + 1) * 128], transpose=True).then_inc(ctx.sem("ald"), 16); n += 1
            for ap, name in [(dtr_, "dtr"), (dtb, "dtb"), (alog, "alog"), (dpc, "dpc64"),
                             (caus, "causT"), (nw64, "nw64")]:
                sy.dma_start(ap[:, :], io[name][:, :]).then_inc(ctx.sem("ald"), 16); n += 1
            sy.wait_ge(ctx.sem("ald"), a0 + 16 * n)
            ctx.cnt["ald"] = a0 + 16 * n
        @blk.vector
        def _(v):
            v.memset(zt[:, :], 0.0)
            v.memset(Ecol[:, :], 0.0)
            v.memset(Sf[:, :], 0.0)
            v.memset(ssq[0:1, :], 0.0)

    steps = []
    def S(eng, fn):
        steps.append((eng, fn))
    S("s", lambda e: e.activation(dts[:, :], dtr_[:, :], AF.Identity, bias=dtb[:, 0:1]))
    S("s", lambda e: e.activation(tmpa[:, :], dts[:, :], AF.Abs))
    S("s", lambda e: e.activation(tmpb[:, :], tmpa[:, :], AF.Exp, scale=-1.0))
    S("s", lambda e: e.activation(tmpa[:, :], tmpb[:, :], AF.Ln, bias=1.0))
    S("v", lambda e: e.tensor_relu(tmpb[:, :], dts[:, :]))
    S("v", lambda e: e.tensor_add(dts[:, :], tmpa[:, :], tmpb[:, :]))
    S("s", lambda e: e.activation(lndt[:, :], dts[:, :], AF.Ln))
    S("s", lambda e: e.activation(Acol[:, 0:1], alog[:, 0:1], AF.Exp))
    S("v", lambda e: e.tensor_scalar(Acol[:, 0:1], Acol[:, 0:1], -1.0, None, ALU.mult))
    S("v", lambda e: e.tensor_scalar(tmpa[:, :], dts[:, :], Acol[:, 0:1], None, ALU.mult))
    S("v", lambda e: e.tensor_tensor_scan(Lg[:, :], tmpa[:, :], tmpa[:, :], 0.0, ALU.add, ALU.bypass))
    S("v", lambda e: e.tensor_copy(Ecol[:, 1:9], Lg[:, 127::128]))
    S("v", lambda e: e.tensor_sub(Rt[:, :], lndt[:, :], Lg[:, :]))
    for c in range(NCH):
        S("v", lambda e, c=c: e.tensor_scalar(tmpa[:, c * 128:(c + 1) * 128], Lg[:, c * 128:(c + 1) * 128], Ecol[:, c:c + 1], None, ALU.subtract))
        S("v", lambda e, c=c: e.tensor_scalar(tmpb[:, c * 128:(c + 1) * 128], Rt[:, c * 128:(c + 1) * 128], Ecol[:, c + 1:c + 2], None, ALU.add))
    S("s", lambda e: e.activation(eloc[:, :], tmpa[:, :], AF.Exp))
    S("s", lambda e: e.activation(eglo[:, :], Lg[:, :], AF.Exp))
    S("s", lambda e: e.activation(wsc[:, :], tmpb[:, :], AF.Exp))
    S("v", lambda e: e.tensor_sub(dE[:, 0:8], Ecol[:, 1:9], Ecol[:, 0:8]))
    S("s", lambda e: e.activation(dE[:, 0:8], dE[:, 0:8], AF.Exp))
    S("s", lambda e: e.activation(dE[:, 8:9], Ecol[:, 0:8][:, 0:1], AF.Identity))
    S("g", lambda e: e.dma_start(dram_tmp[:, 0:8], dE[:, 0:8]))
    for c in range(NCH):
        S("g", lambda e, c=c: e.dma_start(Dflat[0:1, c * 64:(c + 1) * 64],
                                          dram_tmp[:, c:c + 1].rearrange("h c -> c h"),
                                          allow_slow_non_contiguous=True))
    S("s", lambda e: e.activation(dE[:, :], Ecol[:, 0:9], AF.Exp))
    S("g", lambda e: e.dma_start(dram_tmp[:, 0:9], dE[:, 0:9]))
    for c in range(9):
        S("g", lambda e, c=c: e.dma_start(Eflat[0:1, c * 64:(c + 1) * 64],
                                          dram_tmp[:, c:c + 1].rearrange("h c -> c h"),
                                          allow_slow_non_contiguous=True))
    for c in range(NCH):
        S("t", lambda e, c=c: e.transpose(psP[:, 0:64], Lg[:, c * 128:(c + 1) * 128], ctx.eye[0:64, 0:64]))
        S("v", lambda e, c=c: e.tensor_copy(LgT[:, c * 64:(c + 1) * 64], psP[:, 0:64]))
        S("t", lambda e, c=c: e.transpose(psP[:, 0:64], Rt[:, c * 128:(c + 1) * 128], ctx.eye[0:64, 0:64]))
        S("v", lambda e, c=c: e.tensor_copy(RT[:, c * 64:(c + 1) * 64], psP[:, 0:64]))
        S("t", lambda e, c=c: e.transpose(psP[:, 0:64], wsc[:, c * 128:(c + 1) * 128], ctx.eye[0:64, 0:64]))
        S("v", lambda e, c=c: e.tensor_copy(wT[:, c * 64:(c + 1) * 64], psP[:, 0:64]))
        S("t", lambda e, c=c: e.transpose(psP[:, 0:64], eloc[:, c * 128:(c + 1) * 128], ctx.eye[0:64, 0:64]))
        S("v", lambda e, c=c: e.tensor_copy(elocT[:, c * 64:(c + 1) * 64], psP[:, 0:64]))
    aux = ctx.sem("aux")
    A0 = ctx.cnt["aux"]
    incs = [16 if en == "g" else 1 for en, _ in steps]
    tgts = [A0 + sum(incs[:i]) for i in range(len(steps))]
    with nc.Block() as blk:
        for eng_name, body in [("v", "vector"), ("s", "scalar"), ("t", "tensor"), ("g", "gpsimd")]:
            def mk(eng_name=eng_name):
                def emit(e):
                    for i, (en, fn) in enumerate(steps):
                        if en != eng_name:
                            continue
                        e.wait_ge(aux, tgts[i])
                        fn(e).then_inc(aux, incs[i])
                return emit
            getattr(blk, body)(mk())
    ctx.cnt["aux"] = A0 + sum(incs)
    esQ.close()

    esS = contextlib.ExitStack()
    sbS = lambda n, s, d: esS.enter_context(nc.sbuf_tensor(n, s, d))
    bw = [sbS(f"fbw{i}", [128, 64], BF16) for i in range(2)]
    st1 = [sbS(f"fst{i}", [64, 64], F32) for i in range(2)]
    Dbc = sbS("fDbc", [64, 64], F32)
    Dbt = sbS("fDbt", [64, 64], F32)
    psS = [esS.enter_context(nc.psum_tensor(f"fpS{i}", [64, 64], F32)) for i in range(2)]
    psD = esS.enter_context(nc.psum_tensor("fpD", [64, 64], F32))
    ds = DepSteps(ctx)
    Sd = ds.S
    last = {}
    for c in range(NCH):
        i_dm = Sd("t", lambda e, c=c: e.matmul(psD[:, :], ctx.onec[0:1, 0:64],
                                               Dflat[0:1, c * 64:(c + 1) * 64], start=True, stop=True),
                  deps=[last.get("Dbcuse")])
        i_db0 = Sd("v", lambda e: e.tensor_copy(Dbt[:, :], psD[:, :]), deps=[i_dm, last.get("Dbtuse")])
        i_db = Sd("v", lambda e: e.tensor_copy(Dbc[:, :], Dbt[:, :]), deps=[last.get("Dbcuse")])
        last["Dbtuse"] = i_db
        lastdb = i_db
        for h in range(NH):
            p_ = h % 2
            i_bw = Sd("s", lambda e, c=c, h=h, p_=p_: e.activation(
                bw[p_][:, :], btm[c][:, (h // 4) * 64:(h // 4) * 64 + 64], AF.Identity,
                scale=wT[:, c * 64 + h:c * 64 + h + 1]), deps=[last.get(("bwuse", p_))])
            i_ds = Sd("t", lambda e, c=c, h=h, p_=p_: e.matmul(
                psS[p_][:, :], bw[p_][:, :], xtm[c][:, (h // 4) * 64:(h // 4) * 64 + 64],
                start=True, stop=True), deps=[i_bw, last.get(("psS", p_))])
            last[("bwuse", p_)] = i_ds
            i_t1 = Sd("v", lambda e, h=h, p_=p_: e.tensor_scalar(
                st1[p_][:, :], Sf[:, h * 64:(h + 1) * 64], Dbc[:, h:h + 1], None, ALU.mult),
                deps=[i_db, last.get(("Sup", h)), last.get(("st1use", p_))])
            i_t2 = Sd("v", lambda e, h=h, p_=p_: e.tensor_tensor(
                Sf[:, h * 64:(h + 1) * 64], st1[p_][:, :], psS[p_][:, :], ALU.add),
                deps=[i_ds, i_t1])
            last[("psS", p_)] = i_t2
            last[("st1use", p_)] = i_t2
            last[("Sup", h)] = i_t2
        last["Dbcuse"] = last[("Sup", 63)]
    prev = None
    for h in range(NH):
        prev = Sd("g", lambda e, h=h: e.dma_start(io["sfin"][h * 64:(h + 1) * 64, :],
                                                  Sf[:, h * 64:(h + 1) * 64]),
                  deps=[last[("Sup", h)], prev])
    Sd("g", lambda e: e.dma_start(io["elast"][:, 0:1], Ecol[:, 8:9]), deps=[prev])
    ds.emit()
    esS.close()

    gather_cb(ctx)

    esH = contextlib.ExitStack()
    sbH = lambda n, s, d: esH.enter_context(nc.sbuf_tensor(n, s, d))
    sel64 = sbH("fhs64", [64, 64 * 64], F32)
    Esb = sbH("fhE", [64, 8], F32)
    emk = sbH("fhem", [64, 8], F32)
    Wbc = sbH("fhW", [64, 64 * 8], F32)
    acc = sbH("fhacc", [64, 4096], F32)
    skb = sbH("fhskb", [64, 4096], F32)
    sgg = sbH("fhsgg", [64, 4096], F32)
    psW = esH.enter_context(nc.psum_tensor("fhpW", [64, 8], F32))
    a0 = ctx.cnt["ald"]
    with nc.Block() as blk:
        @blk.sync
        def _(sy):
            n = 0
            for k in range(8):
                sy.dma_start(Esb[:, k:k + 1], io["egath"][k * 64:(k + 1) * 64, 0:1]).then_inc(ctx.sem("ald"), 16); n += 1
            sy.dma_start(emk[:, :], io["emask64"][:, :]).then_inc(ctx.sem("ald"), 16); n += 1
            sy.dma_start(sel64[:, :], io["sel64"][:, :]).then_inc(ctx.sem("ald"), 16); n += 1
            sy.wait_ge(ctx.sem("ald"), a0 + 16 * n)
            ctx.cnt["ald"] = a0 + 16 * n
        @blk.vector
        def _(v):
            v.memset(acc[:, :], 0.0)
    steps = []
    S("v", lambda e: e.tensor_mul(Esb[:, :], Esb[:, :], emk[:, :]))
    S("s", lambda e: e.activation(Esb[:, :], Esb[:, :], AF.Exp))
    for h in range(64):
        S("t", lambda e, h=h: e.matmul(psW[:, :], sel64[:, h * 64:(h + 1) * 64], Esb[:, :],
                                       start=True, stop=True))
        S("v", lambda e, h=h: e.tensor_copy(Wbc[:, h * 8:(h + 1) * 8], psW[:, :]))
    for k in range(7):
        sgsrc = io["sgath"].rearrange("(c hh n) p -> c n hh p", c=8, n=64)
        for d in range(4):
            S("g", lambda e, k=k, d=d: e.dma_start(
                skb[:, d * 16 * 64:(d + 1) * 16 * 64].rearrange("n (hh p) -> n hh p", p=64),
                sgsrc[k, :, d * 16:(d + 1) * 16, :]))
        for h in range(64):
            S("v", lambda e, k=k, h=h: e.tensor_scalar(
                acc[:, h * 64:(h + 1) * 64], acc[:, h * 64:(h + 1) * 64],
                Wbc[:, h * 8 + k:h * 8 + k + 1], None, ALU.mult))
        S("v", lambda e, k=k: e.tensor_scalar(sgg[:, :], skb[:, :], emk[:, k:k + 1], None, ALU.mult))
        S("v", lambda e: e.tensor_add(acc[:, :], acc[:, :], sgg[:, :]))
    S("v", lambda e: e.tensor_copy(Sin[:, :], acc[:, :]))
    if "fdbg1" in io:
        S("g", lambda e: e.dma_start(io["fdbg1"][0:1, :], Eflat[0:1, :]))
        S("g", lambda e: e.dma_start(io["fdbg2"][:, :], Sin[:, :]))
    A0 = ctx.cnt["aux"]
    incs = [16 if en == "g" else 1 for en, _ in steps]
    tgts = [A0 + sum(incs[:i]) for i in range(len(steps))]
    with nc.Block() as blk:
        for eng_name, body in [("v", "vector"), ("s", "scalar"), ("t", "tensor"), ("g", "gpsimd")]:
            def mk(eng_name=eng_name):
                def emit(e):
                    for i, (en, fn) in enumerate(steps):
                        if en != eng_name:
                            continue
                        e.wait_ge(aux, tgts[i])
                        fn(e).then_inc(aux, incs[i])
                return emit
            getattr(blk, body)(mk())
    ctx.cnt["aux"] = A0 + sum(incs)
    esH.close()

    esY = contextlib.ExitStack()
    sbY = lambda n, s, d: esY.enter_context(nc.sbuf_tensor(n, s, d))
    cc = [sbY(f"fcc{i}", [64, 64 * 128], BF16) for i in range(2)]
    zc = sbY("fzc", [64, 64 * 128], BF16)
    clb = sbY("fcl", [64, 64 * 128], BF16)
    yfb = [sbY(f"fyf{i}", [64, 64 * 128], BF16) for i in range(2)]
    xfc0 = sbY("fxf", [64, 16 * 128], BF16)
    xfc = [xfc0, xfc0]
    bfc0 = sbY("fbf", [64, 16 * 128], BF16)
    bfc = [bfc0, bfc0]
    Sb = sbY("fSb", [64, 4096], BF16)
    Sgb = [sbY(f"fSg{i}", [64, 64], BF16) for i in range(2)]
    Sgs = [sbY(f"fSgs{i}", [64, 64], F32) for i in range(2)]
    Ebc = sbY("fEbc", [64, 64], F32)
    Dbc2 = sbY("fDbc2", [64, 64], F32)
    Ebt2 = sbY("fEbt2", [64, 64], F32)
    Dbt2 = sbY("fDbt2", [64, 64], F32)
    bct = [sbY(f"fbc{i}", [128, 128], F32) for i in range(2)]
    msk = [sbY(f"fmk{i}", [128, 128], F32) for i in range(2)]
    Mp = [sbY(f"fMp{i}", [128, 128], F32) for i in range(2)]
    Gm = [sbY(f"fGm{i}", [128, 128], BF16) for i in range(2)]
    Gm2 = [sbY(f"fGm2{i}", [128, 128], BF16) for i in range(2)]
    cltmp = [sbY(f"fclt{i}", [64, 128], BF16) for i in range(2)]
    xsc = [sbY(f"fxs{i}", [64, 128], F32) for i in range(2)]
    szb = [sbY(f"fsz{i}", [64, 128], F32) for i in range(2)]
    sqb = [sbY(f"fsq{i}", [64, 128], F32) for i in range(2)]
    ytb = [sbY(f"fyt{i}", [64, 128], F32) for i in range(2)]
    bw2 = [sbY(f"fbw2{i}", [128, 64], BF16) for i in range(2)]
    st2 = [sbY(f"fst2{i}", [64, 64], F32) for i in range(2)]
    psA = [esY.enter_context(nc.psum_tensor(f"fpA{i}", [128, 128], F32)) for i in range(2)]
    psG = [esY.enter_context(nc.psum_tensor(f"fpG{i}", [128, 128], F32)) for i in range(2)]
    psY = [esY.enter_context(nc.psum_tensor(f"fpY{i}", [64, 128], F32)) for i in range(2)]
    psS2 = [esY.enter_context(nc.psum_tensor(f"fpS2{i}", [64, 128], F32)) for i in range(2)]
    csrc = io["cmf"].rearrange("(a p) t -> p a t", p=64)
    zsrc = io["zf"].rearrange("(a p) t -> p a t", p=64)
    xsrc = io["xmf"].rearrange("(a p) t -> p a t", p=64)
    bsrc = io["bmf"].rearrange("(a p) t -> p a t", p=64)
    ydst = io["yfh"].rearrange("(a p) t -> p a t", p=64)

    def hv(buf, h, w=128):
        return buf[:, h * 128:h * 128 + w]

    ds = DepSteps(ctx)
    Sd = ds.S
    last = {}
    with nc.Block() as blk0:
        @blk0.vector
        def _(v):
            v.memset(Sf[:, :], 0.0)
            v.memset(Sb[:, :], 0.0)
    for c in range(NCH):
        p_ = c % 2
        ld_c = []
        for dd in range(4):
            ld_c.append(Sd("g", lambda e, c=c, dd=dd, p_=p_: e.dma_start(
                cc[p_][:, dd * 16 * 128:(dd + 1) * 16 * 128].rearrange("p (a t) -> p a t", t=128),
                csrc[:, dd * 16:(dd + 1) * 16, c * 128:(c + 1) * 128]),
                deps=[last.get(("ccuse", p_))]))
        ld_z = []
        for dd in range(4):
            ld_z.append(Sd("g", lambda e, c=c, dd=dd: e.dma_start(
                zc[:, dd * 16 * 128:(dd + 1) * 16 * 128].rearrange("p (a t) -> p a t", t=128),
                zsrc[:, dd * 16:(dd + 1) * 16, c * 128:(c + 1) * 128]),
                deps=[last.get("zcuse")]))
        i_lx = Sd("g", lambda e, c=c, p_=p_: e.dma_start(
            xfc[p_][:, :].rearrange("p (a t) -> p a t", t=128),
            xsrc[:, :, c * 128:(c + 1) * 128]), deps=[last.get(("xfcuse", 0)), last.get(("xfcuse", 1))])
        i_lb = Sd("g", lambda e, c=c, p_=p_: e.dma_start(
            bfc[p_][:, :].rearrange("p (a t) -> p a t", t=128),
            bsrc[:, :, c * 128:(c + 1) * 128]), deps=[last.get(("bfcuse", 0)), last.get(("bfcuse", 1))])
        i_dm = Sd("t", lambda e, c=c: e.matmul(psS2[0][:, 0:64], ctx.onec[0:1, 0:64],
                                               Dflat[0:1, c * 64:(c + 1) * 64], start=True, stop=True),
                  deps=[last.get(("psS2", 0))])
        i_db0 = Sd("v", lambda e: e.tensor_copy(Dbt2[:, :], psS2[0][:, 0:64]),
                   deps=[i_dm, last.get("Dbt2use")])
        i_db = Sd("v", lambda e: e.tensor_copy(Dbc2[:, :], Dbt2[:, :]),
                  deps=[last.get("Dbc2use")])
        i_em = Sd("t", lambda e, c=c: e.matmul(psS2[1][:, 0:64], ctx.onec[0:1, 0:64],
                                               Eflat[0:1, c * 64:(c + 1) * 64], start=True, stop=True),
                  deps=[last.get(("psS2", 1))])
        i_eb0 = Sd("v", lambda e: e.tensor_copy(Ebt2[:, :], psS2[1][:, 0:64]),
                   deps=[i_em, last.get("Ebt2use")])
        i_eb = Sd("v", lambda e: e.tensor_copy(Ebc[:, :], Ebt2[:, :]),
                  deps=[last.get("Ebcuse")])
        last["Dbt2use"] = i_db
        last["Ebt2use"] = i_eb
        last[("psS2", 0)] = i_db
        last[("psS2", 1)] = i_eb
        cl_done = []
        for h in range(NH):
            hp = h % 2
            i_b2 = Sd("s", lambda e, c=c, h=h, hp=hp: e.activation(
                bct[hp][:, 0:64], zt[:, 0:64], AF.Identity,
                bias=elocT[:, c * 64 + h:c * 64 + h + 1]), deps=[last.get(("bctuse", hp))])
            i_sc = Sd("t", lambda e, hp=hp: e.transpose(
                psA[hp][0:64, :], bct[hp][:, 0:64], ctx.eye[:, :]),
                deps=[i_b2, last.get(("psA", hp))])
            last[("bctuse", hp)] = i_sc
            i_cl0 = Sd("v", lambda e, h=h, hp=hp, p_=p_: e.tensor_tensor(
                cltmp[hp][:, :], hv(cc[p_], h), psA[hp][0:64, :], ALU.mult),
                deps=[i_sc, ld_c[h // 16], last.get(("cltmpuse", hp))])
            last[("psA", hp)] = i_cl0
            i_cl = Sd("v", lambda e, h=h, hp=hp: e.tensor_copy(hv(clb, h), cltmp[hp][:, :]),
                      deps=[i_cl0, last.get(("clbuse", h))])
            last[("cltmpuse", hp)] = i_cl
            cl_done.append(i_cl)
        for h in range(NH):
            hp = h % 2
            kv = h // 4
            g = c * 64 + h
            i_b = Sd("s", lambda e, c=c, h=h, hp=hp: e.activation(
                bct[hp][:, :], caus[:, :], AF.Identity, bias=LgT[:, c * 64 + h:c * 64 + h + 1]),
                deps=[last.get(("bctuse", hp))])
            i_tr = Sd("t", lambda e, hp=hp: e.transpose(psA[hp][:, :], bct[hp][:, :], ctx.eye[:, :]),
                      deps=[i_b, last.get(("psA", hp))])
            last[("bctuse", hp)] = i_tr
            i_mp = Sd("s", lambda e, c=c, h=h, hp=hp: e.activation(
                Mp[hp][:, :], psA[hp][:, :], AF.Exp, bias=RT[:, c * 64 + h:c * 64 + h + 1]),
                deps=[i_tr, last.get(("Mpuse", hp))])
            last[("psA", hp)] = i_mp
            i_gp = Sd("t", lambda e, c=c, h=h, hp=hp, p_=p_: e.matmul(
                psG[hp][:, :], bfc[p_][0:64, (h // 4) * 128:(h // 4) * 128 + 128],
                hv(cc[p_], h), start=True, stop=True),
                deps=[i_lb, ld_c[h // 16], last.get(("psG", hp))])
            i_gm0 = Sd("v", lambda e, hp=hp: e.tensor_tensor(Gm[hp][:, :], psG[hp][:, :], Mp[hp][:, :], ALU.mult),
                       deps=[i_gp, i_mp, last.get(("Gmuse", hp))])
            i_gm = Sd("v", lambda e, hp=hp: e.tensor_copy(Gm2[hp][:, :], Gm[hp][:, :]),
                      deps=[i_gm0, last.get(("Gm2use", hp))])
            last[("psG", hp)] = i_gm0
            last[("Mpuse", hp)] = i_gm0
            last[("Gmuse", hp)] = i_gm
            i_sg = Sd("v", lambda e, h=h, hp=hp: e.tensor_scalar(
                Sgs[hp][:, :], Sin[:, h * 64:(h + 1) * 64], Ebc[:, h:h + 1], None, ALU.mult),
                deps=[i_eb, last.get(("Sgsuse", hp))])
            i_sgb = Sd("v", lambda e, hp=hp: e.tensor_copy(Sgb[hp][:, :], Sgs[hp][:, :]),
                       deps=[i_sg, last.get(("Sgbuse", hp))])
            last[("Sgsuse", hp)] = i_sgb
            if c == 0 and h == 6 and "fdbg3" in io:
                Sd("g", lambda e, hp=hp: e.dma_start(io["fdbg3"][:, :], Sgs[hp][:, :]), deps=[i_sg])
                Sd("g", lambda e: e.dma_start(io["fdbg4"][:, :], Ebc[:, :]), deps=[i_eb])
                Sd("g", lambda e, h=h: e.dma_start(io["fdbg5"][:, :], hv(clb, h)), deps=[cl_done[h]])
            i_y1 = Sd("t", lambda e, c=c, h=h, hp=hp: e.matmul(
                psY[hp][:, :], xtm[c][:, (h // 4) * 64:(h // 4) * 64 + 64], Gm2[hp][:, :],
                start=True, stop=False), deps=[i_gm, last.get(("psY", hp))])
            last[("Gm2use", hp)] = i_y1
            i_y2 = Sd("t", lambda e, h=h, hp=hp: e.matmul(
                psY[hp][:, :], Sb[:, h * 64:(h + 1) * 64], hv(clb, h),
                start=False, stop=False), deps=[cl_done[h], last.get(("Sbup", h))])
            i_y3 = Sd("t", lambda e, h=h, hp=hp: e.matmul(
                psY[hp][:, :], Sgb[hp][:, :], hv(clb, h), start=False, stop=True),
                deps=[i_sgb])
            last[("Sgbuse", hp)] = i_y3
            i_xs = Sd("s", lambda e, c=c, h=h, hp=hp, p_=p_: e.activation(
                xsc[hp][:, :], xfc[p_][0:64, (h // 4) * 128:(h // 4) * 128 + 128],
                AF.Identity, scale=dpc[:, h:h + 1]), deps=[i_lx, last.get(("xscuse", hp))])
            i_yt = Sd("v", lambda e, hp=hp: e.tensor_tensor(ytb[hp][:, :], psY[hp][:, :], xsc[hp][:, :], ALU.add),
                      deps=[i_y3, i_xs, last.get(("ytuse", hp))])
            last[("psY", hp)] = i_yt
            last[("xscuse", hp)] = i_yt
            i_sz = Sd("s", lambda e, h=h, hp=hp: e.activation(szb[hp][:, :], hv(zc, h), AF.Silu),
                      deps=[ld_z[h // 16], last.get(("szuse", hp))])
            i_yf = Sd("v", lambda e, h=h, hp=hp, p_=p_: e.tensor_tensor(
                hv(yfb[p_], h), ytb[hp][:, :], szb[hp][:, :], ALU.mult),
                deps=[i_sz, i_yt, last.get(("yfbuse", p_, h))])
            last[("ytuse", hp)] = i_yf
            last[("szuse", hp)] = i_yf
            i_sq = Sd("v", lambda e, h=h, hp=hp, p_=p_: e.tensor_mul(sqb[hp][:, :], hv(yfb[p_], h), hv(yfb[p_], h)),
                      deps=[last.get(("squse", hp))])
            i_nw = Sd("v", lambda e, h=h, p_=p_: e.tensor_scalar(
                hv(yfb[p_], h), hv(yfb[p_], h), nw64[:, h:h + 1], None, ALU.mult))
            i_mq = Sd("t", lambda e, hp=hp: e.matmul(psS2[hp][0:1, 0:128], ctx.ones128[0:64, 0:1], sqb[hp][:, :],
                                                     start=True, stop=True),
                      deps=[i_sq, last.get(("psS2", hp))])
            last[("squse", hp)] = i_mq
            i_sa = Sd("v", lambda e, c=c, hp=hp: e.tensor_tensor(
                ssq[0:1, c * 128:(c + 1) * 128], ssq[0:1, c * 128:(c + 1) * 128],
                psS2[hp][0:1, 0:128], ALU.add), deps=[i_mq, last.get("ssqlast")])
            last[("psS2", hp)] = i_sa
            last["ssqlast"] = i_sa
            last[("yfbuse2", p_, h)] = i_nw
            i_bw = Sd("s", lambda e, c=c, h=h, hp=hp: e.activation(
                bw2[hp][:, :], btm[c][:, (h // 4) * 64:(h // 4) * 64 + 64], AF.Identity,
                scale=wT[:, c * 64 + h:c * 64 + h + 1]), deps=[last.get(("bw2use", hp))])
            i_ds2 = Sd("t", lambda e, c=c, h=h, hp=hp: e.matmul(
                psS2[hp][:, 0:64], bw2[hp][:, :], xtm[c][:, (h // 4) * 64:(h // 4) * 64 + 64],
                start=True, stop=True), deps=[i_bw, last.get(("psS2", hp))])
            last[("bw2use", hp)] = i_ds2
            i_t1 = Sd("v", lambda e, h=h, hp=hp: e.tensor_scalar(
                st2[hp][:, :], Sf[:, h * 64:(h + 1) * 64], Dbc2[:, h:h + 1], None, ALU.mult),
                deps=[i_db, last.get(("Sup2", h)), last.get(("st2use", hp))])
            i_t2 = Sd("v", lambda e, h=h, hp=hp: e.tensor_tensor(
                Sf[:, h * 64:(h + 1) * 64], st2[hp][:, :], psS2[hp][:, 0:64], ALU.add),
                deps=[i_ds2, i_t1])
            last[("psS2", hp)] = i_t2
            last[("st2use", hp)] = i_t2
            last[("Sup2", h)] = i_t2
            i_sb = Sd("v", lambda e, h=h: e.tensor_copy(Sb[:, h * 64:(h + 1) * 64], Sf[:, h * 64:(h + 1) * 64]),
                      deps=[i_y2, i_t2])
            last[("Sbup", h)] = i_sb
            last[("clbuse", h)] = i_y3
        last["Dbc2use"] = last[("Sup2", 63)]
        last["Ebcuse"] = last[("Sgsuse", 1)]
        st_c = None
        for dd in range(4):
            st_c = Sd("g", lambda e, c=c, dd=dd, p_=p_: e.dma_start(
                ydst[:, dd * 16:(dd + 1) * 16, c * 128:(c + 1) * 128],
                yfb[p_][:, dd * 16 * 128:(dd + 1) * 16 * 128].rearrange("p (a t) -> p a t", t=128)),
                deps=[last[("yfbuse2", p_, 16 * dd + 15)], last["ssqlast"], st_c])
        for h in range(NH):
            last[("yfbuse", p_, h)] = st_c
        last[("ccuse", p_)] = last[("psG", 1)]
        last["zcuse"] = last[("szuse", 1)]
        last[("xfcuse", p_)] = last[("xscuse", 1)]
        last[("bfcuse", p_)] = last[("psG", 1)]
    ds.emit()
    esY.close()
    esP.close()

def emit_p5c(ctx):
    nc, io = ctx.nc, ctx.io
    ssq = ctx.p45["ssq"]
    esC = contextlib.ExitStack()
    sbC = lambda n, s, d: esC.enter_context(nc.sbuf_tensor(n, s, d))
    yft = [sbC(f"c5y{k}", [128, T], BF16) for k in range(32)]
    mos = [sbC(f"c5mo{k}", [128, T], F32) for k in range(16)]
    rrow = sbC("c5rr", [1, T], F32)
    rbc = sbC("c5rb", [128, T], F32)
    wb = [sbC(f"c5wb{i}", [128, 32 * 128], BF16) for i in range(3)]
    mt = sbC("c5mt", [128, T], F32)
    xt_ = sbC("c5xt2", [128, T], F32)
    ps = [esC.enter_context(nc.psum_tensor(f"c5ps{i}", [128, 512], F32)) for i in range(4)]
    psB = esC.enter_context(nc.psum_tensor("c5psB", [128, 512], F32))
    a0 = ctx.cnt["ald"]
    with nc.Block() as blk:
        @blk.sync
        def _(sy):
            for k in range(32):
                sy.dma_start(yft[k][:, :], io["yfh"][k * 128:(k + 1) * 128, :]).then_inc(ctx.sem("ald"), 16)
            sy.wait_ge(ctx.sem("ald"), a0 + 16 * 32)
            ctx.cnt["ald"] = a0 + 16 * 32
    steps = []
    def S(eng, fn):
        steps.append((eng, fn))
    S("s", lambda e: e.activation(rrow[0:1, :], ssq[0:1, :], AF.Ln, scale=1.0 / DI, bias=EPS0))
    S("s", lambda e: e.activation(rrow[0:1, :], rrow[0:1, :], AF.Exp, scale=-0.5))
    for j in range(2):
        S("t", lambda e, j=j: e.matmul(psB[:, :], ctx.onec[0:1, :], rrow[0:1, j * 512:(j + 1) * 512],
                                       start=True, stop=True))
        S("v", lambda e, j=j: e.tensor_copy(rbc[:, j * 512:(j + 1) * 512], psB[:, :]))
    aux = ctx.sem("aux")
    A0 = ctx.cnt["aux"]
    tgts = [A0 + i for i in range(len(steps))]
    with nc.Block() as blk:
        for eng_name, body in [("v", "vector"), ("s", "scalar"), ("t", "tensor")]:
            def mk(eng_name=eng_name):
                def emit(e):
                    for i, (en, fn) in enumerate(steps):
                        if en != eng_name:
                            continue
                        e.wait_ge(aux, tgts[i])
                        fn(e).then_inc(aux, 1)
                return emit
            getattr(blk, body)(mk())
    ctx.cnt["aux"] = A0 + len(steps)

    def evop(ve, ji, ti, p, ot, stile):
        t0 = ti * 512
        return ve.tensor_tensor(mos[ji][:, t0:t0 + 512], p[:, :], rbc[:, t0:t0 + 512], ALU.mult)

    linear(ctx, dict(
        name="outproj", wt=io["wop"], K=DI, N=D, T=T, tw=512,
        x_tile=lambda kc: yft[kc],
        psums=ps, wbufs=wb, evac_v=evop, out_ap=None))

    steps = []
    for kc in range(16):
        S("g", lambda e, kc=kc: e.dma_start(mt[:, :], io["mlp"][kc * 128:(kc + 1) * 128, :]))
        S("g", lambda e, kc=kc: e.dma_start(xt_[:, :], io["xnfm"][kc * 128:(kc + 1) * 128, :]))
        S("v", lambda e, kc=kc: e.tensor_add(mos[kc][:, :], mos[kc][:, :], mt[:, :]))
        S("v", lambda e, kc=kc: e.tensor_add(mos[kc][:, :], mos[kc][:, :], xt_[:, :]))
        S("g", lambda e, kc=kc: e.dma_start(io["sfm"][kc * 128:(kc + 1) * 128, :], mos[kc][:, :]))
    A0 = ctx.cnt["aux"]
    incs = [16 if en == "g" else 1 for en, _ in steps]
    tgts = [A0 + sum(incs[:i]) for i in range(len(steps))]
    with nc.Block() as blk:
        for eng_name, body in [("v", "vector"), ("g", "gpsimd")]:
            def mk(eng_name=eng_name):
                def emit(e):
                    for i, (en, fn) in enumerate(steps):
                        if en != eng_name:
                            continue
                        e.wait_ge(aux, tgts[i])
                        fn(e).then_inc(aux, incs[i])
                return emit
            getattr(blk, body)(mk())
        @blk.sync
        def _(sy):
            sy.wait_ge(aux, A0 + sum(incs))
    ctx.cnt["aux"] = A0 + sum(incs)
    esC.close()
    ctx.p45_es0.close()



from concourse.bass_utils import run_bass_kernel_spmd

_BF = ml_dtypes.bfloat16
_DT = {F32: np.float32, BF16: _BF}

CONSTS_SPEC = [("eye", (128, 128), F32), ("eyeb", (128, 128), BF16),
               ("onescol", (1, 128), F32), ("ones128", (128, 1), F32)]

INTERNAL = {
    "absq": ((1, 4), F32), "sc": ((1, 4), F32),
    "tq1s": ((256, HID), BF16), "tq3s": ((256, HID), BF16), "tq2s": ((W2ROWS, D), BF16),
    "xfm": ((D, T), F32), "xnfm": ((D, T), F32), "xntm": ((T, D), BF16), "htm": ((T, D), BF16),
    "mlp": ((D, T), F32),
    "zf": ((DI, T), BF16), "xmf": ((DXB, T), BF16), "bmf": ((DXB, T), BF16),
    "cmf": ((DI, T), BF16), "dtr": ((64, T), F32),
    "yloc": ((DI, T), BF16), "eglob": ((64, T), F32), "sfin": ((DI, 64), F32), "elast": ((64, 1), F32),
    "sfm": ((D, T), F32), "yfh": ((DI, T), BF16),
    "x1fm": ((D, T), F32), "hsptm": ((T, INTER), BF16), "xn1tm": ((T, D), BF16),
    "qf": ((2048, T), BF16), "kf": ((2048, T), BF16), "vf": ((2048, T), BF16),
    "y1loc": ((2048, T), BF16), "kvfin": ((2048, 128), F32),
    "s1fm": ((D, T), F32), "x2fm": ((D, T), F32),
    "absg": ((8, 4), F32), "sgath": ((8 * DI, 64), F32), "egath": ((512, 1), F32),
    "kvgath": ((8 * 2048, 128), F32),
    "tq1": ((D, HID), BF16), "tq3": ((D, HID), BF16), "tq2": ((HID, D), BF16),
}
WSHAPES = {
    "wip": (D, DINP), "wop": (DI, D), "wfc1": (D, INTER), "wfc2": (INTER, D),
    "wg": (D, INTER), "wu": (D, INTER), "wd": (INTER, D),
    "wqkv": (D, 6144), "wgt": (D, D), "wo1": (D, D),
    "ln2w": (1, D), "ln2b": (1, D), "dtb": (64, 1), "alog": (64, 1),
    "dpc64": (64, 64), "sel64": (64, 64 * 64), "nw64": (64, 64),
    "causal": (128, 128), "causT": (128, 128), "qdec": (2048, 256), "kdec": (2048, 256), "ddecT": (4096, 256),
    "nw1cols": (128, 16), "fc1bc": (128, 64), "fc2bc": (128, 16),
}
PCSHAPES = {"xs": ((T, D), F32), "emask64": ((64, 8), F32), "emask128": ((128, 8), F32),
            "l1w": ((128, NH1 * 8), F32),
            "w1s": ((256, HID), F32), "w3s": ((256, HID), F32), "w2s": ((W2ROWS, D), F32)}


def emit_gather(ctx, pairs, wait=True):
    nc = ctx.nc
    cc = ctx.sem("cc")
    c0 = ctx.cnt["cc"]
    with nc.Block() as blk:
        @blk.gpsimd
        def _(g):
            n = 0
            for src, dst in pairs:
                g.collective_compute(
                    "AllGather", ALU.bypass,
                    replica_groups=[list(range(NCORES))],
                    ins=[src.ap().opt()],
                    outs=[dst.ap().opt()],
                ).then_inc(cc, 1)
                n += 1
            if wait:
                g.wait_ge(cc, c0 + n)
    ctx.cnt["cc"] = c0 + len(pairs)


def emit_gather_wait(ctx):
    nc = ctx.nc
    cc = ctx.sem("cc")
    tgt = ctx.cnt["cc"]
    with nc.Block() as blk:
        @blk.gpsimd
        def _(g):
            g.wait_ge(cc, tgt)


def kernel(**inputs):
    sh, per_core = host_prep(inputs)
    for i in range(NCORES):
        per_core[i]["w1s"] = np.ascontiguousarray(sh["w1t"][256 * i:256 * (i + 1)])
        per_core[i]["w3s"] = np.ascontiguousarray(sh["w3t"][256 * i:256 * (i + 1)])
        per_core[i]["w2s"] = np.ascontiguousarray(sh["w2t"][W2ROWS * i:W2ROWS * (i + 1)])

    nc = bass.Bass("TRN2", target_bir_lowering=False, detect_race_conditions=False)
    io = {}
    handles = {}
    ins_spec = list(CONSTS_SPEC)
    for name, shape in WSHAPES.items():
        ins_spec.append((name, shape, F32))
    for name, (shape, dt) in PCSHAPES.items():
        ins_spec.append((name, shape, dt))
    for name, shape, dt in ins_spec:
        io[name] = nc.dram_tensor(name, list(shape), dt, kind="ExternalInput").ap()
    for name, (shape, dt) in INTERNAL.items():
        h = nc.dram_tensor(name, list(shape), dt, kind="Internal")
        handles[name] = h
        io[name] = h.ap()
    ysh = nc.dram_tensor("ys", [T, D], F32, kind="ExternalOutput")
    io["ys"] = ysh.ap()

    ctx = Ctx(nc, io)
    phases = [
        ("setup", lambda: emit_setup(ctx)),
        ("qa", lambda: emit_qa(ctx)),
        ("cc1", lambda: emit_gather(ctx, [(handles["absq"], handles["absg"])])),
        ("qb", lambda: emit_qb(ctx)),
        ("p1", lambda: emit_p1(ctx)),
        ("cctq", lambda: emit_gather(ctx, [(handles["tq1s"], handles["tq1"]),
                                           (handles["tq3s"], handles["tq3"]),
                                           (handles["tq2s"], handles["tq2"])], wait=False)),
        ("p2", lambda: emit_p2(ctx)),
        ("p3", lambda: emit_p3(ctx)),
        ("p45", lambda: emit_p45(ctx, lambda c2: emit_gather(
            c2, [(handles["sfin"], handles["sgath"]), (handles["elast"], handles["egath"])]))),
        ("p5c", lambda: emit_p5c(ctx)),
        ("p6", lambda: emit_p6(ctx)),
        ("p7", lambda: emit_p7(ctx)),
        ("p8", lambda: emit_p8(ctx)),
        ("p9", lambda: emit_p9(ctx)),
        ("cc3", lambda: emit_gather(ctx, [(handles["kvfin"], handles["kvgath"])])),
        ("p10", lambda: emit_p10(ctx)),
        ("cctqw", lambda: emit_gather_wait(ctx)),
        ("p12", lambda: emit_p12(ctx)),
    ]
    for name, fn in phases:
        with nc.named_scope(name):
            fn()
    ctx.close()

    in_maps = []
    for c in range(NCORES):
        m = {}
        for name, shape, dt in ins_spec:
            a = per_core[c][name] if name in per_core[c] else sh[name]
            m[name] = np.ascontiguousarray(np.asarray(a, dtype=_DT[dt]))
        in_maps.append(m)
    res = run_bass_kernel_spmd(nc, in_maps, core_ids=list(range(NCORES)))
    out = np.concatenate([np.asarray(r["ys"], np.float32) for r in res.results], axis=0)
    return out[None]
